# version 09
# baseline (speedup 1.0000x reference)
import numpy as np
from collections import defaultdict
import contextlib

import concourse.bass as bass
import concourse.mybir as mybir
from concourse.bass_utils import run_bass_kernel_spmd

F32 = mybir.dt.float32
BF16 = mybir.dt.bfloat16
U32 = mybir.dt.uint32
AF = mybir.ActivationFunctionType
ALU = mybir.AluOpType

NCK = 8
H = 4096
DIN = 8192
DSH = DIN // NCK
NDT = DSH // 128
NST = 16
R = 256
E3 = 288
B = 4
L = 2048
TSL = L // NCK
NKT = H // 128
EPS_LN = 1e-5
EPS_RMS = 1e-6
SCALE = 2.0
TQ = 256
TQS = TQ + 1
NTQ = L // TQ


class Buf:
    __slots__ = ("name", "w", "r")

    def __init__(self, name):
        self.name = name
        self.w = None
        self.r = []


class KB:
    CSTREAMS = ("T", "V", "S", "G", "Q")

    def __init__(self):
        self.ops = []

    def op(self, eng, stream, fn, reads=(), writes=(), extra=()):
        idx = len(self.ops)
        waits = set(e for e in extra if e is not None)
        for b in reads:
            if b.w is not None:
                waits.add(b.w)
        for b in writes:
            if b.w is not None:
                waits.add(b.w)
            waits.update(b.r)
        for b in writes:
            b.w = idx
            b.r = []
        for b in reads:
            b.r.append(idx)
        amt = 16 if stream in ("QS", "QA", "QG") else 1
        self.ops.append([eng, stream, fn, waits, amt])
        return idx

    POOLS = {"QS": 24, "QG": 12, "QA": 4}
    STRICT = False

    def finalize(self):
        used = set()
        for _, _, _, waits, _ in self.ops:
            used.update(waits)
        val = {}
        cnt = defaultdict(int)
        pool_ctr = defaultdict(int)
        self.pre_wait = {}
        self.inc_flag = []
        for idx, (eng, stream, fn, waits, amt) in enumerate(self.ops):
            if stream in self.POOLS:
                n = self.POOLS[stream]
                c = pool_ctr[stream]
                pool_ctr[stream] += 1
                sem_name = f"{stream}{c % n}"
                u = c // n
                self.pre_wait[idx] = (sem_name, 16 * u)
                val[idx] = (sem_name, 16 * (u + 1))
                self.inc_flag.append(True)
                cnt[sem_name] = 16 * (u + 1)
            else:
                keep = self.STRICT or (stream not in self.CSTREAMS) or (idx in used)
                self.inc_flag.append(keep)
                if keep:
                    cnt[stream] += amt
                    val[idx] = (stream, cnt[stream])
        self.ev_val = val
        self.final_counts = dict(cnt)

    def emit(self, eng_key, handle, sems):
        waited = defaultdict(int)
        for idx, (eng, stream, fn, waits, amt) in enumerate(self.ops):
            if eng != eng_key:
                continue
            pw = self.pre_wait.get(idx)
            if pw is not None and pw[1] > 0 and waited[pw[0]] < pw[1]:
                handle.wait_ge(sems[pw[0]], pw[1])
                waited[pw[0]] = pw[1]
            for wev in sorted(waits):
                wsv = self.ev_val.get(wev)
                if wsv is None:
                    continue
                ws, wv = wsv
                if (not self.STRICT) and ws == eng and ws in self.CSTREAMS:
                    continue
                if waited[ws] >= wv:
                    continue
                handle.wait_ge(sems[ws], wv)
                waited[ws] = wv
            ins = fn(handle)
            if self.inc_flag[idx]:
                s, v = self.ev_val[idx]
                ins.then_inc(sems[s], amt)


def build_program():
    nc = bass.Bass("TRN2", target_bir_lowering=False, debug=False)

    def ein(name, shp, dt=F32):
        return nc.dram_tensor(name, shp, dt, kind="ExternalInput").ap()

    hs_prepT = ein("hs_prepT", [H, 1024])
    hs_resT = ein("hs_resT", [B, H, TSL])
    w1 = ein("w1", [16, 128, 4096])
    woTs = ein("woTs", [8, 128, 4096])
    wfT = ein("wfT", [32, 128, 4096])
    convw = ein("convw", [DSH, 4])
    convb = ein("convb", [DSH])
    dtbias = ein("dtbias", [DSH])
    dparam = ein("dparam", [DSH])
    alog = ein("alog", [DSH, NST])
    xpwT = ein("xpwT", [DSH, E3])
    dtpT = ein("dtpT", [R, DSH])
    dtlnw = ein("dtlnw", [R])
    blnw = ein("blnw", [NST])
    clnw = ein("clnw", [NST])
    fbi = ein("fb", [H])
    fAWrT = ein("fAWrT", [H, 36])
    fBmT = ein("fBmT", [32, H])
    out_ext = nc.dram_tensor("out", [B, H, TSL], F32, kind="ExternalOutput").ap()

    dr = lambda n, shp, dt: nc.dram_tensor(n, shp, dt).ap()
    ln_in = dr("ln_in", [H, 1024], BF16)
    ln_all = dr("ln_all", [NCK, H, 1024], BF16)
    w1t = dr("w1t", [16, 128, 4096], BF16)
    wos = dr("wos", [8, 128, 4096], BF16)
    wf2 = dr("wf2", [NKT, 128, 4096], BF16)
    arin = [dr(f"arin{b}", [E3, L], F32) for b in range(B)]
    arout = [dr(f"arout{b}", [E3, L], F32) for b in range(B)]
    yin = [dr(f"yin{b}", [DSH, L], BF16) for b in range(B)]
    pmix = [dr(f"pmix{b}", [NCK, H, TSL], BF16) for b in range(B)]
    rsout = [dr(f"rsout{b}", [H, TSL], BF16) for b in range(B)]
    xcsp = dr("xcsp", [B, NDT, 128, L], BF16)
    gsp = dr("gsp", [B, NDT, 128, L], BF16)
    dtlsp = dr("dtlsp", [B, NDT, 128, L], BF16)
    x1sp = dr("x1sp", [B, H, TSL], F32)

    kb = KB()
    st = contextlib.ExitStack()
    T = {}

    def sb(name, shp, dt):
        t = st.enter_context(nc.sbuf_tensor(name, shp, dt))
        T[name] = t
        return t

    def psm(name, shp):
        t = st.enter_context(nc.psum_tensor(name, shp, F32))
        T[name] = t
        return t

    with st:
        SLOT = [sb(f"sl{i}", [128, 8192], BF16) for i in range(5)]
        dAp = sb("dAp", [128, NST * TQS], BF16)
        wl1 = sb("wl1", [128, NST * TQS], BF16)
        hl2 = sb("hl2", [128, NST * TQS], BF16)
        dApB = sb("dApB", [128, NST * TQS], BF16)
        wl1B = sb("wl1B", [128, NST * TQS], BF16)
        hl2B = hl2
        bcB = sb("bcB", [128, NST * TQ], BF16)
        bcC = sb("bcC", [128, NST * TQ], BF16)
        dtl_t = sb("dtl_t", [128, 2 * TQ], BF16)
        xc_t = sb("xc_t", [128, 2 * TQ], BF16)
        g_t = sb("g_t", [128, 2 * TQ], BF16)
        u_t = sb("u_t", [128, TQ], BF16)
        ys_t = sb("ys_t", [128, TQ], F32)
        t2_t = sb("t2_t", [128, TQ], BF16)
        ygd_t = sb("ygd_t", [128, 2 * TQ], BF16)
        hcar = sb("hcar", [128, NDT * NST], BF16)
        rBq = sb("rBq", [128, TQ], BF16)
        rCq = sb("rCq", [128, TQ], BF16)
        xconv = sb("xconv", [128, 2 * (TQ + 3)], BF16)
        tails = sb("tails", [128, NDT * 3], BF16)
        cacc = sb("cacc", [128, 2 * TQ], F32)
        xcw = sb("xcw", [128, TQ], BF16)
        gev = sb("gev", [128, 2 * 512], BF16)
        xcr = sb("xcr", [128, 512], BF16)
        spsc = sb("spsc", [96, 512], F32)
        rowpack = sb("rowpack", [65, 2 * L + 128 + 1024 + 128], BF16)
        bcrows = rowpack[0:16, :]
        spd = sb("spd", [128, 1024], BF16)
        c_sum = sb("c_sum", [1, 512], F32)
        c_ssq = sb("c_ssq", [1, 512], F32)
        c_m = sb("c_m", [1, 512], F32)
        s_scr = sb("s_scr", [1, 512], F32)
        rb_rstd = rowpack[0:1, 4224:5248]
        rb_mrstd = rowpack[32:33, 4224:5248]
        ones32 = rowpack[32:33, 5248:5376]
        rb_rb = rowpack[64:65, 0:L]
        rb_rc = rowpack[64:65, L:2 * L]
        ones64 = rowpack[64:65, 2 * L:2 * L + 128]
        rb_tmp = sb("rb_tmp", [1, TSL], BF16)
        gl4 = sb("gl4", [4, 2 * TSL], F32)
        gb4 = sb("gb4", [4, 2 * TSL], BF16)
        bct4 = sb("bct4", [4, TSL], F32)
        g32 = sb("g32", [32, TSL], BF16)
        prs = sb("prs", [32, TSL], BF16)
        scr36 = sb("scr36", [36, TSL], BF16)
        negA = sb("negA", [128, NDT * NST], F32)
        cw_sb = sb("cw_sb", [128, NDT * 4], F32)
        cb_sb = sb("cb_sb", [128, NDT], F32)
        ndtb = sb("ndtb", [128, NDT], F32)
        d_sb = sb("d_sb", [128, NDT], F32)
        dtlnw_sb = sb("dtlnw_sb", [128, 2], F32)
        bclnw = sb("bclnw", [16, 2], F32)
        dtp_sb = sb("dtp_sb", [128, 16 * 128], BF16)
        xpw_sb = sb("xpw_sb", [128, NDT * E3], BF16)
        awr_sb = sb("awr_sb", [128, NKT * 36], BF16)
        bmt_t = sb("bmt_t", [32, 2 * 128], BF16)
        aug2r = sb("aug2r", [2, TSL], BF16)
        colsA = sb("colsA", [1, 36], BF16)
        ones_c = sb("ones_c", [128, 1], BF16)
        ones_r = sb("ones_r", [1, 128], BF16)
        ones16 = sb("ones16", [16, 128], BF16)

        srow = sb("srow", [1, TQ], BF16)
        sel = sb("sel", [4, 32], BF16)
        sel2 = sb("sel2", [36, 4], BF16)
        x0t = sb("x0t", [128, TSL], F32)
        x1t = sb("x1t", [128, 2 * TSL], F32)
        bct = sb("bct", [128, 3 * TSL], F32)
        aug2 = sb("aug2", [2, H], BF16)

        psA = psm("psA", [128, 512])
        psB = psm("psB", [128, 512])
        psE = psm("psE", [96, 512])
        psF = psm("psF", [128, 512])
        psG = psm("psG", [128, 512])
        psH = psm("psH", [128, 1024])
        psI = psm("psI", [128, 512])

        bufs = {}

        def Bf(n):
            if n not in bufs:
                bufs[n] = Buf(n)
            return bufs[n]

        for n in list(T):
            Bf(n)
        for n in ("ln_in", "ln_all", "w1t", "wos", "wf2", "xcsp", "gsp",
                  "dtlsp", "x1sp", "out", "sl4a", "sl4b",
                  "dtl0", "dtl1", "xct0", "xct1", "gt0", "gt1", "ygd0", "ygd1",
                  "x0t0", "x0t1", "x1t0", "x1t1", "ca0", "ca1", "gev0", "gev1",
                  "xcr0", "xcr1"):
            Bf(n)
        for b in range(B):
            for n in ("arin", "arout", "yin", "pmix", "rsout"):
                Bf(f"{n}{b}")
        SB = [Bf(f"sl{i}") for i in range(5)]
        SB4 = [Bf("sl4a"), Bf("sl4b")]
        DTLB = [Bf("dtl0"), Bf("dtl1")]
        XCTB = [Bf("xct0"), Bf("xct1")]
        GTB = [Bf("gt0"), Bf("gt1")]
        YGDB = [Bf("ygd0"), Bf("ygd1")]
        X0B = [Bf("x0t0"), Bf("x0t1")]
        X1B = [Bf("x1t0"), Bf("x1t1")]
        CAB = [Bf("ca0"), Bf("ca1")]
        GEVB = [Bf("gev0"), Bf("gev1")]
        XCRB = [Bf("xcr0"), Bf("xcr1")]
        spcB = wl1[0:16, 0:L]
        spcC = wl1[0:16, L:2 * L]

        kb.op("V", "V", lambda v: v.memset(ones_c[:, :], 1.0), writes=[Bf("ones_c")])
        kb.op("V", "V", lambda v: v.memset(ones_r[:, :], 1.0), writes=[Bf("ones_r")])
        kb.op("V", "V", lambda v: v.memset(ones64, 1.0), writes=[Bf("ones64")])
        kb.op("V", "V", lambda v: v.memset(ones32, 1.0), writes=[Bf("ones32")])
        kb.op("V", "V", lambda v: v.memset(ones16[:, :], 1.0), writes=[Bf("ones16")])
        kb.op("V", "V", lambda v: v.memset(tails[:, :], 0.0), writes=[Bf("tails")])
        kb.op("V", "V", lambda v: v.memset(sel[:, :], 0.0), writes=[Bf("sel")])
        kb.op("V", "V", lambda v: v.memset(sel2[:, :], 0.0), writes=[Bf("sel2")])
        kb.op("V", "V", lambda v: v.memset(srow[:, :], 1.0), writes=[Bf("srow")])
        for e in range(4):
            kb.op("G", "QG", lambda g, e=e: g.dma_start(
                out=sel[e:e + 1, e * 8:(e + 1) * 8].rearrange("o f -> o f") if False
                else sel[e:e + 1, e * 8:(e + 1) * 8],
                in_=srow[0:1, 0:8]), reads=[Bf("srow")], writes=[Bf("sel")])
            kb.op("G", "QG", lambda g, e=e: g.dma_start(
                out=sel2[32 + e:33 + e, e:e + 1], in_=srow[0:1, 0:1]),
                reads=[Bf("srow")], writes=[Bf("sel2")])
        kb.op("V", "V", lambda v: v.tensor_scalar_mul(sel[:, :], sel[:, :], SCALE),
              reads=[Bf("sel")], writes=[Bf("sel")])

        for dt in range(NDT):
            kb.op("G", "QG", lambda g, dt=dt: g.dma_start(
                out=negA[:, dt * NST:(dt + 1) * NST],
                in_=alog[dt * 128:(dt + 1) * 128, :]), writes=[Bf("negA")])
            kb.op("G", "QG", lambda g, dt=dt: g.dma_start(
                out=cw_sb[:, dt * 4:(dt + 1) * 4],
                in_=convw[dt * 128:(dt + 1) * 128, :]), writes=[Bf("cw_sb")])
            kb.op("G", "QG", lambda g, dt=dt: g.dma_start(
                out=cb_sb[:, dt:dt + 1],
                in_=convb[dt * 128:(dt + 1) * 128].unsqueeze(1)), writes=[Bf("cb_sb")])
            kb.op("G", "QG", lambda g, dt=dt: g.dma_start(
                out=ndtb[:, dt:dt + 1],
                in_=dtbias[dt * 128:(dt + 1) * 128].unsqueeze(1)), writes=[Bf("ndtb")])
            kb.op("G", "QG", lambda g, dt=dt: g.dma_start(
                out=d_sb[:, dt:dt + 1],
                in_=dparam[dt * 128:(dt + 1) * 128].unsqueeze(1)), writes=[Bf("d_sb")])
            for ec in range(3):
                kb.op("G", "QG", lambda g, dt=dt, ec=ec: g.dma_start(
                    out=xpw_sb[:, dt * E3 + ec * 96:dt * E3 + (ec + 1) * 96],
                    in_=xpwT[dt * 128:(dt + 1) * 128, ec * 96:(ec + 1) * 96]),
                    writes=[Bf("xpw_sb")])
        kb.op("S", "S", lambda s: s.activation(negA[:, :], negA[:, :], AF.Exp),
              reads=[Bf("negA")], writes=[Bf("negA")])
        kb.op("V", "V", lambda v: v.tensor_scalar_mul(ndtb[:, :], ndtb[:, :], -1.0),
              reads=[Bf("ndtb")], writes=[Bf("ndtb")])
        for c in range(2):
            kb.op("G", "QG", lambda g, c=c: g.dma_start(
                out=dtlnw_sb[:, c:c + 1],
                in_=dtlnw[c * 128:(c + 1) * 128].unsqueeze(1)),
                writes=[Bf("dtlnw_sb")])
        kb.op("G", "QG", lambda g: g.dma_start(out=bclnw[:, 0:1],
                                               in_=blnw.unsqueeze(1)),
              writes=[Bf("bclnw")])
        kb.op("G", "QG", lambda g: g.dma_start(out=bclnw[:, 1:2],
                                               in_=clnw.unsqueeze(1)),
              writes=[Bf("bclnw")])
        for rt in range(2):
            for dt in range(NDT):
                j = rt * NDT + dt
                kb.op("G", "QG", lambda g, rt=rt, dt=dt, j=j: g.dma_start(
                    out=dtp_sb[:, j * 128:(j + 1) * 128],
                    in_=dtpT[rt * 128:(rt + 1) * 128, dt * 128:(dt + 1) * 128]),
                    writes=[Bf("dtp_sb")])
        for it in range(NKT):
            kb.op("G", "QG", lambda g, it=it: g.dma_start(
                out=awr_sb[:, it * 36:(it + 1) * 36],
                in_=fAWrT[it * 128:(it + 1) * 128, :]), writes=[Bf("awr_sb")])
        kb.op("G", "QG", lambda g: g.dma_start(out=aug2[1:2, :], in_=fbi.unsqueeze(0)),
              writes=[Bf("aug2")])

        def ln_rowmath(width, off=0):
            kb.op("V", "V", lambda v: v.tensor_scalar_mul(
                c_m[0:1, 0:width], c_sum[0:1, 0:width], 1.0 / H),
                reads=[Bf("c_sum")], writes=[Bf("c_m")])
            kb.op("S", "S", lambda s: s.activation(
                c_sum[0:1, 0:width], c_m[0:1, 0:width], AF.Square),
                reads=[Bf("c_m")], writes=[Bf("c_sum")])
            kb.op("V", "V", lambda v: v.scalar_tensor_tensor(
                c_ssq[0:1, 0:width], c_ssq[0:1, 0:width], 1.0 / H,
                c_sum[0:1, 0:width], ALU.mult, ALU.subtract),
                reads=[Bf("c_sum")], writes=[Bf("c_ssq")])
            kb.op("V", "V", lambda v: v.tensor_scalar_add(
                c_ssq[0:1, 0:width], c_ssq[0:1, 0:width], EPS_LN),
                reads=[Bf("c_ssq")], writes=[Bf("c_ssq")])
            kb.op("S", "S", lambda s: s.activation(
                c_ssq[0:1, 0:width], c_ssq[0:1, 0:width], AF.Sqrt),
                reads=[Bf("c_ssq")], writes=[Bf("c_ssq")])
            kb.op("V", "V", lambda v: v.reciprocal(
                c_sum[0:1, 0:width], c_ssq[0:1, 0:width]),
                reads=[Bf("c_ssq")], writes=[Bf("c_sum")])
            kb.op("V", "V", lambda v: v.tensor_copy(
                rb_rstd[0:1, off:off + width], c_sum[0:1, 0:width]),
                reads=[Bf("c_sum")], writes=[Bf("rb_rstd")])
            kb.op("V", "V", lambda v: v.tensor_mul(
                c_m[0:1, 0:width], c_m[0:1, 0:width], c_sum[0:1, 0:width]),
                reads=[Bf("c_sum")], writes=[Bf("c_m")])
            kb.op("V", "V", lambda v: v.tensor_copy(
                rb_mrstd[0:1, off:off + width], c_m[0:1, 0:width]),
                reads=[Bf("c_m")], writes=[Bf("rb_mrstd")])

        for it in range(NKT):
            s, o = SLOT[it // 8], (it % 8) * 1024
            kb.op("G", "QG", lambda g, it=it, s=s, o=o: g.dma_start(
                out=s[:, o:o + 1024], in_=hs_prepT[it * 128:(it + 1) * 128, :]),
                writes=[SB[it // 8]])
        for it in range(NKT):
            s, o = SLOT[it // 8], (it % 8) * 1024
            for ch in range(2):
                kb.op("T", "T", lambda t, s=s, o=o, it=it, ch=ch: t.matmul(
                    psH[0:1, ch * 512:(ch + 1) * 512], ones_c[:, :],
                    s[:, o + ch * 512:o + (ch + 1) * 512],
                    start=(it == 0), stop=(it == NKT - 1)),
                    reads=[SB[it // 8], Bf("ones_c")], writes=[Bf("psH")])
        kb.op("S", "S", lambda s: s.copy(wl1.bitcast(F32)[0:1, 0:1024],
                                         psH[0:1, :]),
              reads=[Bf("psH")], writes=[Bf("wl1")])
        for it in range(NKT):
            s, o = SLOT[it // 8], (it % 8) * 1024
            q, qo = SLOT[4], (it % 4) * 1024
            kb.op("S", "S", lambda sc, s=s, o=o, q=q, qo=qo: sc.activation(
                q[:, qo:qo + 1024], s[:, o:o + 1024], AF.Square),
                reads=[SB[it // 8]], writes=[SB[4]])
            for ch in range(2):
                kb.op("T", "T", lambda t, q=q, qo=qo, it=it, ch=ch: t.matmul(
                    psH[0:1, ch * 512:(ch + 1) * 512], ones_c[:, :],
                    q[:, qo + ch * 512:qo + (ch + 1) * 512],
                    start=(it == 0), stop=(it == NKT - 1)),
                    reads=[SB[4], Bf("ones_c")], writes=[Bf("psH")])
        for ch in range(2):
            kb.op("S", "S", lambda s, ch=ch: s.copy(
                c_sum[:, :], wl1.bitcast(F32)[0:1, ch * 512:(ch + 1) * 512]),
                reads=[Bf("wl1")], writes=[Bf("c_sum")])
            kb.op("S", "S", lambda s, ch=ch: s.copy(
                c_ssq[:, :], psH[0:1, ch * 512:(ch + 1) * 512]),
                reads=[Bf("psH")], writes=[Bf("c_ssq")])
            ln_rowmath(512, off=ch * 512)
        for ch in range(2):
            kb.op("T", "T", lambda t, ch=ch: t.matmul(
                psG[:, :], ones_r[:, :], rb_rstd[0:1, ch * 512:(ch + 1) * 512],
                start=True, stop=True),
                reads=[Bf("rb_rstd"), Bf("ones_r")], writes=[Bf("psG")])
            kb.op("T", "T", lambda t, ch=ch: t.matmul(
                psF[:, :], ones32, rb_mrstd[0:1, ch * 512:(ch + 1) * 512],
                start=True, stop=True),
                reads=[Bf("rb_mrstd"), Bf("ones32")], writes=[Bf("psF")])
            for it in range(NKT):
                s, o = SLOT[it // 8], (it % 8) * 1024 + ch * 512
                q, qo = SLOT[4], 4096 + (it % 4) * 1024 + ch * 512
                kb.op("V", "V", lambda v, s=s, o=o, q=q, qo=qo: v.tensor_tensor(
                    q[:, qo:qo + 512], s[:, o:o + 512], psG[:, :], ALU.mult),
                    reads=[SB[it // 8], Bf("psG")], writes=[SB[4]])
                kb.op("V", "V", lambda v, q=q, qo=qo: v.tensor_tensor(
                    q[:, qo:qo + 512], q[:, qo:qo + 512], psF[:, :], ALU.subtract),
                    reads=[Bf("psF")], writes=[SB[4]])
                kb.op("Q", "QS", lambda qq, it=it, q=q, qo=qo, ch=ch: qq.dma_start(
                    out=ln_in[it * 128:(it + 1) * 128, ch * 512:(ch + 1) * 512],
                    in_=q[:, qo:qo + 512]),
                    reads=[SB[4]], writes=[Bf("ln_in")])
        fl = kb.op("Q", "Q", lambda q: q.nop(), reads=[Bf("ln_in")])
        kb.op("G", "CCL", lambda g: g.collective_compute(
            "AllGather", ALU.bypass, replica_groups=[list(range(NCK))],
            ins=[ln_in], outs=[ln_all]),
            reads=[Bf("ln_in")], writes=[Bf("ln_all")], extra=[fl])

        for kt in range(16):
            s = SLOT[kt % 2]
            kb.op("G", "QG", lambda g, kt=kt, s=s: g.dma_start(
                out=s[:, 0:4096], in_=w1[kt]), writes=[SB[kt % 2]])
            kb.op("Q", "QS", lambda q, kt=kt, s=s: q.dma_start(
                out=w1t[kt], in_=s[:, 0:4096]),
                reads=[SB[kt % 2]], writes=[Bf("w1t")])
        for dt in range(NDT):
            s = SLOT[dt % 2]
            kb.op("G", "QG", lambda g, dt=dt, s=s: g.dma_start(
                out=s[:, 0:H], in_=woTs[dt]), writes=[SB[dt % 2]])
            kb.op("Q", "QS", lambda q, dt=dt, s=s: q.dma_start(
                out=wos[dt], in_=s[:, 0:H]),
                reads=[SB[dt % 2]], writes=[Bf("wos")])
        kb.op("V", "V", lambda v: v.memset(aug2[0:1, :], 0.0), writes=[Bf("aug2")])
        for it in range(NKT):
            s = SLOT[2 + it % 2]
            kb.op("G", "QG", lambda g, it=it, s=s: g.dma_start(
                out=s[:, 0:H], in_=wfT[it]), writes=[SB[2 + it % 2]])
            kb.op("Q", "QS", lambda q, it=it, s=s: q.dma_start(
                out=wf2[it], in_=s[:, 0:H]),
                reads=[SB[2 + it % 2]], writes=[Bf("wf2")])
            for ch in range(NKT):
                kb.op("T", "T", lambda t, s=s, ch=ch, it=it: t.matmul(
                    psH[0:1, 0:128], ones_c[:, :], s[:, ch * 128:(ch + 1) * 128],
                    start=(ch == 0), stop=(ch == NKT - 1)),
                    reads=[SB[2 + it % 2], Bf("ones_c")], writes=[Bf("psH")])
            kb.op("V", "V", lambda v, it=it: v.tensor_copy(
                aug2[0:1, it * 128:(it + 1) * 128], psH[0:1, 0:128]),
                reads=[Bf("psH")], writes=[Bf("aug2")])
        for it in range(NKT):
            kb.op("T", "T", lambda t, it=it: t.matmul(
                psH[0:1, 0:36], ones_c[:, :], awr_sb[:, it * 36:(it + 1) * 36],
                start=(it == 0), stop=(it == NKT - 1)),
                reads=[Bf("awr_sb"), Bf("ones_c")], writes=[Bf("psH")])
        kb.op("V", "V", lambda v: v.tensor_copy(colsA[:, :], psH[0:1, 0:36]),
              reads=[Bf("psH")], writes=[Bf("colsA")])

        def p1_chunk(b, part):
            th, cth = divmod(part, 2)
            if cth == 0:
                for kt in range(NKT):
                    s, o = SLOT[kt // 8], (kt % 8) * 1024
                    chunk = 2 * b + th
                    kb.op("Q", "QS", lambda q, kt=kt, s=s, o=o, chunk=chunk:
                          q.dma_start(
                        out=s[:, o:o + 1024],
                        in_=ln_all[chunk, kt * 128:(kt + 1) * 128, :]),
                        reads=[Bf("ln_all")], writes=[SB[kt // 8]])
            for ci in range(8):
                ct = cth * 8 + ci
                pw = ct % 2
                ws, wof = SLOT[4], pw * 4096
                wb = SB4[pw]
                kb.op("Q", "QS", lambda q, ct=ct, ws=ws, wof=wof: q.dma_start(
                    out=ws[:, wof:wof + 4096], in_=w1t[ct]),
                    reads=[Bf("w1t")], writes=[wb])
                for half in range(2):
                    pst, pB = ((psA, Bf("psA")) if half == 0 else (psB, Bf("psB")))
                    for kt in range(NKT):
                        s, o = SLOT[kt // 8], (kt % 8) * 1024
                        kb.op("T", "T", lambda t, s=s, o=o, ws=ws, wof=wof, kt=kt,
                              half=half, pst=pst: t.matmul(
                            pst[:, :], ws[:, wof + kt * 128:wof + (kt + 1) * 128],
                            s[:, o + half * 512:o + (half + 1) * 512],
                            start=(kt == 0), stop=(kt == NKT - 1)),
                            reads=[SB[kt // 8], wb], writes=[pB])
                    if ct < 8:
                        dtc = ct
                        for tq in range(2):
                            gtq = th * 4 + half * 2 + tq
                            p = gtq % 2
                            xo = p * (TQ + 3)
                            cB = Bf("xconv")
                            kb.op("S", "S", lambda s, xo=xo, pst=pst, tq=tq: s.copy(
                                xconv[:, xo + 3:xo + 3 + TQ],
                                pst[:, tq * TQ:(tq + 1) * TQ]),
                                reads=[pB], writes=[cB])
                            if gtq == 0:
                                kb.op("V", "V", lambda v, xo=xo: v.memset(
                                    xconv[:, xo:xo + 3], 0.0), writes=[cB])
                            else:
                                kb.op("V", "V", lambda v, xo=xo, dtc=dtc:
                                      v.tensor_copy(
                                    xconv[:, xo:xo + 3],
                                    tails[:, dtc * 3:dtc * 3 + 3]),
                                    reads=[Bf("tails")], writes=[cB])
                            kb.op("V", "V", lambda v, xo=xo, dtc=dtc: v.tensor_copy(
                                tails[:, dtc * 3:dtc * 3 + 3],
                                xconv[:, xo + TQ:xo + TQ + 3]),
                                reads=[cB], writes=[Bf("tails")])
                            kb.op("V", "V", lambda v, xo=xo, dtc=dtc: v.tensor_scalar(
                                cacc[:, 0:TQ], xconv[:, xo:xo + TQ],
                                cw_sb[:, dtc * 4:dtc * 4 + 1],
                                cb_sb[:, dtc:dtc + 1], ALU.mult, ALU.add),
                                reads=[cB, Bf("cw_sb"), Bf("cb_sb")],
                                writes=[CAB[0]])
                            for k in (1, 2, 3):
                                src, dst = ((k + 1) % 2) * TQ, (k % 2) * TQ
                                kb.op("V", "V", lambda v, xo=xo, dtc=dtc, k=k,
                                      src=src, dst=dst: v.scalar_tensor_tensor(
                                    cacc[:, dst:dst + TQ],
                                    xconv[:, xo + k:xo + k + TQ],
                                    cw_sb[:, dtc * 4 + k:dtc * 4 + k + 1],
                                    cacc[:, src:src + TQ], ALU.mult, ALU.add),
                                    reads=[cB, Bf("cw_sb"), CAB[(k + 1) % 2]],
                                    writes=[CAB[k % 2]])
                            kb.op("S", "S", lambda s: s.activation(
                                xcw[:, :], cacc[:, TQ:2 * TQ], AF.Silu),
                                reads=[CAB[1]], writes=[Bf("xcw")])
                            kb.op("Q", "QS", lambda q, dtc=dtc, b=b, gtq=gtq:
                                  q.dma_start(
                                out=xcsp[b, dtc, :, gtq * TQ:(gtq + 1) * TQ],
                                in_=xcw[:, :]),
                                reads=[Bf("xcw")], writes=[Bf("xcsp")])
                    else:
                        dtc = ct - 8
                        gp = half
                        go = gp * 512
                        kb.op("S", "S", lambda s, go=go, pst=pst: s.activation(
                            gev[:, go:go + 512], pst[:, :], AF.Silu),
                            reads=[pB], writes=[GEVB[gp]])
                        kb.op("Q", "QS", lambda q, go=go, gp=gp, dtc=dtc, b=b,
                              th=th, half=half: q.dma_start(
                            out=gsp[b, dtc, :, th * 1024 + half * 512:
                                    th * 1024 + (half + 1) * 512],
                            in_=gev[:, go:go + 512]),
                            reads=[GEVB[gp]], writes=[Bf("gsp")])
            if cth == 1:
                for ec in range(3):
                    for hh in range(2):
                        for dt in range(NDT):
                            kb.op("Q", "QS", lambda q, dt=dt, b=b,
                                  th=th, hh=hh: q.dma_start(
                                out=xcr[:, :],
                                in_=xcsp[b, dt, :, th * 1024 + hh * 512:
                                         th * 1024 + (hh + 1) * 512]),
                                reads=[Bf("xcsp")], writes=[XCRB[0]])
                            kb.op("T", "T", lambda t, ec=ec, dt=dt: t.matmul(
                                psE[:, :],
                                xpw_sb[:, dt * E3 + ec * 96:dt * E3 + (ec + 1) * 96],
                                xcr[:, :],
                                start=(dt == 0), stop=(dt == NDT - 1)),
                                reads=[Bf("xpw_sb"), XCRB[0]], writes=[Bf("psE")])
                        kb.op("S", "S", lambda s: s.copy(spsc[:, :], psE[:, :]),
                              reads=[Bf("psE")], writes=[Bf("spsc")])
                        kb.op("Q", "QS", lambda q, ec=ec, hh=hh, b=b, th=th:
                              q.dma_start(
                            out=arin[b][ec * 96:(ec + 1) * 96,
                                        th * 1024 + hh * 512:
                                        th * 1024 + (hh + 1) * 512],
                            in_=spsc[:, :]),
                            reads=[Bf("spsc")], writes=[Bf(f"arin{b}")])

        def p1_ar(b):
            fl = kb.op("Q", "Q", lambda q: q.nop(), reads=[Bf(f"arin{b}")])
            kb.op("G", f"CCA{b}", lambda g, b=b: g.collective_compute(
                "AllReduce", ALU.add, replica_groups=[list(range(NCK))],
                ins=[arin[b]], outs=[arout[b]]),
                reads=[Bf(f"arin{b}")], writes=[Bf(f"arout{b}")], extra=[fl])

        def rows_dt(b):
            kb.op("V", "V", lambda v: v.memset(hcar[:, :], 0.0),
                  writes=[Bf(f"hcar{d}") for d in range(NDT)])
            kb.op("G", "QG", lambda g, b=b: g.dma_start(
                out=spcB, in_=arout[b][256:272, :]),
                reads=[Bf(f"arout{b}")], writes=[Bf("wl1")])
            kb.op("G", "QG", lambda g, b=b: g.dma_start(
                out=spcC, in_=arout[b][272:288, :]),
                reads=[Bf(f"arout{b}")], writes=[Bf("wl1")])
            kb.op("S", "S", lambda s: s.activation(
                dAp[0:16, 0:L], spcB, AF.Square),
                reads=[Bf("wl1")], writes=[Bf("dAp")])
            kb.op("S", "S", lambda s: s.activation(
                dAp[0:16, L:2 * L], spcC, AF.Square),
                reads=[Bf("wl1")], writes=[Bf("dAp")])
            for (co, rbB_) in ((0, Bf("rb_rb")), (L, Bf("rb_rc"))):
                rb = rb_rb if co == 0 else rb_rc
                for ch in range(4):
                    kb.op("T", "T", lambda t, co=co, ch=ch: t.matmul(
                        psH[0:1, 0:512], ones_c[0:16, :],
                        dAp[0:16, co + ch * 512:co + (ch + 1) * 512],
                        start=True, stop=True),
                        reads=[Bf("dAp"), Bf("ones_c")], writes=[Bf("psH")])
                    kb.op("V", "V", lambda v: v.tensor_scalar(
                        s_scr[:, :], psH[0:1, 0:512], 1.0 / NST, EPS_RMS,
                        ALU.mult, ALU.add),
                        reads=[Bf("psH")], writes=[Bf("s_scr")])
                    kb.op("S", "S", lambda s: s.activation(
                        s_scr[:, :], s_scr[:, :], AF.Sqrt),
                        reads=[Bf("s_scr")], writes=[Bf("s_scr")])
                    kb.op("V", "V", lambda v: v.reciprocal(s_scr[:, :], s_scr[:, :]),
                          reads=[Bf("s_scr")], writes=[Bf("s_scr")])
                    kb.op("V", "V", lambda v, rb=rb, ch=ch: v.tensor_copy(
                        rb[0:1, ch * 512:(ch + 1) * 512], s_scr[:, :]),
                        reads=[Bf("s_scr")], writes=[rbB_])
            kb.op("V", "V", lambda v: v.tensor_scalar(
                bcrows[:, 0:L], spcB, bclnw[:, 0:1], -1.0, ALU.mult, ALU.mult),
                reads=[Bf("wl1"), Bf("bclnw")], writes=[Bf("bcrows")])
            kb.op("V", "V", lambda v: v.tensor_scalar(
                bcrows[:, L:2 * L], spcC, bclnw[:, 1:2], None, ALU.mult),
                reads=[Bf("wl1"), Bf("bclnw")], writes=[Bf("bcrows")])
            for kt in range(2):
                for hh in range(2):
                    kb.op("G", "QG", lambda g, kt=kt, hh=hh, b=b: g.dma_start(
                        out=spd[:, :],
                        in_=arout[b][kt * 128:(kt + 1) * 128,
                                     hh * 1024:(hh + 1) * 1024]),
                        reads=[Bf(f"arout{b}")], writes=[Bf("spd")])
                    kb.op("S", "S", lambda s, kt=kt, hh=hh: s.activation(
                        wl1[:, kt * L + hh * 1024:kt * L + (hh + 1) * 1024],
                        spd[:, :], AF.Square),
                        reads=[Bf("spd")], writes=[Bf("wl1")])
            for ch in range(4):
                for kt in range(2):
                    kb.op("T", "T", lambda t, ch=ch, kt=kt: t.matmul(
                        psH[0:1, 0:512], ones_c[:, :],
                        wl1[:, kt * L + ch * 512:kt * L + (ch + 1) * 512],
                        start=(kt == 0), stop=(kt == 1)),
                        reads=[Bf("wl1"), Bf("ones_c")], writes=[Bf("psH")])
                kb.op("V", "V", lambda v: v.tensor_scalar(
                    s_scr[:, :], psH[0:1, 0:512], 1.0 / R, EPS_RMS,
                    ALU.mult, ALU.add),
                    reads=[Bf("psH")], writes=[Bf("s_scr")])
                kb.op("S", "S", lambda s: s.activation(
                    s_scr[:, :], s_scr[:, :], AF.Sqrt),
                    reads=[Bf("s_scr")], writes=[Bf("s_scr")])
                kb.op("V", "V", lambda v: v.reciprocal(s_scr[:, :], s_scr[:, :]),
                      reads=[Bf("s_scr")], writes=[Bf("s_scr")])
                kb.op("V", "V", lambda v, ch=ch: v.tensor_copy(
                    dAp[0:1, ch * 512:(ch + 1) * 512], s_scr[:, :]),
                    reads=[Bf("s_scr")], writes=[Bf("dAp")])
            for kt in range(2):
                for hh in range(2):
                    kb.op("G", "QG", lambda g, kt=kt, hh=hh, b=b: g.dma_start(
                        out=spd[:, :],
                        in_=arout[b][kt * 128:(kt + 1) * 128,
                                     hh * 1024:(hh + 1) * 1024]),
                        reads=[Bf(f"arout{b}")], writes=[Bf("spd")])
                    for c2 in range(2):
                        ch = hh * 2 + c2
                        kb.op("T", "T", lambda t, ch=ch: t.matmul(
                            psG[:, :], ones_r[:, :],
                            dAp[0:1, ch * 512:(ch + 1) * 512],
                            start=True, stop=True),
                            reads=[Bf("dAp"), Bf("ones_r")], writes=[Bf("psG")])
                        kb.op("V", "V", lambda v, kt=kt, ch=ch, c2=c2:
                              v.scalar_tensor_tensor(
                            hl2[:, kt * L + ch * 512:kt * L + (ch + 1) * 512],
                            spd[:, c2 * 512:(c2 + 1) * 512],
                            dtlnw_sb[:, kt:kt + 1], psG[:, :], ALU.mult, ALU.mult),
                            reads=[Bf("spd"), Bf("dtlnw_sb"), Bf("psG")],
                            writes=[Bf("hl2")])
            for dt in range(NDT):
                for tc in range(4):
                    for rt in range(2):
                        j = rt * NDT + dt
                        kb.op("T", "T", lambda t, j=j, rt=rt, tc=tc: t.matmul(
                            psF[:, :], dtp_sb[:, j * 128:(j + 1) * 128],
                            hl2[:, rt * L + tc * 512:rt * L + (tc + 1) * 512],
                            start=(rt == 0), stop=(rt == 1)),
                            reads=[Bf("dtp_sb"), Bf("hl2")], writes=[Bf("psF")])
                    kb.op("S", "S", lambda s, dt=dt: s.activation(
                        bct[:, 0:512], psF[:, :], AF.Sigmoid,
                        scale=-1.0, bias=ndtb[:, dt:dt + 1]),
                        reads=[Bf("psF"), Bf("ndtb")], writes=[Bf("bct")])
                    kb.op("S", "S", lambda s: s.activation(
                        gev[:, 0:512], bct[:, 0:512], AF.Ln),
                        reads=[Bf("bct")], writes=[GEVB[0]])
                    kb.op("Q", "QS", lambda q, b=b, dt=dt, tc=tc: q.dma_start(
                        out=dtlsp[b, dt, :, tc * 512:(tc + 1) * 512],
                        in_=gev[:, 0:512]),
                        reads=[GEVB[0]], writes=[Bf("dtlsp")])
            for n in range(NST):
                kb.op("V", "V", lambda v, n=n: v.memset(
                    dAp[:, n * TQS:n * TQS + 1], 0.0), writes=[Bf("dAp")])
            if b == 0:
                for n in range(NST):
                    kb.op("V", "V", lambda v, n=n: v.memset(
                        dApB[:, n * TQS:n * TQS + 1], 0.0),
                        writes=[Bf("dApB")])

        def bc_tq(b, tq):
            t0 = tq * TQ
            for (dst, dB, co) in ((bcB, Bf("bcB"), 0), (bcC, Bf("bcC"), L)):
                kb.op("V", "V", lambda v: v.memset(
                    hl2[0:16, 0:NST * TQ], 0.0), writes=[Bf("hl2")])
                for n in range(NST):
                    kb.op("Q", "QS", lambda q, n=n, co=co, t0=t0: q.dma_start(
                        out=hl2[n:n + 1, n * TQ:(n + 1) * TQ],
                        in_=bcrows[n:n + 1, co + t0:co + t0 + TQ]),
                        reads=[Bf("bcrows")], writes=[Bf("hl2")])
                for ch in range(4):
                    for c2 in range(2):
                        kb.op("T", "T", lambda t, ch=ch, c2=c2: t.matmul(
                            psH[:, c2 * 512:(c2 + 1) * 512], ones16[:, :],
                            hl2[0:16, ch * 1024 + c2 * 512:
                                ch * 1024 + (c2 + 1) * 512],
                            start=True, stop=True),
                            reads=[Bf("hl2"), Bf("ones16")], writes=[Bf("psH")])
                    kb.op("S", "S", lambda s, dst=dst, ch=ch: s.copy(
                        dst[:, ch * 1024:(ch + 1) * 1024], psH[:, :]),
                        reads=[Bf("psH")], writes=[dB])
            for (dst, dB, rb) in ((rBq, Bf("rBq"), rb_rb), (rCq, Bf("rCq"), rb_rc)):
                kb.op("T", "T", lambda t, rb=rb, t0=t0: t.matmul(
                    psG[:, 0:TQ], ones64, rb[0:1, t0:t0 + TQ],
                    start=True, stop=True),
                    reads=[Bf("rb_rb") if rb is rb_rb else Bf("rb_rc"),
                           Bf("ones64")], writes=[Bf("psG")])
                kb.op("S", "S", lambda s, dst=dst: s.copy(dst[:, :], psG[:, 0:TQ]),
                      reads=[Bf("psG")], writes=[dB])

        def scan_tq(b, tq):
            t0 = tq * TQ
            for dt in range(NDT):
                p = dt % 2
                po = p * TQ
                for (dst, dB, src, sB) in (
                        (dtl_t, DTLB[p], dtlsp, Bf("dtlsp")),
                        (xc_t, XCTB[p], xcsp, Bf("xcsp")),
                        (g_t, GTB[p], gsp, Bf("gsp"))):
                    kb.op("Q", "QS", lambda q, dst=dst, src=src, po=po, b=b, dt=dt,
                          t0=t0: q.dma_start(
                        out=dst[:, po:po + TQ], in_=src[b, dt, :, t0:t0 + TQ]),
                        reads=[sB], writes=[dB])
                kb.op("G", "G", lambda g, po=po: g.tensor_tensor(
                    u_t[:, :], dtl_t[:, po:po + TQ], xc_t[:, po:po + TQ], ALU.mult),
                    reads=[DTLB[p], XCTB[p]], writes=[Bf("u_t")])
                kb.op("G", "G", lambda g: g.tensor_tensor(
                    u_t[:, :], u_t[:, :], rBq[:, :], ALU.mult),
                    reads=[Bf("rBq")], writes=[Bf("u_t")])
                dApX, wlX, hlX = ((dAp, wl1, hl2), (dApB, wl1B, hl2))[dt % 2]
                dApN, wlN, hlN = (("dAp", "wl1", "hl2"),
                                  ("dApB", "wl1B", "hl2"))[dt % 2]
                hcN = f"hcar{dt}"
                kb.op("Q", "QS", lambda q, dt=dt, wlX=wlX: q.dma_start(
                    out=wlX[:, :].rearrange("p (n t) -> p n t", n=NST)
                    [:, :, 0:1],
                    in_=hcar[:, dt * NST:(dt + 1) * NST].unsqueeze(2),
                    allow_slow_non_contiguous=True),
                    reads=[Bf(hcN)], writes=[Bf(wlN)])
                for n in range(NST):
                    kb.op("S", "S", lambda s, po=po, n=n, dt=dt, dApX=dApX:
                          s.activation(
                        dApX[:, n * TQS + 1:n * TQS + 1 + TQ],
                        dtl_t[:, po:po + TQ], AF.Exp,
                        scale=negA[:, dt * NST + n:dt * NST + n + 1]),
                        reads=[DTLB[p], Bf("negA")], writes=[Bf(dApN)])
                uview = u_t[:, :].unsqueeze(1).broadcast_to([128, NST, TQ])
                bview = bcB[:, :].rearrange("p (n t) -> p n t", n=NST)
                wview = wlX[:, :].rearrange("p (n t) -> p n t", n=NST)[:, :, 1:]
                kb.op("G", "G", lambda g, uview=uview, bview=bview, wview=wview:
                      g.tensor_tensor(wview, uview, bview, ALU.mult),
                      reads=[Bf("u_t"), Bf("bcB")], writes=[Bf(wlN)])
                kb.op("V", "V", lambda v, dApX=dApX, wlX=wlX, hlX=hlX:
                      v.tensor_tensor_scan(
                    hlX[:, :], dApX[:, :], wlX[:, :], 0.0, ALU.mult, ALU.add),
                    reads=[Bf(dApN), Bf(wlN)], writes=[Bf(hlN)])
                kb.op("Q", "QS", lambda q, dt=dt, hlX=hlX: q.dma_start(
                    out=hcar[:, dt * NST:(dt + 1) * NST].unsqueeze(2),
                    in_=hlX[:, :].rearrange("p (n t) -> p n t", n=NST)
                    [:, :, TQ:TQ + 1],
                    allow_slow_non_contiguous=True),
                    reads=[Bf(hlN)], writes=[Bf(hcN)])
                cview = bcC[:, :].rearrange("p (n t) -> p n t", n=NST)
                hview = hlX[:, :].rearrange("p (n t) -> p n t", n=NST)[:, :, 1:]
                pview = dApX[:, :].rearrange("p (n t) -> p n t", n=NST)[:, :, 1:]
                kb.op("V", "V", lambda v, pview=pview, hview=hview, cview=cview:
                      v.tensor_tensor(pview, hview, cview, ALU.mult),
                      reads=[Bf(hlN), Bf("bcC")], writes=[Bf(dApN)])
                pv = dApX[:, :].rearrange("p (n t) -> p n t", n=NST)
                lv = wlX[:, :].rearrange("p (n t) -> p n t", n=NST)
                kb.op("V", "V", lambda v, pv=pv, lv=lv: v.tensor_tensor(
                    lv[:, 0:8, 1:], pv[:, 0:16:2, 1:], pv[:, 1:16:2, 1:],
                    ALU.add),
                    reads=[Bf(dApN)], writes=[Bf(wlN)])
                kb.op("V", "V", lambda v, lv=lv: v.tensor_tensor(
                    lv[:, 8:12, 1:], lv[:, 0:8:2, 1:], lv[:, 1:8:2, 1:],
                    ALU.add), writes=[Bf(wlN)])
                kb.op("V", "V", lambda v, lv=lv: v.tensor_tensor(
                    lv[:, 12:14, 1:], lv[:, 8:12:2, 1:], lv[:, 9:12:2, 1:],
                    ALU.add), writes=[Bf(wlN)])
                kb.op("V", "V", lambda v, wlX=wlX: v.tensor_tensor(
                    ys_t[:, :], wlX[:, 12 * TQS + 1:12 * TQS + 1 + TQ],
                    wlX[:, 13 * TQS + 1:13 * TQS + 1 + TQ], ALU.add),
                    reads=[Bf(wlN)], writes=[Bf("ys_t")])
                kb.op("V", "V", lambda v: v.tensor_tensor(
                    ys_t[:, :], ys_t[:, :], rCq[:, :], ALU.mult),
                    reads=[Bf("rCq")], writes=[Bf("ys_t")])
                kb.op("V", "V", lambda v, po=po, dt=dt: v.scalar_tensor_tensor(
                    t2_t[:, :], xc_t[:, po:po + TQ], d_sb[:, dt:dt + 1], ys_t[:, :],
                    ALU.mult, ALU.add),
                    reads=[XCTB[p], Bf("d_sb"), Bf("ys_t")], writes=[Bf("t2_t")])
                kb.op("V", "V", lambda v, po=po: v.tensor_tensor(
                    ygd_t[:, po:po + TQ], t2_t[:, :], g_t[:, po:po + TQ], ALU.mult),
                    reads=[Bf("t2_t"), GTB[p]], writes=[YGDB[p]])
                kb.op("Q", "QS", lambda q, po=po, b=b, dt=dt, t0=t0: q.dma_start(
                    out=yin[b][dt * 128:(dt + 1) * 128, t0:t0 + TQ],
                    in_=ygd_t[:, po:po + TQ]),
                    reads=[YGDB[p]], writes=[Bf(f"yin{b}")])

        def y_rs(b):
            fl = kb.op("Q", "Q", lambda q: q.nop(), reads=[Bf(f"pmix{b}")])
            kb.op("G", f"CCR{b}", lambda g, b=b: g.collective_compute(
                "ReduceScatter", ALU.add, replica_groups=[list(range(NCK))],
                ins=[pmix[b]], outs=[rsout[b]]),
                reads=[Bf(f"pmix{b}")], writes=[Bf(f"rsout{b}")], extra=[fl])

        def ln_stats_T(mslot, mbuf, sqslot, sqbuf):
            for it in range(NKT):
                mo = it * TSL
                kb.op("S", "S", lambda s, mo=mo, mslot=mslot, sqslot=sqslot:
                      s.activation(sqslot[:, mo:mo + TSL], mslot[:, mo:mo + TSL],
                                   AF.Square),
                      reads=[mbuf], writes=[sqbuf])
            for it in range(NKT):
                mo = it * TSL
                kb.op("T", "T", lambda t, mo=mo, mslot=mslot, it=it: t.matmul(
                    psH[0:1, 0:TSL], ones_c[:, :], mslot[:, mo:mo + TSL],
                    start=(it == 0), stop=(it == NKT - 1)),
                    reads=[mbuf, Bf("ones_c")], writes=[Bf("psH")])
            kb.op("S", "S", lambda s: s.copy(c_sum[0:1, 0:TSL], psH[0:1, 0:TSL]),
                  reads=[Bf("psH")], writes=[Bf("c_sum")])
            for it in range(NKT):
                mo = it * TSL
                kb.op("T", "T", lambda t, mo=mo, sqslot=sqslot, it=it: t.matmul(
                    psH[0:1, TSL:2 * TSL], ones_c[:, :], sqslot[:, mo:mo + TSL],
                    start=(it == 0), stop=(it == NKT - 1)),
                    reads=[sqbuf, Bf("ones_c")], writes=[Bf("psH")])
            kb.op("S", "S", lambda s: s.copy(c_ssq[0:1, 0:TSL],
                                             psH[0:1, TSL:2 * TSL]),
                  reads=[Bf("psH")], writes=[Bf("c_ssq")])
            ln_rowmath(TSL)

        def stat_bcast(which):
            pairs = (((rb_rstd, 0), (rb_mrstd, 1)) if which != 1
                     else ((rb_rstd, 2),))
            for rb, pos in pairs:
                rbB = Bf("rb_rstd") if rb is rb_rstd else Bf("rb_mrstd")
                on = ones_r[:, :] if rb is rb_rstd else ones32
                onB = Bf("ones_r") if rb is rb_rstd else Bf("ones32")
                kb.op("T", "T", lambda t, rb=rb, on=on: t.matmul(
                    psG[:, 0:TSL], on, rb[0:1, 0:TSL],
                    start=True, stop=True),
                    reads=[rbB, onB], writes=[Bf("psG")])
                kb.op("S", "S", lambda s, pos=pos: s.copy(
                    bct[:, pos * TSL:(pos + 1) * TSL], psG[:, 0:TSL]),
                    reads=[Bf("psG")], writes=[Bf("bct")])

        def epi_chunk(b, part):
            if part in (0, 1):
                for sub in range(2):
                    tch = part * 2 + sub
                    t0 = tch * 512
                    rp = tch % 2
                    ro = rp * 4096
                    for dt in range(NDT):
                        kb.op("Q", "QS", lambda q, dt=dt, b=b, t0=t0, ro=ro:
                              q.dma_start(
                            out=SLOT[0][:, ro + dt * 512:ro + (dt + 1) * 512],
                            in_=yin[b][dt * 128:(dt + 1) * 128, t0:t0 + 512]),
                            reads=[Bf(f"yin{b}")], writes=[SB4[rp] if False
                                                           else SB[0]])
                    for htg in range(8):
                        wp = (tch * 8 + htg) % 2
                        wof = wp * 4096
                        kb.op("Q", "QS", lambda q, htg=htg, wof=wof: q.dma_start(
                            out=SLOT[4][:, wof:wof + 4096], in_=wos[htg]),
                            reads=[Bf("wos")], writes=[SB4[wp]])
                        for hh in range(4):
                            pst, pB = ((psA, Bf("psA")), (psB, Bf("psB")),
                                       (psF, Bf("psF")), (psI, Bf("psI")))[hh]
                            for dt in range(NDT):
                                kb.op("T", "T", lambda t, pst=pst, wof=wof, dt=dt,
                                      hh=hh, ro=ro: t.matmul(
                                    pst[:, :],
                                    SLOT[4][:, wof + dt * 512 + hh * 128:
                                            wof + dt * 512 + (hh + 1) * 128],
                                    SLOT[0][:, ro + dt * 512:ro + (dt + 1) * 512],
                                    start=(dt == 0), stop=(dt == NDT - 1)),
                                    reads=[SB[0], SB4[wp]], writes=[pB])
                            ht = htg * 4 + hh
                            gp = hh % 2
                            go = gp * 512
                            kb.op("S", "S", lambda s, pst=pst, go=go: s.copy(
                                gev[:, go:go + 512], pst[:, :]),
                                reads=[pB], writes=[GEVB[gp]])
                            for jj in range(2):
                                jb = tch * 2 + jj
                                kb.op("Q", "QS", lambda q, b=b, jb=jb, ht=ht,
                                      go=go, jj=jj: q.dma_start(
                                    out=pmix[b][jb, ht * 128:(ht + 1) * 128, :],
                                    in_=gev[:, go + jj * 256:go + (jj + 1) * 256]),
                                    reads=[GEVB[gp]], writes=[Bf(f"pmix{b}")])
            if part == 2:
                for ht in range(NKT):
                    mo = ht * TSL
                    kb.op("Q", "QS", lambda q, ht=ht, mo=mo, b=b: q.dma_start(
                        out=SLOT[2][:, mo:mo + TSL],
                        in_=rsout[b][ht * 128:(ht + 1) * 128, :]),
                        reads=[Bf(f"rsout{b}")], writes=[SB[2]])
            if part == 2:
                ln_stats_T(SLOT[2], SB[2], SLOT[1], SB[1])
                stat_bcast(0)
                for ht in range(NKT):
                    p = ht % 2
                    po = p * TSL
                    mo = ht * TSL
                    kb.op("Q", "QS", lambda q, ht=ht, b=b: q.dma_start(
                        out=x0t[:, :],
                        in_=hs_resT[b, ht * 128:(ht + 1) * 128, :]),
                        writes=[X0B[0]])
                    kb.op("V", "V", lambda v, po=po, mo=mo: v.tensor_tensor(
                        cacc[:, po:po + TSL], SLOT[2][:, mo:mo + TSL],
                        bct[:, 0:TSL], ALU.mult),
                        reads=[SB[2], Bf("bct")], writes=[CAB[p]])
                    kb.op("V", "V", lambda v, po=po: v.scalar_tensor_tensor(
                        cacc[:, po:po + TSL], cacc[:, po:po + TSL], 1.0,
                        bct[:, TSL:2 * TSL], ALU.mult, ALU.subtract),
                        reads=[Bf("bct")], writes=[CAB[p]])
                    kb.op("V", "V", lambda v, po=po: v.tensor_tensor(
                        x1t[:, po:po + TSL], cacc[:, po:po + TSL],
                        x0t[:, :], ALU.add),
                        reads=[CAB[p], X0B[0]], writes=[X1B[p]])
                    kb.op("Q", "QS", lambda q, po=po, ht=ht, b=b: q.dma_start(
                        out=x1sp[b, ht * 128:(ht + 1) * 128, :],
                        in_=x1t[:, po:po + TSL]),
                        reads=[X1B[p]], writes=[Bf("x1sp")])
                for it in range(NKT):
                    o = it * TSL
                    kb.op("G", "QG", lambda g, it=it, o=o, b=b: g.dma_start(
                        out=SLOT[0][:, o:o + TSL],
                        in_=x1sp[b, it * 128:(it + 1) * 128, :]),
                        reads=[Bf("x1sp")], writes=[SB[0]])
                ln_stats_T(SLOT[0], SB[0], SLOT[1], SB[1])
                stat_bcast(1)
                kb.op("V", "V", lambda v: v.tensor_scalar_mul(
                    rb_tmp[0:1, 0:TSL], c_m[0:1, 0:TSL], -1.0),
                    reads=[Bf("c_m")], writes=[Bf("rb_tmp")])
                kb.op("V", "V", lambda v: v.tensor_tensor(
                    aug2r[0:1, :], rb_tmp[0:1, 0:TSL], c_ssq[0:1, 0:TSL], ALU.mult),
                    reads=[Bf("rb_tmp"), Bf("c_ssq")], writes=[Bf("aug2r")])
                kb.op("G", "QG", lambda g: g.dma_start(
                    out=aug2r[1:2, :], in_=c_ssq[0:1, 0:TSL]),
                    reads=[Bf("c_ssq")], writes=[Bf("aug2r")])
                gates_lora(b)
            if part == 3:
                ffn_and_out(b)

        def gates_lora(b):
            for it in range(NKT):
                o = it * TSL
                kb.op("T", "T", lambda t, it=it, o=o: t.matmul(
                    psH[0:36, 0:TSL], awr_sb[:, it * 36:(it + 1) * 36],
                    SLOT[0][:, o:o + TSL], start=(it == 0), stop=False),
                    reads=[SB[0], Bf("awr_sb")], writes=[Bf("psH")])
            kb.op("T", "T", lambda t: t.matmul(
                psH[0:36, 0:TSL], colsA[:, :], aug2r[0:1, :],
                start=False, stop=True),
                reads=[Bf("colsA"), Bf("aug2r")], writes=[Bf("psH")])
            kb.op("V", "V", lambda v: v.tensor_copy(scr36[:, :], psH[0:36, 0:TSL]),
                  reads=[Bf("psH")], writes=[Bf("scr36")])
            kb.op("T", "T", lambda t: t.matmul(
                psG[0:4, 0:TSL], sel2[:, :], scr36[:, :], start=True, stop=True),
                reads=[Bf("sel2"), Bf("scr36")], writes=[Bf("psG")])
            kb.op("V", "V", lambda v: v.tensor_copy(gl4[:, 0:TSL], psG[0:4, 0:TSL]),
                  reads=[Bf("psG")], writes=[Bf("gl4")])
            kb.op("T", "T", lambda t: t.matmul(
                psG[0:4, 0:TSL], ones_r[:, 0:4], rb_rstd[0:1, 0:TSL],
                start=True, stop=True),
                reads=[Bf("rb_rstd"), Bf("ones_r")], writes=[Bf("psG")])
            kb.op("V", "V", lambda v: v.tensor_copy(bct4[:, :], psG[0:4, 0:TSL]),
                  reads=[Bf("psG")], writes=[Bf("bct4")])
            kb.op("V", "V", lambda v: v.tensor_tensor(
                gl4[:, 0:TSL], gl4[:, 0:TSL], bct4[:, :], ALU.mult),
                reads=[Bf("bct4")], writes=[Bf("gl4")])
            kb.op("S", "S", lambda s: s.activation(
                gb4[:, 0:TSL], gl4[:, 0:TSL], AF.Exp),
                reads=[Bf("gl4")], writes=[Bf("gb4")])
            kb.op("T", "T", lambda t: t.matmul(
                psG[0:1, 0:TSL], ones_c[0:4, :], gb4[:, 0:TSL],
                start=True, stop=True),
                reads=[Bf("gb4"), Bf("ones_c")], writes=[Bf("psG")])
            kb.op("V", "V", lambda v: v.tensor_copy(
                gl4[0:1, TSL:2 * TSL], psG[0:1, 0:TSL]),
                reads=[Bf("psG")], writes=[Bf("gl4")])
            kb.op("V", "V", lambda v: v.reciprocal(
                gl4[0:1, TSL:2 * TSL], gl4[0:1, TSL:2 * TSL]),
                reads=[Bf("gl4")], writes=[Bf("gl4")])
            kb.op("V", "V", lambda v: v.tensor_copy(
                rb_tmp[0:1, 0:TSL], gl4[0:1, TSL:2 * TSL]),
                reads=[Bf("gl4")], writes=[Bf("rb_tmp")])
            kb.op("T", "T", lambda t: t.matmul(
                psG[0:4, 0:TSL], ones_r[:, 0:4], rb_tmp[0:1, 0:TSL],
                start=True, stop=True),
                reads=[Bf("rb_tmp"), Bf("ones_r")], writes=[Bf("psG")])
            kb.op("V", "V", lambda v: v.tensor_copy(bct4[:, :], psG[0:4, 0:TSL]),
                  reads=[Bf("psG")], writes=[Bf("bct4")])
            kb.op("V", "V", lambda v: v.tensor_tensor(
                gb4[:, TSL:2 * TSL], gb4[:, 0:TSL], bct4[:, :], ALU.mult),
                reads=[Bf("bct4")], writes=[Bf("gb4")])
            kb.op("T", "T", lambda t: t.matmul(
                psH[0:32, 0:TSL], sel[:, :], gb4[:, TSL:2 * TSL],
                start=True, stop=True),
                reads=[Bf("sel"), Bf("gb4")], writes=[Bf("psH")])
            kb.op("V", "V", lambda v: v.tensor_copy(g32[:, :], psH[0:32, 0:TSL]),
                  reads=[Bf("psH")], writes=[Bf("g32")])
            kb.op("V", "V", lambda v: v.tensor_tensor(
                prs[:, :], scr36[0:32, :], g32[:, :], ALU.mult),
                reads=[Bf("scr36"), Bf("g32")], writes=[Bf("prs")])

        def ffn_and_out(b):
            for ot in range(NKT):
                pw = ot % 2
                ws, wof = SLOT[4], pw * 4096
                wb = SB4[pw]
                kb.op("Q", "QS", lambda q, ot=ot, ws=ws, wof=wof: q.dma_start(
                    out=ws[:, wof:wof + 4096], in_=wf2[ot]),
                    reads=[Bf("wf2")], writes=[wb])
                kb.op("G", "QG", lambda g, ot=ot, pw=pw: g.dma_start(
                    out=bmt_t[:, pw * 128:(pw + 1) * 128],
                    in_=fBmT[:, ot * 128:(ot + 1) * 128]),
                    writes=[Bf("bmt_t")])
                for it in range(NKT):
                    o = it * TSL
                    kb.op("T", "T", lambda t, ws=ws, wof=wof, it=it, o=o: t.matmul(
                        psA[:, 0:TSL], ws[:, wof + it * 128:wof + (it + 1) * 128],
                        SLOT[0][:, o:o + TSL], start=(it == 0), stop=False),
                        reads=[SB[0], wb], writes=[Bf("psA")])
                kb.op("T", "T", lambda t, ot=ot: t.matmul(
                    psA[:, 0:TSL], aug2[:, ot * 128:(ot + 1) * 128],
                    aug2r[:, :], start=False, stop=False),
                    reads=[Bf("aug2"), Bf("aug2r")], writes=[Bf("psA")])
                kb.op("T", "T", lambda t, ot=ot, pw=pw: t.matmul(
                    psA[:, 0:TSL], bmt_t[:, pw * 128:(pw + 1) * 128], prs[:, :],
                    start=False, stop=True),
                    reads=[Bf("bmt_t"), Bf("prs")], writes=[Bf("psA")])
                mo = ot * TSL
                kb.op("V", "V", lambda v, mo=mo: v.tensor_tensor(
                    SLOT[2][:, mo:mo + TSL], psA[:, 0:TSL],
                    bct[:, 2 * TSL:3 * TSL], ALU.mult),
                    reads=[Bf("psA"), Bf("bct")], writes=[SB[2]])
            ln_stats_T(SLOT[2], SB[2], SLOT[1], SB[1])
            stat_bcast(2)
            for ht in range(NKT):
                p = ht % 2
                po = p * TSL
                mo = ht * TSL
                kb.op("G", "QG", lambda g, po=po, ht=ht, b=b: g.dma_start(
                    out=x1t[:, po:po + TSL],
                    in_=x1sp[b, ht * 128:(ht + 1) * 128, :]),
                    reads=[Bf("x1sp")], writes=[X1B[p]])
                kb.op("V", "V", lambda v, po=po, mo=mo: v.tensor_tensor(
                    cacc[:, po:po + TSL], SLOT[2][:, mo:mo + TSL],
                    bct[:, 0:TSL], ALU.mult),
                    reads=[SB[2], Bf("bct")], writes=[CAB[p]])
                kb.op("V", "V", lambda v, po=po: v.scalar_tensor_tensor(
                    cacc[:, po:po + TSL], cacc[:, po:po + TSL], 1.0,
                    bct[:, TSL:2 * TSL], ALU.mult, ALU.subtract),
                    reads=[Bf("bct")], writes=[CAB[p]])
                kb.op("V", "V", lambda v, po=po: v.tensor_tensor(
                    cacc[:, po:po + TSL], cacc[:, po:po + TSL],
                    x1t[:, po:po + TSL], ALU.add),
                    reads=[X1B[p]], writes=[CAB[p]])
                kb.op("Q", "QS", lambda q, po=po, ht=ht, b=b: q.dma_start(
                    out=out_ext[b, ht * 128:(ht + 1) * 128, :],
                    in_=cacc[:, po:po + TSL]),
                    reads=[CAB[p]], writes=[Bf("out")])

        for part in range(4):
            p1_chunk(0, part)
        p1_ar(0)
        for b in range(B):
            rows_dt(b)
            bc_tq(b, 0)
            scan_tq(b, 0)
            bc_tq(b, 1)
            scan_tq(b, 1)
            if b >= 1:
                epi_chunk(b - 1, 0)
            bc_tq(b, 2)
            scan_tq(b, 2)
            if b >= 1:
                epi_chunk(b - 1, 1)
                y_rs(b - 1)
            bc_tq(b, 3)
            scan_tq(b, 3)
            if b >= 1:
                epi_chunk(b - 1, 2)
                epi_chunk(b - 1, 3)
            bc_tq(b, 4)
            scan_tq(b, 4)
            bc_tq(b, 5)
            scan_tq(b, 5)
            if b + 1 < B:
                p1_chunk(b + 1, 0)
            bc_tq(b, 6)
            scan_tq(b, 6)
            if b + 1 < B:
                p1_chunk(b + 1, 1)
            bc_tq(b, 7)
            scan_tq(b, 7)
            if b + 1 < B:
                p1_chunk(b + 1, 2)
                p1_chunk(b + 1, 3)
                p1_ar(b + 1)
        epi_chunk(B - 1, 0)
        epi_chunk(B - 1, 1)
        y_rs(B - 1)
        epi_chunk(B - 1, 2)
        epi_chunk(B - 1, 3)

        kb.op("G", "G", lambda g: g.nop(),
              extra=[x for x in (bufs["out"].w,) if x is not None])

        kb.finalize()

        with contextlib.ExitStack() as st2:
            sems = {}
            ccs = ["CCL"] + [f"CCA{b}" for b in range(B)] + \
                  [f"CCR{b}" for b in range(B)]
            pools = [f"{q}{i}" for q, n in KB.POOLS.items() for i in range(n)]
            for s in ["T", "V", "S", "G", "Q", "QS", "QA", "QG"] + ccs + pools:
                sems[s] = st2.enter_context(nc.semaphore(f"sem_{s}"))
            with nc.Block() as block:
                @block.tensor
                def _(e):
                    kb.emit("T", e, sems)

                @block.vector
                def _(e):
                    kb.emit("V", e, sems)

                @block.scalar
                def _(e):
                    kb.emit("S", e, sems)

                @block.gpsimd
                def _(e):
                    kb.emit("G", e, sems)

                @block.sync
                def _(e):
                    kb.emit("Q", e, sems)
    return nc


_NC_CACHE = None


def kernel(**inputs):
    global _NC_CACHE
    f32 = np.float32

    hs = np.asarray(inputs["L0_attn_hidden_states"], f32)
    W1 = np.asarray(inputs["L0_attn_in_proj_w"], f32)
    convw = np.asarray(inputs["L0_attn_conv_w"], f32)
    convb = np.asarray(inputs["L0_attn_conv_b"], f32)
    xpw = np.asarray(inputs["L0_attn_x_proj_w"], f32)
    dtp = np.asarray(inputs["L0_attn_dt_proj_w"], f32)
    dtbias = np.asarray(inputs["L0_attn_dt_bias"], f32)
    alog = np.asarray(inputs["L0_attn_A_log"], f32)
    dparam = np.asarray(inputs["L0_attn_D_param"], f32)
    wo = np.asarray(inputs["L0_attn_out_proj_w"], f32)
    dtlnw = np.asarray(inputs["L0_attn_dt_ln_w"], f32)
    blnw = np.asarray(inputs["L0_attn_B_ln_w"], f32)
    clnw = np.asarray(inputs["L0_attn_C_ln_w"], f32)
    wf = np.asarray(inputs["L0_ffn_W"], f32)
    fbv = np.asarray(inputs["L0_ffn_b"], f32)
    fA = np.asarray(inputs["L0_ffn_A"], f32)
    fBm = np.asarray(inputs["L0_ffn_Bm"], f32)
    fWr = np.asarray(inputs["L0_ffn_Wr"], f32)

    hsT = np.ascontiguousarray(hs.reshape(B * L, H).T)
    woT_full = np.ascontiguousarray(wo.T)
    wfT_full = np.ascontiguousarray(wf.T)
    wf_sh = np.ascontiguousarray(
        wfT_full.reshape(32, 128, 32, 128).transpose(2, 1, 0, 3)
        .reshape(32, 128, 4096))
    dtpT_full = np.ascontiguousarray(dtp.T)
    xpwT_full = np.ascontiguousarray(xpw.T)
    fAWr = np.concatenate([fA.reshape(32, H), fWr], axis=0)
    fAWrT = np.ascontiguousarray(fAWr.T)
    fBmT = np.ascontiguousarray(fBm.transpose(0, 2, 1).reshape(32, H))

    in_maps = []
    for k in range(NCK):
        dsl = slice(k * DSH, (k + 1) * DSH)
        res = np.stack([
            np.ascontiguousarray(hsT[:, b * L + k * TSL:b * L + (k + 1) * TSL])
            for b in range(B)])
        w1k = np.concatenate(
            [W1[:, dsl], W1[:, DIN + k * DSH:DIN + (k + 1) * DSH]], axis=1)
        w1_sh = np.ascontiguousarray(
            w1k.reshape(32, 128, 16, 128).transpose(2, 1, 0, 3)
            .reshape(16, 128, 4096))
        wo_sh = np.ascontiguousarray(
            woT_full[dsl].reshape(8, 128, 8, 512).transpose(2, 1, 0, 3)
            .reshape(8, 128, 4096))
        in_maps.append({
            "hs_prepT": np.ascontiguousarray(hsT[:, k * 1024:(k + 1) * 1024]),
            "hs_resT": res,
            "w1": w1_sh,
            "woTs": wo_sh,
            "wfT": wf_sh,
            "convw": np.ascontiguousarray(convw[dsl]),
            "convb": np.ascontiguousarray(convb[dsl]),
            "dtbias": np.ascontiguousarray(dtbias[dsl]),
            "dparam": np.ascontiguousarray(dparam[dsl]),
            "alog": np.ascontiguousarray(alog[dsl]),
            "xpwT": np.ascontiguousarray(xpwT_full[dsl]),
            "dtpT": np.ascontiguousarray(dtpT_full[:, dsl]),
            "dtlnw": dtlnw, "blnw": blnw, "clnw": clnw,
            "fb": fbv, "fAWrT": fAWrT, "fBmT": fBmT,
        })

    if _NC_CACHE is None:
        _NC_CACHE = build_program()
    res = run_bass_kernel_spmd(_NC_CACHE, in_maps, core_ids=list(range(NCK)))
    full = np.empty((B, L, H), np.float32)
    for k in range(NCK):
        full[:, k * TSL:(k + 1) * TSL, :] = res.results[k]["out"].transpose(0, 2, 1)
    return full


# version 10
# speedup vs baseline: 1.0343x; 1.0343x over previous
import numpy as np
from collections import defaultdict
import contextlib

import concourse.bass as bass
import concourse.mybir as mybir
from concourse.bass_utils import run_bass_kernel_spmd

F32 = mybir.dt.float32
BF16 = mybir.dt.bfloat16
U32 = mybir.dt.uint32
AF = mybir.ActivationFunctionType
ALU = mybir.AluOpType

NCK = 8
H = 4096
DIN = 8192
DSH = DIN // NCK
NDT = DSH // 128
NST = 16
R = 256
E3 = 288
B = 4
L = 2048
TSL = L // NCK
NKT = H // 128
EPS_LN = 1e-5
EPS_RMS = 1e-6
SCALE = 2.0
TQ = 256
TQS = TQ + 1
NTQ = L // TQ


class Buf:
    __slots__ = ("name", "w", "r")

    def __init__(self, name):
        self.name = name
        self.w = None
        self.r = []


class KB:
    CSTREAMS = ("T", "V", "S", "G", "Q")

    def __init__(self):
        self.ops = []

    def op(self, eng, stream, fn, reads=(), writes=(), extra=()):
        idx = len(self.ops)
        waits = set(e for e in extra if e is not None)
        for b in reads:
            if b.w is not None:
                waits.add(b.w)
        for b in writes:
            if b.w is not None:
                waits.add(b.w)
            waits.update(b.r)
        for b in writes:
            b.w = idx
            b.r = []
        for b in reads:
            b.r.append(idx)
        amt = 16 if stream in ("QS", "QA", "QG") else 1
        self.ops.append([eng, stream, fn, waits, amt])
        return idx

    POOLS = {"QS": 24, "QG": 12, "QA": 4}
    STRICT = False

    def finalize(self):
        used = set()
        for _, _, _, waits, _ in self.ops:
            used.update(waits)
        val = {}
        cnt = defaultdict(int)
        pool_ctr = defaultdict(int)
        self.pre_wait = {}
        self.inc_flag = []
        for idx, (eng, stream, fn, waits, amt) in enumerate(self.ops):
            if stream in self.POOLS:
                n = self.POOLS[stream]
                c = pool_ctr[stream]
                pool_ctr[stream] += 1
                sem_name = f"{stream}{c % n}"
                u = c // n
                self.pre_wait[idx] = (sem_name, 16 * u)
                val[idx] = (sem_name, 16 * (u + 1))
                self.inc_flag.append(True)
                cnt[sem_name] = 16 * (u + 1)
            else:
                keep = self.STRICT or (stream not in self.CSTREAMS) or (idx in used)
                self.inc_flag.append(keep)
                if keep:
                    cnt[stream] += amt
                    val[idx] = (stream, cnt[stream])
        self.ev_val = val
        self.final_counts = dict(cnt)

    def emit(self, eng_key, handle, sems):
        waited = defaultdict(int)
        for idx, (eng, stream, fn, waits, amt) in enumerate(self.ops):
            if eng != eng_key:
                continue
            pw = self.pre_wait.get(idx)
            if pw is not None and pw[1] > 0 and waited[pw[0]] < pw[1]:
                handle.wait_ge(sems[pw[0]], pw[1])
                waited[pw[0]] = pw[1]
            for wev in sorted(waits):
                wsv = self.ev_val.get(wev)
                if wsv is None:
                    continue
                ws, wv = wsv
                if (not self.STRICT) and ws == eng and ws in self.CSTREAMS:
                    continue
                if waited[ws] >= wv:
                    continue
                handle.wait_ge(sems[ws], wv)
                waited[ws] = wv
            ins = fn(handle)
            if self.inc_flag[idx]:
                s, v = self.ev_val[idx]
                ins.then_inc(sems[s], amt)


def build_program():
    nc = bass.Bass("TRN2", target_bir_lowering=False, debug=False)

    def ein(name, shp, dt=F32):
        return nc.dram_tensor(name, shp, dt, kind="ExternalInput").ap()

    hs_prepT = ein("hs_prepT", [H, 1024])
    hs_resT = ein("hs_resT", [B, H, TSL])
    w1 = ein("w1", [16, 128, 4096])
    woTs = ein("woTs", [8, 128, 4096])
    wfT = ein("wfT", [32, 128, 4096])
    convw = ein("convw", [DSH, 4])
    convb = ein("convb", [DSH])
    dtbias = ein("dtbias", [DSH])
    dparam = ein("dparam", [DSH])
    alog = ein("alog", [DSH, NST])
    xpwT = ein("xpwT", [DSH, E3])
    dtpT = ein("dtpT", [R, DSH])
    dtlnw = ein("dtlnw", [R])
    blnw = ein("blnw", [NST])
    clnw = ein("clnw", [NST])
    fbi = ein("fb", [H])
    fAWrT = ein("fAWrT", [H, 36])
    fBmT = ein("fBmT", [32, H])
    out_ext = nc.dram_tensor("out", [B, H, TSL], F32, kind="ExternalOutput").ap()

    dr = lambda n, shp, dt: nc.dram_tensor(n, shp, dt).ap()
    ln_in = dr("ln_in", [H, 1024], BF16)
    ln_all = dr("ln_all", [NCK, H, 1024], BF16)
    w1t = dr("w1t", [16, 128, 4096], BF16)
    wos = dr("wos", [8, 128, 4096], BF16)
    wf2 = dr("wf2", [NKT, 128, 4096], BF16)
    arin = [dr(f"arin{b}", [E3, L], F32) for b in range(B)]
    arout = [dr(f"arout{b}", [E3, L], F32) for b in range(B)]
    yin = [dr(f"yin{b}", [DSH, L], BF16) for b in range(B)]
    pmix = [dr(f"pmix{b}", [NCK, H, TSL], BF16) for b in range(B)]
    rsout = [dr(f"rsout{b}", [H, TSL], BF16) for b in range(B)]
    xcsp = dr("xcsp", [B, NDT, 128, L], BF16)
    gsp = dr("gsp", [B, NDT, 128, L], BF16)
    dtlsp = dr("dtlsp", [B, NDT, 128, L], BF16)
    x1sp = dr("x1sp", [B, H, TSL], F32)

    kb = KB()
    st = contextlib.ExitStack()
    T = {}

    def sb(name, shp, dt):
        t = st.enter_context(nc.sbuf_tensor(name, shp, dt))
        T[name] = t
        return t

    def psm(name, shp):
        t = st.enter_context(nc.psum_tensor(name, shp, F32))
        T[name] = t
        return t

    with st:
        SLOT = [sb(f"sl{i}", [128, 8192], BF16) for i in range(5)]
        dAp = sb("dAp", [128, NST * TQS], BF16)
        wl1 = sb("wl1", [128, NST * TQS], BF16)
        hl2 = sb("hl2", [128, NST * TQS], BF16)
        dApB = sb("dApB", [128, NST * TQS], BF16)
        wl1B = sb("wl1B", [128, NST * TQS], BF16)
        hl2B = hl2
        bcB = sb("bcB", [128, NST * TQ], BF16)
        bcC = sb("bcC", [128, NST * TQ], BF16)
        dtl_t = sb("dtl_t", [128, 2 * TQ], BF16)
        xc_t = sb("xc_t", [128, 2 * TQ], BF16)
        g_t = sb("g_t", [128, 2 * TQ], BF16)
        u_t = sb("u_t", [128, TQ], BF16)
        ys_t = sb("ys_t", [128, TQ], F32)
        t2_t = sb("t2_t", [128, TQ], BF16)
        ygd_t = sb("ygd_t", [128, 2 * TQ], BF16)
        hcar = sb("hcar", [128, NDT * NST], BF16)
        rBq = sb("rBq", [128, TQ], BF16)
        rCq = sb("rCq", [128, TQ], BF16)
        xconv = sb("xconv", [128, 2 * (TQ + 3)], BF16)
        tails = sb("tails", [128, NDT * 3], BF16)
        cacc = sb("cacc", [128, 2 * TQ], F32)
        xcw = sb("xcw", [128, TQ], BF16)
        gev = sb("gev", [128, 2 * 512], BF16)
        xcr = sb("xcr", [128, 512], BF16)
        spsc = sb("spsc", [96, 512], F32)
        rowpack = sb("rowpack", [65, 2 * L + 128 + 1024 + 128], BF16)
        bcrows = rowpack[0:16, :]
        spd = sb("spd", [128, 1024], BF16)
        c_sum = sb("c_sum", [1, 512], F32)
        c_ssq = sb("c_ssq", [1, 512], F32)
        c_m = sb("c_m", [1, 512], F32)
        s_scr = sb("s_scr", [1, 512], F32)
        rb_rstd = rowpack[0:1, 4224:5248]
        rb_mrstd = rowpack[32:33, 4224:5248]
        ones32 = rowpack[32:33, 5248:5376]
        rb_rb = rowpack[64:65, 0:L]
        rb_rc = rowpack[64:65, L:2 * L]
        ones64 = rowpack[64:65, 2 * L:2 * L + 128]
        rb_tmp = sb("rb_tmp", [1, TSL], BF16)
        gl4 = sb("gl4", [4, 2 * TSL], F32)
        gb4 = sb("gb4", [4, 2 * TSL], BF16)
        bct4 = sb("bct4", [4, TSL], F32)
        g32 = sb("g32", [32, TSL], BF16)
        prs = sb("prs", [32, TSL], BF16)
        scr36 = sb("scr36", [36, TSL], BF16)
        negA = sb("negA", [128, NDT * NST], F32)
        cw_sb = sb("cw_sb", [128, NDT * 4], F32)
        cb_sb = sb("cb_sb", [128, NDT], F32)
        ndtb = sb("ndtb", [128, NDT], F32)
        d_sb = sb("d_sb", [128, NDT], F32)
        dtlnw_sb = sb("dtlnw_sb", [128, 2], F32)
        bclnw = sb("bclnw", [16, 2], F32)
        dtp_sb = sb("dtp_sb", [128, 16 * 128], BF16)
        xpw_sb = sb("xpw_sb", [128, NDT * E3], BF16)
        awr_sb = sb("awr_sb", [128, NKT * 36], BF16)
        bmt_t = sb("bmt_t", [32, 2 * 128], BF16)
        aug2r = sb("aug2r", [2, TSL], BF16)
        colsA = sb("colsA", [1, 36], BF16)
        ones_c = sb("ones_c", [128, 1], BF16)
        ones_r = sb("ones_r", [1, 128], BF16)
        ones16 = sb("ones16", [16, 128], BF16)

        srow = sb("srow", [1, TQ], BF16)
        sel = sb("sel", [4, 32], BF16)
        sel2 = sb("sel2", [36, 4], BF16)
        x0t = sb("x0t", [128, TSL], F32)
        x1t = sb("x1t", [128, 2 * TSL], F32)
        bct = sb("bct", [128, 3 * TSL], F32)
        aug2 = sb("aug2", [2, H], BF16)

        psA = psm("psA", [128, 512])
        psB = psm("psB", [128, 512])
        psE = psm("psE", [96, 512])
        psF = psm("psF", [128, 512])
        psG = psm("psG", [128, 512])
        psH = psm("psH", [128, 1024])
        psI = psm("psI", [128, 512])

        bufs = {}

        def Bf(n):
            if n not in bufs:
                bufs[n] = Buf(n)
            return bufs[n]

        for n in list(T):
            Bf(n)
        for n in ("ln_in", "ln_all", "w1t", "wos", "wf2", "xcsp", "gsp",
                  "dtlsp", "x1sp", "out", "sl4a", "sl4b",
                  "dtl0", "dtl1", "xct0", "xct1", "gt0", "gt1", "ygd0", "ygd1",
                  "x0t0", "x0t1", "x1t0", "x1t1", "ca0", "ca1", "gev0", "gev1",
                  "xcr0", "xcr1"):
            Bf(n)
        for b in range(B):
            for n in ("arin", "arout", "yin", "pmix", "rsout"):
                Bf(f"{n}{b}")
        SB = [Bf(f"sl{i}") for i in range(5)]
        SB4 = [Bf("sl4a"), Bf("sl4b")]
        DTLB = [Bf("dtl0"), Bf("dtl1")]
        XCTB = [Bf("xct0"), Bf("xct1")]
        GTB = [Bf("gt0"), Bf("gt1")]
        YGDB = [Bf("ygd0"), Bf("ygd1")]
        X0B = [Bf("x0t0"), Bf("x0t1")]
        X1B = [Bf("x1t0"), Bf("x1t1")]
        CAB = [Bf("ca0"), Bf("ca1")]
        GEVB = [Bf("gev0"), Bf("gev1")]
        XCRB = [Bf("xcr0"), Bf("xcr1")]
        spcB = wl1[0:16, 0:L]
        spcC = wl1[0:16, L:2 * L]

        kb.op("V", "V", lambda v: v.memset(ones_c[:, :], 1.0), writes=[Bf("ones_c")])
        kb.op("V", "V", lambda v: v.memset(ones_r[:, :], 1.0), writes=[Bf("ones_r")])
        kb.op("V", "V", lambda v: v.memset(ones64, 1.0), writes=[Bf("ones64")])
        kb.op("V", "V", lambda v: v.memset(ones32, 1.0), writes=[Bf("ones32")])
        kb.op("V", "V", lambda v: v.memset(ones16[:, :], 1.0), writes=[Bf("ones16")])
        kb.op("V", "V", lambda v: v.memset(tails[:, :], 0.0), writes=[Bf("tails")])
        kb.op("V", "V", lambda v: v.memset(sel[:, :], 0.0), writes=[Bf("sel")])
        kb.op("V", "V", lambda v: v.memset(sel2[:, :], 0.0), writes=[Bf("sel2")])
        kb.op("V", "V", lambda v: v.memset(srow[:, :], 1.0), writes=[Bf("srow")])
        for e in range(4):
            kb.op("G", "QG", lambda g, e=e: g.dma_start(
                out=sel[e:e + 1, e * 8:(e + 1) * 8].rearrange("o f -> o f") if False
                else sel[e:e + 1, e * 8:(e + 1) * 8],
                in_=srow[0:1, 0:8]), reads=[Bf("srow")], writes=[Bf("sel")])
            kb.op("G", "QG", lambda g, e=e: g.dma_start(
                out=sel2[32 + e:33 + e, e:e + 1], in_=srow[0:1, 0:1]),
                reads=[Bf("srow")], writes=[Bf("sel2")])
        kb.op("V", "V", lambda v: v.tensor_scalar_mul(sel[:, :], sel[:, :], SCALE),
              reads=[Bf("sel")], writes=[Bf("sel")])

        for dt in range(NDT):
            kb.op("G", "QG", lambda g, dt=dt: g.dma_start(
                out=negA[:, dt * NST:(dt + 1) * NST],
                in_=alog[dt * 128:(dt + 1) * 128, :]), writes=[Bf("negA")])
            kb.op("G", "QG", lambda g, dt=dt: g.dma_start(
                out=cw_sb[:, dt * 4:(dt + 1) * 4],
                in_=convw[dt * 128:(dt + 1) * 128, :]), writes=[Bf("cw_sb")])
            kb.op("G", "QG", lambda g, dt=dt: g.dma_start(
                out=cb_sb[:, dt:dt + 1],
                in_=convb[dt * 128:(dt + 1) * 128].unsqueeze(1)), writes=[Bf("cb_sb")])
            kb.op("G", "QG", lambda g, dt=dt: g.dma_start(
                out=ndtb[:, dt:dt + 1],
                in_=dtbias[dt * 128:(dt + 1) * 128].unsqueeze(1)), writes=[Bf("ndtb")])
            kb.op("G", "QG", lambda g, dt=dt: g.dma_start(
                out=d_sb[:, dt:dt + 1],
                in_=dparam[dt * 128:(dt + 1) * 128].unsqueeze(1)), writes=[Bf("d_sb")])
            for ec in range(3):
                kb.op("G", "QG", lambda g, dt=dt, ec=ec: g.dma_start(
                    out=xpw_sb[:, dt * E3 + ec * 96:dt * E3 + (ec + 1) * 96],
                    in_=xpwT[dt * 128:(dt + 1) * 128, ec * 96:(ec + 1) * 96]),
                    writes=[Bf("xpw_sb")])
        kb.op("S", "S", lambda s: s.activation(negA[:, :], negA[:, :], AF.Exp),
              reads=[Bf("negA")], writes=[Bf("negA")])
        kb.op("V", "V", lambda v: v.tensor_scalar_mul(ndtb[:, :], ndtb[:, :], -1.0),
              reads=[Bf("ndtb")], writes=[Bf("ndtb")])
        for c in range(2):
            kb.op("G", "QG", lambda g, c=c: g.dma_start(
                out=dtlnw_sb[:, c:c + 1],
                in_=dtlnw[c * 128:(c + 1) * 128].unsqueeze(1)),
                writes=[Bf("dtlnw_sb")])
        kb.op("G", "QG", lambda g: g.dma_start(out=bclnw[:, 0:1],
                                               in_=blnw.unsqueeze(1)),
              writes=[Bf("bclnw")])
        kb.op("G", "QG", lambda g: g.dma_start(out=bclnw[:, 1:2],
                                               in_=clnw.unsqueeze(1)),
              writes=[Bf("bclnw")])
        for rt in range(2):
            for dt in range(NDT):
                j = rt * NDT + dt
                kb.op("G", "QG", lambda g, rt=rt, dt=dt, j=j: g.dma_start(
                    out=dtp_sb[:, j * 128:(j + 1) * 128],
                    in_=dtpT[rt * 128:(rt + 1) * 128, dt * 128:(dt + 1) * 128]),
                    writes=[Bf("dtp_sb")])
        for it in range(NKT):
            kb.op("G", "QG", lambda g, it=it: g.dma_start(
                out=awr_sb[:, it * 36:(it + 1) * 36],
                in_=fAWrT[it * 128:(it + 1) * 128, :]), writes=[Bf("awr_sb")])
        kb.op("G", "QG", lambda g: g.dma_start(out=aug2[1:2, :], in_=fbi.unsqueeze(0)),
              writes=[Bf("aug2")])

        def ln_rowmath(width, off=0):
            kb.op("V", "V", lambda v: v.tensor_scalar_mul(
                c_m[0:1, 0:width], c_sum[0:1, 0:width], 1.0 / H),
                reads=[Bf("c_sum")], writes=[Bf("c_m")])
            kb.op("S", "S", lambda s: s.activation(
                c_sum[0:1, 0:width], c_m[0:1, 0:width], AF.Square),
                reads=[Bf("c_m")], writes=[Bf("c_sum")])
            kb.op("V", "V", lambda v: v.scalar_tensor_tensor(
                c_ssq[0:1, 0:width], c_ssq[0:1, 0:width], 1.0 / H,
                c_sum[0:1, 0:width], ALU.mult, ALU.subtract),
                reads=[Bf("c_sum")], writes=[Bf("c_ssq")])
            kb.op("V", "V", lambda v: v.tensor_scalar_add(
                c_ssq[0:1, 0:width], c_ssq[0:1, 0:width], EPS_LN),
                reads=[Bf("c_ssq")], writes=[Bf("c_ssq")])
            kb.op("S", "S", lambda s: s.activation(
                c_ssq[0:1, 0:width], c_ssq[0:1, 0:width], AF.Sqrt),
                reads=[Bf("c_ssq")], writes=[Bf("c_ssq")])
            kb.op("V", "V", lambda v: v.reciprocal(
                c_sum[0:1, 0:width], c_ssq[0:1, 0:width]),
                reads=[Bf("c_ssq")], writes=[Bf("c_sum")])
            kb.op("V", "V", lambda v: v.tensor_copy(
                rb_rstd[0:1, off:off + width], c_sum[0:1, 0:width]),
                reads=[Bf("c_sum")], writes=[Bf("rb_rstd")])
            kb.op("V", "V", lambda v: v.tensor_mul(
                c_m[0:1, 0:width], c_m[0:1, 0:width], c_sum[0:1, 0:width]),
                reads=[Bf("c_sum")], writes=[Bf("c_m")])
            kb.op("V", "V", lambda v: v.tensor_copy(
                rb_mrstd[0:1, off:off + width], c_m[0:1, 0:width]),
                reads=[Bf("c_m")], writes=[Bf("rb_mrstd")])

        for it in range(NKT):
            s, o = SLOT[it // 8], (it % 8) * 1024
            kb.op("G", "QG", lambda g, it=it, s=s, o=o: g.dma_start(
                out=s[:, o:o + 1024], in_=hs_prepT[it * 128:(it + 1) * 128, :]),
                writes=[SB[it // 8]])
        for it in range(NKT):
            s, o = SLOT[it // 8], (it % 8) * 1024
            for ch in range(2):
                kb.op("T", "T", lambda t, s=s, o=o, it=it, ch=ch: t.matmul(
                    psH[0:1, ch * 512:(ch + 1) * 512], ones_c[:, :],
                    s[:, o + ch * 512:o + (ch + 1) * 512],
                    start=(it == 0), stop=(it == NKT - 1)),
                    reads=[SB[it // 8], Bf("ones_c")], writes=[Bf("psH")])
        kb.op("S", "S", lambda s: s.copy(wl1.bitcast(F32)[0:1, 0:1024],
                                         psH[0:1, :]),
              reads=[Bf("psH")], writes=[Bf("wl1")])
        for it in range(NKT):
            s, o = SLOT[it // 8], (it % 8) * 1024
            q, qo = SLOT[4], (it % 4) * 1024
            kb.op("S", "S", lambda sc, s=s, o=o, q=q, qo=qo: sc.activation(
                q[:, qo:qo + 1024], s[:, o:o + 1024], AF.Square),
                reads=[SB[it // 8]], writes=[SB[4]])
            for ch in range(2):
                kb.op("T", "T", lambda t, q=q, qo=qo, it=it, ch=ch: t.matmul(
                    psH[0:1, ch * 512:(ch + 1) * 512], ones_c[:, :],
                    q[:, qo + ch * 512:qo + (ch + 1) * 512],
                    start=(it == 0), stop=(it == NKT - 1)),
                    reads=[SB[4], Bf("ones_c")], writes=[Bf("psH")])
        for ch in range(2):
            kb.op("S", "S", lambda s, ch=ch: s.copy(
                c_sum[:, :], wl1.bitcast(F32)[0:1, ch * 512:(ch + 1) * 512]),
                reads=[Bf("wl1")], writes=[Bf("c_sum")])
            kb.op("S", "S", lambda s, ch=ch: s.copy(
                c_ssq[:, :], psH[0:1, ch * 512:(ch + 1) * 512]),
                reads=[Bf("psH")], writes=[Bf("c_ssq")])
            ln_rowmath(512, off=ch * 512)
        for ch in range(2):
            kb.op("T", "T", lambda t, ch=ch: t.matmul(
                psG[:, :], ones_r[:, :], rb_rstd[0:1, ch * 512:(ch + 1) * 512],
                start=True, stop=True),
                reads=[Bf("rb_rstd"), Bf("ones_r")], writes=[Bf("psG")])
            kb.op("T", "T", lambda t, ch=ch: t.matmul(
                psF[:, :], ones32, rb_mrstd[0:1, ch * 512:(ch + 1) * 512],
                start=True, stop=True),
                reads=[Bf("rb_mrstd"), Bf("ones32")], writes=[Bf("psF")])
            for it in range(NKT):
                s, o = SLOT[it // 8], (it % 8) * 1024 + ch * 512
                q, qo = SLOT[4], 4096 + (it % 4) * 1024 + ch * 512
                kb.op("V", "V", lambda v, s=s, o=o, q=q, qo=qo: v.tensor_tensor(
                    q[:, qo:qo + 512], s[:, o:o + 512], psG[:, :], ALU.mult),
                    reads=[SB[it // 8], Bf("psG")], writes=[SB[4]])
                kb.op("V", "V", lambda v, q=q, qo=qo: v.tensor_tensor(
                    q[:, qo:qo + 512], q[:, qo:qo + 512], psF[:, :], ALU.subtract),
                    reads=[Bf("psF")], writes=[SB[4]])
                kb.op("Q", "QS", lambda qq, it=it, q=q, qo=qo, ch=ch: qq.dma_start(
                    out=ln_in[it * 128:(it + 1) * 128, ch * 512:(ch + 1) * 512],
                    in_=q[:, qo:qo + 512]),
                    reads=[SB[4]], writes=[Bf("ln_in")])
        fl = kb.op("Q", "Q", lambda q: q.nop(), reads=[Bf("ln_in")])
        kb.op("G", "CCL", lambda g: g.collective_compute(
            "AllGather", ALU.bypass, replica_groups=[list(range(NCK))],
            ins=[ln_in], outs=[ln_all]),
            reads=[Bf("ln_in")], writes=[Bf("ln_all")], extra=[fl])

        for kt in range(16):
            s = SLOT[kt % 2]
            kb.op("G", "QG", lambda g, kt=kt, s=s: g.dma_start(
                out=s[:, 0:4096], in_=w1[kt]), writes=[SB[kt % 2]])
            kb.op("Q", "QS", lambda q, kt=kt, s=s: q.dma_start(
                out=w1t[kt], in_=s[:, 0:4096]),
                reads=[SB[kt % 2]], writes=[Bf("w1t")])
        for dt in range(NDT):
            s = SLOT[dt % 2]
            kb.op("G", "QG", lambda g, dt=dt, s=s: g.dma_start(
                out=s[:, 0:H], in_=woTs[dt]), writes=[SB[dt % 2]])
            kb.op("Q", "QS", lambda q, dt=dt, s=s: q.dma_start(
                out=wos[dt], in_=s[:, 0:H]),
                reads=[SB[dt % 2]], writes=[Bf("wos")])
        kb.op("V", "V", lambda v: v.memset(aug2[0:1, :], 0.0), writes=[Bf("aug2")])
        for it in range(NKT):
            s = SLOT[2 + it % 2]
            kb.op("G", "QG", lambda g, it=it, s=s: g.dma_start(
                out=s[:, 0:H], in_=wfT[it]), writes=[SB[2 + it % 2]])
            kb.op("Q", "QS", lambda q, it=it, s=s: q.dma_start(
                out=wf2[it], in_=s[:, 0:H]),
                reads=[SB[2 + it % 2]], writes=[Bf("wf2")])
            for ch in range(NKT):
                kb.op("T", "T", lambda t, s=s, ch=ch, it=it: t.matmul(
                    psH[0:1, 0:128], ones_c[:, :], s[:, ch * 128:(ch + 1) * 128],
                    start=(ch == 0), stop=(ch == NKT - 1)),
                    reads=[SB[2 + it % 2], Bf("ones_c")], writes=[Bf("psH")])
            kb.op("V", "V", lambda v, it=it: v.tensor_copy(
                aug2[0:1, it * 128:(it + 1) * 128], psH[0:1, 0:128]),
                reads=[Bf("psH")], writes=[Bf("aug2")])
        for it in range(NKT):
            kb.op("T", "T", lambda t, it=it: t.matmul(
                psH[0:1, 0:36], ones_c[:, :], awr_sb[:, it * 36:(it + 1) * 36],
                start=(it == 0), stop=(it == NKT - 1)),
                reads=[Bf("awr_sb"), Bf("ones_c")], writes=[Bf("psH")])
        kb.op("V", "V", lambda v: v.tensor_copy(colsA[:, :], psH[0:1, 0:36]),
              reads=[Bf("psH")], writes=[Bf("colsA")])

        def p1_chunk(b, part):
            th, cth = divmod(part, 2)
            if cth == 0:
                for kt in range(NKT):
                    s, o = SLOT[kt // 8], (kt % 8) * 1024
                    chunk = 2 * b + th
                    kb.op("Q", "QS", lambda q, kt=kt, s=s, o=o, chunk=chunk:
                          q.dma_start(
                        out=s[:, o:o + 1024],
                        in_=ln_all[chunk, kt * 128:(kt + 1) * 128, :]),
                        reads=[Bf("ln_all")], writes=[SB[kt // 8]])
            for ci in range(8):
                ct = cth * 8 + ci
                pw = ct % 2
                ws, wof = SLOT[4], pw * 4096
                wb = SB4[pw]
                kb.op("Q", "QS", lambda q, ct=ct, ws=ws, wof=wof: q.dma_start(
                    out=ws[:, wof:wof + 4096], in_=w1t[ct]),
                    reads=[Bf("w1t")], writes=[wb])
                for half in range(2):
                    pst, pB = ((psA, Bf("psA")) if half == 0 else (psB, Bf("psB")))
                    for kt in range(NKT):
                        s, o = SLOT[kt // 8], (kt % 8) * 1024
                        kb.op("T", "T", lambda t, s=s, o=o, ws=ws, wof=wof, kt=kt,
                              half=half, pst=pst: t.matmul(
                            pst[:, :], ws[:, wof + kt * 128:wof + (kt + 1) * 128],
                            s[:, o + half * 512:o + (half + 1) * 512],
                            start=(kt == 0), stop=(kt == NKT - 1)),
                            reads=[SB[kt // 8], wb], writes=[pB])
                    if ct < 8:
                        dtc = ct
                        for tq in range(2):
                            gtq = th * 4 + half * 2 + tq
                            p = gtq % 2
                            xo = p * (TQ + 3)
                            cB = Bf("xconv")
                            kb.op("S", "S", lambda s, xo=xo, pst=pst, tq=tq: s.copy(
                                xconv[:, xo + 3:xo + 3 + TQ],
                                pst[:, tq * TQ:(tq + 1) * TQ]),
                                reads=[pB], writes=[cB])
                            if gtq == 0:
                                kb.op("V", "V", lambda v, xo=xo: v.memset(
                                    xconv[:, xo:xo + 3], 0.0), writes=[cB])
                            else:
                                kb.op("V", "V", lambda v, xo=xo, dtc=dtc:
                                      v.tensor_copy(
                                    xconv[:, xo:xo + 3],
                                    tails[:, dtc * 3:dtc * 3 + 3]),
                                    reads=[Bf("tails")], writes=[cB])
                            kb.op("V", "V", lambda v, xo=xo, dtc=dtc: v.tensor_copy(
                                tails[:, dtc * 3:dtc * 3 + 3],
                                xconv[:, xo + TQ:xo + TQ + 3]),
                                reads=[cB], writes=[Bf("tails")])
                            kb.op("V", "V", lambda v, xo=xo, dtc=dtc: v.tensor_scalar(
                                cacc[:, 0:TQ], xconv[:, xo:xo + TQ],
                                cw_sb[:, dtc * 4:dtc * 4 + 1],
                                cb_sb[:, dtc:dtc + 1], ALU.mult, ALU.add),
                                reads=[cB, Bf("cw_sb"), Bf("cb_sb")],
                                writes=[CAB[0]])
                            for k in (1, 2, 3):
                                src, dst = ((k + 1) % 2) * TQ, (k % 2) * TQ
                                kb.op("V", "V", lambda v, xo=xo, dtc=dtc, k=k,
                                      src=src, dst=dst: v.scalar_tensor_tensor(
                                    cacc[:, dst:dst + TQ],
                                    xconv[:, xo + k:xo + k + TQ],
                                    cw_sb[:, dtc * 4 + k:dtc * 4 + k + 1],
                                    cacc[:, src:src + TQ], ALU.mult, ALU.add),
                                    reads=[cB, Bf("cw_sb"), CAB[(k + 1) % 2]],
                                    writes=[CAB[k % 2]])
                            kb.op("S", "S", lambda s: s.activation(
                                xcw[:, :], cacc[:, TQ:2 * TQ], AF.Silu),
                                reads=[CAB[1]], writes=[Bf("xcw")])
                            kb.op("Q", "QS", lambda q, dtc=dtc, b=b, gtq=gtq:
                                  q.dma_start(
                                out=xcsp[b, dtc, :, gtq * TQ:(gtq + 1) * TQ],
                                in_=xcw[:, :]),
                                reads=[Bf("xcw")], writes=[Bf("xcsp")])
                    else:
                        dtc = ct - 8
                        gp = half
                        go = gp * 512
                        kb.op("S", "S", lambda s, go=go, pst=pst: s.activation(
                            gev[:, go:go + 512], pst[:, :], AF.Silu),
                            reads=[pB], writes=[GEVB[gp]])
                        kb.op("Q", "QS", lambda q, go=go, gp=gp, dtc=dtc, b=b,
                              th=th, half=half: q.dma_start(
                            out=gsp[b, dtc, :, th * 1024 + half * 512:
                                    th * 1024 + (half + 1) * 512],
                            in_=gev[:, go:go + 512]),
                            reads=[GEVB[gp]], writes=[Bf("gsp")])
            if cth == 1:
                for ec in range(3):
                    for hh in range(2):
                        for dt in range(NDT):
                            kb.op("Q", "QS", lambda q, dt=dt, b=b,
                                  th=th, hh=hh: q.dma_start(
                                out=xcr[:, :],
                                in_=xcsp[b, dt, :, th * 1024 + hh * 512:
                                         th * 1024 + (hh + 1) * 512]),
                                reads=[Bf("xcsp")], writes=[XCRB[0]])
                            kb.op("T", "T", lambda t, ec=ec, dt=dt: t.matmul(
                                psE[:, :],
                                xpw_sb[:, dt * E3 + ec * 96:dt * E3 + (ec + 1) * 96],
                                xcr[:, :],
                                start=(dt == 0), stop=(dt == NDT - 1)),
                                reads=[Bf("xpw_sb"), XCRB[0]], writes=[Bf("psE")])
                        kb.op("S", "S", lambda s: s.copy(spsc[:, :], psE[:, :]),
                              reads=[Bf("psE")], writes=[Bf("spsc")])
                        kb.op("Q", "QS", lambda q, ec=ec, hh=hh, b=b, th=th:
                              q.dma_start(
                            out=arin[b][ec * 96:(ec + 1) * 96,
                                        th * 1024 + hh * 512:
                                        th * 1024 + (hh + 1) * 512],
                            in_=spsc[:, :]),
                            reads=[Bf("spsc")], writes=[Bf(f"arin{b}")])

        def p1_ar(b):
            fl = kb.op("Q", "Q", lambda q: q.nop(), reads=[Bf(f"arin{b}")])
            kb.op("G", f"CCA{b}", lambda g, b=b: g.collective_compute(
                "AllReduce", ALU.add, replica_groups=[list(range(NCK))],
                ins=[arin[b]], outs=[arout[b]]),
                reads=[Bf(f"arin{b}")], writes=[Bf(f"arout{b}")], extra=[fl])

        def rows_dt(b):
            kb.op("V", "V", lambda v: v.memset(hcar[:, :], 0.0),
                  writes=[Bf(f"hcar{d}") for d in range(NDT)])
            kb.op("G", "QG", lambda g, b=b: g.dma_start(
                out=spcB, in_=arout[b][256:272, :]),
                reads=[Bf(f"arout{b}")], writes=[Bf("wl1")])
            kb.op("G", "QG", lambda g, b=b: g.dma_start(
                out=spcC, in_=arout[b][272:288, :]),
                reads=[Bf(f"arout{b}")], writes=[Bf("wl1")])
            kb.op("S", "S", lambda s: s.activation(
                dAp[0:16, 0:L], spcB, AF.Square),
                reads=[Bf("wl1")], writes=[Bf("dAp")])
            kb.op("S", "S", lambda s: s.activation(
                dAp[0:16, L:2 * L], spcC, AF.Square),
                reads=[Bf("wl1")], writes=[Bf("dAp")])
            for (co, rbB_) in ((0, Bf("rb_rb")), (L, Bf("rb_rc"))):
                rb = rb_rb if co == 0 else rb_rc
                for ch in range(4):
                    kb.op("T", "T", lambda t, co=co, ch=ch: t.matmul(
                        psH[0:1, 0:512], ones_c[0:16, :],
                        dAp[0:16, co + ch * 512:co + (ch + 1) * 512],
                        start=True, stop=True),
                        reads=[Bf("dAp"), Bf("ones_c")], writes=[Bf("psH")])
                    kb.op("V", "V", lambda v: v.tensor_scalar(
                        s_scr[:, :], psH[0:1, 0:512], 1.0 / NST, EPS_RMS,
                        ALU.mult, ALU.add),
                        reads=[Bf("psH")], writes=[Bf("s_scr")])
                    kb.op("S", "S", lambda s: s.activation(
                        s_scr[:, :], s_scr[:, :], AF.Sqrt),
                        reads=[Bf("s_scr")], writes=[Bf("s_scr")])
                    kb.op("V", "V", lambda v: v.reciprocal(s_scr[:, :], s_scr[:, :]),
                          reads=[Bf("s_scr")], writes=[Bf("s_scr")])
                    kb.op("V", "V", lambda v, rb=rb, ch=ch: v.tensor_copy(
                        rb[0:1, ch * 512:(ch + 1) * 512], s_scr[:, :]),
                        reads=[Bf("s_scr")], writes=[rbB_])
            kb.op("V", "V", lambda v: v.tensor_scalar(
                bcrows[:, 0:L], spcB, bclnw[:, 0:1], -1.0, ALU.mult, ALU.mult),
                reads=[Bf("wl1"), Bf("bclnw")], writes=[Bf("bcrows")])
            kb.op("V", "V", lambda v: v.tensor_scalar(
                bcrows[:, L:2 * L], spcC, bclnw[:, 1:2], None, ALU.mult),
                reads=[Bf("wl1"), Bf("bclnw")], writes=[Bf("bcrows")])
            for kt in range(2):
                for hh in range(2):
                    kb.op("G", "QG", lambda g, kt=kt, hh=hh, b=b: g.dma_start(
                        out=spd[:, :],
                        in_=arout[b][kt * 128:(kt + 1) * 128,
                                     hh * 1024:(hh + 1) * 1024]),
                        reads=[Bf(f"arout{b}")], writes=[Bf("spd")])
                    kb.op("S", "S", lambda s, kt=kt, hh=hh: s.activation(
                        wl1[:, kt * L + hh * 1024:kt * L + (hh + 1) * 1024],
                        spd[:, :], AF.Square),
                        reads=[Bf("spd")], writes=[Bf("wl1")])
            for ch in range(4):
                for kt in range(2):
                    kb.op("T", "T", lambda t, ch=ch, kt=kt: t.matmul(
                        psH[0:1, 0:512], ones_c[:, :],
                        wl1[:, kt * L + ch * 512:kt * L + (ch + 1) * 512],
                        start=(kt == 0), stop=(kt == 1)),
                        reads=[Bf("wl1"), Bf("ones_c")], writes=[Bf("psH")])
                kb.op("V", "V", lambda v: v.tensor_scalar(
                    s_scr[:, :], psH[0:1, 0:512], 1.0 / R, EPS_RMS,
                    ALU.mult, ALU.add),
                    reads=[Bf("psH")], writes=[Bf("s_scr")])
                kb.op("S", "S", lambda s: s.activation(
                    s_scr[:, :], s_scr[:, :], AF.Sqrt),
                    reads=[Bf("s_scr")], writes=[Bf("s_scr")])
                kb.op("V", "V", lambda v: v.reciprocal(s_scr[:, :], s_scr[:, :]),
                      reads=[Bf("s_scr")], writes=[Bf("s_scr")])
                kb.op("V", "V", lambda v, ch=ch: v.tensor_copy(
                    dAp[0:1, ch * 512:(ch + 1) * 512], s_scr[:, :]),
                    reads=[Bf("s_scr")], writes=[Bf("dAp")])
            for kt in range(2):
                for hh in range(2):
                    kb.op("G", "QG", lambda g, kt=kt, hh=hh, b=b: g.dma_start(
                        out=spd[:, :],
                        in_=arout[b][kt * 128:(kt + 1) * 128,
                                     hh * 1024:(hh + 1) * 1024]),
                        reads=[Bf(f"arout{b}")], writes=[Bf("spd")])
                    for c2 in range(2):
                        ch = hh * 2 + c2
                        kb.op("T", "T", lambda t, ch=ch: t.matmul(
                            psG[:, :], ones_r[:, :],
                            dAp[0:1, ch * 512:(ch + 1) * 512],
                            start=True, stop=True),
                            reads=[Bf("dAp"), Bf("ones_r")], writes=[Bf("psG")])
                        kb.op("V", "V", lambda v, kt=kt, ch=ch, c2=c2:
                              v.scalar_tensor_tensor(
                            hl2[:, kt * L + ch * 512:kt * L + (ch + 1) * 512],
                            spd[:, c2 * 512:(c2 + 1) * 512],
                            dtlnw_sb[:, kt:kt + 1], psG[:, :], ALU.mult, ALU.mult),
                            reads=[Bf("spd"), Bf("dtlnw_sb"), Bf("psG")],
                            writes=[Bf("hl2")])
            for dt in range(NDT):
                for tc in range(4):
                    for rt in range(2):
                        j = rt * NDT + dt
                        kb.op("T", "T", lambda t, j=j, rt=rt, tc=tc: t.matmul(
                            psF[:, :], dtp_sb[:, j * 128:(j + 1) * 128],
                            hl2[:, rt * L + tc * 512:rt * L + (tc + 1) * 512],
                            start=(rt == 0), stop=(rt == 1)),
                            reads=[Bf("dtp_sb"), Bf("hl2")], writes=[Bf("psF")])
                    kb.op("S", "S", lambda s, dt=dt: s.activation(
                        bct[:, 0:512], psF[:, :], AF.Sigmoid,
                        scale=-1.0, bias=ndtb[:, dt:dt + 1]),
                        reads=[Bf("psF"), Bf("ndtb")], writes=[Bf("bct")])
                    kb.op("S", "S", lambda s: s.activation(
                        gev[:, 0:512], bct[:, 0:512], AF.Ln),
                        reads=[Bf("bct")], writes=[GEVB[0]])
                    kb.op("Q", "QS", lambda q, b=b, dt=dt, tc=tc: q.dma_start(
                        out=dtlsp[b, dt, :, tc * 512:(tc + 1) * 512],
                        in_=gev[:, 0:512]),
                        reads=[GEVB[0]], writes=[Bf("dtlsp")])
            for n in range(NST):
                kb.op("V", "V", lambda v, n=n: v.memset(
                    dAp[:, n * TQS:n * TQS + 1], 0.0), writes=[Bf("dAp")])
            if b == 0:
                for n in range(NST):
                    kb.op("V", "V", lambda v, n=n: v.memset(
                        dApB[:, n * TQS:n * TQS + 1], 0.0),
                        writes=[Bf("dApB")])

        def bc_tq(b, tq):
            t0 = tq * TQ
            for (dst, dB, co) in ((bcB, Bf("bcB"), 0), (bcC, Bf("bcC"), L)):
                kb.op("V", "V", lambda v: v.memset(
                    hl2[0:16, 0:NST * TQ], 0.0), writes=[Bf("hl2")])
                for n in range(NST):
                    kb.op("Q", "QS", lambda q, n=n, co=co, t0=t0: q.dma_start(
                        out=hl2[n:n + 1, n * TQ:(n + 1) * TQ],
                        in_=bcrows[n:n + 1, co + t0:co + t0 + TQ]),
                        reads=[Bf("bcrows")], writes=[Bf("hl2")])
                for ch in range(4):
                    for c2 in range(2):
                        kb.op("T", "T", lambda t, ch=ch, c2=c2: t.matmul(
                            psH[:, c2 * 512:(c2 + 1) * 512], ones16[:, :],
                            hl2[0:16, ch * 1024 + c2 * 512:
                                ch * 1024 + (c2 + 1) * 512],
                            start=True, stop=True),
                            reads=[Bf("hl2"), Bf("ones16")], writes=[Bf("psH")])
                    kb.op("S", "S", lambda s, dst=dst, ch=ch: s.copy(
                        dst[:, ch * 1024:(ch + 1) * 1024], psH[:, :]),
                        reads=[Bf("psH")], writes=[dB])
            for (dst, dB, rb) in ((rBq, Bf("rBq"), rb_rb), (rCq, Bf("rCq"), rb_rc)):
                kb.op("T", "T", lambda t, rb=rb, t0=t0: t.matmul(
                    psG[:, 0:TQ], ones64, rb[0:1, t0:t0 + TQ],
                    start=True, stop=True),
                    reads=[Bf("rb_rb") if rb is rb_rb else Bf("rb_rc"),
                           Bf("ones64")], writes=[Bf("psG")])
                kb.op("S", "S", lambda s, dst=dst: s.copy(dst[:, :], psG[:, 0:TQ]),
                      reads=[Bf("psG")], writes=[dB])

        def scan_tq(b, tq):
            t0 = tq * TQ
            for dt in range(NDT):
                p = dt % 2
                po = p * TQ
                for (dst, dB, src, sB) in (
                        (dtl_t, DTLB[p], dtlsp, Bf("dtlsp")),
                        (xc_t, XCTB[p], xcsp, Bf("xcsp")),
                        (g_t, GTB[p], gsp, Bf("gsp"))):
                    kb.op("Q", "QS", lambda q, dst=dst, src=src, po=po, b=b, dt=dt,
                          t0=t0: q.dma_start(
                        out=dst[:, po:po + TQ], in_=src[b, dt, :, t0:t0 + TQ]),
                        reads=[sB], writes=[dB])
                kb.op("V", "V", lambda v, po=po: v.tensor_tensor(
                    u_t[:, :], dtl_t[:, po:po + TQ], xc_t[:, po:po + TQ], ALU.mult),
                    reads=[DTLB[p], XCTB[p]], writes=[Bf("u_t")])
                kb.op("V", "V", lambda v: v.tensor_tensor(
                    u_t[:, :], u_t[:, :], rBq[:, :], ALU.mult),
                    reads=[Bf("rBq")], writes=[Bf("u_t")])
                dApX, wlX, hlX = ((dAp, wl1, hl2), (dApB, wl1B, hl2))[dt % 2]
                dApN, wlN, hlN = (("dAp", "wl1", "hl2"),
                                  ("dApB", "wl1B", "hl2"))[dt % 2]
                hcN = f"hcar{dt}"
                kb.op("Q", "QS", lambda q, dt=dt, wlX=wlX: q.dma_start(
                    out=wlX[:, :].rearrange("p (n t) -> p n t", n=NST)
                    [:, :, 0:1],
                    in_=hcar[:, dt * NST:(dt + 1) * NST].unsqueeze(2),
                    allow_slow_non_contiguous=True),
                    reads=[Bf(hcN)], writes=[Bf(wlN)])
                for n in range(NST):
                    kb.op("S", "S", lambda s, po=po, n=n, dt=dt, dApX=dApX:
                          s.activation(
                        dApX[:, n * TQS + 1:n * TQS + 1 + TQ],
                        dtl_t[:, po:po + TQ], AF.Exp,
                        scale=negA[:, dt * NST + n:dt * NST + n + 1]),
                        reads=[DTLB[p], Bf("negA")], writes=[Bf(dApN)])
                uview = u_t[:, :].unsqueeze(1).broadcast_to([128, NST, TQ])
                bview = bcB[:, :].rearrange("p (n t) -> p n t", n=NST)
                wview = wlX[:, :].rearrange("p (n t) -> p n t", n=NST)[:, :, 1:]
                kb.op("V", "V", lambda v, uview=uview, bview=bview, wview=wview:
                      v.tensor_tensor(wview, uview, bview, ALU.mult),
                      reads=[Bf("u_t"), Bf("bcB")], writes=[Bf(wlN)])
                kb.op("V", "V", lambda v, dApX=dApX, wlX=wlX, hlX=hlX:
                      v.tensor_tensor_scan(
                    hlX[:, :], dApX[:, :], wlX[:, :], 0.0, ALU.mult, ALU.add),
                    reads=[Bf(dApN), Bf(wlN)], writes=[Bf(hlN)])
                kb.op("Q", "QS", lambda q, dt=dt, hlX=hlX: q.dma_start(
                    out=hcar[:, dt * NST:(dt + 1) * NST].unsqueeze(2),
                    in_=hlX[:, :].rearrange("p (n t) -> p n t", n=NST)
                    [:, :, TQ:TQ + 1],
                    allow_slow_non_contiguous=True),
                    reads=[Bf(hlN)], writes=[Bf(hcN)])
                cview = bcC[:, :].rearrange("p (n t) -> p n t", n=NST)
                hview = hlX[:, :].rearrange("p (n t) -> p n t", n=NST)[:, :, 1:]
                pview = dApX[:, :].rearrange("p (n t) -> p n t", n=NST)[:, :, 1:]
                kb.op("V", "V", lambda v, pview=pview, hview=hview, cview=cview:
                      v.tensor_tensor(pview, hview, cview, ALU.mult),
                      reads=[Bf(hlN), Bf("bcC")], writes=[Bf(dApN)])
                pv = dApX[:, :].rearrange("p (n t) -> p n t", n=NST)
                lv = wlX[:, :].rearrange("p (n t) -> p n t", n=NST)
                kb.op("V", "V", lambda v, pv=pv, lv=lv: v.tensor_tensor(
                    lv[:, 0:8, 1:], pv[:, 0:16:2, 1:], pv[:, 1:16:2, 1:],
                    ALU.add),
                    reads=[Bf(dApN)], writes=[Bf(wlN)])
                kb.op("V", "V", lambda v, lv=lv: v.tensor_tensor(
                    lv[:, 8:12, 1:], lv[:, 0:8:2, 1:], lv[:, 1:8:2, 1:],
                    ALU.add), writes=[Bf(wlN)])
                kb.op("V", "V", lambda v, lv=lv: v.tensor_tensor(
                    lv[:, 12:14, 1:], lv[:, 8:12:2, 1:], lv[:, 9:12:2, 1:],
                    ALU.add), writes=[Bf(wlN)])
                kb.op("V", "V", lambda v, wlX=wlX: v.tensor_tensor(
                    ys_t[:, :], wlX[:, 12 * TQS + 1:12 * TQS + 1 + TQ],
                    wlX[:, 13 * TQS + 1:13 * TQS + 1 + TQ], ALU.add),
                    reads=[Bf(wlN)], writes=[Bf("ys_t")])
                kb.op("V", "V", lambda v: v.tensor_tensor(
                    ys_t[:, :], ys_t[:, :], rCq[:, :], ALU.mult),
                    reads=[Bf("rCq")], writes=[Bf("ys_t")])
                kb.op("V", "V", lambda v, po=po, dt=dt: v.scalar_tensor_tensor(
                    t2_t[:, :], xc_t[:, po:po + TQ], d_sb[:, dt:dt + 1], ys_t[:, :],
                    ALU.mult, ALU.add),
                    reads=[XCTB[p], Bf("d_sb"), Bf("ys_t")], writes=[Bf("t2_t")])
                kb.op("V", "V", lambda v, po=po: v.tensor_tensor(
                    ygd_t[:, po:po + TQ], t2_t[:, :], g_t[:, po:po + TQ], ALU.mult),
                    reads=[Bf("t2_t"), GTB[p]], writes=[YGDB[p]])
                kb.op("Q", "QS", lambda q, po=po, b=b, dt=dt, t0=t0: q.dma_start(
                    out=yin[b][dt * 128:(dt + 1) * 128, t0:t0 + TQ],
                    in_=ygd_t[:, po:po + TQ]),
                    reads=[YGDB[p]], writes=[Bf(f"yin{b}")])

        def y_rs(b):
            fl = kb.op("Q", "Q", lambda q: q.nop(), reads=[Bf(f"pmix{b}")])
            kb.op("G", f"CCR{b}", lambda g, b=b: g.collective_compute(
                "ReduceScatter", ALU.add, replica_groups=[list(range(NCK))],
                ins=[pmix[b]], outs=[rsout[b]]),
                reads=[Bf(f"pmix{b}")], writes=[Bf(f"rsout{b}")], extra=[fl])

        def ln_stats_T(mslot, mbuf, sqslot, sqbuf):
            for it in range(NKT):
                mo = it * TSL
                kb.op("S", "S", lambda s, mo=mo, mslot=mslot, sqslot=sqslot:
                      s.activation(sqslot[:, mo:mo + TSL], mslot[:, mo:mo + TSL],
                                   AF.Square),
                      reads=[mbuf], writes=[sqbuf])
            for it in range(NKT):
                mo = it * TSL
                kb.op("T", "T", lambda t, mo=mo, mslot=mslot, it=it: t.matmul(
                    psH[0:1, 0:TSL], ones_c[:, :], mslot[:, mo:mo + TSL],
                    start=(it == 0), stop=(it == NKT - 1)),
                    reads=[mbuf, Bf("ones_c")], writes=[Bf("psH")])
            kb.op("S", "S", lambda s: s.copy(c_sum[0:1, 0:TSL], psH[0:1, 0:TSL]),
                  reads=[Bf("psH")], writes=[Bf("c_sum")])
            for it in range(NKT):
                mo = it * TSL
                kb.op("T", "T", lambda t, mo=mo, sqslot=sqslot, it=it: t.matmul(
                    psH[0:1, TSL:2 * TSL], ones_c[:, :], sqslot[:, mo:mo + TSL],
                    start=(it == 0), stop=(it == NKT - 1)),
                    reads=[sqbuf, Bf("ones_c")], writes=[Bf("psH")])
            kb.op("S", "S", lambda s: s.copy(c_ssq[0:1, 0:TSL],
                                             psH[0:1, TSL:2 * TSL]),
                  reads=[Bf("psH")], writes=[Bf("c_ssq")])
            ln_rowmath(TSL)

        def stat_bcast(which):
            pairs = (((rb_rstd, 0), (rb_mrstd, 1)) if which != 1
                     else ((rb_rstd, 2),))
            for rb, pos in pairs:
                rbB = Bf("rb_rstd") if rb is rb_rstd else Bf("rb_mrstd")
                on = ones_r[:, :] if rb is rb_rstd else ones32
                onB = Bf("ones_r") if rb is rb_rstd else Bf("ones32")
                kb.op("T", "T", lambda t, rb=rb, on=on: t.matmul(
                    psG[:, 0:TSL], on, rb[0:1, 0:TSL],
                    start=True, stop=True),
                    reads=[rbB, onB], writes=[Bf("psG")])
                kb.op("S", "S", lambda s, pos=pos: s.copy(
                    bct[:, pos * TSL:(pos + 1) * TSL], psG[:, 0:TSL]),
                    reads=[Bf("psG")], writes=[Bf("bct")])

        def epi_chunk(b, part):
            if part in (0, 1):
                for sub in range(2):
                    tch = part * 2 + sub
                    t0 = tch * 512
                    rp = tch % 2
                    ro = rp * 4096
                    for dt in range(NDT):
                        kb.op("Q", "QS", lambda q, dt=dt, b=b, t0=t0, ro=ro:
                              q.dma_start(
                            out=SLOT[0][:, ro + dt * 512:ro + (dt + 1) * 512],
                            in_=yin[b][dt * 128:(dt + 1) * 128, t0:t0 + 512]),
                            reads=[Bf(f"yin{b}")], writes=[SB4[rp] if False
                                                           else SB[0]])
                    for htg in range(8):
                        wp = (tch * 8 + htg) % 2
                        wof = wp * 4096
                        kb.op("Q", "QS", lambda q, htg=htg, wof=wof: q.dma_start(
                            out=SLOT[4][:, wof:wof + 4096], in_=wos[htg]),
                            reads=[Bf("wos")], writes=[SB4[wp]])
                        for hh in range(4):
                            pst, pB = ((psA, Bf("psA")), (psB, Bf("psB")),
                                       (psF, Bf("psF")), (psI, Bf("psI")))[hh]
                            for dt in range(NDT):
                                kb.op("T", "T", lambda t, pst=pst, wof=wof, dt=dt,
                                      hh=hh, ro=ro: t.matmul(
                                    pst[:, :],
                                    SLOT[4][:, wof + dt * 512 + hh * 128:
                                            wof + dt * 512 + (hh + 1) * 128],
                                    SLOT[0][:, ro + dt * 512:ro + (dt + 1) * 512],
                                    start=(dt == 0), stop=(dt == NDT - 1)),
                                    reads=[SB[0], SB4[wp]], writes=[pB])
                            ht = htg * 4 + hh
                            gp = hh % 2
                            go = gp * 512
                            kb.op("S", "S", lambda s, pst=pst, go=go: s.copy(
                                gev[:, go:go + 512], pst[:, :]),
                                reads=[pB], writes=[GEVB[gp]])
                            for jj in range(2):
                                jb = tch * 2 + jj
                                kb.op("Q", "QS", lambda q, b=b, jb=jb, ht=ht,
                                      go=go, jj=jj: q.dma_start(
                                    out=pmix[b][jb, ht * 128:(ht + 1) * 128, :],
                                    in_=gev[:, go + jj * 256:go + (jj + 1) * 256]),
                                    reads=[GEVB[gp]], writes=[Bf(f"pmix{b}")])
            if part == 2:
                for ht in range(NKT):
                    mo = ht * TSL
                    kb.op("Q", "QS", lambda q, ht=ht, mo=mo, b=b: q.dma_start(
                        out=SLOT[2][:, mo:mo + TSL],
                        in_=rsout[b][ht * 128:(ht + 1) * 128, :]),
                        reads=[Bf(f"rsout{b}")], writes=[SB[2]])
            if part == 2:
                ln_stats_T(SLOT[2], SB[2], SLOT[1], SB[1])
                stat_bcast(0)
                for ht in range(NKT):
                    p = ht % 2
                    po = p * TSL
                    mo = ht * TSL
                    kb.op("Q", "QS", lambda q, ht=ht, b=b: q.dma_start(
                        out=x0t[:, :],
                        in_=hs_resT[b, ht * 128:(ht + 1) * 128, :]),
                        writes=[X0B[0]])
                    kb.op("V", "V", lambda v, po=po, mo=mo: v.tensor_tensor(
                        cacc[:, po:po + TSL], SLOT[2][:, mo:mo + TSL],
                        bct[:, 0:TSL], ALU.mult),
                        reads=[SB[2], Bf("bct")], writes=[CAB[p]])
                    kb.op("V", "V", lambda v, po=po: v.scalar_tensor_tensor(
                        cacc[:, po:po + TSL], cacc[:, po:po + TSL], 1.0,
                        bct[:, TSL:2 * TSL], ALU.mult, ALU.subtract),
                        reads=[Bf("bct")], writes=[CAB[p]])
                    kb.op("V", "V", lambda v, po=po: v.tensor_tensor(
                        x1t[:, po:po + TSL], cacc[:, po:po + TSL],
                        x0t[:, :], ALU.add),
                        reads=[CAB[p], X0B[0]], writes=[X1B[p]])
                    kb.op("Q", "QS", lambda q, po=po, ht=ht, b=b: q.dma_start(
                        out=x1sp[b, ht * 128:(ht + 1) * 128, :],
                        in_=x1t[:, po:po + TSL]),
                        reads=[X1B[p]], writes=[Bf("x1sp")])
                for it in range(NKT):
                    o = it * TSL
                    kb.op("G", "QG", lambda g, it=it, o=o, b=b: g.dma_start(
                        out=SLOT[0][:, o:o + TSL],
                        in_=x1sp[b, it * 128:(it + 1) * 128, :]),
                        reads=[Bf("x1sp")], writes=[SB[0]])
                ln_stats_T(SLOT[0], SB[0], SLOT[1], SB[1])
                stat_bcast(1)
                kb.op("V", "V", lambda v: v.tensor_scalar_mul(
                    rb_tmp[0:1, 0:TSL], c_m[0:1, 0:TSL], -1.0),
                    reads=[Bf("c_m")], writes=[Bf("rb_tmp")])
                kb.op("V", "V", lambda v: v.tensor_tensor(
                    aug2r[0:1, :], rb_tmp[0:1, 0:TSL], c_ssq[0:1, 0:TSL], ALU.mult),
                    reads=[Bf("rb_tmp"), Bf("c_ssq")], writes=[Bf("aug2r")])
                kb.op("G", "QG", lambda g: g.dma_start(
                    out=aug2r[1:2, :], in_=c_ssq[0:1, 0:TSL]),
                    reads=[Bf("c_ssq")], writes=[Bf("aug2r")])
                gates_lora(b)
            if part == 3:
                ffn_and_out(b)

        def gates_lora(b):
            for it in range(NKT):
                o = it * TSL
                kb.op("T", "T", lambda t, it=it, o=o: t.matmul(
                    psH[0:36, 0:TSL], awr_sb[:, it * 36:(it + 1) * 36],
                    SLOT[0][:, o:o + TSL], start=(it == 0), stop=False),
                    reads=[SB[0], Bf("awr_sb")], writes=[Bf("psH")])
            kb.op("T", "T", lambda t: t.matmul(
                psH[0:36, 0:TSL], colsA[:, :], aug2r[0:1, :],
                start=False, stop=True),
                reads=[Bf("colsA"), Bf("aug2r")], writes=[Bf("psH")])
            kb.op("V", "V", lambda v: v.tensor_copy(scr36[:, :], psH[0:36, 0:TSL]),
                  reads=[Bf("psH")], writes=[Bf("scr36")])
            kb.op("T", "T", lambda t: t.matmul(
                psG[0:4, 0:TSL], sel2[:, :], scr36[:, :], start=True, stop=True),
                reads=[Bf("sel2"), Bf("scr36")], writes=[Bf("psG")])
            kb.op("V", "V", lambda v: v.tensor_copy(gl4[:, 0:TSL], psG[0:4, 0:TSL]),
                  reads=[Bf("psG")], writes=[Bf("gl4")])
            kb.op("T", "T", lambda t: t.matmul(
                psG[0:4, 0:TSL], ones_r[:, 0:4], rb_rstd[0:1, 0:TSL],
                start=True, stop=True),
                reads=[Bf("rb_rstd"), Bf("ones_r")], writes=[Bf("psG")])
            kb.op("V", "V", lambda v: v.tensor_copy(bct4[:, :], psG[0:4, 0:TSL]),
                  reads=[Bf("psG")], writes=[Bf("bct4")])
            kb.op("V", "V", lambda v: v.tensor_tensor(
                gl4[:, 0:TSL], gl4[:, 0:TSL], bct4[:, :], ALU.mult),
                reads=[Bf("bct4")], writes=[Bf("gl4")])
            kb.op("S", "S", lambda s: s.activation(
                gb4[:, 0:TSL], gl4[:, 0:TSL], AF.Exp),
                reads=[Bf("gl4")], writes=[Bf("gb4")])
            kb.op("T", "T", lambda t: t.matmul(
                psG[0:1, 0:TSL], ones_c[0:4, :], gb4[:, 0:TSL],
                start=True, stop=True),
                reads=[Bf("gb4"), Bf("ones_c")], writes=[Bf("psG")])
            kb.op("V", "V", lambda v: v.tensor_copy(
                gl4[0:1, TSL:2 * TSL], psG[0:1, 0:TSL]),
                reads=[Bf("psG")], writes=[Bf("gl4")])
            kb.op("V", "V", lambda v: v.reciprocal(
                gl4[0:1, TSL:2 * TSL], gl4[0:1, TSL:2 * TSL]),
                reads=[Bf("gl4")], writes=[Bf("gl4")])
            kb.op("V", "V", lambda v: v.tensor_copy(
                rb_tmp[0:1, 0:TSL], gl4[0:1, TSL:2 * TSL]),
                reads=[Bf("gl4")], writes=[Bf("rb_tmp")])
            kb.op("T", "T", lambda t: t.matmul(
                psG[0:4, 0:TSL], ones_r[:, 0:4], rb_tmp[0:1, 0:TSL],
                start=True, stop=True),
                reads=[Bf("rb_tmp"), Bf("ones_r")], writes=[Bf("psG")])
            kb.op("V", "V", lambda v: v.tensor_copy(bct4[:, :], psG[0:4, 0:TSL]),
                  reads=[Bf("psG")], writes=[Bf("bct4")])
            kb.op("V", "V", lambda v: v.tensor_tensor(
                gb4[:, TSL:2 * TSL], gb4[:, 0:TSL], bct4[:, :], ALU.mult),
                reads=[Bf("bct4")], writes=[Bf("gb4")])
            kb.op("T", "T", lambda t: t.matmul(
                psH[0:32, 0:TSL], sel[:, :], gb4[:, TSL:2 * TSL],
                start=True, stop=True),
                reads=[Bf("sel"), Bf("gb4")], writes=[Bf("psH")])
            kb.op("V", "V", lambda v: v.tensor_copy(g32[:, :], psH[0:32, 0:TSL]),
                  reads=[Bf("psH")], writes=[Bf("g32")])
            kb.op("V", "V", lambda v: v.tensor_tensor(
                prs[:, :], scr36[0:32, :], g32[:, :], ALU.mult),
                reads=[Bf("scr36"), Bf("g32")], writes=[Bf("prs")])

        def ffn_and_out(b):
            for ot in range(NKT):
                pw = ot % 2
                ws, wof = SLOT[4], pw * 4096
                wb = SB4[pw]
                kb.op("Q", "QS", lambda q, ot=ot, ws=ws, wof=wof: q.dma_start(
                    out=ws[:, wof:wof + 4096], in_=wf2[ot]),
                    reads=[Bf("wf2")], writes=[wb])
                kb.op("G", "QG", lambda g, ot=ot, pw=pw: g.dma_start(
                    out=bmt_t[:, pw * 128:(pw + 1) * 128],
                    in_=fBmT[:, ot * 128:(ot + 1) * 128]),
                    writes=[Bf("bmt_t")])
                for it in range(NKT):
                    o = it * TSL
                    kb.op("T", "T", lambda t, ws=ws, wof=wof, it=it, o=o: t.matmul(
                        psA[:, 0:TSL], ws[:, wof + it * 128:wof + (it + 1) * 128],
                        SLOT[0][:, o:o + TSL], start=(it == 0), stop=False),
                        reads=[SB[0], wb], writes=[Bf("psA")])
                kb.op("T", "T", lambda t, ot=ot: t.matmul(
                    psA[:, 0:TSL], aug2[:, ot * 128:(ot + 1) * 128],
                    aug2r[:, :], start=False, stop=False),
                    reads=[Bf("aug2"), Bf("aug2r")], writes=[Bf("psA")])
                kb.op("T", "T", lambda t, ot=ot, pw=pw: t.matmul(
                    psA[:, 0:TSL], bmt_t[:, pw * 128:(pw + 1) * 128], prs[:, :],
                    start=False, stop=True),
                    reads=[Bf("bmt_t"), Bf("prs")], writes=[Bf("psA")])
                mo = ot * TSL
                kb.op("V", "V", lambda v, mo=mo: v.tensor_tensor(
                    SLOT[2][:, mo:mo + TSL], psA[:, 0:TSL],
                    bct[:, 2 * TSL:3 * TSL], ALU.mult),
                    reads=[Bf("psA"), Bf("bct")], writes=[SB[2]])
            ln_stats_T(SLOT[2], SB[2], SLOT[1], SB[1])
            stat_bcast(2)
            for ht in range(NKT):
                p = ht % 2
                po = p * TSL
                mo = ht * TSL
                kb.op("G", "QG", lambda g, po=po, ht=ht, b=b: g.dma_start(
                    out=x1t[:, po:po + TSL],
                    in_=x1sp[b, ht * 128:(ht + 1) * 128, :]),
                    reads=[Bf("x1sp")], writes=[X1B[p]])
                kb.op("V", "V", lambda v, po=po, mo=mo: v.tensor_tensor(
                    cacc[:, po:po + TSL], SLOT[2][:, mo:mo + TSL],
                    bct[:, 0:TSL], ALU.mult),
                    reads=[SB[2], Bf("bct")], writes=[CAB[p]])
                kb.op("V", "V", lambda v, po=po: v.scalar_tensor_tensor(
                    cacc[:, po:po + TSL], cacc[:, po:po + TSL], 1.0,
                    bct[:, TSL:2 * TSL], ALU.mult, ALU.subtract),
                    reads=[Bf("bct")], writes=[CAB[p]])
                kb.op("V", "V", lambda v, po=po: v.tensor_tensor(
                    cacc[:, po:po + TSL], cacc[:, po:po + TSL],
                    x1t[:, po:po + TSL], ALU.add),
                    reads=[X1B[p]], writes=[CAB[p]])
                kb.op("Q", "QS", lambda q, po=po, ht=ht, b=b: q.dma_start(
                    out=out_ext[b, ht * 128:(ht + 1) * 128, :],
                    in_=cacc[:, po:po + TSL]),
                    reads=[CAB[p]], writes=[Bf("out")])

        for part in range(4):
            p1_chunk(0, part)
        p1_ar(0)
        for b in range(B):
            rows_dt(b)
            bc_tq(b, 0)
            scan_tq(b, 0)
            bc_tq(b, 1)
            scan_tq(b, 1)
            if b >= 1:
                epi_chunk(b - 1, 0)
            bc_tq(b, 2)
            scan_tq(b, 2)
            if b >= 1:
                epi_chunk(b - 1, 1)
                y_rs(b - 1)
            bc_tq(b, 3)
            scan_tq(b, 3)
            if b >= 1:
                epi_chunk(b - 1, 2)
                epi_chunk(b - 1, 3)
            bc_tq(b, 4)
            scan_tq(b, 4)
            bc_tq(b, 5)
            scan_tq(b, 5)
            if b + 1 < B:
                p1_chunk(b + 1, 0)
            bc_tq(b, 6)
            scan_tq(b, 6)
            if b + 1 < B:
                p1_chunk(b + 1, 1)
            bc_tq(b, 7)
            scan_tq(b, 7)
            if b + 1 < B:
                p1_chunk(b + 1, 2)
                p1_chunk(b + 1, 3)
                p1_ar(b + 1)
        epi_chunk(B - 1, 0)
        epi_chunk(B - 1, 1)
        y_rs(B - 1)
        epi_chunk(B - 1, 2)
        epi_chunk(B - 1, 3)

        kb.op("G", "G", lambda g: g.nop(),
              extra=[x for x in (bufs["out"].w,) if x is not None])

        kb.finalize()

        with contextlib.ExitStack() as st2:
            sems = {}
            ccs = ["CCL"] + [f"CCA{b}" for b in range(B)] + \
                  [f"CCR{b}" for b in range(B)]
            pools = [f"{q}{i}" for q, n in KB.POOLS.items() for i in range(n)]
            for s in ["T", "V", "S", "G", "Q", "QS", "QA", "QG"] + ccs + pools:
                sems[s] = st2.enter_context(nc.semaphore(f"sem_{s}"))
            with nc.Block() as block:
                @block.tensor
                def _(e):
                    kb.emit("T", e, sems)

                @block.vector
                def _(e):
                    kb.emit("V", e, sems)

                @block.scalar
                def _(e):
                    kb.emit("S", e, sems)

                @block.gpsimd
                def _(e):
                    kb.emit("G", e, sems)

                @block.sync
                def _(e):
                    kb.emit("Q", e, sems)
    return nc


_NC_CACHE = None


def kernel(**inputs):
    global _NC_CACHE
    f32 = np.float32

    hs = np.asarray(inputs["L0_attn_hidden_states"], f32)
    W1 = np.asarray(inputs["L0_attn_in_proj_w"], f32)
    convw = np.asarray(inputs["L0_attn_conv_w"], f32)
    convb = np.asarray(inputs["L0_attn_conv_b"], f32)
    xpw = np.asarray(inputs["L0_attn_x_proj_w"], f32)
    dtp = np.asarray(inputs["L0_attn_dt_proj_w"], f32)
    dtbias = np.asarray(inputs["L0_attn_dt_bias"], f32)
    alog = np.asarray(inputs["L0_attn_A_log"], f32)
    dparam = np.asarray(inputs["L0_attn_D_param"], f32)
    wo = np.asarray(inputs["L0_attn_out_proj_w"], f32)
    dtlnw = np.asarray(inputs["L0_attn_dt_ln_w"], f32)
    blnw = np.asarray(inputs["L0_attn_B_ln_w"], f32)
    clnw = np.asarray(inputs["L0_attn_C_ln_w"], f32)
    wf = np.asarray(inputs["L0_ffn_W"], f32)
    fbv = np.asarray(inputs["L0_ffn_b"], f32)
    fA = np.asarray(inputs["L0_ffn_A"], f32)
    fBm = np.asarray(inputs["L0_ffn_Bm"], f32)
    fWr = np.asarray(inputs["L0_ffn_Wr"], f32)

    hsT = np.ascontiguousarray(hs.reshape(B * L, H).T)
    woT_full = np.ascontiguousarray(wo.T)
    wfT_full = np.ascontiguousarray(wf.T)
    wf_sh = np.ascontiguousarray(
        wfT_full.reshape(32, 128, 32, 128).transpose(2, 1, 0, 3)
        .reshape(32, 128, 4096))
    dtpT_full = np.ascontiguousarray(dtp.T)
    xpwT_full = np.ascontiguousarray(xpw.T)
    fAWr = np.concatenate([fA.reshape(32, H), fWr], axis=0)
    fAWrT = np.ascontiguousarray(fAWr.T)
    fBmT = np.ascontiguousarray(fBm.transpose(0, 2, 1).reshape(32, H))

    in_maps = []
    for k in range(NCK):
        dsl = slice(k * DSH, (k + 1) * DSH)
        res = np.stack([
            np.ascontiguousarray(hsT[:, b * L + k * TSL:b * L + (k + 1) * TSL])
            for b in range(B)])
        w1k = np.concatenate(
            [W1[:, dsl], W1[:, DIN + k * DSH:DIN + (k + 1) * DSH]], axis=1)
        w1_sh = np.ascontiguousarray(
            w1k.reshape(32, 128, 16, 128).transpose(2, 1, 0, 3)
            .reshape(16, 128, 4096))
        wo_sh = np.ascontiguousarray(
            woT_full[dsl].reshape(8, 128, 8, 512).transpose(2, 1, 0, 3)
            .reshape(8, 128, 4096))
        in_maps.append({
            "hs_prepT": np.ascontiguousarray(hsT[:, k * 1024:(k + 1) * 1024]),
            "hs_resT": res,
            "w1": w1_sh,
            "woTs": wo_sh,
            "wfT": wf_sh,
            "convw": np.ascontiguousarray(convw[dsl]),
            "convb": np.ascontiguousarray(convb[dsl]),
            "dtbias": np.ascontiguousarray(dtbias[dsl]),
            "dparam": np.ascontiguousarray(dparam[dsl]),
            "alog": np.ascontiguousarray(alog[dsl]),
            "xpwT": np.ascontiguousarray(xpwT_full[dsl]),
            "dtpT": np.ascontiguousarray(dtpT_full[:, dsl]),
            "dtlnw": dtlnw, "blnw": blnw, "clnw": clnw,
            "fb": fbv, "fAWrT": fAWrT, "fBmT": fBmT,
        })

    if _NC_CACHE is None:
        _NC_CACHE = build_program()
    res = run_bass_kernel_spmd(_NC_CACHE, in_maps, core_ids=list(range(NCK)))
    full = np.empty((B, L, H), np.float32)
    for k in range(NCK):
        full[:, k * TSL:(k + 1) * TSL, :] = res.results[k]["out"].transpose(0, 2, 1)
    return full


# version 12
# speedup vs baseline: 1.1187x; 1.0816x over previous
import numpy as np
from collections import defaultdict
import contextlib

import concourse.bass as bass
import concourse.mybir as mybir
from concourse.bass_utils import run_bass_kernel_spmd

F32 = mybir.dt.float32
BF16 = mybir.dt.bfloat16
U32 = mybir.dt.uint32
AF = mybir.ActivationFunctionType
ALU = mybir.AluOpType

NCK = 8
H = 4096
DIN = 8192
DSH = DIN // NCK
NDT = DSH // 128
NST = 16
R = 256
E3 = 288
B = 4
L = 2048
TSL = L // NCK
NKT = H // 128
EPS_LN = 1e-5
EPS_RMS = 1e-6
SCALE = 2.0
TQ = 256
TQS = TQ + 1
NTQ = L // TQ


class Buf:
    __slots__ = ("name", "w", "r")

    def __init__(self, name):
        self.name = name
        self.w = None
        self.r = []


class KB:
    CSTREAMS = ("T", "V", "S", "G", "Q")

    def __init__(self):
        self.ops = []

    def op(self, eng, stream, fn, reads=(), writes=(), extra=()):
        idx = len(self.ops)
        waits = set(e for e in extra if e is not None)
        for b in reads:
            if b.w is not None:
                waits.add(b.w)
        for b in writes:
            if b.w is not None:
                waits.add(b.w)
            waits.update(b.r)
        for b in writes:
            b.w = idx
            b.r = []
        for b in reads:
            b.r.append(idx)
        amt = 16 if stream in ("QS", "QA", "QG") else 1
        self.ops.append([eng, stream, fn, waits, amt])
        return idx

    POOLS = {"QS": 24, "QG": 12, "QA": 4}
    STRICT = False

    def finalize(self):
        used = set()
        for _, _, _, waits, _ in self.ops:
            used.update(waits)
        val = {}
        cnt = defaultdict(int)
        pool_ctr = defaultdict(int)
        self.pre_wait = {}
        self.inc_flag = []
        for idx, (eng, stream, fn, waits, amt) in enumerate(self.ops):
            if stream in self.POOLS:
                n = self.POOLS[stream]
                c = pool_ctr[stream]
                pool_ctr[stream] += 1
                sem_name = f"{stream}{c % n}"
                u = c // n
                self.pre_wait[idx] = (sem_name, 16 * u)
                val[idx] = (sem_name, 16 * (u + 1))
                self.inc_flag.append(True)
                cnt[sem_name] = 16 * (u + 1)
            else:
                keep = self.STRICT or (stream not in self.CSTREAMS) or (idx in used)
                self.inc_flag.append(keep)
                if keep:
                    cnt[stream] += amt
                    val[idx] = (stream, cnt[stream])
        self.ev_val = val
        self.final_counts = dict(cnt)

    def emit(self, eng_key, handle, sems):
        waited = defaultdict(int)
        for idx, (eng, stream, fn, waits, amt) in enumerate(self.ops):
            if eng != eng_key:
                continue
            pw = self.pre_wait.get(idx)
            if pw is not None and pw[1] > 0 and waited[pw[0]] < pw[1]:
                handle.wait_ge(sems[pw[0]], pw[1])
                waited[pw[0]] = pw[1]
            for wev in sorted(waits):
                wsv = self.ev_val.get(wev)
                if wsv is None:
                    continue
                ws, wv = wsv
                if (not self.STRICT) and ws == eng and ws in self.CSTREAMS:
                    continue
                if waited[ws] >= wv:
                    continue
                handle.wait_ge(sems[ws], wv)
                waited[ws] = wv
            ins = fn(handle)
            if self.inc_flag[idx]:
                s, v = self.ev_val[idx]
                ins.then_inc(sems[s], amt)


def build_program():
    nc = bass.Bass("TRN2", target_bir_lowering=False, debug=False)

    def ein(name, shp, dt=F32):
        return nc.dram_tensor(name, shp, dt, kind="ExternalInput").ap()

    hs_prepT = ein("hs_prepT", [H, 1024])
    hs_resT = ein("hs_resT", [B, H, TSL])
    w1 = ein("w1", [16, 128, 4096])
    woTs = ein("woTs", [8, 128, 4096])
    wfT = ein("wfT", [32, 128, 4096])
    convw = ein("convw", [DSH, 4])
    convb = ein("convb", [DSH])
    dtbias = ein("dtbias", [DSH])
    dparam = ein("dparam", [DSH])
    alog = ein("alog", [DSH, NST])
    xpwT = ein("xpwT", [DSH, E3])
    dtpT = ein("dtpT", [R, DSH])
    dtlnw = ein("dtlnw", [R])
    blnw = ein("blnw", [NST])
    clnw = ein("clnw", [NST])
    fbi = ein("fb", [H])
    fAWrT = ein("fAWrT", [H, 36])
    fBmT = ein("fBmT", [32, H])
    out_ext = nc.dram_tensor("out", [B, H, TSL], F32, kind="ExternalOutput").ap()

    dr = lambda n, shp, dt: nc.dram_tensor(n, shp, dt).ap()
    ln_in = dr("ln_in", [H, 1024], BF16)
    ln_all = dr("ln_all", [NCK, H, 1024], BF16)
    w1t = dr("w1t", [16, 128, 4096], BF16)
    wos = dr("wos", [8, 128, 4096], BF16)
    wf2 = dr("wf2", [NKT, 128, 4096], BF16)
    arin = [dr(f"arin{b}", [E3, L], F32) for b in range(B)]
    arout = [dr(f"arout{b}", [E3, L], F32) for b in range(B)]
    yin = [dr(f"yin{b}", [DSH, L], BF16) for b in range(B)]
    pmix = [dr(f"pmix{b}", [NCK, H, TSL], BF16) for b in range(B)]
    rsout = [dr(f"rsout{b}", [H, TSL], BF16) for b in range(B)]
    xcsp = dr("xcsp", [B, NDT, 128, L], BF16)
    gsp = dr("gsp", [B, NDT, 128, L], BF16)
    dtlsp = dr("dtlsp", [B, NDT, 128, L], BF16)
    x1sp = dr("x1sp", [B, H, TSL], F32)

    kb = KB()
    st = contextlib.ExitStack()
    T = {}

    def sb(name, shp, dt):
        t = st.enter_context(nc.sbuf_tensor(name, shp, dt))
        T[name] = t
        return t

    def psm(name, shp):
        t = st.enter_context(nc.psum_tensor(name, shp, F32))
        T[name] = t
        return t

    with st:
        SLOT = [sb(f"sl{i}", [128, 8192], BF16) for i in range(5)]
        dAp = sb("dAp", [128, NST * TQS], BF16)
        wl1 = sb("wl1", [128, NST * TQS], BF16)
        hl2 = sb("hl2", [128, NST * TQS], BF16)
        dApB = sb("dApB", [128, NST * TQS], BF16)
        wl1B = sb("wl1B", [128, NST * TQS], BF16)
        hl2B = hl2
        bcB = sb("bcB", [128, NST * TQ], BF16)
        bcC = sb("bcC", [128, NST * TQ], BF16)
        dtl_t = sb("dtl_t", [128, 2 * TQ], BF16)
        xc_t = sb("xc_t", [128, 2 * TQ], BF16)
        g_t = sb("g_t", [128, 2 * TQ], BF16)
        u_t = sb("u_t", [128, TQ], BF16)
        ys_t = sb("ys_t", [128, TQ], F32)
        t2_t = sb("t2_t", [128, TQ], BF16)
        ygd_t = sb("ygd_t", [128, 2 * TQ], BF16)
        hcar = sb("hcar", [128, NDT * NST], BF16)
        xconv = sb("xconv", [128, 2 * (TQ + 3)], BF16)
        tails = sb("tails", [128, NDT * 3], BF16)
        cacc = sb("cacc", [128, 2 * TQ], F32)
        xcw = sb("xcw", [128, TQ], BF16)
        gev = sb("gev", [128, 2 * 512], BF16)
        xcr = sb("xcr", [128, 512], BF16)
        spsc = sb("spsc", [96, 512], F32)
        rowpack = sb("rowpack", [65, 2 * L + 128 + 1024 + 128], BF16)
        bcrows = rowpack[0:16, :]
        spd = sb("spd", [128, 1024], BF16)
        c_sum = sb("c_sum", [1, 512], F32)
        c_ssq = sb("c_ssq", [1, 512], F32)
        c_m = sb("c_m", [1, 512], F32)
        s_scr = sb("s_scr", [1, 512], F32)
        rb_rstd = rowpack[0:1, 4224:5248]
        rb_mrstd = rowpack[32:33, 4224:5248]
        ones32 = rowpack[32:33, 5248:5376]
        rb_rb = rowpack[64:65, 0:L]
        rb_rc = rowpack[64:65, L:2 * L]
        ones64 = rowpack[64:65, 2 * L:2 * L + 128]
        rb_tmp = sb("rb_tmp", [1, TSL], BF16)
        gl4 = sb("gl4", [4, 2 * TSL], F32)
        gb4 = sb("gb4", [4, 2 * TSL], BF16)
        bct4 = sb("bct4", [4, TSL], F32)
        g32 = sb("g32", [32, TSL], BF16)
        prs = sb("prs", [32, TSL], BF16)
        scr36 = sb("scr36", [36, TSL], BF16)
        negA = sb("negA", [128, NDT * NST], F32)
        cw_sb = sb("cw_sb", [128, NDT * 4], F32)
        cb_sb = sb("cb_sb", [128, NDT], F32)
        ndtb = sb("ndtb", [128, NDT], F32)
        d_sb = sb("d_sb", [128, NDT], F32)
        dtlnw_sb = sb("dtlnw_sb", [128, 2], F32)
        bclnw = sb("bclnw", [16, 2], F32)
        dtp_sb = sb("dtp_sb", [128, 16 * 128], BF16)
        xpw_sb = sb("xpw_sb", [128, NDT * E3], BF16)
        awr_sb = sb("awr_sb", [128, NKT * 36], BF16)
        bmt_t = sb("bmt_t", [32, 2 * 128], BF16)
        aug2r = sb("aug2r", [2, TSL], BF16)
        colsA = sb("colsA", [1, 36], BF16)
        ones_c = sb("ones_c", [128, 1], BF16)
        ones_r = sb("ones_r", [1, 128], BF16)
        ones16 = sb("ones16", [16, 128], BF16)

        srow = sb("srow", [1, TQ], BF16)
        sel = sb("sel", [4, 32], BF16)
        sel2 = sb("sel2", [36, 4], BF16)
        x0t = sb("x0t", [128, TSL], F32)
        x1t = sb("x1t", [128, 2 * TSL], F32)
        bct = sb("bct", [128, 3 * TSL], F32)
        aug2 = sb("aug2", [2, H], BF16)
        dmask = sb("dmask", [16, NST], BF16)

        psA = psm("psA", [128, 512])
        psB = psm("psB", [128, 512])
        psE = psm("psE", [96, 512])
        psF = psm("psF", [128, 512])
        psG = psm("psG", [128, 512])
        psH = psm("psH", [128, 1024])
        psI = psm("psI", [128, 512])

        bufs = {}

        def Bf(n):
            if n not in bufs:
                bufs[n] = Buf(n)
            return bufs[n]

        for n in list(T):
            Bf(n)
        for n in ("ln_in", "ln_all", "w1t", "wos", "wf2", "xcsp", "gsp",
                  "dtlsp", "x1sp", "out", "sl4a", "sl4b",
                  "dtl0", "dtl1", "xct0", "xct1", "gt0", "gt1", "ygd0", "ygd1",
                  "x0t0", "x0t1", "x1t0", "x1t1", "ca0", "ca1", "gev0", "gev1",
                  "xcr0", "xcr1"):
            Bf(n)
        for b in range(B):
            for n in ("arin", "arout", "yin", "pmix", "rsout"):
                Bf(f"{n}{b}")
        SB = [Bf(f"sl{i}") for i in range(5)]
        SB4 = [Bf("sl4a"), Bf("sl4b")]
        DTLB = [Bf("dtl0"), Bf("dtl1")]
        XCTB = [Bf("xct0"), Bf("xct1")]
        GTB = [Bf("gt0"), Bf("gt1")]
        YGDB = [Bf("ygd0"), Bf("ygd1")]
        X0B = [Bf("x0t0"), Bf("x0t1")]
        X1B = [Bf("x1t0"), Bf("x1t1")]
        CAB = [Bf("ca0"), Bf("ca1")]
        GEVB = [Bf("gev0"), Bf("gev1")]
        XCRB = [Bf("xcr0"), Bf("xcr1")]
        spcB = wl1[0:16, 0:L]
        spcC = wl1[0:16, L:2 * L]

        kb.op("V", "V", lambda v: v.memset(ones_c[:, :], 1.0), writes=[Bf("ones_c")])
        kb.op("V", "V", lambda v: v.memset(ones_r[:, :], 1.0), writes=[Bf("ones_r")])
        kb.op("V", "V", lambda v: v.memset(ones64, 1.0), writes=[Bf("ones64")])
        kb.op("V", "V", lambda v: v.memset(ones32, 1.0), writes=[Bf("ones32")])
        kb.op("V", "V", lambda v: v.memset(ones16[:, :], 1.0), writes=[Bf("ones16")])
        kb.op("V", "V", lambda v: v.memset(tails[:, :], 0.0), writes=[Bf("tails")])
        kb.op("V", "V", lambda v: v.memset(sel[:, :], 0.0), writes=[Bf("sel")])
        kb.op("V", "V", lambda v: v.memset(dmask[:, :], 0.0), writes=[Bf("dmask")])
        kb.op("V", "V", lambda v: v.memset(sel2[:, :], 0.0), writes=[Bf("sel2")])
        kb.op("V", "V", lambda v: v.memset(srow[:, :], 1.0), writes=[Bf("srow")])
        for e in range(4):
            kb.op("G", "QG", lambda g, e=e: g.dma_start(
                out=sel[e:e + 1, e * 8:(e + 1) * 8].rearrange("o f -> o f") if False
                else sel[e:e + 1, e * 8:(e + 1) * 8],
                in_=srow[0:1, 0:8]), reads=[Bf("srow")], writes=[Bf("sel")])
            kb.op("G", "QG", lambda g, e=e: g.dma_start(
                out=sel2[32 + e:33 + e, e:e + 1], in_=srow[0:1, 0:1]),
                reads=[Bf("srow")], writes=[Bf("sel2")])
        for n in range(NST):
            kb.op("G", "QG", lambda g, n=n: g.dma_start(
                out=dmask[n:n + 1, n:n + 1], in_=srow[0:1, 0:1]),
                reads=[Bf("srow")], writes=[Bf("dmask")])
        kb.op("V", "V", lambda v: v.tensor_scalar_mul(sel[:, :], sel[:, :], SCALE),
              reads=[Bf("sel")], writes=[Bf("sel")])

        for dt in range(NDT):
            kb.op("G", "QG", lambda g, dt=dt: g.dma_start(
                out=negA[:, dt * NST:(dt + 1) * NST],
                in_=alog[dt * 128:(dt + 1) * 128, :]), writes=[Bf("negA")])
            kb.op("G", "QG", lambda g, dt=dt: g.dma_start(
                out=cw_sb[:, dt * 4:(dt + 1) * 4],
                in_=convw[dt * 128:(dt + 1) * 128, :]), writes=[Bf("cw_sb")])
            kb.op("G", "QG", lambda g, dt=dt: g.dma_start(
                out=cb_sb[:, dt:dt + 1],
                in_=convb[dt * 128:(dt + 1) * 128].unsqueeze(1)), writes=[Bf("cb_sb")])
            kb.op("G", "QG", lambda g, dt=dt: g.dma_start(
                out=ndtb[:, dt:dt + 1],
                in_=dtbias[dt * 128:(dt + 1) * 128].unsqueeze(1)), writes=[Bf("ndtb")])
            kb.op("G", "QG", lambda g, dt=dt: g.dma_start(
                out=d_sb[:, dt:dt + 1],
                in_=dparam[dt * 128:(dt + 1) * 128].unsqueeze(1)), writes=[Bf("d_sb")])
            for ec in range(3):
                kb.op("G", "QG", lambda g, dt=dt, ec=ec: g.dma_start(
                    out=xpw_sb[:, dt * E3 + ec * 96:dt * E3 + (ec + 1) * 96],
                    in_=xpwT[dt * 128:(dt + 1) * 128, ec * 96:(ec + 1) * 96]),
                    writes=[Bf("xpw_sb")])
        kb.op("S", "S", lambda s: s.activation(negA[:, :], negA[:, :], AF.Exp),
              reads=[Bf("negA")], writes=[Bf("negA")])
        kb.op("V", "V", lambda v: v.tensor_scalar_mul(ndtb[:, :], ndtb[:, :], -1.0),
              reads=[Bf("ndtb")], writes=[Bf("ndtb")])
        for c in range(2):
            kb.op("G", "QG", lambda g, c=c: g.dma_start(
                out=dtlnw_sb[:, c:c + 1],
                in_=dtlnw[c * 128:(c + 1) * 128].unsqueeze(1)),
                writes=[Bf("dtlnw_sb")])
        kb.op("G", "QG", lambda g: g.dma_start(out=bclnw[:, 0:1],
                                               in_=blnw.unsqueeze(1)),
              writes=[Bf("bclnw")])
        kb.op("G", "QG", lambda g: g.dma_start(out=bclnw[:, 1:2],
                                               in_=clnw.unsqueeze(1)),
              writes=[Bf("bclnw")])
        for rt in range(2):
            for dt in range(NDT):
                j = rt * NDT + dt
                kb.op("G", "QG", lambda g, rt=rt, dt=dt, j=j: g.dma_start(
                    out=dtp_sb[:, j * 128:(j + 1) * 128],
                    in_=dtpT[rt * 128:(rt + 1) * 128, dt * 128:(dt + 1) * 128]),
                    writes=[Bf("dtp_sb")])
        for it in range(NKT):
            kb.op("G", "QG", lambda g, it=it: g.dma_start(
                out=awr_sb[:, it * 36:(it + 1) * 36],
                in_=fAWrT[it * 128:(it + 1) * 128, :]), writes=[Bf("awr_sb")])
        kb.op("G", "QG", lambda g: g.dma_start(out=aug2[1:2, :], in_=fbi.unsqueeze(0)),
              writes=[Bf("aug2")])

        def ln_rowmath(width, off=0):
            kb.op("V", "V", lambda v: v.tensor_scalar_mul(
                c_m[0:1, 0:width], c_sum[0:1, 0:width], 1.0 / H),
                reads=[Bf("c_sum")], writes=[Bf("c_m")])
            kb.op("S", "S", lambda s: s.activation(
                c_sum[0:1, 0:width], c_m[0:1, 0:width], AF.Square),
                reads=[Bf("c_m")], writes=[Bf("c_sum")])
            kb.op("V", "V", lambda v: v.scalar_tensor_tensor(
                c_ssq[0:1, 0:width], c_ssq[0:1, 0:width], 1.0 / H,
                c_sum[0:1, 0:width], ALU.mult, ALU.subtract),
                reads=[Bf("c_sum")], writes=[Bf("c_ssq")])
            kb.op("V", "V", lambda v: v.tensor_scalar_add(
                c_ssq[0:1, 0:width], c_ssq[0:1, 0:width], EPS_LN),
                reads=[Bf("c_ssq")], writes=[Bf("c_ssq")])
            kb.op("S", "S", lambda s: s.activation(
                c_ssq[0:1, 0:width], c_ssq[0:1, 0:width], AF.Sqrt),
                reads=[Bf("c_ssq")], writes=[Bf("c_ssq")])
            kb.op("V", "V", lambda v: v.reciprocal(
                c_sum[0:1, 0:width], c_ssq[0:1, 0:width]),
                reads=[Bf("c_ssq")], writes=[Bf("c_sum")])
            kb.op("V", "V", lambda v: v.tensor_copy(
                rb_rstd[0:1, off:off + width], c_sum[0:1, 0:width]),
                reads=[Bf("c_sum")], writes=[Bf("rb_rstd")])
            kb.op("V", "V", lambda v: v.tensor_mul(
                c_m[0:1, 0:width], c_m[0:1, 0:width], c_sum[0:1, 0:width]),
                reads=[Bf("c_sum")], writes=[Bf("c_m")])
            kb.op("V", "V", lambda v: v.tensor_copy(
                rb_mrstd[0:1, off:off + width], c_m[0:1, 0:width]),
                reads=[Bf("c_m")], writes=[Bf("rb_mrstd")])

        for it in range(NKT):
            s, o = SLOT[it // 8], (it % 8) * 1024
            kb.op("G", "QG", lambda g, it=it, s=s, o=o: g.dma_start(
                out=s[:, o:o + 1024], in_=hs_prepT[it * 128:(it + 1) * 128, :]),
                writes=[SB[it // 8]])
        for it in range(NKT):
            s, o = SLOT[it // 8], (it % 8) * 1024
            for ch in range(2):
                kb.op("T", "T", lambda t, s=s, o=o, it=it, ch=ch: t.matmul(
                    psH[0:1, ch * 512:(ch + 1) * 512], ones_c[:, :],
                    s[:, o + ch * 512:o + (ch + 1) * 512],
                    start=(it == 0), stop=(it == NKT - 1)),
                    reads=[SB[it // 8], Bf("ones_c")], writes=[Bf("psH")])
        kb.op("S", "S", lambda s: s.copy(wl1.bitcast(F32)[0:1, 0:1024],
                                         psH[0:1, :]),
              reads=[Bf("psH")], writes=[Bf("wl1")])
        for it in range(NKT):
            s, o = SLOT[it // 8], (it % 8) * 1024
            q, qo = SLOT[4], (it % 4) * 1024
            kb.op("S", "S", lambda sc, s=s, o=o, q=q, qo=qo: sc.activation(
                q[:, qo:qo + 1024], s[:, o:o + 1024], AF.Square),
                reads=[SB[it // 8]], writes=[SB[4]])
            for ch in range(2):
                kb.op("T", "T", lambda t, q=q, qo=qo, it=it, ch=ch: t.matmul(
                    psH[0:1, ch * 512:(ch + 1) * 512], ones_c[:, :],
                    q[:, qo + ch * 512:qo + (ch + 1) * 512],
                    start=(it == 0), stop=(it == NKT - 1)),
                    reads=[SB[4], Bf("ones_c")], writes=[Bf("psH")])
        for ch in range(2):
            kb.op("S", "S", lambda s, ch=ch: s.copy(
                c_sum[:, :], wl1.bitcast(F32)[0:1, ch * 512:(ch + 1) * 512]),
                reads=[Bf("wl1")], writes=[Bf("c_sum")])
            kb.op("S", "S", lambda s, ch=ch: s.copy(
                c_ssq[:, :], psH[0:1, ch * 512:(ch + 1) * 512]),
                reads=[Bf("psH")], writes=[Bf("c_ssq")])
            ln_rowmath(512, off=ch * 512)
        for ch in range(2):
            kb.op("T", "T", lambda t, ch=ch: t.matmul(
                psG[:, :], ones_r[:, :], rb_rstd[0:1, ch * 512:(ch + 1) * 512],
                start=True, stop=True),
                reads=[Bf("rb_rstd"), Bf("ones_r")], writes=[Bf("psG")])
            kb.op("T", "T", lambda t, ch=ch: t.matmul(
                psF[:, :], ones32, rb_mrstd[0:1, ch * 512:(ch + 1) * 512],
                start=True, stop=True),
                reads=[Bf("rb_mrstd"), Bf("ones32")], writes=[Bf("psF")])
            for it in range(NKT):
                s, o = SLOT[it // 8], (it % 8) * 1024 + ch * 512
                q, qo = SLOT[4], 4096 + (it % 4) * 1024 + ch * 512
                kb.op("V", "V", lambda v, s=s, o=o, q=q, qo=qo: v.tensor_tensor(
                    q[:, qo:qo + 512], s[:, o:o + 512], psG[:, :], ALU.mult),
                    reads=[SB[it // 8], Bf("psG")], writes=[SB[4]])
                kb.op("V", "V", lambda v, q=q, qo=qo: v.tensor_tensor(
                    q[:, qo:qo + 512], q[:, qo:qo + 512], psF[:, :], ALU.subtract),
                    reads=[Bf("psF")], writes=[SB[4]])
                kb.op("Q", "QS", lambda qq, it=it, q=q, qo=qo, ch=ch: qq.dma_start(
                    out=ln_in[it * 128:(it + 1) * 128, ch * 512:(ch + 1) * 512],
                    in_=q[:, qo:qo + 512]),
                    reads=[SB[4]], writes=[Bf("ln_in")])
        fl = kb.op("Q", "Q", lambda q: q.nop(), reads=[Bf("ln_in")])
        kb.op("G", "CCL", lambda g: g.collective_compute(
            "AllGather", ALU.bypass, replica_groups=[list(range(NCK))],
            ins=[ln_in], outs=[ln_all]),
            reads=[Bf("ln_in")], writes=[Bf("ln_all")], extra=[fl])

        for kt in range(16):
            s = SLOT[kt % 2]
            kb.op("G", "QG", lambda g, kt=kt, s=s: g.dma_start(
                out=s[:, 0:4096], in_=w1[kt]), writes=[SB[kt % 2]])
            kb.op("Q", "QS", lambda q, kt=kt, s=s: q.dma_start(
                out=w1t[kt], in_=s[:, 0:4096]),
                reads=[SB[kt % 2]], writes=[Bf("w1t")])
        for dt in range(NDT):
            s = SLOT[dt % 2]
            kb.op("G", "QG", lambda g, dt=dt, s=s: g.dma_start(
                out=s[:, 0:H], in_=woTs[dt]), writes=[SB[dt % 2]])
            kb.op("Q", "QS", lambda q, dt=dt, s=s: q.dma_start(
                out=wos[dt], in_=s[:, 0:H]),
                reads=[SB[dt % 2]], writes=[Bf("wos")])
        kb.op("V", "V", lambda v: v.memset(aug2[0:1, :], 0.0), writes=[Bf("aug2")])
        for it in range(NKT):
            s = SLOT[2 + it % 2]
            kb.op("G", "QG", lambda g, it=it, s=s: g.dma_start(
                out=s[:, 0:H], in_=wfT[it]), writes=[SB[2 + it % 2]])
            kb.op("Q", "QS", lambda q, it=it, s=s: q.dma_start(
                out=wf2[it], in_=s[:, 0:H]),
                reads=[SB[2 + it % 2]], writes=[Bf("wf2")])
            for ch in range(NKT):
                kb.op("T", "T", lambda t, s=s, ch=ch, it=it: t.matmul(
                    psH[0:1, 0:128], ones_c[:, :], s[:, ch * 128:(ch + 1) * 128],
                    start=(ch == 0), stop=(ch == NKT - 1)),
                    reads=[SB[2 + it % 2], Bf("ones_c")], writes=[Bf("psH")])
            kb.op("V", "V", lambda v, it=it: v.tensor_copy(
                aug2[0:1, it * 128:(it + 1) * 128], psH[0:1, 0:128]),
                reads=[Bf("psH")], writes=[Bf("aug2")])
        for it in range(NKT):
            kb.op("T", "T", lambda t, it=it: t.matmul(
                psH[0:1, 0:36], ones_c[:, :], awr_sb[:, it * 36:(it + 1) * 36],
                start=(it == 0), stop=(it == NKT - 1)),
                reads=[Bf("awr_sb"), Bf("ones_c")], writes=[Bf("psH")])
        kb.op("V", "V", lambda v: v.tensor_copy(colsA[:, :], psH[0:1, 0:36]),
              reads=[Bf("psH")], writes=[Bf("colsA")])

        def p1_chunk(b, part):
            th, cth = divmod(part, 2)
            if cth == 0:
                for kt in range(NKT):
                    s, o = SLOT[kt // 8], (kt % 8) * 1024
                    chunk = 2 * b + th
                    kb.op("Q", "QS", lambda q, kt=kt, s=s, o=o, chunk=chunk:
                          q.dma_start(
                        out=s[:, o:o + 1024],
                        in_=ln_all[chunk, kt * 128:(kt + 1) * 128, :]),
                        reads=[Bf("ln_all")], writes=[SB[kt // 8]])
            for ci in range(8):
                ct = cth * 8 + ci
                pw = ct % 2
                ws, wof = SLOT[4], pw * 4096
                wb = SB4[pw]
                kb.op("Q", "QS", lambda q, ct=ct, ws=ws, wof=wof: q.dma_start(
                    out=ws[:, wof:wof + 4096], in_=w1t[ct]),
                    reads=[Bf("w1t")], writes=[wb])
                for half in range(2):
                    pst, pB = ((psA, Bf("psA")) if half == 0 else (psB, Bf("psB")))
                    for kt in range(NKT):
                        s, o = SLOT[kt // 8], (kt % 8) * 1024
                        kb.op("T", "T", lambda t, s=s, o=o, ws=ws, wof=wof, kt=kt,
                              half=half, pst=pst: t.matmul(
                            pst[:, :], ws[:, wof + kt * 128:wof + (kt + 1) * 128],
                            s[:, o + half * 512:o + (half + 1) * 512],
                            start=(kt == 0), stop=(kt == NKT - 1)),
                            reads=[SB[kt // 8], wb], writes=[pB])
                    if ct < 8:
                        dtc = ct
                        for tq in range(2):
                            gtq = th * 4 + half * 2 + tq
                            p = gtq % 2
                            xo = p * (TQ + 3)
                            cB = Bf("xconv")
                            kb.op("S", "S", lambda s, xo=xo, pst=pst, tq=tq: s.copy(
                                xconv[:, xo + 3:xo + 3 + TQ],
                                pst[:, tq * TQ:(tq + 1) * TQ]),
                                reads=[pB], writes=[cB])
                            if gtq == 0:
                                kb.op("V", "V", lambda v, xo=xo: v.memset(
                                    xconv[:, xo:xo + 3], 0.0), writes=[cB])
                            else:
                                kb.op("V", "V", lambda v, xo=xo, dtc=dtc:
                                      v.tensor_copy(
                                    xconv[:, xo:xo + 3],
                                    tails[:, dtc * 3:dtc * 3 + 3]),
                                    reads=[Bf("tails")], writes=[cB])
                            kb.op("V", "V", lambda v, xo=xo, dtc=dtc: v.tensor_copy(
                                tails[:, dtc * 3:dtc * 3 + 3],
                                xconv[:, xo + TQ:xo + TQ + 3]),
                                reads=[cB], writes=[Bf("tails")])
                            kb.op("V", "V", lambda v, xo=xo, dtc=dtc: v.tensor_scalar(
                                cacc[:, 0:TQ], xconv[:, xo:xo + TQ],
                                cw_sb[:, dtc * 4:dtc * 4 + 1],
                                cb_sb[:, dtc:dtc + 1], ALU.mult, ALU.add),
                                reads=[cB, Bf("cw_sb"), Bf("cb_sb")],
                                writes=[CAB[0]])
                            for k in (1, 2, 3):
                                src, dst = ((k + 1) % 2) * TQ, (k % 2) * TQ
                                kb.op("V", "V", lambda v, xo=xo, dtc=dtc, k=k,
                                      src=src, dst=dst: v.scalar_tensor_tensor(
                                    cacc[:, dst:dst + TQ],
                                    xconv[:, xo + k:xo + k + TQ],
                                    cw_sb[:, dtc * 4 + k:dtc * 4 + k + 1],
                                    cacc[:, src:src + TQ], ALU.mult, ALU.add),
                                    reads=[cB, Bf("cw_sb"), CAB[(k + 1) % 2]],
                                    writes=[CAB[k % 2]])
                            kb.op("S", "S", lambda s: s.activation(
                                xcw[:, :], cacc[:, TQ:2 * TQ], AF.Silu),
                                reads=[CAB[1]], writes=[Bf("xcw")])
                            kb.op("Q", "QS", lambda q, dtc=dtc, b=b, gtq=gtq:
                                  q.dma_start(
                                out=xcsp[b, dtc, :, gtq * TQ:(gtq + 1) * TQ],
                                in_=xcw[:, :]),
                                reads=[Bf("xcw")], writes=[Bf("xcsp")])
                    else:
                        dtc = ct - 8
                        gp = half
                        go = gp * 512
                        kb.op("S", "S", lambda s, go=go, pst=pst: s.activation(
                            gev[:, go:go + 512], pst[:, :], AF.Silu),
                            reads=[pB], writes=[GEVB[gp]])
                        kb.op("Q", "QS", lambda q, go=go, gp=gp, dtc=dtc, b=b,
                              th=th, half=half: q.dma_start(
                            out=gsp[b, dtc, :, th * 1024 + half * 512:
                                    th * 1024 + (half + 1) * 512],
                            in_=gev[:, go:go + 512]),
                            reads=[GEVB[gp]], writes=[Bf("gsp")])
            if cth == 1:
                for ec in range(3):
                    for hh in range(2):
                        for dt in range(NDT):
                            kb.op("Q", "QS", lambda q, dt=dt, b=b,
                                  th=th, hh=hh: q.dma_start(
                                out=xcr[:, :],
                                in_=xcsp[b, dt, :, th * 1024 + hh * 512:
                                         th * 1024 + (hh + 1) * 512]),
                                reads=[Bf("xcsp")], writes=[XCRB[0]])
                            kb.op("T", "T", lambda t, ec=ec, dt=dt: t.matmul(
                                psE[:, :],
                                xpw_sb[:, dt * E3 + ec * 96:dt * E3 + (ec + 1) * 96],
                                xcr[:, :],
                                start=(dt == 0), stop=(dt == NDT - 1)),
                                reads=[Bf("xpw_sb"), XCRB[0]], writes=[Bf("psE")])
                        kb.op("S", "S", lambda s: s.copy(spsc[:, :], psE[:, :]),
                              reads=[Bf("psE")], writes=[Bf("spsc")])
                        kb.op("Q", "QS", lambda q, ec=ec, hh=hh, b=b, th=th:
                              q.dma_start(
                            out=arin[b][ec * 96:(ec + 1) * 96,
                                        th * 1024 + hh * 512:
                                        th * 1024 + (hh + 1) * 512],
                            in_=spsc[:, :]),
                            reads=[Bf("spsc")], writes=[Bf(f"arin{b}")])

        def p1_ar(b):
            fl = kb.op("Q", "Q", lambda q: q.nop(), reads=[Bf(f"arin{b}")])
            kb.op("G", f"CCA{b}", lambda g, b=b: g.collective_compute(
                "AllReduce", ALU.add, replica_groups=[list(range(NCK))],
                ins=[arin[b]], outs=[arout[b]]),
                reads=[Bf(f"arin{b}")], writes=[Bf(f"arout{b}")], extra=[fl])

        def rows_dt(b):
            kb.op("V", "V", lambda v: v.memset(hcar[:, :], 0.0),
                  writes=[Bf(f"hcar{d}") for d in range(NDT)])
            kb.op("G", "QG", lambda g, b=b: g.dma_start(
                out=spcB, in_=arout[b][256:272, :]),
                reads=[Bf(f"arout{b}")], writes=[Bf("wl1")])
            kb.op("G", "QG", lambda g, b=b: g.dma_start(
                out=spcC, in_=arout[b][272:288, :]),
                reads=[Bf(f"arout{b}")], writes=[Bf("wl1")])
            kb.op("S", "S", lambda s: s.activation(
                dAp[0:16, 0:L], spcB, AF.Square),
                reads=[Bf("wl1")], writes=[Bf("dAp")])
            kb.op("S", "S", lambda s: s.activation(
                dAp[0:16, L:2 * L], spcC, AF.Square),
                reads=[Bf("wl1")], writes=[Bf("dAp")])
            for (co, rbB_) in ((0, Bf("rb_rb")), (L, Bf("rb_rc"))):
                rb = rb_rb if co == 0 else rb_rc
                for ch in range(4):
                    kb.op("T", "T", lambda t, co=co, ch=ch: t.matmul(
                        psH[0:1, 0:512], ones_c[0:16, :],
                        dAp[0:16, co + ch * 512:co + (ch + 1) * 512],
                        start=True, stop=True),
                        reads=[Bf("dAp"), Bf("ones_c")], writes=[Bf("psH")])
                    kb.op("V", "V", lambda v: v.tensor_scalar(
                        s_scr[:, :], psH[0:1, 0:512], 1.0 / NST, EPS_RMS,
                        ALU.mult, ALU.add),
                        reads=[Bf("psH")], writes=[Bf("s_scr")])
                    kb.op("S", "S", lambda s: s.activation(
                        s_scr[:, :], s_scr[:, :], AF.Sqrt),
                        reads=[Bf("s_scr")], writes=[Bf("s_scr")])
                    kb.op("V", "V", lambda v: v.reciprocal(s_scr[:, :], s_scr[:, :]),
                          reads=[Bf("s_scr")], writes=[Bf("s_scr")])
                    kb.op("V", "V", lambda v, rb=rb, ch=ch: v.tensor_copy(
                        rb[0:1, ch * 512:(ch + 1) * 512], s_scr[:, :]),
                        reads=[Bf("s_scr")], writes=[rbB_])
            kb.op("V", "V", lambda v: v.tensor_scalar(
                bcrows[:, 0:L], spcB, bclnw[:, 0:1], -1.0, ALU.mult, ALU.mult),
                reads=[Bf("wl1"), Bf("bclnw")], writes=[Bf("bcrows")])
            kb.op("V", "V", lambda v: v.tensor_scalar(
                bcrows[:, L:2 * L], spcC, bclnw[:, 1:2], None, ALU.mult),
                reads=[Bf("wl1"), Bf("bclnw")], writes=[Bf("bcrows")])
            for (co, rb, rbB_) in ((0, rb_rb, Bf("rb_rb")),
                                   (L, rb_rc, Bf("rb_rc"))):
                for ch in range(4):
                    kb.op("T", "T", lambda t, co=co, ch=ch, rb=rb: t.matmul(
                        psH[0:16, 0:512], ones64[0:1, 0:16],
                        rb[0:1, ch * 512:(ch + 1) * 512],
                        start=True, stop=True),
                        reads=[rbB_, Bf("ones64")], writes=[Bf("psH")])
                    kb.op("S", "S", lambda s, ch=ch: s.copy(
                        hl2[0:16, ch * 512:(ch + 1) * 512],
                        psH[0:16, 0:512]),
                        reads=[Bf("psH")], writes=[Bf("hl2")])
                kb.op("V", "V", lambda v, co=co: v.tensor_tensor(
                    bcrows[:, co:co + L], bcrows[:, co:co + L],
                    hl2[0:16, 0:L], ALU.mult),
                    reads=[Bf("hl2")], writes=[Bf("bcrows")])
            for kt in range(2):
                for hh in range(2):
                    kb.op("G", "QG", lambda g, kt=kt, hh=hh, b=b: g.dma_start(
                        out=spd[:, :],
                        in_=arout[b][kt * 128:(kt + 1) * 128,
                                     hh * 1024:(hh + 1) * 1024]),
                        reads=[Bf(f"arout{b}")], writes=[Bf("spd")])
                    kb.op("S", "S", lambda s, kt=kt, hh=hh: s.activation(
                        wl1[:, kt * L + hh * 1024:kt * L + (hh + 1) * 1024],
                        spd[:, :], AF.Square),
                        reads=[Bf("spd")], writes=[Bf("wl1")])
            for ch in range(4):
                for kt in range(2):
                    kb.op("T", "T", lambda t, ch=ch, kt=kt: t.matmul(
                        psH[0:1, 0:512], ones_c[:, :],
                        wl1[:, kt * L + ch * 512:kt * L + (ch + 1) * 512],
                        start=(kt == 0), stop=(kt == 1)),
                        reads=[Bf("wl1"), Bf("ones_c")], writes=[Bf("psH")])
                kb.op("V", "V", lambda v: v.tensor_scalar(
                    s_scr[:, :], psH[0:1, 0:512], 1.0 / R, EPS_RMS,
                    ALU.mult, ALU.add),
                    reads=[Bf("psH")], writes=[Bf("s_scr")])
                kb.op("S", "S", lambda s: s.activation(
                    s_scr[:, :], s_scr[:, :], AF.Sqrt),
                    reads=[Bf("s_scr")], writes=[Bf("s_scr")])
                kb.op("V", "V", lambda v: v.reciprocal(s_scr[:, :], s_scr[:, :]),
                      reads=[Bf("s_scr")], writes=[Bf("s_scr")])
                kb.op("V", "V", lambda v, ch=ch: v.tensor_copy(
                    dAp[0:1, ch * 512:(ch + 1) * 512], s_scr[:, :]),
                    reads=[Bf("s_scr")], writes=[Bf("dAp")])
            for kt in range(2):
                for hh in range(2):
                    kb.op("G", "QG", lambda g, kt=kt, hh=hh, b=b: g.dma_start(
                        out=spd[:, :],
                        in_=arout[b][kt * 128:(kt + 1) * 128,
                                     hh * 1024:(hh + 1) * 1024]),
                        reads=[Bf(f"arout{b}")], writes=[Bf("spd")])
                    for c2 in range(2):
                        ch = hh * 2 + c2
                        kb.op("T", "T", lambda t, ch=ch: t.matmul(
                            psG[:, :], ones_r[:, :],
                            dAp[0:1, ch * 512:(ch + 1) * 512],
                            start=True, stop=True),
                            reads=[Bf("dAp"), Bf("ones_r")], writes=[Bf("psG")])
                        kb.op("V", "V", lambda v, kt=kt, ch=ch, c2=c2:
                              v.scalar_tensor_tensor(
                            hl2[:, kt * L + ch * 512:kt * L + (ch + 1) * 512],
                            spd[:, c2 * 512:(c2 + 1) * 512],
                            dtlnw_sb[:, kt:kt + 1], psG[:, :], ALU.mult, ALU.mult),
                            reads=[Bf("spd"), Bf("dtlnw_sb"), Bf("psG")],
                            writes=[Bf("hl2")])
            for dt in range(NDT):
                for tc in range(4):
                    for rt in range(2):
                        j = rt * NDT + dt
                        kb.op("T", "T", lambda t, j=j, rt=rt, tc=tc: t.matmul(
                            psF[:, :], dtp_sb[:, j * 128:(j + 1) * 128],
                            hl2[:, rt * L + tc * 512:rt * L + (tc + 1) * 512],
                            start=(rt == 0), stop=(rt == 1)),
                            reads=[Bf("dtp_sb"), Bf("hl2")], writes=[Bf("psF")])
                    kb.op("S", "S", lambda s, dt=dt: s.activation(
                        bct[:, 0:512], psF[:, :], AF.Sigmoid,
                        scale=-1.0, bias=ndtb[:, dt:dt + 1]),
                        reads=[Bf("psF"), Bf("ndtb")], writes=[Bf("bct")])
                    kb.op("S", "S", lambda s: s.activation(
                        gev[:, 0:512], bct[:, 0:512], AF.Ln),
                        reads=[Bf("bct")], writes=[GEVB[0]])
                    kb.op("Q", "QS", lambda q, b=b, dt=dt, tc=tc: q.dma_start(
                        out=dtlsp[b, dt, :, tc * 512:(tc + 1) * 512],
                        in_=gev[:, 0:512]),
                        reads=[GEVB[0]], writes=[Bf("dtlsp")])
            for n in range(NST):
                kb.op("V", "V", lambda v, n=n: v.memset(
                    dAp[:, n * TQS:n * TQS + 1], 0.0), writes=[Bf("dAp")])
            if b == 0:
                for n in range(NST):
                    kb.op("V", "V", lambda v, n=n: v.memset(
                        dApB[:, n * TQS:n * TQS + 1], 0.0),
                        writes=[Bf("dApB")])

        def bc_tq(b, tq):
            t0 = tq * TQ
            for (dst, dB, co) in ((bcB, Bf("bcB"), 0), (bcC, Bf("bcC"), L)):
                rows_v = bcrows[:, co + t0:co + t0 + TQ].unsqueeze(1) \
                    .broadcast_to([NST, NST, TQ])
                mask_v = dmask[:, :].unsqueeze(2).broadcast_to([NST, NST, TQ])
                diag_v = hl2[0:16, 0:NST * TQ].rearrange(
                    "p (n t) -> p n t", n=NST)
                kb.op("V", "V", lambda v, rows_v=rows_v, mask_v=mask_v,
                      diag_v=diag_v: v.tensor_tensor(
                    diag_v, rows_v, mask_v, ALU.mult),
                    reads=[Bf("bcrows"), Bf("dmask")], writes=[Bf("hl2")])
                for ch in range(4):
                    for c2 in range(2):
                        kb.op("T", "T", lambda t, ch=ch, c2=c2: t.matmul(
                            psH[:, c2 * 512:(c2 + 1) * 512], ones16[:, :],
                            hl2[0:16, ch * 1024 + c2 * 512:
                                ch * 1024 + (c2 + 1) * 512],
                            start=True, stop=True),
                            reads=[Bf("hl2"), Bf("ones16")], writes=[Bf("psH")])
                    kb.op("S", "S", lambda s, dst=dst, ch=ch: s.copy(
                        dst[:, ch * 1024:(ch + 1) * 1024], psH[:, :]),
                        reads=[Bf("psH")], writes=[dB])

        def scan_tq(b, tq):
            t0 = tq * TQ
            for dt in range(NDT):
                p = dt % 2
                po = p * TQ
                for (dst, dB, src, sB) in (
                        (dtl_t, DTLB[p], dtlsp, Bf("dtlsp")),
                        (xc_t, XCTB[p], xcsp, Bf("xcsp")),
                        (g_t, GTB[p], gsp, Bf("gsp"))):
                    kb.op("Q", "QS", lambda q, dst=dst, src=src, po=po, b=b, dt=dt,
                          t0=t0: q.dma_start(
                        out=dst[:, po:po + TQ], in_=src[b, dt, :, t0:t0 + TQ]),
                        reads=[sB], writes=[dB])
                kb.op("V", "V", lambda v, po=po: v.tensor_tensor(
                    u_t[:, :], dtl_t[:, po:po + TQ], xc_t[:, po:po + TQ], ALU.mult),
                    reads=[DTLB[p], XCTB[p]], writes=[Bf("u_t")])
                dApX, wlX, hlX = ((dAp, wl1, hl2), (dApB, wl1B, hl2))[dt % 2]
                dApN, wlN, hlN = (("dAp", "wl1", "hl2"),
                                  ("dApB", "wl1B", "hl2"))[dt % 2]
                hcN = f"hcar{dt}"
                kb.op("Q", "QS", lambda q, dt=dt, wlX=wlX: q.dma_start(
                    out=wlX[:, :].rearrange("p (n t) -> p n t", n=NST)
                    [:, :, 0:1],
                    in_=hcar[:, dt * NST:(dt + 1) * NST].unsqueeze(2),
                    allow_slow_non_contiguous=True),
                    reads=[Bf(hcN)], writes=[Bf(wlN)])
                for n in range(NST):
                    kb.op("S", "S", lambda s, po=po, n=n, dt=dt, dApX=dApX:
                          s.activation(
                        dApX[:, n * TQS + 1:n * TQS + 1 + TQ],
                        dtl_t[:, po:po + TQ], AF.Exp,
                        scale=negA[:, dt * NST + n:dt * NST + n + 1]),
                        reads=[DTLB[p], Bf("negA")], writes=[Bf(dApN)])
                uview = u_t[:, :].unsqueeze(1).broadcast_to([128, NST, TQ])
                bview = bcB[:, :].rearrange("p (n t) -> p n t", n=NST)
                wview = wlX[:, :].rearrange("p (n t) -> p n t", n=NST)[:, :, 1:]
                kb.op("V", "V", lambda v, uview=uview, bview=bview, wview=wview:
                      v.tensor_tensor(wview, uview, bview, ALU.mult),
                      reads=[Bf("u_t"), Bf("bcB")], writes=[Bf(wlN)])
                kb.op("V", "V", lambda v, dApX=dApX, wlX=wlX, hlX=hlX:
                      v.tensor_tensor_scan(
                    hlX[:, :], dApX[:, :], wlX[:, :], 0.0, ALU.mult, ALU.add),
                    reads=[Bf(dApN), Bf(wlN)], writes=[Bf(hlN)])
                kb.op("Q", "QS", lambda q, dt=dt, hlX=hlX: q.dma_start(
                    out=hcar[:, dt * NST:(dt + 1) * NST].unsqueeze(2),
                    in_=hlX[:, :].rearrange("p (n t) -> p n t", n=NST)
                    [:, :, TQ:TQ + 1],
                    allow_slow_non_contiguous=True),
                    reads=[Bf(hlN)], writes=[Bf(hcN)])
                cview = bcC[:, :].rearrange("p (n t) -> p n t", n=NST)
                hview = hlX[:, :].rearrange("p (n t) -> p n t", n=NST)[:, :, 1:]
                pview = dApX[:, :].rearrange("p (n t) -> p n t", n=NST)[:, :, 1:]
                kb.op("V", "V", lambda v, pview=pview, hview=hview, cview=cview:
                      v.tensor_tensor(pview, hview, cview, ALU.mult),
                      reads=[Bf(hlN), Bf("bcC")], writes=[Bf(dApN)])
                pv = dApX[:, :].rearrange("p (n t) -> p n t", n=NST)
                lv = wlX[:, :].rearrange("p (n t) -> p n t", n=NST)
                kb.op("V", "V", lambda v, pv=pv, lv=lv: v.tensor_tensor(
                    lv[:, 0:8, 1:], pv[:, 0:16:2, 1:], pv[:, 1:16:2, 1:],
                    ALU.add),
                    reads=[Bf(dApN)], writes=[Bf(wlN)])
                kb.op("V", "V", lambda v, lv=lv: v.tensor_tensor(
                    lv[:, 8:12, 1:], lv[:, 0:8:2, 1:], lv[:, 1:8:2, 1:],
                    ALU.add), writes=[Bf(wlN)])
                kb.op("V", "V", lambda v, lv=lv: v.tensor_tensor(
                    lv[:, 12:14, 1:], lv[:, 8:12:2, 1:], lv[:, 9:12:2, 1:],
                    ALU.add), writes=[Bf(wlN)])
                kb.op("V", "V", lambda v, wlX=wlX: v.tensor_tensor(
                    ys_t[:, :], wlX[:, 12 * TQS + 1:12 * TQS + 1 + TQ],
                    wlX[:, 13 * TQS + 1:13 * TQS + 1 + TQ], ALU.add),
                    reads=[Bf(wlN)], writes=[Bf("ys_t")])
                kb.op("V", "V", lambda v, po=po, dt=dt: v.scalar_tensor_tensor(
                    t2_t[:, :], xc_t[:, po:po + TQ], d_sb[:, dt:dt + 1], ys_t[:, :],
                    ALU.mult, ALU.add),
                    reads=[XCTB[p], Bf("d_sb"), Bf("ys_t")], writes=[Bf("t2_t")])
                kb.op("V", "V", lambda v, po=po: v.tensor_tensor(
                    ygd_t[:, po:po + TQ], t2_t[:, :], g_t[:, po:po + TQ], ALU.mult),
                    reads=[Bf("t2_t"), GTB[p]], writes=[YGDB[p]])
                kb.op("Q", "QS", lambda q, po=po, b=b, dt=dt, t0=t0: q.dma_start(
                    out=yin[b][dt * 128:(dt + 1) * 128, t0:t0 + TQ],
                    in_=ygd_t[:, po:po + TQ]),
                    reads=[YGDB[p]], writes=[Bf(f"yin{b}")])

        def y_rs(b):
            fl = kb.op("Q", "Q", lambda q: q.nop(), reads=[Bf(f"pmix{b}")])
            kb.op("G", f"CCR{b}", lambda g, b=b: g.collective_compute(
                "ReduceScatter", ALU.add, replica_groups=[list(range(NCK))],
                ins=[pmix[b]], outs=[rsout[b]]),
                reads=[Bf(f"pmix{b}")], writes=[Bf(f"rsout{b}")], extra=[fl])

        def ln_stats_T(mslot, mbuf, sqslot, sqbuf):
            for it in range(NKT):
                mo = it * TSL
                kb.op("S", "S", lambda s, mo=mo, mslot=mslot, sqslot=sqslot:
                      s.activation(sqslot[:, mo:mo + TSL], mslot[:, mo:mo + TSL],
                                   AF.Square),
                      reads=[mbuf], writes=[sqbuf])
            for it in range(NKT):
                mo = it * TSL
                kb.op("T", "T", lambda t, mo=mo, mslot=mslot, it=it: t.matmul(
                    psH[0:1, 0:TSL], ones_c[:, :], mslot[:, mo:mo + TSL],
                    start=(it == 0), stop=(it == NKT - 1)),
                    reads=[mbuf, Bf("ones_c")], writes=[Bf("psH")])
            kb.op("S", "S", lambda s: s.copy(c_sum[0:1, 0:TSL], psH[0:1, 0:TSL]),
                  reads=[Bf("psH")], writes=[Bf("c_sum")])
            for it in range(NKT):
                mo = it * TSL
                kb.op("T", "T", lambda t, mo=mo, sqslot=sqslot, it=it: t.matmul(
                    psH[0:1, TSL:2 * TSL], ones_c[:, :], sqslot[:, mo:mo + TSL],
                    start=(it == 0), stop=(it == NKT - 1)),
                    reads=[sqbuf, Bf("ones_c")], writes=[Bf("psH")])
            kb.op("S", "S", lambda s: s.copy(c_ssq[0:1, 0:TSL],
                                             psH[0:1, TSL:2 * TSL]),
                  reads=[Bf("psH")], writes=[Bf("c_ssq")])
            ln_rowmath(TSL)

        def stat_bcast(which):
            pairs = (((rb_rstd, 0), (rb_mrstd, 1)) if which != 1
                     else ((rb_rstd, 2),))
            for rb, pos in pairs:
                rbB = Bf("rb_rstd") if rb is rb_rstd else Bf("rb_mrstd")
                on = ones_r[:, :] if rb is rb_rstd else ones32
                onB = Bf("ones_r") if rb is rb_rstd else Bf("ones32")
                kb.op("T", "T", lambda t, rb=rb, on=on: t.matmul(
                    psG[:, 0:TSL], on, rb[0:1, 0:TSL],
                    start=True, stop=True),
                    reads=[rbB, onB], writes=[Bf("psG")])
                kb.op("S", "S", lambda s, pos=pos: s.copy(
                    bct[:, pos * TSL:(pos + 1) * TSL], psG[:, 0:TSL]),
                    reads=[Bf("psG")], writes=[Bf("bct")])

        def epi_chunk(b, part):
            if part in (0, 1):
                for sub in range(2):
                    tch = part * 2 + sub
                    t0 = tch * 512
                    rp = tch % 2
                    ro = rp * 4096
                    for dt in range(NDT):
                        kb.op("Q", "QS", lambda q, dt=dt, b=b, t0=t0, ro=ro:
                              q.dma_start(
                            out=SLOT[0][:, ro + dt * 512:ro + (dt + 1) * 512],
                            in_=yin[b][dt * 128:(dt + 1) * 128, t0:t0 + 512]),
                            reads=[Bf(f"yin{b}")], writes=[SB4[rp] if False
                                                           else SB[0]])
                    for htg in range(8):
                        wp = (tch * 8 + htg) % 2
                        wof = wp * 4096
                        kb.op("Q", "QS", lambda q, htg=htg, wof=wof: q.dma_start(
                            out=SLOT[4][:, wof:wof + 4096], in_=wos[htg]),
                            reads=[Bf("wos")], writes=[SB4[wp]])
                        for hh in range(4):
                            pst, pB = ((psA, Bf("psA")), (psB, Bf("psB")),
                                       (psF, Bf("psF")), (psI, Bf("psI")))[hh]
                            for dt in range(NDT):
                                kb.op("T", "T", lambda t, pst=pst, wof=wof, dt=dt,
                                      hh=hh, ro=ro: t.matmul(
                                    pst[:, :],
                                    SLOT[4][:, wof + dt * 512 + hh * 128:
                                            wof + dt * 512 + (hh + 1) * 128],
                                    SLOT[0][:, ro + dt * 512:ro + (dt + 1) * 512],
                                    start=(dt == 0), stop=(dt == NDT - 1)),
                                    reads=[SB[0], SB4[wp]], writes=[pB])
                            ht = htg * 4 + hh
                            gp = hh % 2
                            go = gp * 512
                            kb.op("S", "S", lambda s, pst=pst, go=go: s.copy(
                                gev[:, go:go + 512], pst[:, :]),
                                reads=[pB], writes=[GEVB[gp]])
                            for jj in range(2):
                                jb = tch * 2 + jj
                                kb.op("Q", "QS", lambda q, b=b, jb=jb, ht=ht,
                                      go=go, jj=jj: q.dma_start(
                                    out=pmix[b][jb, ht * 128:(ht + 1) * 128, :],
                                    in_=gev[:, go + jj * 256:go + (jj + 1) * 256]),
                                    reads=[GEVB[gp]], writes=[Bf(f"pmix{b}")])
            if part == 2:
                for ht in range(NKT):
                    mo = ht * TSL
                    kb.op("Q", "QS", lambda q, ht=ht, mo=mo, b=b: q.dma_start(
                        out=SLOT[2][:, mo:mo + TSL],
                        in_=rsout[b][ht * 128:(ht + 1) * 128, :]),
                        reads=[Bf(f"rsout{b}")], writes=[SB[2]])
            if part == 2:
                ln_stats_T(SLOT[2], SB[2], SLOT[1], SB[1])
                stat_bcast(0)
                for ht in range(NKT):
                    p = ht % 2
                    po = p * TSL
                    mo = ht * TSL
                    kb.op("Q", "QS", lambda q, ht=ht, b=b: q.dma_start(
                        out=x0t[:, :],
                        in_=hs_resT[b, ht * 128:(ht + 1) * 128, :]),
                        writes=[X0B[0]])
                    kb.op("V", "V", lambda v, po=po, mo=mo: v.tensor_tensor(
                        cacc[:, po:po + TSL], SLOT[2][:, mo:mo + TSL],
                        bct[:, 0:TSL], ALU.mult),
                        reads=[SB[2], Bf("bct")], writes=[CAB[p]])
                    kb.op("V", "V", lambda v, po=po: v.scalar_tensor_tensor(
                        cacc[:, po:po + TSL], cacc[:, po:po + TSL], 1.0,
                        bct[:, TSL:2 * TSL], ALU.mult, ALU.subtract),
                        reads=[Bf("bct")], writes=[CAB[p]])
                    kb.op("V", "V", lambda v, po=po: v.tensor_tensor(
                        x1t[:, po:po + TSL], cacc[:, po:po + TSL],
                        x0t[:, :], ALU.add),
                        reads=[CAB[p], X0B[0]], writes=[X1B[p]])
                    kb.op("Q", "QS", lambda q, po=po, ht=ht, b=b: q.dma_start(
                        out=x1sp[b, ht * 128:(ht + 1) * 128, :],
                        in_=x1t[:, po:po + TSL]),
                        reads=[X1B[p]], writes=[Bf("x1sp")])
                for it in range(NKT):
                    o = it * TSL
                    kb.op("G", "QG", lambda g, it=it, o=o, b=b: g.dma_start(
                        out=SLOT[0][:, o:o + TSL],
                        in_=x1sp[b, it * 128:(it + 1) * 128, :]),
                        reads=[Bf("x1sp")], writes=[SB[0]])
                ln_stats_T(SLOT[0], SB[0], SLOT[1], SB[1])
                stat_bcast(1)
                kb.op("V", "V", lambda v: v.tensor_scalar_mul(
                    rb_tmp[0:1, 0:TSL], c_m[0:1, 0:TSL], -1.0),
                    reads=[Bf("c_m")], writes=[Bf("rb_tmp")])
                kb.op("V", "V", lambda v: v.tensor_tensor(
                    aug2r[0:1, :], rb_tmp[0:1, 0:TSL], c_ssq[0:1, 0:TSL], ALU.mult),
                    reads=[Bf("rb_tmp"), Bf("c_ssq")], writes=[Bf("aug2r")])
                kb.op("G", "QG", lambda g: g.dma_start(
                    out=aug2r[1:2, :], in_=c_ssq[0:1, 0:TSL]),
                    reads=[Bf("c_ssq")], writes=[Bf("aug2r")])
                gates_lora(b)
            if part == 3:
                ffn_and_out(b)

        def gates_lora(b):
            for it in range(NKT):
                o = it * TSL
                kb.op("T", "T", lambda t, it=it, o=o: t.matmul(
                    psH[0:36, 0:TSL], awr_sb[:, it * 36:(it + 1) * 36],
                    SLOT[0][:, o:o + TSL], start=(it == 0), stop=False),
                    reads=[SB[0], Bf("awr_sb")], writes=[Bf("psH")])
            kb.op("T", "T", lambda t: t.matmul(
                psH[0:36, 0:TSL], colsA[:, :], aug2r[0:1, :],
                start=False, stop=True),
                reads=[Bf("colsA"), Bf("aug2r")], writes=[Bf("psH")])
            kb.op("V", "V", lambda v: v.tensor_copy(scr36[:, :], psH[0:36, 0:TSL]),
                  reads=[Bf("psH")], writes=[Bf("scr36")])
            kb.op("T", "T", lambda t: t.matmul(
                psG[0:4, 0:TSL], sel2[:, :], scr36[:, :], start=True, stop=True),
                reads=[Bf("sel2"), Bf("scr36")], writes=[Bf("psG")])
            kb.op("V", "V", lambda v: v.tensor_copy(gl4[:, 0:TSL], psG[0:4, 0:TSL]),
                  reads=[Bf("psG")], writes=[Bf("gl4")])
            kb.op("T", "T", lambda t: t.matmul(
                psG[0:4, 0:TSL], ones_r[:, 0:4], rb_rstd[0:1, 0:TSL],
                start=True, stop=True),
                reads=[Bf("rb_rstd"), Bf("ones_r")], writes=[Bf("psG")])
            kb.op("V", "V", lambda v: v.tensor_copy(bct4[:, :], psG[0:4, 0:TSL]),
                  reads=[Bf("psG")], writes=[Bf("bct4")])
            kb.op("V", "V", lambda v: v.tensor_tensor(
                gl4[:, 0:TSL], gl4[:, 0:TSL], bct4[:, :], ALU.mult),
                reads=[Bf("bct4")], writes=[Bf("gl4")])
            kb.op("S", "S", lambda s: s.activation(
                gb4[:, 0:TSL], gl4[:, 0:TSL], AF.Exp),
                reads=[Bf("gl4")], writes=[Bf("gb4")])
            kb.op("T", "T", lambda t: t.matmul(
                psG[0:1, 0:TSL], ones_c[0:4, :], gb4[:, 0:TSL],
                start=True, stop=True),
                reads=[Bf("gb4"), Bf("ones_c")], writes=[Bf("psG")])
            kb.op("V", "V", lambda v: v.tensor_copy(
                gl4[0:1, TSL:2 * TSL], psG[0:1, 0:TSL]),
                reads=[Bf("psG")], writes=[Bf("gl4")])
            kb.op("V", "V", lambda v: v.reciprocal(
                gl4[0:1, TSL:2 * TSL], gl4[0:1, TSL:2 * TSL]),
                reads=[Bf("gl4")], writes=[Bf("gl4")])
            kb.op("V", "V", lambda v: v.tensor_copy(
                rb_tmp[0:1, 0:TSL], gl4[0:1, TSL:2 * TSL]),
                reads=[Bf("gl4")], writes=[Bf("rb_tmp")])
            kb.op("T", "T", lambda t: t.matmul(
                psG[0:4, 0:TSL], ones_r[:, 0:4], rb_tmp[0:1, 0:TSL],
                start=True, stop=True),
                reads=[Bf("rb_tmp"), Bf("ones_r")], writes=[Bf("psG")])
            kb.op("V", "V", lambda v: v.tensor_copy(bct4[:, :], psG[0:4, 0:TSL]),
                  reads=[Bf("psG")], writes=[Bf("bct4")])
            kb.op("V", "V", lambda v: v.tensor_tensor(
                gb4[:, TSL:2 * TSL], gb4[:, 0:TSL], bct4[:, :], ALU.mult),
                reads=[Bf("bct4")], writes=[Bf("gb4")])
            kb.op("T", "T", lambda t: t.matmul(
                psH[0:32, 0:TSL], sel[:, :], gb4[:, TSL:2 * TSL],
                start=True, stop=True),
                reads=[Bf("sel"), Bf("gb4")], writes=[Bf("psH")])
            kb.op("V", "V", lambda v: v.tensor_copy(g32[:, :], psH[0:32, 0:TSL]),
                  reads=[Bf("psH")], writes=[Bf("g32")])
            kb.op("V", "V", lambda v: v.tensor_tensor(
                prs[:, :], scr36[0:32, :], g32[:, :], ALU.mult),
                reads=[Bf("scr36"), Bf("g32")], writes=[Bf("prs")])

        def ffn_and_out(b):
            for ot in range(NKT):
                pw = ot % 2
                ws, wof = SLOT[4], pw * 4096
                wb = SB4[pw]
                kb.op("Q", "QS", lambda q, ot=ot, ws=ws, wof=wof: q.dma_start(
                    out=ws[:, wof:wof + 4096], in_=wf2[ot]),
                    reads=[Bf("wf2")], writes=[wb])
                kb.op("G", "QG", lambda g, ot=ot, pw=pw: g.dma_start(
                    out=bmt_t[:, pw * 128:(pw + 1) * 128],
                    in_=fBmT[:, ot * 128:(ot + 1) * 128]),
                    writes=[Bf("bmt_t")])
                for it in range(NKT):
                    o = it * TSL
                    kb.op("T", "T", lambda t, ws=ws, wof=wof, it=it, o=o: t.matmul(
                        psA[:, 0:TSL], ws[:, wof + it * 128:wof + (it + 1) * 128],
                        SLOT[0][:, o:o + TSL], start=(it == 0), stop=False),
                        reads=[SB[0], wb], writes=[Bf("psA")])
                kb.op("T", "T", lambda t, ot=ot: t.matmul(
                    psA[:, 0:TSL], aug2[:, ot * 128:(ot + 1) * 128],
                    aug2r[:, :], start=False, stop=False),
                    reads=[Bf("aug2"), Bf("aug2r")], writes=[Bf("psA")])
                kb.op("T", "T", lambda t, ot=ot, pw=pw: t.matmul(
                    psA[:, 0:TSL], bmt_t[:, pw * 128:(pw + 1) * 128], prs[:, :],
                    start=False, stop=True),
                    reads=[Bf("bmt_t"), Bf("prs")], writes=[Bf("psA")])
                mo = ot * TSL
                kb.op("V", "V", lambda v, mo=mo: v.tensor_tensor(
                    SLOT[2][:, mo:mo + TSL], psA[:, 0:TSL],
                    bct[:, 2 * TSL:3 * TSL], ALU.mult),
                    reads=[Bf("psA"), Bf("bct")], writes=[SB[2]])
            ln_stats_T(SLOT[2], SB[2], SLOT[1], SB[1])
            stat_bcast(2)
            for ht in range(NKT):
                p = ht % 2
                po = p * TSL
                mo = ht * TSL
                kb.op("G", "QG", lambda g, po=po, ht=ht, b=b: g.dma_start(
                    out=x1t[:, po:po + TSL],
                    in_=x1sp[b, ht * 128:(ht + 1) * 128, :]),
                    reads=[Bf("x1sp")], writes=[X1B[p]])
                kb.op("V", "V", lambda v, po=po, mo=mo: v.tensor_tensor(
                    cacc[:, po:po + TSL], SLOT[2][:, mo:mo + TSL],
                    bct[:, 0:TSL], ALU.mult),
                    reads=[SB[2], Bf("bct")], writes=[CAB[p]])
                kb.op("V", "V", lambda v, po=po: v.scalar_tensor_tensor(
                    cacc[:, po:po + TSL], cacc[:, po:po + TSL], 1.0,
                    bct[:, TSL:2 * TSL], ALU.mult, ALU.subtract),
                    reads=[Bf("bct")], writes=[CAB[p]])
                kb.op("V", "V", lambda v, po=po: v.tensor_tensor(
                    cacc[:, po:po + TSL], cacc[:, po:po + TSL],
                    x1t[:, po:po + TSL], ALU.add),
                    reads=[X1B[p]], writes=[CAB[p]])
                kb.op("Q", "QS", lambda q, po=po, ht=ht, b=b: q.dma_start(
                    out=out_ext[b, ht * 128:(ht + 1) * 128, :],
                    in_=cacc[:, po:po + TSL]),
                    reads=[CAB[p]], writes=[Bf("out")])

        for part in range(4):
            p1_chunk(0, part)
        p1_ar(0)
        for b in range(B):
            rows_dt(b)
            bc_tq(b, 0)
            scan_tq(b, 0)
            bc_tq(b, 1)
            scan_tq(b, 1)
            if b >= 1:
                epi_chunk(b - 1, 0)
            bc_tq(b, 2)
            scan_tq(b, 2)
            if b >= 1:
                epi_chunk(b - 1, 1)
                y_rs(b - 1)
            bc_tq(b, 3)
            scan_tq(b, 3)
            if b >= 1:
                epi_chunk(b - 1, 2)
                epi_chunk(b - 1, 3)
            bc_tq(b, 4)
            scan_tq(b, 4)
            bc_tq(b, 5)
            scan_tq(b, 5)
            if b + 1 < B:
                p1_chunk(b + 1, 0)
            bc_tq(b, 6)
            scan_tq(b, 6)
            if b + 1 < B:
                p1_chunk(b + 1, 1)
            bc_tq(b, 7)
            scan_tq(b, 7)
            if b + 1 < B:
                p1_chunk(b + 1, 2)
                p1_chunk(b + 1, 3)
                p1_ar(b + 1)
        epi_chunk(B - 1, 0)
        epi_chunk(B - 1, 1)
        y_rs(B - 1)
        epi_chunk(B - 1, 2)
        epi_chunk(B - 1, 3)

        kb.op("G", "G", lambda g: g.nop(),
              extra=[x for x in (bufs["out"].w,) if x is not None])

        kb.finalize()

        with contextlib.ExitStack() as st2:
            sems = {}
            ccs = ["CCL"] + [f"CCA{b}" for b in range(B)] + \
                  [f"CCR{b}" for b in range(B)]
            pools = [f"{q}{i}" for q, n in KB.POOLS.items() for i in range(n)]
            for s in ["T", "V", "S", "G", "Q", "QS", "QA", "QG"] + ccs + pools:
                sems[s] = st2.enter_context(nc.semaphore(f"sem_{s}"))
            with nc.Block() as block:
                @block.tensor
                def _(e):
                    kb.emit("T", e, sems)

                @block.vector
                def _(e):
                    kb.emit("V", e, sems)

                @block.scalar
                def _(e):
                    kb.emit("S", e, sems)

                @block.gpsimd
                def _(e):
                    kb.emit("G", e, sems)

                @block.sync
                def _(e):
                    kb.emit("Q", e, sems)
    return nc


_NC_CACHE = None


def kernel(**inputs):
    global _NC_CACHE
    f32 = np.float32

    hs = np.asarray(inputs["L0_attn_hidden_states"], f32)
    W1 = np.asarray(inputs["L0_attn_in_proj_w"], f32)
    convw = np.asarray(inputs["L0_attn_conv_w"], f32)
    convb = np.asarray(inputs["L0_attn_conv_b"], f32)
    xpw = np.asarray(inputs["L0_attn_x_proj_w"], f32)
    dtp = np.asarray(inputs["L0_attn_dt_proj_w"], f32)
    dtbias = np.asarray(inputs["L0_attn_dt_bias"], f32)
    alog = np.asarray(inputs["L0_attn_A_log"], f32)
    dparam = np.asarray(inputs["L0_attn_D_param"], f32)
    wo = np.asarray(inputs["L0_attn_out_proj_w"], f32)
    dtlnw = np.asarray(inputs["L0_attn_dt_ln_w"], f32)
    blnw = np.asarray(inputs["L0_attn_B_ln_w"], f32)
    clnw = np.asarray(inputs["L0_attn_C_ln_w"], f32)
    wf = np.asarray(inputs["L0_ffn_W"], f32)
    fbv = np.asarray(inputs["L0_ffn_b"], f32)
    fA = np.asarray(inputs["L0_ffn_A"], f32)
    fBm = np.asarray(inputs["L0_ffn_Bm"], f32)
    fWr = np.asarray(inputs["L0_ffn_Wr"], f32)

    hsT = np.ascontiguousarray(hs.reshape(B * L, H).T)
    woT_full = np.ascontiguousarray(wo.T)
    wfT_full = np.ascontiguousarray(wf.T)
    wf_sh = np.ascontiguousarray(
        wfT_full.reshape(32, 128, 32, 128).transpose(2, 1, 0, 3)
        .reshape(32, 128, 4096))
    dtpT_full = np.ascontiguousarray(dtp.T)
    xpwT_full = np.ascontiguousarray(xpw.T)
    fAWr = np.concatenate([fA.reshape(32, H), fWr], axis=0)
    fAWrT = np.ascontiguousarray(fAWr.T)
    fBmT = np.ascontiguousarray(fBm.transpose(0, 2, 1).reshape(32, H))

    in_maps = []
    for k in range(NCK):
        dsl = slice(k * DSH, (k + 1) * DSH)
        res = np.stack([
            np.ascontiguousarray(hsT[:, b * L + k * TSL:b * L + (k + 1) * TSL])
            for b in range(B)])
        w1k = np.concatenate(
            [W1[:, dsl], W1[:, DIN + k * DSH:DIN + (k + 1) * DSH]], axis=1)
        w1_sh = np.ascontiguousarray(
            w1k.reshape(32, 128, 16, 128).transpose(2, 1, 0, 3)
            .reshape(16, 128, 4096))
        wo_sh = np.ascontiguousarray(
            woT_full[dsl].reshape(8, 128, 8, 512).transpose(2, 1, 0, 3)
            .reshape(8, 128, 4096))
        in_maps.append({
            "hs_prepT": np.ascontiguousarray(hsT[:, k * 1024:(k + 1) * 1024]),
            "hs_resT": res,
            "w1": w1_sh,
            "woTs": wo_sh,
            "wfT": wf_sh,
            "convw": np.ascontiguousarray(convw[dsl]),
            "convb": np.ascontiguousarray(convb[dsl]),
            "dtbias": np.ascontiguousarray(dtbias[dsl]),
            "dparam": np.ascontiguousarray(dparam[dsl]),
            "alog": np.ascontiguousarray(alog[dsl]),
            "xpwT": np.ascontiguousarray(xpwT_full[dsl]),
            "dtpT": np.ascontiguousarray(dtpT_full[:, dsl]),
            "dtlnw": dtlnw, "blnw": blnw, "clnw": clnw,
            "fb": fbv, "fAWrT": fAWrT, "fBmT": fBmT,
        })

    if _NC_CACHE is None:
        _NC_CACHE = build_program()
    res = run_bass_kernel_spmd(_NC_CACHE, in_maps, core_ids=list(range(NCK)))
    full = np.empty((B, L, H), np.float32)
    for k in range(NCK):
        full[:, k * TSL:(k + 1) * TSL, :] = res.results[k]["out"].transpose(0, 2, 1)
    return full


# version 13
# speedup vs baseline: 1.2715x; 1.1366x over previous
import numpy as np
from collections import defaultdict
import contextlib

import concourse.bass as bass
import concourse.mybir as mybir
from concourse.bass_utils import run_bass_kernel_spmd

F32 = mybir.dt.float32
BF16 = mybir.dt.bfloat16
U32 = mybir.dt.uint32
AF = mybir.ActivationFunctionType
ALU = mybir.AluOpType

NCK = 8
H = 4096
DIN = 8192
DSH = DIN // NCK
NDT = DSH // 128
NST = 16
R = 256
E3 = 288
B = 4
L = 2048
TSL = L // NCK
NKT = H // 128
EPS_LN = 1e-5
EPS_RMS = 1e-6
SCALE = 2.0
TQ = 256
TQS = TQ + 1
NTQ = L // TQ


class Buf:
    __slots__ = ("name", "w", "r")

    def __init__(self, name):
        self.name = name
        self.w = None
        self.r = []


class KB:
    CSTREAMS = ("T", "V", "S", "G", "Q")

    def __init__(self):
        self.ops = []

    def op(self, eng, stream, fn, reads=(), writes=(), extra=()):
        idx = len(self.ops)
        waits = set(e for e in extra if e is not None)
        for b in reads:
            if b.w is not None:
                waits.add(b.w)
        for b in writes:
            if b.w is not None:
                waits.add(b.w)
            waits.update(b.r)
        for b in writes:
            b.w = idx
            b.r = []
        for b in reads:
            b.r.append(idx)
        amt = 16 if stream in ("QS", "QA", "QG") else 1
        self.ops.append([eng, stream, fn, waits, amt])
        return idx

    POOLS = {"QS": 24, "QG": 16, "QA": 8}
    STRICT = False

    def finalize(self):
        used = set()
        for _, _, _, waits, _ in self.ops:
            used.update(waits)
        val = {}
        cnt = defaultdict(int)
        pool_ctr = defaultdict(int)
        self.pre_wait = {}
        self.inc_flag = []
        for idx, (eng, stream, fn, waits, amt) in enumerate(self.ops):
            if stream in self.POOLS:
                n = self.POOLS[stream]
                c = pool_ctr[stream]
                pool_ctr[stream] += 1
                sem_name = f"{stream}{c % n}"
                u = c // n
                self.pre_wait[idx] = (sem_name, 16 * u)
                val[idx] = (sem_name, 16 * (u + 1))
                self.inc_flag.append(True)
                cnt[sem_name] = 16 * (u + 1)
            else:
                keep = self.STRICT or (stream not in self.CSTREAMS) or (idx in used)
                self.inc_flag.append(keep)
                if keep:
                    cnt[stream] += amt
                    val[idx] = (stream, cnt[stream])
        self.ev_val = val
        self.final_counts = dict(cnt)

    def emit(self, eng_key, handle, sems):
        waited = defaultdict(int)
        for idx, (eng, stream, fn, waits, amt) in enumerate(self.ops):
            if eng != eng_key:
                continue
            pw = self.pre_wait.get(idx)
            if pw is not None and pw[1] > 0 and waited[pw[0]] < pw[1]:
                handle.wait_ge(sems[pw[0]], pw[1])
                waited[pw[0]] = pw[1]
            for wev in sorted(waits):
                wsv = self.ev_val.get(wev)
                if wsv is None:
                    continue
                ws, wv = wsv
                if (not self.STRICT) and ws == eng and ws in self.CSTREAMS:
                    continue
                if waited[ws] >= wv:
                    continue
                handle.wait_ge(sems[ws], wv)
                waited[ws] = wv
            ins = fn(handle)
            if self.inc_flag[idx]:
                s, v = self.ev_val[idx]
                ins.then_inc(sems[s], amt)


def build_program():
    nc = bass.Bass("TRN2", target_bir_lowering=False, debug=False)

    def ein(name, shp, dt=F32):
        return nc.dram_tensor(name, shp, dt, kind="ExternalInput").ap()

    hs_prepT = ein("hs_prepT", [H, 1024])
    hs_resT = ein("hs_resT", [B, H, TSL])
    w1 = ein("w1", [16, 128, 4096])
    woTs = ein("woTs", [8, 128, 4096])
    wfT = ein("wfT", [32, 128, 4096])
    convw = ein("convw", [DSH, 4])
    convb = ein("convb", [DSH])
    dtbias = ein("dtbias", [DSH])
    dparam = ein("dparam", [DSH])
    alog = ein("alog", [DSH, NST])
    xpwT = ein("xpwT", [DSH, E3])
    dtpT = ein("dtpT", [R, DSH])
    dtlnw = ein("dtlnw", [R])
    blnw = ein("blnw", [NST])
    clnw = ein("clnw", [NST])
    fbi = ein("fb", [H])
    fAWrT = ein("fAWrT", [H, 36])
    fBmT = ein("fBmT", [32, H])
    out_ext = nc.dram_tensor("out", [B, H, TSL], F32, kind="ExternalOutput").ap()

    dr = lambda n, shp, dt: nc.dram_tensor(n, shp, dt).ap()
    ln_in = dr("ln_in", [H, 1024], BF16)
    ln_all = dr("ln_all", [NCK, H, 1024], BF16)
    w1t = dr("w1t", [16, 128, 4096], BF16)
    wos = dr("wos", [8, 128, 4096], BF16)
    wf2 = dr("wf2", [NKT, 128, 4096], BF16)
    arin = [dr(f"arin{b}", [E3, L], F32) for b in range(B)]
    arout = [dr(f"arout{b}", [E3, L], F32) for b in range(B)]
    yin = [dr(f"yin{b}", [DSH, L], BF16) for b in range(B)]
    pmix = [dr(f"pmix{b}", [NCK, H, TSL], BF16) for b in range(B)]
    rsout = [dr(f"rsout{b}", [H, TSL], BF16) for b in range(B)]
    xcsp = dr("xcsp", [B, NDT, 128, L], BF16)
    gsp = dr("gsp", [B, NDT, 128, L], BF16)
    dtlsp = dr("dtlsp", [B, NDT, 128, L], BF16)
    x1sp = dr("x1sp", [B, H, TSL], F32)

    kb = KB()
    st = contextlib.ExitStack()
    T = {}

    def sb(name, shp, dt):
        t = st.enter_context(nc.sbuf_tensor(name, shp, dt))
        T[name] = t
        return t

    def psm(name, shp):
        t = st.enter_context(nc.psum_tensor(name, shp, F32))
        T[name] = t
        return t

    with st:
        SLOT = [sb(f"sl{i}", [128, 8192], BF16) for i in range(5)]
        dAp = sb("dAp", [128, NST * TQS], BF16)
        wl1 = sb("wl1", [128, NST * TQS], BF16)
        hl2 = sb("hl2", [128, NST * TQS], BF16)
        dApB = sb("dApB", [128, NST * TQS], BF16)
        wl1B = sb("wl1B", [128, NST * TQS], BF16)
        hl2B = hl2
        bcB = sb("bcB", [128, NST * TQ], BF16)
        bcC = sb("bcC", [128, NST * TQ], BF16)
        dtl_t = sb("dtl_t", [128, 2 * TQ], BF16)
        xc_t = sb("xc_t", [128, 2 * TQ], BF16)
        g_t = sb("g_t", [128, 2 * TQ], BF16)
        u_t = sb("u_t", [128, TQ], BF16)
        ys_t = sb("ys_t", [128, TQ], F32)
        t2_t = sb("t2_t", [128, TQ], BF16)
        ygd_t = sb("ygd_t", [128, 2 * TQ], BF16)
        hcar = sb("hcar", [128, NDT * NST], BF16)
        xconv = sb("xconv", [128, 2 * (TQ + 3)], BF16)
        tails = sb("tails", [128, NDT * 3], BF16)
        cacc = sb("cacc", [128, 2 * TQ], F32)
        xcw = sb("xcw", [128, 2 * TQ], BF16)
        gev = sb("gev", [128, 2 * 512], BF16)
        xcr = sb("xcr", [128, 512], BF16)
        spsc = sb("spsc", [96, 512], F32)
        rowpack = sb("rowpack", [65, 2 * L + 128 + 1024 + 128], BF16)
        bcrows = rowpack[0:16, :]
        spd = sb("spd", [128, 1024], BF16)
        c_sum = sb("c_sum", [1, 512], F32)
        c_ssq = sb("c_ssq", [1, 512], F32)
        c_m = sb("c_m", [1, 512], F32)
        s_scr = sb("s_scr", [1, 512], F32)
        rb_rstd = rowpack[0:1, 4224:5248]
        rb_mrstd = rowpack[32:33, 4224:5248]
        ones32 = rowpack[32:33, 5248:5376]
        rb_rb = rowpack[64:65, 0:L]
        rb_rc = rowpack[64:65, L:2 * L]
        ones64 = rowpack[64:65, 2 * L:2 * L + 128]
        rb_tmp = sb("rb_tmp", [1, TSL], BF16)
        gl4 = sb("gl4", [4, 2 * TSL], F32)
        gb4 = sb("gb4", [4, 2 * TSL], BF16)
        bct4 = sb("bct4", [4, TSL], F32)
        g32 = sb("g32", [32, TSL], BF16)
        prs = sb("prs", [32, TSL], BF16)
        scr36 = sb("scr36", [36, TSL], BF16)
        negA = sb("negA", [128, NDT * NST], F32)
        cw_sb = sb("cw_sb", [128, NDT * 4], F32)
        cb_sb = sb("cb_sb", [128, NDT], F32)
        ndtb = sb("ndtb", [128, NDT], F32)
        d_sb = sb("d_sb", [128, NDT], F32)
        dtlnw_sb = sb("dtlnw_sb", [128, 2], F32)
        bclnw = sb("bclnw", [16, 2], F32)
        dtp_sb = sb("dtp_sb", [128, 16 * 128], BF16)
        xpw_sb = sb("xpw_sb", [128, NDT * E3], BF16)
        awr_sb = sb("awr_sb", [128, NKT * 36], BF16)
        bmt_t = sb("bmt_t", [32, 2 * 128], BF16)
        aug2r = sb("aug2r", [2, TSL], BF16)
        colsA = sb("colsA", [1, 36], BF16)
        ones_c = sb("ones_c", [128, 1], BF16)
        ones_r = sb("ones_r", [1, 128], BF16)
        ones16 = sb("ones16", [16, 128], BF16)

        srow = sb("srow", [1, TQ], BF16)
        sel = sb("sel", [4, 32], BF16)
        sel2 = sb("sel2", [36, 4], BF16)
        x0t = sb("x0t", [128, TSL], F32)
        x1t = sb("x1t", [128, 2 * TSL], F32)
        bct = sb("bct", [128, 3 * TSL], F32)
        aug2 = sb("aug2", [2, H], BF16)
        dmask = sb("dmask", [16, NST], BF16)

        psA = psm("psA", [128, 512])
        psB = psm("psB", [128, 512])
        psE = psm("psE", [96, 512])
        psF = psm("psF", [128, 512])
        psG = psm("psG", [128, 512])
        psH = psm("psH", [128, 1024])
        psI = psm("psI", [128, 512])

        bufs = {}

        def Bf(n):
            if n not in bufs:
                bufs[n] = Buf(n)
            return bufs[n]

        for n in list(T):
            Bf(n)
        for n in ("ln_in", "ln_all", "w1t", "wos", "wf2", "xcsp", "gsp",
                  "dtlsp", "x1sp", "out", "sl4a", "sl4b",
                  "dtl0", "dtl1", "xct0", "xct1", "gt0", "gt1", "ygd0", "ygd1",
                  "x0t0", "x0t1", "x1t0", "x1t1", "ca0", "ca1", "gev0", "gev1",
                  "xcr0", "xcr1"):
            Bf(n)
        for b in range(B):
            for n in ("arin", "arout", "yin", "pmix", "rsout"):
                Bf(f"{n}{b}")
        SB = [Bf(f"sl{i}") for i in range(5)]
        SB4 = [Bf("sl4a"), Bf("sl4b")]
        DTLB = [Bf("dtl0"), Bf("dtl1")]
        XCTB = [Bf("xct0"), Bf("xct1")]
        GTB = [Bf("gt0"), Bf("gt1")]
        YGDB = [Bf("ygd0"), Bf("ygd1")]
        X0B = [Bf("x0t0"), Bf("x0t1")]
        X1B = [Bf("x1t0"), Bf("x1t1")]
        CAB = [Bf("ca0"), Bf("ca1")]
        GEVB = [Bf("gev0"), Bf("gev1")]
        XCRB = [Bf("xcr0"), Bf("xcr1")]
        spcB = wl1[0:16, 0:L]
        spcC = wl1[0:16, L:2 * L]

        kb.op("V", "V", lambda v: v.memset(ones_c[:, :], 1.0), writes=[Bf("ones_c")])
        kb.op("V", "V", lambda v: v.memset(ones_r[:, :], 1.0), writes=[Bf("ones_r")])
        kb.op("V", "V", lambda v: v.memset(ones64, 1.0), writes=[Bf("ones64")])
        kb.op("V", "V", lambda v: v.memset(ones32, 1.0), writes=[Bf("ones32")])
        kb.op("V", "V", lambda v: v.memset(ones16[:, :], 1.0), writes=[Bf("ones16")])
        kb.op("V", "V", lambda v: v.memset(tails[:, :], 0.0), writes=[Bf("tails")])
        kb.op("V", "V", lambda v: v.memset(sel[:, :], 0.0), writes=[Bf("sel")])
        kb.op("V", "V", lambda v: v.memset(dmask[:, :], 0.0), writes=[Bf("dmask")])
        kb.op("V", "V", lambda v: v.memset(sel2[:, :], 0.0), writes=[Bf("sel2")])
        kb.op("V", "V", lambda v: v.memset(srow[:, :], 1.0), writes=[Bf("srow")])
        for e in range(4):
            kb.op("G", "QG", lambda g, e=e: g.dma_start(
                out=sel[e:e + 1, e * 8:(e + 1) * 8].rearrange("o f -> o f") if False
                else sel[e:e + 1, e * 8:(e + 1) * 8],
                in_=srow[0:1, 0:8]), reads=[Bf("srow")], writes=[Bf("sel")])
            kb.op("G", "QG", lambda g, e=e: g.dma_start(
                out=sel2[32 + e:33 + e, e:e + 1], in_=srow[0:1, 0:1]),
                reads=[Bf("srow")], writes=[Bf("sel2")])
        for n in range(NST):
            kb.op("G", "QG", lambda g, n=n: g.dma_start(
                out=dmask[n:n + 1, n:n + 1], in_=srow[0:1, 0:1]),
                reads=[Bf("srow")], writes=[Bf("dmask")])
        kb.op("V", "V", lambda v: v.tensor_scalar_mul(sel[:, :], sel[:, :], SCALE),
              reads=[Bf("sel")], writes=[Bf("sel")])

        for dt in range(NDT):
            kb.op("G", "QG", lambda g, dt=dt: g.dma_start(
                out=negA[:, dt * NST:(dt + 1) * NST],
                in_=alog[dt * 128:(dt + 1) * 128, :]), writes=[Bf("negA")])
            kb.op("G", "QG", lambda g, dt=dt: g.dma_start(
                out=cw_sb[:, dt * 4:(dt + 1) * 4],
                in_=convw[dt * 128:(dt + 1) * 128, :]), writes=[Bf("cw_sb")])
            kb.op("G", "QG", lambda g, dt=dt: g.dma_start(
                out=cb_sb[:, dt:dt + 1],
                in_=convb[dt * 128:(dt + 1) * 128].unsqueeze(1)), writes=[Bf("cb_sb")])
            kb.op("G", "QG", lambda g, dt=dt: g.dma_start(
                out=ndtb[:, dt:dt + 1],
                in_=dtbias[dt * 128:(dt + 1) * 128].unsqueeze(1)), writes=[Bf("ndtb")])
            kb.op("G", "QG", lambda g, dt=dt: g.dma_start(
                out=d_sb[:, dt:dt + 1],
                in_=dparam[dt * 128:(dt + 1) * 128].unsqueeze(1)), writes=[Bf("d_sb")])
            for ec in range(3):
                kb.op("G", "QG", lambda g, dt=dt, ec=ec: g.dma_start(
                    out=xpw_sb[:, dt * E3 + ec * 96:dt * E3 + (ec + 1) * 96],
                    in_=xpwT[dt * 128:(dt + 1) * 128, ec * 96:(ec + 1) * 96]),
                    writes=[Bf("xpw_sb")])
        kb.op("S", "S", lambda s: s.activation(negA[:, :], negA[:, :], AF.Exp),
              reads=[Bf("negA")], writes=[Bf("negA")])
        kb.op("V", "V", lambda v: v.tensor_scalar_mul(ndtb[:, :], ndtb[:, :], -1.0),
              reads=[Bf("ndtb")], writes=[Bf("ndtb")])
        for c in range(2):
            kb.op("G", "QG", lambda g, c=c: g.dma_start(
                out=dtlnw_sb[:, c:c + 1],
                in_=dtlnw[c * 128:(c + 1) * 128].unsqueeze(1)),
                writes=[Bf("dtlnw_sb")])
        kb.op("G", "QG", lambda g: g.dma_start(out=bclnw[:, 0:1],
                                               in_=blnw.unsqueeze(1)),
              writes=[Bf("bclnw")])
        kb.op("G", "QG", lambda g: g.dma_start(out=bclnw[:, 1:2],
                                               in_=clnw.unsqueeze(1)),
              writes=[Bf("bclnw")])
        for rt in range(2):
            for dt in range(NDT):
                j = rt * NDT + dt
                kb.op("G", "QG", lambda g, rt=rt, dt=dt, j=j: g.dma_start(
                    out=dtp_sb[:, j * 128:(j + 1) * 128],
                    in_=dtpT[rt * 128:(rt + 1) * 128, dt * 128:(dt + 1) * 128]),
                    writes=[Bf("dtp_sb")])
        for it in range(NKT):
            kb.op("G", "QG", lambda g, it=it: g.dma_start(
                out=awr_sb[:, it * 36:(it + 1) * 36],
                in_=fAWrT[it * 128:(it + 1) * 128, :]), writes=[Bf("awr_sb")])
        kb.op("G", "QG", lambda g: g.dma_start(out=aug2[1:2, :], in_=fbi.unsqueeze(0)),
              writes=[Bf("aug2")])

        def ln_rowmath(width, off=0):
            kb.op("V", "V", lambda v: v.tensor_scalar_mul(
                c_m[0:1, 0:width], c_sum[0:1, 0:width], 1.0 / H),
                reads=[Bf("c_sum")], writes=[Bf("c_m")])
            kb.op("S", "S", lambda s: s.activation(
                c_sum[0:1, 0:width], c_m[0:1, 0:width], AF.Square),
                reads=[Bf("c_m")], writes=[Bf("c_sum")])
            kb.op("V", "V", lambda v: v.scalar_tensor_tensor(
                c_ssq[0:1, 0:width], c_ssq[0:1, 0:width], 1.0 / H,
                c_sum[0:1, 0:width], ALU.mult, ALU.subtract),
                reads=[Bf("c_sum")], writes=[Bf("c_ssq")])
            kb.op("V", "V", lambda v: v.tensor_scalar_add(
                c_ssq[0:1, 0:width], c_ssq[0:1, 0:width], EPS_LN),
                reads=[Bf("c_ssq")], writes=[Bf("c_ssq")])
            kb.op("S", "S", lambda s: s.activation(
                c_ssq[0:1, 0:width], c_ssq[0:1, 0:width], AF.Sqrt),
                reads=[Bf("c_ssq")], writes=[Bf("c_ssq")])
            kb.op("V", "V", lambda v: v.reciprocal(
                c_sum[0:1, 0:width], c_ssq[0:1, 0:width]),
                reads=[Bf("c_ssq")], writes=[Bf("c_sum")])
            kb.op("V", "V", lambda v: v.tensor_copy(
                rb_rstd[0:1, off:off + width], c_sum[0:1, 0:width]),
                reads=[Bf("c_sum")], writes=[Bf("rb_rstd")])
            kb.op("V", "V", lambda v: v.tensor_mul(
                c_m[0:1, 0:width], c_m[0:1, 0:width], c_sum[0:1, 0:width]),
                reads=[Bf("c_sum")], writes=[Bf("c_m")])
            kb.op("V", "V", lambda v: v.tensor_copy(
                rb_mrstd[0:1, off:off + width], c_m[0:1, 0:width]),
                reads=[Bf("c_m")], writes=[Bf("rb_mrstd")])

        for it in range(NKT):
            s, o = SLOT[it // 8], (it % 8) * 1024
            kb.op("G", "QG", lambda g, it=it, s=s, o=o: g.dma_start(
                out=s[:, o:o + 1024], in_=hs_prepT[it * 128:(it + 1) * 128, :]),
                writes=[SB[it // 8]])
        for it in range(NKT):
            s, o = SLOT[it // 8], (it % 8) * 1024
            for ch in range(2):
                kb.op("T", "T", lambda t, s=s, o=o, it=it, ch=ch: t.matmul(
                    psH[0:1, ch * 512:(ch + 1) * 512], ones_c[:, :],
                    s[:, o + ch * 512:o + (ch + 1) * 512],
                    start=(it == 0), stop=(it == NKT - 1)),
                    reads=[SB[it // 8], Bf("ones_c")], writes=[Bf("psH")])
        kb.op("S", "S", lambda s: s.copy(wl1.bitcast(F32)[0:1, 0:1024],
                                         psH[0:1, :]),
              reads=[Bf("psH")], writes=[Bf("wl1")])
        for it in range(NKT):
            s, o = SLOT[it // 8], (it % 8) * 1024
            q, qo = SLOT[4], (it % 4) * 1024
            kb.op("S", "S", lambda sc, s=s, o=o, q=q, qo=qo: sc.activation(
                q[:, qo:qo + 1024], s[:, o:o + 1024], AF.Square),
                reads=[SB[it // 8]], writes=[SB[4]])
            for ch in range(2):
                kb.op("T", "T", lambda t, q=q, qo=qo, it=it, ch=ch: t.matmul(
                    psH[0:1, ch * 512:(ch + 1) * 512], ones_c[:, :],
                    q[:, qo + ch * 512:qo + (ch + 1) * 512],
                    start=(it == 0), stop=(it == NKT - 1)),
                    reads=[SB[4], Bf("ones_c")], writes=[Bf("psH")])
        for ch in range(2):
            kb.op("S", "S", lambda s, ch=ch: s.copy(
                c_sum[:, :], wl1.bitcast(F32)[0:1, ch * 512:(ch + 1) * 512]),
                reads=[Bf("wl1")], writes=[Bf("c_sum")])
            kb.op("S", "S", lambda s, ch=ch: s.copy(
                c_ssq[:, :], psH[0:1, ch * 512:(ch + 1) * 512]),
                reads=[Bf("psH")], writes=[Bf("c_ssq")])
            ln_rowmath(512, off=ch * 512)
        for ch in range(2):
            kb.op("T", "T", lambda t, ch=ch: t.matmul(
                psG[:, :], ones_r[:, :], rb_rstd[0:1, ch * 512:(ch + 1) * 512],
                start=True, stop=True),
                reads=[Bf("rb_rstd"), Bf("ones_r")], writes=[Bf("psG")])
            kb.op("T", "T", lambda t, ch=ch: t.matmul(
                psF[:, :], ones32, rb_mrstd[0:1, ch * 512:(ch + 1) * 512],
                start=True, stop=True),
                reads=[Bf("rb_mrstd"), Bf("ones32")], writes=[Bf("psF")])
            for it in range(NKT):
                s, o = SLOT[it // 8], (it % 8) * 1024 + ch * 512
                q, qo = SLOT[4], 4096 + (it % 4) * 1024 + ch * 512
                kb.op("V", "V", lambda v, s=s, o=o, q=q, qo=qo: v.tensor_tensor(
                    q[:, qo:qo + 512], s[:, o:o + 512], psG[:, :], ALU.mult),
                    reads=[SB[it // 8], Bf("psG")], writes=[SB[4]])
                kb.op("V", "V", lambda v, q=q, qo=qo: v.tensor_tensor(
                    q[:, qo:qo + 512], q[:, qo:qo + 512], psF[:, :], ALU.subtract),
                    reads=[Bf("psF")], writes=[SB[4]])
                kb.op("Q", "QS", lambda qq, it=it, q=q, qo=qo, ch=ch: qq.dma_start(
                    out=ln_in[it * 128:(it + 1) * 128, ch * 512:(ch + 1) * 512],
                    in_=q[:, qo:qo + 512]),
                    reads=[SB[4]], writes=[Bf("ln_in")])
        fl = kb.op("Q", "Q", lambda q: q.nop(), reads=[Bf("ln_in")])
        kb.op("G", "CCL", lambda g: g.collective_compute(
            "AllGather", ALU.bypass, replica_groups=[list(range(NCK))],
            ins=[ln_in], outs=[ln_all]),
            reads=[Bf("ln_in")], writes=[Bf("ln_all")], extra=[fl])

        for kt in range(16):
            s = SLOT[kt % 2]
            kb.op("G", "QG", lambda g, kt=kt, s=s: g.dma_start(
                out=s[:, 0:4096], in_=w1[kt]), writes=[SB[kt % 2]])
            kb.op("Q", "QS", lambda q, kt=kt, s=s: q.dma_start(
                out=w1t[kt], in_=s[:, 0:4096]),
                reads=[SB[kt % 2]], writes=[Bf("w1t")])
        for dt in range(NDT):
            s = SLOT[dt % 2]
            kb.op("G", "QG", lambda g, dt=dt, s=s: g.dma_start(
                out=s[:, 0:H], in_=woTs[dt]), writes=[SB[dt % 2]])
            kb.op("Q", "QS", lambda q, dt=dt, s=s: q.dma_start(
                out=wos[dt], in_=s[:, 0:H]),
                reads=[SB[dt % 2]], writes=[Bf("wos")])
        kb.op("V", "V", lambda v: v.memset(aug2[0:1, :], 0.0), writes=[Bf("aug2")])
        for it in range(NKT):
            s = SLOT[2 + it % 2]
            kb.op("G", "QG", lambda g, it=it, s=s: g.dma_start(
                out=s[:, 0:H], in_=wfT[it]), writes=[SB[2 + it % 2]])
            kb.op("Q", "QS", lambda q, it=it, s=s: q.dma_start(
                out=wf2[it], in_=s[:, 0:H]),
                reads=[SB[2 + it % 2]], writes=[Bf("wf2")])
            for ch in range(NKT):
                kb.op("T", "T", lambda t, s=s, ch=ch, it=it: t.matmul(
                    psH[0:1, 0:128], ones_c[:, :], s[:, ch * 128:(ch + 1) * 128],
                    start=(ch == 0), stop=(ch == NKT - 1)),
                    reads=[SB[2 + it % 2], Bf("ones_c")], writes=[Bf("psH")])
            kb.op("V", "V", lambda v, it=it: v.tensor_copy(
                aug2[0:1, it * 128:(it + 1) * 128], psH[0:1, 0:128]),
                reads=[Bf("psH")], writes=[Bf("aug2")])
        for it in range(NKT):
            kb.op("T", "T", lambda t, it=it: t.matmul(
                psH[0:1, 0:36], ones_c[:, :], awr_sb[:, it * 36:(it + 1) * 36],
                start=(it == 0), stop=(it == NKT - 1)),
                reads=[Bf("awr_sb"), Bf("ones_c")], writes=[Bf("psH")])
        kb.op("V", "V", lambda v: v.tensor_copy(colsA[:, :], psH[0:1, 0:36]),
              reads=[Bf("psH")], writes=[Bf("colsA")])

        def p1_chunk(b, part):
            th, cth = divmod(part, 2)
            if cth == 0:
                for kt in range(NKT):
                    s, o = SLOT[kt // 8], (kt % 8) * 1024
                    chunk = 2 * b + th
                    kb.op("Q", "QS", lambda q, kt=kt, s=s, o=o, chunk=chunk:
                          q.dma_start(
                        out=s[:, o:o + 1024],
                        in_=ln_all[chunk, kt * 128:(kt + 1) * 128, :]),
                        reads=[Bf("ln_all")], writes=[SB[kt // 8]])
            for ci in range(8):
                ct = cth * 8 + ci
                pw = ct % 2
                ws, wof = SLOT[4], pw * 4096
                wb = SB4[pw]
                kb.op("S", "QA", lambda q, ct=ct, ws=ws, wof=wof: q.dma_start(
                    out=ws[:, wof:wof + 4096], in_=w1t[ct]),
                    reads=[Bf("w1t")], writes=[wb])
                for half in range(2):
                    pst, pB = ((psA, Bf("psA")) if half == 0 else (psB, Bf("psB")))
                    for kt in range(NKT):
                        s, o = SLOT[kt // 8], (kt % 8) * 1024
                        kb.op("T", "T", lambda t, s=s, o=o, ws=ws, wof=wof, kt=kt,
                              half=half, pst=pst: t.matmul(
                            pst[:, :], ws[:, wof + kt * 128:wof + (kt + 1) * 128],
                            s[:, o + half * 512:o + (half + 1) * 512],
                            start=(kt == 0), stop=(kt == NKT - 1)),
                            reads=[SB[kt // 8], wb], writes=[pB])
                    if ct < 8:
                        dtc = ct
                        for tq in range(2):
                            gtq = th * 4 + half * 2 + tq
                            p = gtq % 2
                            xo = p * (TQ + 3)
                            cB = Bf("xconv")
                            kb.op("S", "S", lambda s, xo=xo, pst=pst, tq=tq: s.copy(
                                xconv[:, xo + 3:xo + 3 + TQ],
                                pst[:, tq * TQ:(tq + 1) * TQ]),
                                reads=[pB], writes=[cB])
                            if gtq == 0:
                                kb.op("V", "V", lambda v, xo=xo: v.memset(
                                    xconv[:, xo:xo + 3], 0.0), writes=[cB])
                            else:
                                kb.op("V", "V", lambda v, xo=xo, dtc=dtc:
                                      v.tensor_copy(
                                    xconv[:, xo:xo + 3],
                                    tails[:, dtc * 3:dtc * 3 + 3]),
                                    reads=[Bf("tails")], writes=[cB])
                            kb.op("V", "V", lambda v, xo=xo, dtc=dtc: v.tensor_copy(
                                tails[:, dtc * 3:dtc * 3 + 3],
                                xconv[:, xo + TQ:xo + TQ + 3]),
                                reads=[cB], writes=[Bf("tails")])
                            kb.op("V", "V", lambda v, xo=xo, dtc=dtc: v.tensor_scalar(
                                cacc[:, 0:TQ], xconv[:, xo:xo + TQ],
                                cw_sb[:, dtc * 4:dtc * 4 + 1],
                                cb_sb[:, dtc:dtc + 1], ALU.mult, ALU.add),
                                reads=[cB, Bf("cw_sb"), Bf("cb_sb")],
                                writes=[CAB[0]])
                            for k in (1, 2, 3):
                                src, dst = ((k + 1) % 2) * TQ, (k % 2) * TQ
                                kb.op("V", "V", lambda v, xo=xo, dtc=dtc, k=k,
                                      src=src, dst=dst: v.scalar_tensor_tensor(
                                    cacc[:, dst:dst + TQ],
                                    xconv[:, xo + k:xo + k + TQ],
                                    cw_sb[:, dtc * 4 + k:dtc * 4 + k + 1],
                                    cacc[:, src:src + TQ], ALU.mult, ALU.add),
                                    reads=[cB, Bf("cw_sb"), CAB[(k + 1) % 2]],
                                    writes=[CAB[k % 2]])
                            kb.op("S", "S", lambda s, tq=tq: s.activation(
                                xcw[:, tq * TQ:(tq + 1) * TQ],
                                cacc[:, TQ:2 * TQ], AF.Silu),
                                reads=[CAB[1]], writes=[Bf("xcw")])
                            if tq == 1:
                                kb.op("Q", "QS", lambda q, dtc=dtc, b=b, gtq=gtq:
                                      q.dma_start(
                                    out=xcsp[b, dtc, :,
                                             (gtq - 1) * TQ:(gtq + 1) * TQ],
                                    in_=xcw[:, :]),
                                    reads=[Bf("xcw")], writes=[Bf("xcsp")])
                    else:
                        dtc = ct - 8
                        gp = half
                        go = gp * 512
                        kb.op("S", "S", lambda s, go=go, pst=pst: s.activation(
                            gev[:, go:go + 512], pst[:, :], AF.Silu),
                            reads=[pB], writes=[GEVB[gp]])
                        kb.op("Q", "QS", lambda q, go=go, gp=gp, dtc=dtc, b=b,
                              th=th, half=half: q.dma_start(
                            out=gsp[b, dtc, :, th * 1024 + half * 512:
                                    th * 1024 + (half + 1) * 512],
                            in_=gev[:, go:go + 512]),
                            reads=[GEVB[gp]], writes=[Bf("gsp")])
            if cth == 1:
                for ec in range(3):
                    for hh in range(2):
                        for dt in range(NDT):
                            kb.op("Q", "QS", lambda q, dt=dt, b=b,
                                  th=th, hh=hh: q.dma_start(
                                out=xcr[:, :],
                                in_=xcsp[b, dt, :, th * 1024 + hh * 512:
                                         th * 1024 + (hh + 1) * 512]),
                                reads=[Bf("xcsp")], writes=[XCRB[0]])
                            kb.op("T", "T", lambda t, ec=ec, dt=dt: t.matmul(
                                psE[:, :],
                                xpw_sb[:, dt * E3 + ec * 96:dt * E3 + (ec + 1) * 96],
                                xcr[:, :],
                                start=(dt == 0), stop=(dt == NDT - 1)),
                                reads=[Bf("xpw_sb"), XCRB[0]], writes=[Bf("psE")])
                        kb.op("S", "S", lambda s: s.copy(spsc[:, :], psE[:, :]),
                              reads=[Bf("psE")], writes=[Bf("spsc")])
                        kb.op("Q", "QS", lambda q, ec=ec, hh=hh, b=b, th=th:
                              q.dma_start(
                            out=arin[b][ec * 96:(ec + 1) * 96,
                                        th * 1024 + hh * 512:
                                        th * 1024 + (hh + 1) * 512],
                            in_=spsc[:, :]),
                            reads=[Bf("spsc")], writes=[Bf(f"arin{b}")])

        def p1_ar(b):
            fl = kb.op("Q", "Q", lambda q: q.nop(), reads=[Bf(f"arin{b}")])
            kb.op("G", f"CCA{b}", lambda g, b=b: g.collective_compute(
                "AllReduce", ALU.add, replica_groups=[list(range(NCK))],
                ins=[arin[b]], outs=[arout[b]]),
                reads=[Bf(f"arin{b}")], writes=[Bf(f"arout{b}")], extra=[fl])

        def rows_dt(b):
            kb.op("V", "V", lambda v: v.memset(hcar[:, :], 0.0),
                  writes=[Bf(f"hcar{d}") for d in range(NDT)])
            kb.op("G", "QG", lambda g, b=b: g.dma_start(
                out=spcB, in_=arout[b][256:272, :]),
                reads=[Bf(f"arout{b}")], writes=[Bf("wl1")])
            kb.op("G", "QG", lambda g, b=b: g.dma_start(
                out=spcC, in_=arout[b][272:288, :]),
                reads=[Bf(f"arout{b}")], writes=[Bf("wl1")])
            kb.op("S", "S", lambda s: s.activation(
                dAp[0:16, 0:L], spcB, AF.Square),
                reads=[Bf("wl1")], writes=[Bf("dAp")])
            kb.op("S", "S", lambda s: s.activation(
                dAp[0:16, L:2 * L], spcC, AF.Square),
                reads=[Bf("wl1")], writes=[Bf("dAp")])
            for (co, rbB_) in ((0, Bf("rb_rb")), (L, Bf("rb_rc"))):
                rb = rb_rb if co == 0 else rb_rc
                for ch in range(4):
                    kb.op("T", "T", lambda t, co=co, ch=ch: t.matmul(
                        psH[0:1, 0:512], ones_c[0:16, :],
                        dAp[0:16, co + ch * 512:co + (ch + 1) * 512],
                        start=True, stop=True),
                        reads=[Bf("dAp"), Bf("ones_c")], writes=[Bf("psH")])
                    kb.op("V", "V", lambda v: v.tensor_scalar(
                        s_scr[:, :], psH[0:1, 0:512], 1.0 / NST, EPS_RMS,
                        ALU.mult, ALU.add),
                        reads=[Bf("psH")], writes=[Bf("s_scr")])
                    kb.op("S", "S", lambda s: s.activation(
                        s_scr[:, :], s_scr[:, :], AF.Sqrt),
                        reads=[Bf("s_scr")], writes=[Bf("s_scr")])
                    kb.op("V", "V", lambda v: v.reciprocal(s_scr[:, :], s_scr[:, :]),
                          reads=[Bf("s_scr")], writes=[Bf("s_scr")])
                    kb.op("V", "V", lambda v, rb=rb, ch=ch: v.tensor_copy(
                        rb[0:1, ch * 512:(ch + 1) * 512], s_scr[:, :]),
                        reads=[Bf("s_scr")], writes=[rbB_])
            kb.op("V", "V", lambda v: v.tensor_scalar(
                bcrows[:, 0:L], spcB, bclnw[:, 0:1], -1.0, ALU.mult, ALU.mult),
                reads=[Bf("wl1"), Bf("bclnw")], writes=[Bf("bcrows")])
            kb.op("V", "V", lambda v: v.tensor_scalar(
                bcrows[:, L:2 * L], spcC, bclnw[:, 1:2], None, ALU.mult),
                reads=[Bf("wl1"), Bf("bclnw")], writes=[Bf("bcrows")])
            for (co, rb, rbB_) in ((0, rb_rb, Bf("rb_rb")),
                                   (L, rb_rc, Bf("rb_rc"))):
                for ch in range(4):
                    kb.op("T", "T", lambda t, co=co, ch=ch, rb=rb: t.matmul(
                        psH[0:16, 0:512], ones64[0:1, 0:16],
                        rb[0:1, ch * 512:(ch + 1) * 512],
                        start=True, stop=True),
                        reads=[rbB_, Bf("ones64")], writes=[Bf("psH")])
                    kb.op("S", "S", lambda s, ch=ch: s.copy(
                        hl2[0:16, ch * 512:(ch + 1) * 512],
                        psH[0:16, 0:512]),
                        reads=[Bf("psH")], writes=[Bf("hl2")])
                kb.op("V", "V", lambda v, co=co: v.tensor_tensor(
                    bcrows[:, co:co + L], bcrows[:, co:co + L],
                    hl2[0:16, 0:L], ALU.mult),
                    reads=[Bf("hl2")], writes=[Bf("bcrows")])
            for kt in range(2):
                for hh in range(2):
                    kb.op("G", "QG", lambda g, kt=kt, hh=hh, b=b: g.dma_start(
                        out=spd[:, :],
                        in_=arout[b][kt * 128:(kt + 1) * 128,
                                     hh * 1024:(hh + 1) * 1024]),
                        reads=[Bf(f"arout{b}")], writes=[Bf("spd")])
                    kb.op("S", "S", lambda s, kt=kt, hh=hh: s.activation(
                        wl1[:, kt * L + hh * 1024:kt * L + (hh + 1) * 1024],
                        spd[:, :], AF.Square),
                        reads=[Bf("spd")], writes=[Bf("wl1")])
            for ch in range(4):
                for kt in range(2):
                    kb.op("T", "T", lambda t, ch=ch, kt=kt: t.matmul(
                        psH[0:1, 0:512], ones_c[:, :],
                        wl1[:, kt * L + ch * 512:kt * L + (ch + 1) * 512],
                        start=(kt == 0), stop=(kt == 1)),
                        reads=[Bf("wl1"), Bf("ones_c")], writes=[Bf("psH")])
                kb.op("V", "V", lambda v: v.tensor_scalar(
                    s_scr[:, :], psH[0:1, 0:512], 1.0 / R, EPS_RMS,
                    ALU.mult, ALU.add),
                    reads=[Bf("psH")], writes=[Bf("s_scr")])
                kb.op("S", "S", lambda s: s.activation(
                    s_scr[:, :], s_scr[:, :], AF.Sqrt),
                    reads=[Bf("s_scr")], writes=[Bf("s_scr")])
                kb.op("V", "V", lambda v: v.reciprocal(s_scr[:, :], s_scr[:, :]),
                      reads=[Bf("s_scr")], writes=[Bf("s_scr")])
                kb.op("V", "V", lambda v, ch=ch: v.tensor_copy(
                    dAp[0:1, ch * 512:(ch + 1) * 512], s_scr[:, :]),
                    reads=[Bf("s_scr")], writes=[Bf("dAp")])
            for kt in range(2):
                for hh in range(2):
                    kb.op("G", "QG", lambda g, kt=kt, hh=hh, b=b: g.dma_start(
                        out=spd[:, :],
                        in_=arout[b][kt * 128:(kt + 1) * 128,
                                     hh * 1024:(hh + 1) * 1024]),
                        reads=[Bf(f"arout{b}")], writes=[Bf("spd")])
                    for c2 in range(2):
                        ch = hh * 2 + c2
                        kb.op("T", "T", lambda t, ch=ch: t.matmul(
                            psG[:, :], ones_r[:, :],
                            dAp[0:1, ch * 512:(ch + 1) * 512],
                            start=True, stop=True),
                            reads=[Bf("dAp"), Bf("ones_r")], writes=[Bf("psG")])
                        kb.op("V", "V", lambda v, kt=kt, ch=ch, c2=c2:
                              v.scalar_tensor_tensor(
                            hl2[:, kt * L + ch * 512:kt * L + (ch + 1) * 512],
                            spd[:, c2 * 512:(c2 + 1) * 512],
                            dtlnw_sb[:, kt:kt + 1], psG[:, :], ALU.mult, ALU.mult),
                            reads=[Bf("spd"), Bf("dtlnw_sb"), Bf("psG")],
                            writes=[Bf("hl2")])
            for dt in range(NDT):
                for tc in range(4):
                    for rt in range(2):
                        j = rt * NDT + dt
                        kb.op("T", "T", lambda t, j=j, rt=rt, tc=tc: t.matmul(
                            psF[:, :], dtp_sb[:, j * 128:(j + 1) * 128],
                            hl2[:, rt * L + tc * 512:rt * L + (tc + 1) * 512],
                            start=(rt == 0), stop=(rt == 1)),
                            reads=[Bf("dtp_sb"), Bf("hl2")], writes=[Bf("psF")])
                    kb.op("S", "S", lambda s, dt=dt: s.activation(
                        bct[:, 0:512], psF[:, :], AF.Sigmoid,
                        scale=-1.0, bias=ndtb[:, dt:dt + 1]),
                        reads=[Bf("psF"), Bf("ndtb")], writes=[Bf("bct")])
                    kb.op("S", "S", lambda s: s.activation(
                        gev[:, 0:512], bct[:, 0:512], AF.Ln),
                        reads=[Bf("bct")], writes=[GEVB[0]])
                    kb.op("Q", "QS", lambda q, b=b, dt=dt, tc=tc: q.dma_start(
                        out=dtlsp[b, dt, :, tc * 512:(tc + 1) * 512],
                        in_=gev[:, 0:512]),
                        reads=[GEVB[0]], writes=[Bf("dtlsp")])
            for n in range(NST):
                kb.op("V", "V", lambda v, n=n: v.memset(
                    dAp[:, n * TQS:n * TQS + 1], 0.0), writes=[Bf("dAp")])
            if b == 0:
                for n in range(NST):
                    kb.op("V", "V", lambda v, n=n: v.memset(
                        dApB[:, n * TQS:n * TQS + 1], 0.0),
                        writes=[Bf("dApB")])

        def bc_tq(b, tq):
            t0 = tq * TQ
            for (dst, dB, co) in ((bcB, Bf("bcB"), 0), (bcC, Bf("bcC"), L)):
                rows_v = bcrows[:, co + t0:co + t0 + TQ].unsqueeze(1) \
                    .broadcast_to([NST, NST, TQ])
                mask_v = dmask[:, :].unsqueeze(2).broadcast_to([NST, NST, TQ])
                diag_v = hl2[0:16, 0:NST * TQ].rearrange(
                    "p (n t) -> p n t", n=NST)
                kb.op("V", "V", lambda v, rows_v=rows_v, mask_v=mask_v,
                      diag_v=diag_v: v.tensor_tensor(
                    diag_v, rows_v, mask_v, ALU.mult),
                    reads=[Bf("bcrows"), Bf("dmask")], writes=[Bf("hl2")])
                for ch in range(4):
                    for c2 in range(2):
                        kb.op("T", "T", lambda t, ch=ch, c2=c2: t.matmul(
                            psH[:, c2 * 512:(c2 + 1) * 512], ones16[:, :],
                            hl2[0:16, ch * 1024 + c2 * 512:
                                ch * 1024 + (c2 + 1) * 512],
                            start=True, stop=True),
                            reads=[Bf("hl2"), Bf("ones16")], writes=[Bf("psH")])
                    kb.op("S", "S", lambda s, dst=dst, ch=ch: s.copy(
                        dst[:, ch * 1024:(ch + 1) * 1024], psH[:, :]),
                        reads=[Bf("psH")], writes=[dB])

        def scan_tq(b, tq):
            t0 = tq * TQ
            for dt in range(NDT):
                p = dt % 2
                po = p * TQ
                if p == 0:
                    for (dst, dB2, src, sB) in (
                            (dtl_t, (DTLB[0], DTLB[1]), dtlsp, Bf("dtlsp")),
                            (xc_t, (XCTB[0], XCTB[1]), xcsp, Bf("xcsp")),
                            (g_t, (GTB[0], GTB[1]), gsp, Bf("gsp"))):
                        kb.op("G", "QG", lambda g, dst=dst, src=src, b=b, dt=dt,
                              t0=t0: g.dma_start(
                            out=dst[:, :].rearrange("p (j t) -> p j t", j=2),
                            in_=src[b, dt:dt + 2, :, t0:t0 + TQ]
                                .transpose([1, 0, 2])),
                            reads=[sB], writes=[dB2[0], dB2[1]])
                kb.op("V", "V", lambda v, po=po: v.tensor_tensor(
                    u_t[:, :], dtl_t[:, po:po + TQ], xc_t[:, po:po + TQ], ALU.mult),
                    reads=[DTLB[p], XCTB[p]], writes=[Bf("u_t")])
                dApX, wlX, hlX = ((dAp, wl1, hl2), (dApB, wl1B, hl2))[dt % 2]
                dApN, wlN, hlN = (("dAp", "wl1", "hl2"),
                                  ("dApB", "wl1B", "hl2"))[dt % 2]
                hcN = f"hcar{dt}"
                kb.op("G", "QG", lambda q, dt=dt, wlX=wlX: q.dma_start(
                    out=wlX[:, :].rearrange("p (n t) -> p n t", n=NST)
                    [:, :, 0:1],
                    in_=hcar[:, dt * NST:(dt + 1) * NST].unsqueeze(2),
                    allow_slow_non_contiguous=True),
                    reads=[Bf(hcN)], writes=[Bf(wlN)])
                for n in range(NST):
                    kb.op("S", "S", lambda s, po=po, n=n, dt=dt, dApX=dApX:
                          s.activation(
                        dApX[:, n * TQS + 1:n * TQS + 1 + TQ],
                        dtl_t[:, po:po + TQ], AF.Exp,
                        scale=negA[:, dt * NST + n:dt * NST + n + 1]),
                        reads=[DTLB[p], Bf("negA")], writes=[Bf(dApN)])
                uview = u_t[:, :].unsqueeze(1).broadcast_to([128, NST, TQ])
                bview = bcB[:, :].rearrange("p (n t) -> p n t", n=NST)
                wview = wlX[:, :].rearrange("p (n t) -> p n t", n=NST)[:, :, 1:]
                kb.op("V", "V", lambda v, uview=uview, bview=bview, wview=wview:
                      v.tensor_tensor(wview, uview, bview, ALU.mult),
                      reads=[Bf("u_t"), Bf("bcB")], writes=[Bf(wlN)])
                kb.op("V", "V", lambda v, dApX=dApX, wlX=wlX, hlX=hlX:
                      v.tensor_tensor_scan(
                    hlX[:, :], dApX[:, :], wlX[:, :], 0.0, ALU.mult, ALU.add),
                    reads=[Bf(dApN), Bf(wlN)], writes=[Bf(hlN)])
                kb.op("G", "QG", lambda q, dt=dt, hlX=hlX: q.dma_start(
                    out=hcar[:, dt * NST:(dt + 1) * NST].unsqueeze(2),
                    in_=hlX[:, :].rearrange("p (n t) -> p n t", n=NST)
                    [:, :, TQ:TQ + 1],
                    allow_slow_non_contiguous=True),
                    reads=[Bf(hlN)], writes=[Bf(hcN)])
                cview = bcC[:, :].rearrange("p (n t) -> p n t", n=NST)
                hview = hlX[:, :].rearrange("p (n t) -> p n t", n=NST)[:, :, 1:]
                pview = dApX[:, :].rearrange("p (n t) -> p n t", n=NST)[:, :, 1:]
                kb.op("V", "V", lambda v, pview=pview, hview=hview, cview=cview:
                      v.tensor_tensor(pview, hview, cview, ALU.mult),
                      reads=[Bf(hlN), Bf("bcC")], writes=[Bf(dApN)])
                pv = dApX[:, :].rearrange("p (n t) -> p n t", n=NST)
                lv = wlX[:, :].rearrange("p (n t) -> p n t", n=NST)
                kb.op("V", "V", lambda v, pv=pv, lv=lv: v.tensor_tensor(
                    lv[:, 0:8, 1:], pv[:, 0:16:2, 1:], pv[:, 1:16:2, 1:],
                    ALU.add),
                    reads=[Bf(dApN)], writes=[Bf(wlN)])
                kb.op("V", "V", lambda v, lv=lv: v.tensor_tensor(
                    lv[:, 8:12, 1:], lv[:, 0:8:2, 1:], lv[:, 1:8:2, 1:],
                    ALU.add), writes=[Bf(wlN)])
                kb.op("V", "V", lambda v, lv=lv: v.tensor_tensor(
                    lv[:, 12:14, 1:], lv[:, 8:12:2, 1:], lv[:, 9:12:2, 1:],
                    ALU.add), writes=[Bf(wlN)])
                kb.op("V", "V", lambda v, wlX=wlX: v.tensor_tensor(
                    ys_t[:, :], wlX[:, 12 * TQS + 1:12 * TQS + 1 + TQ],
                    wlX[:, 13 * TQS + 1:13 * TQS + 1 + TQ], ALU.add),
                    reads=[Bf(wlN)], writes=[Bf("ys_t")])
                kb.op("V", "V", lambda v, po=po, dt=dt: v.scalar_tensor_tensor(
                    t2_t[:, :], xc_t[:, po:po + TQ], d_sb[:, dt:dt + 1], ys_t[:, :],
                    ALU.mult, ALU.add),
                    reads=[XCTB[p], Bf("d_sb"), Bf("ys_t")], writes=[Bf("t2_t")])
                kb.op("V", "V", lambda v, po=po: v.tensor_tensor(
                    ygd_t[:, po:po + TQ], t2_t[:, :], g_t[:, po:po + TQ], ALU.mult),
                    reads=[Bf("t2_t"), GTB[p]], writes=[YGDB[p]])
                kb.op("Q", "QS", lambda q, po=po, b=b, dt=dt, t0=t0: q.dma_start(
                    out=yin[b][dt * 128:(dt + 1) * 128, t0:t0 + TQ],
                    in_=ygd_t[:, po:po + TQ]),
                    reads=[YGDB[p]], writes=[Bf(f"yin{b}")])

        def y_rs(b):
            fl = kb.op("Q", "Q", lambda q: q.nop(), reads=[Bf(f"pmix{b}")])
            kb.op("G", f"CCR{b}", lambda g, b=b: g.collective_compute(
                "ReduceScatter", ALU.add, replica_groups=[list(range(NCK))],
                ins=[pmix[b]], outs=[rsout[b]]),
                reads=[Bf(f"pmix{b}")], writes=[Bf(f"rsout{b}")], extra=[fl])

        def ln_stats_T(mslot, mbuf, sqslot, sqbuf):
            for it in range(NKT):
                mo = it * TSL
                kb.op("S", "S", lambda s, mo=mo, mslot=mslot, sqslot=sqslot:
                      s.activation(sqslot[:, mo:mo + TSL], mslot[:, mo:mo + TSL],
                                   AF.Square),
                      reads=[mbuf], writes=[sqbuf])
            for it in range(NKT):
                mo = it * TSL
                kb.op("T", "T", lambda t, mo=mo, mslot=mslot, it=it: t.matmul(
                    psH[0:1, 0:TSL], ones_c[:, :], mslot[:, mo:mo + TSL],
                    start=(it == 0), stop=(it == NKT - 1)),
                    reads=[mbuf, Bf("ones_c")], writes=[Bf("psH")])
            kb.op("S", "S", lambda s: s.copy(c_sum[0:1, 0:TSL], psH[0:1, 0:TSL]),
                  reads=[Bf("psH")], writes=[Bf("c_sum")])
            for it in range(NKT):
                mo = it * TSL
                kb.op("T", "T", lambda t, mo=mo, sqslot=sqslot, it=it: t.matmul(
                    psH[0:1, TSL:2 * TSL], ones_c[:, :], sqslot[:, mo:mo + TSL],
                    start=(it == 0), stop=(it == NKT - 1)),
                    reads=[sqbuf, Bf("ones_c")], writes=[Bf("psH")])
            kb.op("S", "S", lambda s: s.copy(c_ssq[0:1, 0:TSL],
                                             psH[0:1, TSL:2 * TSL]),
                  reads=[Bf("psH")], writes=[Bf("c_ssq")])
            ln_rowmath(TSL)

        def stat_bcast(which):
            pairs = (((rb_rstd, 0), (rb_mrstd, 1)) if which != 1
                     else ((rb_rstd, 2),))
            for rb, pos in pairs:
                rbB = Bf("rb_rstd") if rb is rb_rstd else Bf("rb_mrstd")
                on = ones_r[:, :] if rb is rb_rstd else ones32
                onB = Bf("ones_r") if rb is rb_rstd else Bf("ones32")
                kb.op("T", "T", lambda t, rb=rb, on=on: t.matmul(
                    psG[:, 0:TSL], on, rb[0:1, 0:TSL],
                    start=True, stop=True),
                    reads=[rbB, onB], writes=[Bf("psG")])
                kb.op("S", "S", lambda s, pos=pos: s.copy(
                    bct[:, pos * TSL:(pos + 1) * TSL], psG[:, 0:TSL]),
                    reads=[Bf("psG")], writes=[Bf("bct")])

        def epi_chunk(b, part):
            if part in (0, 1):
                for sub in range(2):
                    tch = part * 2 + sub
                    t0 = tch * 512
                    rp = tch % 2
                    ro = rp * 4096
                    for dt in range(NDT):
                        kb.op("Q", "QS", lambda q, dt=dt, b=b, t0=t0, ro=ro:
                              q.dma_start(
                            out=SLOT[0][:, ro + dt * 512:ro + (dt + 1) * 512],
                            in_=yin[b][dt * 128:(dt + 1) * 128, t0:t0 + 512]),
                            reads=[Bf(f"yin{b}")], writes=[SB4[rp] if False
                                                           else SB[0]])
                    for htg in range(8):
                        wp = (tch * 8 + htg) % 2
                        wof = wp * 4096
                        kb.op("Q", "QS", lambda q, htg=htg, wof=wof: q.dma_start(
                            out=SLOT[4][:, wof:wof + 4096], in_=wos[htg]),
                            reads=[Bf("wos")], writes=[SB4[wp]])
                        for hh in range(4):
                            pst, pB = ((psA, Bf("psA")), (psB, Bf("psB")),
                                       (psF, Bf("psF")), (psI, Bf("psI")))[hh]
                            for dt in range(NDT):
                                kb.op("T", "T", lambda t, pst=pst, wof=wof, dt=dt,
                                      hh=hh, ro=ro: t.matmul(
                                    pst[:, :],
                                    SLOT[4][:, wof + dt * 512 + hh * 128:
                                            wof + dt * 512 + (hh + 1) * 128],
                                    SLOT[0][:, ro + dt * 512:ro + (dt + 1) * 512],
                                    start=(dt == 0), stop=(dt == NDT - 1)),
                                    reads=[SB[0], SB4[wp]], writes=[pB])
                            ht = htg * 4 + hh
                            gp = hh % 2
                            go = gp * 512
                            kb.op("S", "S", lambda s, pst=pst, go=go: s.copy(
                                gev[:, go:go + 512], pst[:, :]),
                                reads=[pB], writes=[GEVB[gp]])
                            kb.op("Q", "QS", lambda q, b=b, tch=tch, ht=ht,
                                  go=go: q.dma_start(
                                out=pmix[b][tch * 2:tch * 2 + 2,
                                            ht * 128:(ht + 1) * 128, :]
                                    .transpose([1, 0, 2]),
                                in_=gev[:, go:go + 512]
                                    .rearrange("p (j t) -> p j t", j=2)),
                                reads=[GEVB[gp]], writes=[Bf(f"pmix{b}")])
            if part == 2:
                for ht in range(NKT):
                    mo = ht * TSL
                    kb.op("Q", "QS", lambda q, ht=ht, mo=mo, b=b: q.dma_start(
                        out=SLOT[2][:, mo:mo + TSL],
                        in_=rsout[b][ht * 128:(ht + 1) * 128, :]),
                        reads=[Bf(f"rsout{b}")], writes=[SB[2]])
            if part == 2:
                ln_stats_T(SLOT[2], SB[2], SLOT[1], SB[1])
                stat_bcast(0)
                for ht in range(NKT):
                    p = ht % 2
                    po = p * TSL
                    mo = ht * TSL
                    kb.op("Q", "QS", lambda q, ht=ht, b=b: q.dma_start(
                        out=x0t[:, :],
                        in_=hs_resT[b, ht * 128:(ht + 1) * 128, :]),
                        writes=[X0B[0]])
                    kb.op("V", "V", lambda v, po=po, mo=mo: v.tensor_tensor(
                        cacc[:, po:po + TSL], SLOT[2][:, mo:mo + TSL],
                        bct[:, 0:TSL], ALU.mult),
                        reads=[SB[2], Bf("bct")], writes=[CAB[p]])
                    kb.op("V", "V", lambda v, po=po: v.scalar_tensor_tensor(
                        cacc[:, po:po + TSL], cacc[:, po:po + TSL], 1.0,
                        bct[:, TSL:2 * TSL], ALU.mult, ALU.subtract),
                        reads=[Bf("bct")], writes=[CAB[p]])
                    kb.op("V", "V", lambda v, po=po: v.tensor_tensor(
                        x1t[:, po:po + TSL], cacc[:, po:po + TSL],
                        x0t[:, :], ALU.add),
                        reads=[CAB[p], X0B[0]], writes=[X1B[p]])
                    kb.op("Q", "QS", lambda q, po=po, ht=ht, b=b: q.dma_start(
                        out=x1sp[b, ht * 128:(ht + 1) * 128, :],
                        in_=x1t[:, po:po + TSL]),
                        reads=[X1B[p]], writes=[Bf("x1sp")])
                for it in range(NKT):
                    o = it * TSL
                    kb.op("G", "QG", lambda g, it=it, o=o, b=b: g.dma_start(
                        out=SLOT[0][:, o:o + TSL],
                        in_=x1sp[b, it * 128:(it + 1) * 128, :]),
                        reads=[Bf("x1sp")], writes=[SB[0]])
                ln_stats_T(SLOT[0], SB[0], SLOT[1], SB[1])
                stat_bcast(1)
                kb.op("V", "V", lambda v: v.tensor_scalar_mul(
                    rb_tmp[0:1, 0:TSL], c_m[0:1, 0:TSL], -1.0),
                    reads=[Bf("c_m")], writes=[Bf("rb_tmp")])
                kb.op("V", "V", lambda v: v.tensor_tensor(
                    aug2r[0:1, :], rb_tmp[0:1, 0:TSL], c_ssq[0:1, 0:TSL], ALU.mult),
                    reads=[Bf("rb_tmp"), Bf("c_ssq")], writes=[Bf("aug2r")])
                kb.op("G", "QG", lambda g: g.dma_start(
                    out=aug2r[1:2, :], in_=c_ssq[0:1, 0:TSL]),
                    reads=[Bf("c_ssq")], writes=[Bf("aug2r")])
                gates_lora(b)
            if part == 3:
                ffn_and_out(b)

        def gates_lora(b):
            for it in range(NKT):
                o = it * TSL
                kb.op("T", "T", lambda t, it=it, o=o: t.matmul(
                    psH[0:36, 0:TSL], awr_sb[:, it * 36:(it + 1) * 36],
                    SLOT[0][:, o:o + TSL], start=(it == 0), stop=False),
                    reads=[SB[0], Bf("awr_sb")], writes=[Bf("psH")])
            kb.op("T", "T", lambda t: t.matmul(
                psH[0:36, 0:TSL], colsA[:, :], aug2r[0:1, :],
                start=False, stop=True),
                reads=[Bf("colsA"), Bf("aug2r")], writes=[Bf("psH")])
            kb.op("V", "V", lambda v: v.tensor_copy(scr36[:, :], psH[0:36, 0:TSL]),
                  reads=[Bf("psH")], writes=[Bf("scr36")])
            kb.op("T", "T", lambda t: t.matmul(
                psG[0:4, 0:TSL], sel2[:, :], scr36[:, :], start=True, stop=True),
                reads=[Bf("sel2"), Bf("scr36")], writes=[Bf("psG")])
            kb.op("V", "V", lambda v: v.tensor_copy(gl4[:, 0:TSL], psG[0:4, 0:TSL]),
                  reads=[Bf("psG")], writes=[Bf("gl4")])
            kb.op("T", "T", lambda t: t.matmul(
                psG[0:4, 0:TSL], ones_r[:, 0:4], rb_rstd[0:1, 0:TSL],
                start=True, stop=True),
                reads=[Bf("rb_rstd"), Bf("ones_r")], writes=[Bf("psG")])
            kb.op("V", "V", lambda v: v.tensor_copy(bct4[:, :], psG[0:4, 0:TSL]),
                  reads=[Bf("psG")], writes=[Bf("bct4")])
            kb.op("V", "V", lambda v: v.tensor_tensor(
                gl4[:, 0:TSL], gl4[:, 0:TSL], bct4[:, :], ALU.mult),
                reads=[Bf("bct4")], writes=[Bf("gl4")])
            kb.op("S", "S", lambda s: s.activation(
                gb4[:, 0:TSL], gl4[:, 0:TSL], AF.Exp),
                reads=[Bf("gl4")], writes=[Bf("gb4")])
            kb.op("T", "T", lambda t: t.matmul(
                psG[0:1, 0:TSL], ones_c[0:4, :], gb4[:, 0:TSL],
                start=True, stop=True),
                reads=[Bf("gb4"), Bf("ones_c")], writes=[Bf("psG")])
            kb.op("V", "V", lambda v: v.tensor_copy(
                gl4[0:1, TSL:2 * TSL], psG[0:1, 0:TSL]),
                reads=[Bf("psG")], writes=[Bf("gl4")])
            kb.op("V", "V", lambda v: v.reciprocal(
                gl4[0:1, TSL:2 * TSL], gl4[0:1, TSL:2 * TSL]),
                reads=[Bf("gl4")], writes=[Bf("gl4")])
            kb.op("V", "V", lambda v: v.tensor_copy(
                rb_tmp[0:1, 0:TSL], gl4[0:1, TSL:2 * TSL]),
                reads=[Bf("gl4")], writes=[Bf("rb_tmp")])
            kb.op("T", "T", lambda t: t.matmul(
                psG[0:4, 0:TSL], ones_r[:, 0:4], rb_tmp[0:1, 0:TSL],
                start=True, stop=True),
                reads=[Bf("rb_tmp"), Bf("ones_r")], writes=[Bf("psG")])
            kb.op("V", "V", lambda v: v.tensor_copy(bct4[:, :], psG[0:4, 0:TSL]),
                  reads=[Bf("psG")], writes=[Bf("bct4")])
            kb.op("V", "V", lambda v: v.tensor_tensor(
                gb4[:, TSL:2 * TSL], gb4[:, 0:TSL], bct4[:, :], ALU.mult),
                reads=[Bf("bct4")], writes=[Bf("gb4")])
            kb.op("T", "T", lambda t: t.matmul(
                psH[0:32, 0:TSL], sel[:, :], gb4[:, TSL:2 * TSL],
                start=True, stop=True),
                reads=[Bf("sel"), Bf("gb4")], writes=[Bf("psH")])
            kb.op("V", "V", lambda v: v.tensor_copy(g32[:, :], psH[0:32, 0:TSL]),
                  reads=[Bf("psH")], writes=[Bf("g32")])
            kb.op("V", "V", lambda v: v.tensor_tensor(
                prs[:, :], scr36[0:32, :], g32[:, :], ALU.mult),
                reads=[Bf("scr36"), Bf("g32")], writes=[Bf("prs")])

        def ffn_and_out(b):
            for ot in range(NKT):
                pw = ot % 2
                ws, wof = SLOT[4], pw * 4096
                wb = SB4[pw]
                kb.op("Q", "QS", lambda q, ot=ot, ws=ws, wof=wof: q.dma_start(
                    out=ws[:, wof:wof + 4096], in_=wf2[ot]),
                    reads=[Bf("wf2")], writes=[wb])
                kb.op("G", "QG", lambda g, ot=ot, pw=pw: g.dma_start(
                    out=bmt_t[:, pw * 128:(pw + 1) * 128],
                    in_=fBmT[:, ot * 128:(ot + 1) * 128]),
                    writes=[Bf("bmt_t")])
                for it in range(NKT):
                    o = it * TSL
                    kb.op("T", "T", lambda t, ws=ws, wof=wof, it=it, o=o: t.matmul(
                        psA[:, 0:TSL], ws[:, wof + it * 128:wof + (it + 1) * 128],
                        SLOT[0][:, o:o + TSL], start=(it == 0), stop=False),
                        reads=[SB[0], wb], writes=[Bf("psA")])
                kb.op("T", "T", lambda t, ot=ot: t.matmul(
                    psA[:, 0:TSL], aug2[:, ot * 128:(ot + 1) * 128],
                    aug2r[:, :], start=False, stop=False),
                    reads=[Bf("aug2"), Bf("aug2r")], writes=[Bf("psA")])
                kb.op("T", "T", lambda t, ot=ot, pw=pw: t.matmul(
                    psA[:, 0:TSL], bmt_t[:, pw * 128:(pw + 1) * 128], prs[:, :],
                    start=False, stop=True),
                    reads=[Bf("bmt_t"), Bf("prs")], writes=[Bf("psA")])
                mo = ot * TSL
                kb.op("V", "V", lambda v, mo=mo: v.tensor_tensor(
                    SLOT[2][:, mo:mo + TSL], psA[:, 0:TSL],
                    bct[:, 2 * TSL:3 * TSL], ALU.mult),
                    reads=[Bf("psA"), Bf("bct")], writes=[SB[2]])
            ln_stats_T(SLOT[2], SB[2], SLOT[1], SB[1])
            stat_bcast(2)
            for ht in range(NKT):
                p = ht % 2
                po = p * TSL
                mo = ht * TSL
                kb.op("G", "QG", lambda g, po=po, ht=ht, b=b: g.dma_start(
                    out=x1t[:, po:po + TSL],
                    in_=x1sp[b, ht * 128:(ht + 1) * 128, :]),
                    reads=[Bf("x1sp")], writes=[X1B[p]])
                kb.op("V", "V", lambda v, po=po, mo=mo: v.tensor_tensor(
                    cacc[:, po:po + TSL], SLOT[2][:, mo:mo + TSL],
                    bct[:, 0:TSL], ALU.mult),
                    reads=[SB[2], Bf("bct")], writes=[CAB[p]])
                kb.op("V", "V", lambda v, po=po: v.scalar_tensor_tensor(
                    cacc[:, po:po + TSL], cacc[:, po:po + TSL], 1.0,
                    bct[:, TSL:2 * TSL], ALU.mult, ALU.subtract),
                    reads=[Bf("bct")], writes=[CAB[p]])
                kb.op("V", "V", lambda v, po=po: v.tensor_tensor(
                    cacc[:, po:po + TSL], cacc[:, po:po + TSL],
                    x1t[:, po:po + TSL], ALU.add),
                    reads=[X1B[p]], writes=[CAB[p]])
                kb.op("Q", "QS", lambda q, po=po, ht=ht, b=b: q.dma_start(
                    out=out_ext[b, ht * 128:(ht + 1) * 128, :],
                    in_=cacc[:, po:po + TSL]),
                    reads=[CAB[p]], writes=[Bf("out")])

        for part in range(4):
            p1_chunk(0, part)
        p1_ar(0)
        for b in range(B):
            rows_dt(b)
            bc_tq(b, 0)
            scan_tq(b, 0)
            bc_tq(b, 1)
            scan_tq(b, 1)
            if b >= 1:
                epi_chunk(b - 1, 0)
            bc_tq(b, 2)
            scan_tq(b, 2)
            if b >= 1:
                epi_chunk(b - 1, 1)
                y_rs(b - 1)
            bc_tq(b, 3)
            scan_tq(b, 3)
            if b >= 1:
                epi_chunk(b - 1, 2)
                epi_chunk(b - 1, 3)
            bc_tq(b, 4)
            scan_tq(b, 4)
            bc_tq(b, 5)
            scan_tq(b, 5)
            if b + 1 < B:
                p1_chunk(b + 1, 0)
            bc_tq(b, 6)
            scan_tq(b, 6)
            if b + 1 < B:
                p1_chunk(b + 1, 1)
            bc_tq(b, 7)
            scan_tq(b, 7)
            if b + 1 < B:
                p1_chunk(b + 1, 2)
                p1_chunk(b + 1, 3)
                p1_ar(b + 1)
        epi_chunk(B - 1, 0)
        epi_chunk(B - 1, 1)
        y_rs(B - 1)
        epi_chunk(B - 1, 2)
        epi_chunk(B - 1, 3)

        kb.op("G", "G", lambda g: g.nop(),
              extra=[x for x in (bufs["out"].w,) if x is not None])

        kb.finalize()

        with contextlib.ExitStack() as st2:
            sems = {}
            ccs = ["CCL"] + [f"CCA{b}" for b in range(B)] + \
                  [f"CCR{b}" for b in range(B)]
            pools = [f"{q}{i}" for q, n in KB.POOLS.items() for i in range(n)]
            for s in ["T", "V", "S", "G", "Q", "QS", "QA", "QG"] + ccs + pools:
                sems[s] = st2.enter_context(nc.semaphore(f"sem_{s}"))
            with nc.Block() as block:
                @block.tensor
                def _(e):
                    kb.emit("T", e, sems)

                @block.vector
                def _(e):
                    kb.emit("V", e, sems)

                @block.scalar
                def _(e):
                    kb.emit("S", e, sems)

                @block.gpsimd
                def _(e):
                    kb.emit("G", e, sems)

                @block.sync
                def _(e):
                    kb.emit("Q", e, sems)
    return nc


_NC_CACHE = None


def kernel(**inputs):
    global _NC_CACHE
    f32 = np.float32

    hs = np.asarray(inputs["L0_attn_hidden_states"], f32)
    W1 = np.asarray(inputs["L0_attn_in_proj_w"], f32)
    convw = np.asarray(inputs["L0_attn_conv_w"], f32)
    convb = np.asarray(inputs["L0_attn_conv_b"], f32)
    xpw = np.asarray(inputs["L0_attn_x_proj_w"], f32)
    dtp = np.asarray(inputs["L0_attn_dt_proj_w"], f32)
    dtbias = np.asarray(inputs["L0_attn_dt_bias"], f32)
    alog = np.asarray(inputs["L0_attn_A_log"], f32)
    dparam = np.asarray(inputs["L0_attn_D_param"], f32)
    wo = np.asarray(inputs["L0_attn_out_proj_w"], f32)
    dtlnw = np.asarray(inputs["L0_attn_dt_ln_w"], f32)
    blnw = np.asarray(inputs["L0_attn_B_ln_w"], f32)
    clnw = np.asarray(inputs["L0_attn_C_ln_w"], f32)
    wf = np.asarray(inputs["L0_ffn_W"], f32)
    fbv = np.asarray(inputs["L0_ffn_b"], f32)
    fA = np.asarray(inputs["L0_ffn_A"], f32)
    fBm = np.asarray(inputs["L0_ffn_Bm"], f32)
    fWr = np.asarray(inputs["L0_ffn_Wr"], f32)

    hsT = np.ascontiguousarray(hs.reshape(B * L, H).T)
    woT_full = np.ascontiguousarray(wo.T)
    wfT_full = np.ascontiguousarray(wf.T)
    wf_sh = np.ascontiguousarray(
        wfT_full.reshape(32, 128, 32, 128).transpose(2, 1, 0, 3)
        .reshape(32, 128, 4096))
    dtpT_full = np.ascontiguousarray(dtp.T)
    xpwT_full = np.ascontiguousarray(xpw.T)
    fAWr = np.concatenate([fA.reshape(32, H), fWr], axis=0)
    fAWrT = np.ascontiguousarray(fAWr.T)
    fBmT = np.ascontiguousarray(fBm.transpose(0, 2, 1).reshape(32, H))

    in_maps = []
    for k in range(NCK):
        dsl = slice(k * DSH, (k + 1) * DSH)
        res = np.stack([
            np.ascontiguousarray(hsT[:, b * L + k * TSL:b * L + (k + 1) * TSL])
            for b in range(B)])
        w1k = np.concatenate(
            [W1[:, dsl], W1[:, DIN + k * DSH:DIN + (k + 1) * DSH]], axis=1)
        w1_sh = np.ascontiguousarray(
            w1k.reshape(32, 128, 16, 128).transpose(2, 1, 0, 3)
            .reshape(16, 128, 4096))
        wo_sh = np.ascontiguousarray(
            woT_full[dsl].reshape(8, 128, 8, 512).transpose(2, 1, 0, 3)
            .reshape(8, 128, 4096))
        in_maps.append({
            "hs_prepT": np.ascontiguousarray(hsT[:, k * 1024:(k + 1) * 1024]),
            "hs_resT": res,
            "w1": w1_sh,
            "woTs": wo_sh,
            "wfT": wf_sh,
            "convw": np.ascontiguousarray(convw[dsl]),
            "convb": np.ascontiguousarray(convb[dsl]),
            "dtbias": np.ascontiguousarray(dtbias[dsl]),
            "dparam": np.ascontiguousarray(dparam[dsl]),
            "alog": np.ascontiguousarray(alog[dsl]),
            "xpwT": np.ascontiguousarray(xpwT_full[dsl]),
            "dtpT": np.ascontiguousarray(dtpT_full[:, dsl]),
            "dtlnw": dtlnw, "blnw": blnw, "clnw": clnw,
            "fb": fbv, "fAWrT": fAWrT, "fBmT": fBmT,
        })

    if _NC_CACHE is None:
        _NC_CACHE = build_program()
    res = run_bass_kernel_spmd(_NC_CACHE, in_maps, core_ids=list(range(NCK)))
    full = np.empty((B, L, H), np.float32)
    for k in range(NCK):
        full[:, k * TSL:(k + 1) * TSL, :] = res.results[k]["out"].transpose(0, 2, 1)
    return full


# version 16 (final)
# speedup vs baseline: 1.2786x; 1.0056x over previous
import numpy as np
from collections import defaultdict
import contextlib

import concourse.bass as bass
import concourse.mybir as mybir
from concourse.bass_utils import run_bass_kernel_spmd

F32 = mybir.dt.float32
BF16 = mybir.dt.bfloat16
U32 = mybir.dt.uint32
AF = mybir.ActivationFunctionType
ALU = mybir.AluOpType

NCK = 8
H = 4096
DIN = 8192
DSH = DIN // NCK
NDT = DSH // 128
NST = 16
R = 256
E3 = 288
B = 4
L = 2048
TSL = L // NCK
NKT = H // 128
EPS_LN = 1e-5
EPS_RMS = 1e-6
SCALE = 2.0
TQ = 256
TQS = TQ + 1
NTQ = L // TQ


class Buf:
    __slots__ = ("name", "w", "r")

    def __init__(self, name):
        self.name = name
        self.w = None
        self.r = []


class KB:
    CSTREAMS = ("T", "V", "S", "G", "Q")

    def __init__(self):
        self.ops = []

    def op(self, eng, stream, fn, reads=(), writes=(), extra=()):
        idx = len(self.ops)
        waits = set(e for e in extra if e is not None)
        for b in reads:
            if b.w is not None:
                waits.add(b.w)
        for b in writes:
            if b.w is not None:
                waits.add(b.w)
            waits.update(b.r)
        for b in writes:
            b.w = idx
            b.r = []
        for b in reads:
            b.r.append(idx)
        amt = 16 if stream in ("QS", "QA", "QG") else 1
        self.ops.append([eng, stream, fn, waits, amt])
        return idx

    POOLS = {"QS": 24, "QG": 16, "QA": 8}
    STRICT = False

    def finalize(self):
        used = set()
        for _, _, _, waits, _ in self.ops:
            used.update(waits)
        val = {}
        cnt = defaultdict(int)
        pool_ctr = defaultdict(int)
        self.pre_wait = {}
        self.inc_flag = []
        for idx, (eng, stream, fn, waits, amt) in enumerate(self.ops):
            if stream in self.POOLS:
                n = self.POOLS[stream]
                c = pool_ctr[stream]
                pool_ctr[stream] += 1
                sem_name = f"{stream}{c % n}"
                u = c // n
                self.pre_wait[idx] = (sem_name, 16 * u)
                val[idx] = (sem_name, 16 * (u + 1))
                self.inc_flag.append(True)
                cnt[sem_name] = 16 * (u + 1)
            else:
                keep = self.STRICT or (stream not in self.CSTREAMS) or (idx in used)
                self.inc_flag.append(keep)
                if keep:
                    cnt[stream] += amt
                    val[idx] = (stream, cnt[stream])
        self.ev_val = val
        self.final_counts = dict(cnt)

    def emit(self, eng_key, handle, sems):
        waited = defaultdict(int)
        for idx, (eng, stream, fn, waits, amt) in enumerate(self.ops):
            if eng != eng_key:
                continue
            pw = self.pre_wait.get(idx)
            if pw is not None and pw[1] > 0 and waited[pw[0]] < pw[1]:
                handle.wait_ge(sems[pw[0]], pw[1])
                waited[pw[0]] = pw[1]
            for wev in sorted(waits):
                wsv = self.ev_val.get(wev)
                if wsv is None:
                    continue
                ws, wv = wsv
                if (not self.STRICT) and ws == eng and ws in self.CSTREAMS:
                    continue
                if waited[ws] >= wv:
                    continue
                handle.wait_ge(sems[ws], wv)
                waited[ws] = wv
            ins = fn(handle)
            if self.inc_flag[idx]:
                s, v = self.ev_val[idx]
                ins.then_inc(sems[s], amt)


def build_program():
    nc = bass.Bass("TRN2", target_bir_lowering=False, debug=False)

    def ein(name, shp, dt=F32):
        return nc.dram_tensor(name, shp, dt, kind="ExternalInput").ap()

    hs_prepT = ein("hs_prepT", [H, 1024])
    hs_resT = ein("hs_resT", [B, H, TSL])
    w1 = ein("w1", [16, 128, 4096])
    woTs = ein("woTs", [8, 128, 4096])
    wfT = ein("wfT", [32, 128, 4096])
    convw = ein("convw", [DSH, 4])
    convb = ein("convb", [DSH])
    dtbias = ein("dtbias", [DSH])
    dparam = ein("dparam", [DSH])
    alog = ein("alog", [DSH, NST])
    xpwT = ein("xpwT", [DSH, E3])
    dtpT = ein("dtpT", [R, DSH])
    dtlnw = ein("dtlnw", [R])
    blnw = ein("blnw", [NST])
    clnw = ein("clnw", [NST])
    fbi = ein("fb", [H])
    fAWrT = ein("fAWrT", [H, 36])
    fBmT = ein("fBmT", [32, H])
    out_ext = nc.dram_tensor("out", [B, H, TSL], F32, kind="ExternalOutput").ap()

    dr = lambda n, shp, dt: nc.dram_tensor(n, shp, dt).ap()
    ln_in = dr("ln_in", [H, 1024], BF16)
    ln_all = dr("ln_all", [NCK, H, 1024], BF16)
    w1t = dr("w1t", [16, 128, 4096], BF16)
    wos = dr("wos", [8, 128, 4096], BF16)
    wf2 = dr("wf2", [NKT, 128, 4096], BF16)
    arin = [dr(f"arin{b}", [E3, L], F32) for b in range(B)]
    arout = [dr(f"arout{b}", [E3, L], F32) for b in range(B)]
    yin = [dr(f"yin{b}", [DSH, L], BF16) for b in range(B)]
    pmix = [dr(f"pmix{b}", [NCK, H, TSL], BF16) for b in range(B)]
    rsout = [dr(f"rsout{b}", [H, TSL], BF16) for b in range(B)]
    xcsp = dr("xcsp", [B, NDT, 128, L], BF16)
    gsp = dr("gsp", [B, NDT, 128, L], BF16)
    dtlsp = dr("dtlsp", [B, NDT, 128, L], BF16)
    x1sp = dr("x1sp", [B, H, TSL], F32)

    kb = KB()
    st = contextlib.ExitStack()
    T = {}

    def sb(name, shp, dt):
        t = st.enter_context(nc.sbuf_tensor(name, shp, dt))
        T[name] = t
        return t

    def psm(name, shp):
        t = st.enter_context(nc.psum_tensor(name, shp, F32))
        T[name] = t
        return t

    with st:
        SLOT = [sb(f"sl{i}", [128, 8192], BF16) for i in range(5)]
        dAp = sb("dAp", [128, NST * TQS], BF16)
        wl1 = sb("wl1", [128, NST * TQS], BF16)
        hl2 = sb("hl2", [128, NST * TQS], BF16)
        dApB = sb("dApB", [128, NST * TQS], BF16)
        wl1B = sb("wl1B", [128, NST * TQS], BF16)
        hl2B = hl2
        bcB = sb("bcB", [128, NST * TQ], BF16)
        bcC = sb("bcC", [128, NST * TQ], BF16)
        dtl_t = sb("dtl_t", [128, 2 * TQ], BF16)
        xc_t = sb("xc_t", [128, 2 * TQ], BF16)
        g_t = sb("g_t", [128, 2 * TQ], BF16)
        u_t = sb("u_t", [128, TQ], BF16)
        ys_t = sb("ys_t", [128, TQ], F32)
        t2_t = sb("t2_t", [128, TQ], BF16)
        ygd_t = sb("ygd_t", [128, 2 * TQ], BF16)
        hcar = sb("hcar", [128, NDT * NST], BF16)
        xconv = sb("xconv", [128, 2 * (TQ + 3)], BF16)
        tails = sb("tails", [128, NDT * 3], BF16)
        cacc = sb("cacc", [128, 2 * TQ], F32)
        xcw = sb("xcw", [128, 2 * TQ], BF16)
        gev = sb("gev", [128, 2 * 512], BF16)
        xcr = sb("xcr", [128, 512], BF16)
        spsc = sb("spsc", [96, 512], F32)
        rowpack = sb("rowpack", [65, 2 * L + 128 + 1024 + 128], BF16)
        bcrows = rowpack[0:16, :]
        spd = sb("spd", [128, 1024], BF16)
        c_sum = sb("c_sum", [1, 512], F32)
        c_ssq = sb("c_ssq", [1, 512], F32)
        c_m = sb("c_m", [1, 512], F32)
        s_scr = sb("s_scr", [1, 512], F32)
        rb_rstd = rowpack[0:1, 4224:5248]
        rb_mrstd = rowpack[32:33, 4224:5248]
        ones32 = rowpack[32:33, 5248:5376]
        rb_rb = rowpack[64:65, 0:L]
        rb_rc = rowpack[64:65, L:2 * L]
        ones64 = rowpack[64:65, 2 * L:2 * L + 128]
        rb_tmp = sb("rb_tmp", [1, TSL], BF16)
        gl4 = sb("gl4", [4, 2 * TSL], F32)
        gb4 = sb("gb4", [4, 2 * TSL], BF16)
        bct4 = sb("bct4", [4, TSL], F32)
        g32 = sb("g32", [32, TSL], BF16)
        prs = sb("prs", [32, TSL], BF16)
        scr36 = sb("scr36", [36, TSL], BF16)
        negA = sb("negA", [128, NDT * NST], F32)
        cw_sb = sb("cw_sb", [128, NDT * 4], F32)
        cb_sb = sb("cb_sb", [128, NDT], F32)
        ndtb = sb("ndtb", [128, NDT], F32)
        d_sb = sb("d_sb", [128, NDT], F32)
        dtlnw_sb = sb("dtlnw_sb", [128, 2], F32)
        bclnw = sb("bclnw", [16, 2], F32)
        dtp_sb = sb("dtp_sb", [128, 16 * 128], BF16)
        xpw_sb = sb("xpw_sb", [128, NDT * E3], BF16)
        awr_sb = sb("awr_sb", [128, NKT * 36], BF16)
        bmt_t = sb("bmt_t", [32, 2 * 128], BF16)
        aug2r = sb("aug2r", [2, TSL], BF16)
        colsA = sb("colsA", [1, 36], BF16)
        ones_c = sb("ones_c", [128, 1], BF16)
        ones_r = sb("ones_r", [1, 128], BF16)
        ones16 = sb("ones16", [16, 128], BF16)

        srow = sb("srow", [1, TQ], BF16)
        sel = sb("sel", [4, 32], BF16)
        sel2 = sb("sel2", [36, 4], BF16)
        x0t = sb("x0t", [128, TSL], F32)
        x1t = sb("x1t", [128, 2 * TSL], F32)
        bct = sb("bct", [128, 3 * TSL], F32)
        aug2 = sb("aug2", [2, H], BF16)
        dmask = sb("dmask", [16, NST], BF16)

        psA = psm("psA", [128, 512])
        psB = psm("psB", [128, 512])
        psE = psm("psE", [96, 512])
        psF = psm("psF", [128, 512])
        psG = psm("psG", [128, 512])
        psH = psm("psH", [128, 1024])
        psI = psm("psI", [128, 512])

        bufs = {}

        def Bf(n):
            if n not in bufs:
                bufs[n] = Buf(n)
            return bufs[n]

        for n in list(T):
            Bf(n)
        for n in ("ln_in", "ln_all", "w1t", "wos", "wf2", "xcsp", "gsp",
                  "dtlsp", "x1sp", "out", "sl4a", "sl4b",
                  "dtl0", "dtl1", "xct0", "xct1", "gt0", "gt1", "ygd0", "ygd1",
                  "x0t0", "x0t1", "x1t0", "x1t1", "ca0", "ca1", "gev0", "gev1",
                  "xcr0", "xcr1"):
            Bf(n)
        for b in range(B):
            for n in ("arin", "arout", "yin", "pmix", "rsout"):
                Bf(f"{n}{b}")
        SB = [Bf(f"sl{i}") for i in range(5)]
        SB4 = [Bf("sl4a"), Bf("sl4b")]
        DTLB = [Bf("dtl0"), Bf("dtl1")]
        XCTB = [Bf("xct0"), Bf("xct1")]
        GTB = [Bf("gt0"), Bf("gt1")]
        YGDB = [Bf("ygd0"), Bf("ygd1")]
        X0B = [Bf("x0t0"), Bf("x0t1")]
        X1B = [Bf("x1t0"), Bf("x1t1")]
        CAB = [Bf("ca0"), Bf("ca1")]
        GEVB = [Bf("gev0"), Bf("gev1")]
        XCRB = [Bf("xcr0"), Bf("xcr1")]
        spcB = wl1[0:16, 0:L]
        spcC = wl1[0:16, L:2 * L]

        kb.op("V", "V", lambda v: v.memset(ones_c[:, :], 1.0), writes=[Bf("ones_c")])
        kb.op("V", "V", lambda v: v.memset(ones_r[:, :], 1.0), writes=[Bf("ones_r")])
        kb.op("V", "V", lambda v: v.memset(ones64, 1.0), writes=[Bf("ones64")])
        kb.op("V", "V", lambda v: v.memset(ones32, 1.0), writes=[Bf("ones32")])
        kb.op("V", "V", lambda v: v.memset(ones16[:, :], 1.0), writes=[Bf("ones16")])
        kb.op("V", "V", lambda v: v.memset(tails[:, :], 0.0), writes=[Bf("tails")])
        kb.op("V", "V", lambda v: v.memset(sel[:, :], 0.0), writes=[Bf("sel")])
        kb.op("V", "V", lambda v: v.memset(dmask[:, :], 0.0), writes=[Bf("dmask")])
        kb.op("V", "V", lambda v: v.memset(sel2[:, :], 0.0), writes=[Bf("sel2")])
        kb.op("V", "V", lambda v: v.memset(srow[:, :], 1.0), writes=[Bf("srow")])
        for e in range(4):
            kb.op("G", "QG", lambda g, e=e: g.dma_start(
                out=sel[e:e + 1, e * 8:(e + 1) * 8].rearrange("o f -> o f") if False
                else sel[e:e + 1, e * 8:(e + 1) * 8],
                in_=srow[0:1, 0:8]), reads=[Bf("srow")], writes=[Bf("sel")])
            kb.op("G", "QG", lambda g, e=e: g.dma_start(
                out=sel2[32 + e:33 + e, e:e + 1], in_=srow[0:1, 0:1]),
                reads=[Bf("srow")], writes=[Bf("sel2")])
        for n in range(NST):
            kb.op("G", "QG", lambda g, n=n: g.dma_start(
                out=dmask[n:n + 1, n:n + 1], in_=srow[0:1, 0:1]),
                reads=[Bf("srow")], writes=[Bf("dmask")])
        kb.op("V", "V", lambda v: v.tensor_scalar_mul(sel[:, :], sel[:, :], SCALE),
              reads=[Bf("sel")], writes=[Bf("sel")])

        for dt in range(NDT):
            kb.op("G", "QG", lambda g, dt=dt: g.dma_start(
                out=negA[:, dt * NST:(dt + 1) * NST],
                in_=alog[dt * 128:(dt + 1) * 128, :]), writes=[Bf("negA")])
            kb.op("G", "QG", lambda g, dt=dt: g.dma_start(
                out=cw_sb[:, dt * 4:(dt + 1) * 4],
                in_=convw[dt * 128:(dt + 1) * 128, :]), writes=[Bf("cw_sb")])
            kb.op("G", "QG", lambda g, dt=dt: g.dma_start(
                out=cb_sb[:, dt:dt + 1],
                in_=convb[dt * 128:(dt + 1) * 128].unsqueeze(1)), writes=[Bf("cb_sb")])
            kb.op("G", "QG", lambda g, dt=dt: g.dma_start(
                out=ndtb[:, dt:dt + 1],
                in_=dtbias[dt * 128:(dt + 1) * 128].unsqueeze(1)), writes=[Bf("ndtb")])
            kb.op("G", "QG", lambda g, dt=dt: g.dma_start(
                out=d_sb[:, dt:dt + 1],
                in_=dparam[dt * 128:(dt + 1) * 128].unsqueeze(1)), writes=[Bf("d_sb")])
            for ec in range(3):
                kb.op("G", "QG", lambda g, dt=dt, ec=ec: g.dma_start(
                    out=xpw_sb[:, dt * E3 + ec * 96:dt * E3 + (ec + 1) * 96],
                    in_=xpwT[dt * 128:(dt + 1) * 128, ec * 96:(ec + 1) * 96]),
                    writes=[Bf("xpw_sb")])
        kb.op("S", "S", lambda s: s.activation(negA[:, :], negA[:, :], AF.Exp),
              reads=[Bf("negA")], writes=[Bf("negA")])
        kb.op("V", "V", lambda v: v.tensor_scalar_mul(ndtb[:, :], ndtb[:, :], -1.0),
              reads=[Bf("ndtb")], writes=[Bf("ndtb")])
        for c in range(2):
            kb.op("G", "QG", lambda g, c=c: g.dma_start(
                out=dtlnw_sb[:, c:c + 1],
                in_=dtlnw[c * 128:(c + 1) * 128].unsqueeze(1)),
                writes=[Bf("dtlnw_sb")])
        kb.op("G", "QG", lambda g: g.dma_start(out=bclnw[:, 0:1],
                                               in_=blnw.unsqueeze(1)),
              writes=[Bf("bclnw")])
        kb.op("G", "QG", lambda g: g.dma_start(out=bclnw[:, 1:2],
                                               in_=clnw.unsqueeze(1)),
              writes=[Bf("bclnw")])
        for rt in range(2):
            for dt in range(NDT):
                j = rt * NDT + dt
                kb.op("G", "QG", lambda g, rt=rt, dt=dt, j=j: g.dma_start(
                    out=dtp_sb[:, j * 128:(j + 1) * 128],
                    in_=dtpT[rt * 128:(rt + 1) * 128, dt * 128:(dt + 1) * 128]),
                    writes=[Bf("dtp_sb")])
        for it in range(NKT):
            kb.op("G", "QG", lambda g, it=it: g.dma_start(
                out=awr_sb[:, it * 36:(it + 1) * 36],
                in_=fAWrT[it * 128:(it + 1) * 128, :]), writes=[Bf("awr_sb")])
        kb.op("G", "QG", lambda g: g.dma_start(out=aug2[1:2, :], in_=fbi.unsqueeze(0)),
              writes=[Bf("aug2")])

        def ln_rowmath(width, off=0):
            kb.op("V", "V", lambda v: v.tensor_scalar_mul(
                c_m[0:1, 0:width], c_sum[0:1, 0:width], 1.0 / H),
                reads=[Bf("c_sum")], writes=[Bf("c_m")])
            kb.op("S", "S", lambda s: s.activation(
                c_sum[0:1, 0:width], c_m[0:1, 0:width], AF.Square),
                reads=[Bf("c_m")], writes=[Bf("c_sum")])
            kb.op("V", "V", lambda v: v.scalar_tensor_tensor(
                c_ssq[0:1, 0:width], c_ssq[0:1, 0:width], 1.0 / H,
                c_sum[0:1, 0:width], ALU.mult, ALU.subtract),
                reads=[Bf("c_sum")], writes=[Bf("c_ssq")])
            kb.op("V", "V", lambda v: v.tensor_scalar_add(
                c_ssq[0:1, 0:width], c_ssq[0:1, 0:width], EPS_LN),
                reads=[Bf("c_ssq")], writes=[Bf("c_ssq")])
            kb.op("S", "S", lambda s: s.activation(
                c_ssq[0:1, 0:width], c_ssq[0:1, 0:width], AF.Sqrt),
                reads=[Bf("c_ssq")], writes=[Bf("c_ssq")])
            kb.op("V", "V", lambda v: v.reciprocal(
                c_sum[0:1, 0:width], c_ssq[0:1, 0:width]),
                reads=[Bf("c_ssq")], writes=[Bf("c_sum")])
            kb.op("V", "V", lambda v: v.tensor_copy(
                rb_rstd[0:1, off:off + width], c_sum[0:1, 0:width]),
                reads=[Bf("c_sum")], writes=[Bf("rb_rstd")])
            kb.op("V", "V", lambda v: v.tensor_mul(
                c_m[0:1, 0:width], c_m[0:1, 0:width], c_sum[0:1, 0:width]),
                reads=[Bf("c_sum")], writes=[Bf("c_m")])
            kb.op("V", "V", lambda v: v.tensor_copy(
                rb_mrstd[0:1, off:off + width], c_m[0:1, 0:width]),
                reads=[Bf("c_m")], writes=[Bf("rb_mrstd")])

        for it in range(NKT):
            s, o = SLOT[it // 8], (it % 8) * 1024
            kb.op("G", "QG", lambda g, it=it, s=s, o=o: g.dma_start(
                out=s[:, o:o + 1024], in_=hs_prepT[it * 128:(it + 1) * 128, :]),
                writes=[SB[it // 8]])
        for it in range(NKT):
            s, o = SLOT[it // 8], (it % 8) * 1024
            for ch in range(2):
                kb.op("T", "T", lambda t, s=s, o=o, it=it, ch=ch: t.matmul(
                    psH[0:1, ch * 512:(ch + 1) * 512], ones_c[:, :],
                    s[:, o + ch * 512:o + (ch + 1) * 512],
                    start=(it == 0), stop=(it == NKT - 1)),
                    reads=[SB[it // 8], Bf("ones_c")], writes=[Bf("psH")])
        kb.op("S", "S", lambda s: s.copy(wl1.bitcast(F32)[0:1, 0:1024],
                                         psH[0:1, :]),
              reads=[Bf("psH")], writes=[Bf("wl1")])
        for it in range(NKT):
            s, o = SLOT[it // 8], (it % 8) * 1024
            q, qo = SLOT[4], (it % 4) * 1024
            kb.op("S", "S", lambda sc, s=s, o=o, q=q, qo=qo: sc.activation(
                q[:, qo:qo + 1024], s[:, o:o + 1024], AF.Square),
                reads=[SB[it // 8]], writes=[SB[4]])
            for ch in range(2):
                kb.op("T", "T", lambda t, q=q, qo=qo, it=it, ch=ch: t.matmul(
                    psH[0:1, ch * 512:(ch + 1) * 512], ones_c[:, :],
                    q[:, qo + ch * 512:qo + (ch + 1) * 512],
                    start=(it == 0), stop=(it == NKT - 1)),
                    reads=[SB[4], Bf("ones_c")], writes=[Bf("psH")])
        for ch in range(2):
            kb.op("S", "S", lambda s, ch=ch: s.copy(
                c_sum[:, :], wl1.bitcast(F32)[0:1, ch * 512:(ch + 1) * 512]),
                reads=[Bf("wl1")], writes=[Bf("c_sum")])
            kb.op("S", "S", lambda s, ch=ch: s.copy(
                c_ssq[:, :], psH[0:1, ch * 512:(ch + 1) * 512]),
                reads=[Bf("psH")], writes=[Bf("c_ssq")])
            ln_rowmath(512, off=ch * 512)
        for ch in range(2):
            kb.op("T", "T", lambda t, ch=ch: t.matmul(
                psG[:, :], ones_r[:, :], rb_rstd[0:1, ch * 512:(ch + 1) * 512],
                start=True, stop=True),
                reads=[Bf("rb_rstd"), Bf("ones_r")], writes=[Bf("psG")])
            kb.op("T", "T", lambda t, ch=ch: t.matmul(
                psF[:, :], ones32, rb_mrstd[0:1, ch * 512:(ch + 1) * 512],
                start=True, stop=True),
                reads=[Bf("rb_mrstd"), Bf("ones32")], writes=[Bf("psF")])
            for it in range(NKT):
                s, o = SLOT[it // 8], (it % 8) * 1024 + ch * 512
                q, qo = SLOT[4], 4096 + (it % 4) * 1024 + ch * 512
                kb.op("V", "V", lambda v, s=s, o=o, q=q, qo=qo: v.tensor_tensor(
                    q[:, qo:qo + 512], s[:, o:o + 512], psG[:, :], ALU.mult),
                    reads=[SB[it // 8], Bf("psG")], writes=[SB[4]])
                kb.op("V", "V", lambda v, q=q, qo=qo: v.tensor_tensor(
                    q[:, qo:qo + 512], q[:, qo:qo + 512], psF[:, :], ALU.subtract),
                    reads=[Bf("psF")], writes=[SB[4]])
                kb.op("Q", "QS", lambda qq, it=it, q=q, qo=qo, ch=ch: qq.dma_start(
                    out=ln_in[it * 128:(it + 1) * 128, ch * 512:(ch + 1) * 512],
                    in_=q[:, qo:qo + 512]),
                    reads=[SB[4]], writes=[Bf("ln_in")])
        fl = kb.op("Q", "Q", lambda q: q.nop(), reads=[Bf("ln_in")])
        kb.op("G", "CCL", lambda g: g.collective_compute(
            "AllGather", ALU.bypass, replica_groups=[list(range(NCK))],
            ins=[ln_in], outs=[ln_all]),
            reads=[Bf("ln_in")], writes=[Bf("ln_all")], extra=[fl])

        for kt in range(16):
            s = SLOT[kt % 2]
            kb.op("G", "QG", lambda g, kt=kt, s=s: g.dma_start(
                out=s[:, 0:4096], in_=w1[kt]), writes=[SB[kt % 2]])
            kb.op("Q", "QS", lambda q, kt=kt, s=s: q.dma_start(
                out=w1t[kt], in_=s[:, 0:4096]),
                reads=[SB[kt % 2]], writes=[Bf("w1t")])
        for dt in range(NDT):
            s = SLOT[dt % 2]
            kb.op("G", "QG", lambda g, dt=dt, s=s: g.dma_start(
                out=s[:, 0:H], in_=woTs[dt]), writes=[SB[dt % 2]])
            kb.op("Q", "QS", lambda q, dt=dt, s=s: q.dma_start(
                out=wos[dt], in_=s[:, 0:H]),
                reads=[SB[dt % 2]], writes=[Bf("wos")])
        kb.op("V", "V", lambda v: v.memset(aug2[0:1, :], 0.0), writes=[Bf("aug2")])
        for it in range(NKT):
            s = SLOT[2 + it % 2]
            kb.op("G", "QG", lambda g, it=it, s=s: g.dma_start(
                out=s[:, 0:H], in_=wfT[it]), writes=[SB[2 + it % 2]])
            kb.op("Q", "QS", lambda q, it=it, s=s: q.dma_start(
                out=wf2[it], in_=s[:, 0:H]),
                reads=[SB[2 + it % 2]], writes=[Bf("wf2")])
            for ch in range(NKT):
                kb.op("T", "T", lambda t, s=s, ch=ch, it=it: t.matmul(
                    psH[0:1, 0:128], ones_c[:, :], s[:, ch * 128:(ch + 1) * 128],
                    start=(ch == 0), stop=(ch == NKT - 1)),
                    reads=[SB[2 + it % 2], Bf("ones_c")], writes=[Bf("psH")])
            kb.op("V", "V", lambda v, it=it: v.tensor_copy(
                aug2[0:1, it * 128:(it + 1) * 128], psH[0:1, 0:128]),
                reads=[Bf("psH")], writes=[Bf("aug2")])
        for it in range(NKT):
            kb.op("T", "T", lambda t, it=it: t.matmul(
                psH[0:1, 0:36], ones_c[:, :], awr_sb[:, it * 36:(it + 1) * 36],
                start=(it == 0), stop=(it == NKT - 1)),
                reads=[Bf("awr_sb"), Bf("ones_c")], writes=[Bf("psH")])
        kb.op("V", "V", lambda v: v.tensor_copy(colsA[:, :], psH[0:1, 0:36]),
              reads=[Bf("psH")], writes=[Bf("colsA")])

        def p1_chunk(b, part):
            th, cth = divmod(part, 2)
            if cth == 0:
                for kt in range(NKT):
                    s, o = SLOT[kt // 8], (kt % 8) * 1024
                    chunk = 2 * b + th
                    kb.op("Q", "QS", lambda q, kt=kt, s=s, o=o, chunk=chunk:
                          q.dma_start(
                        out=s[:, o:o + 1024],
                        in_=ln_all[chunk, kt * 128:(kt + 1) * 128, :]),
                        reads=[Bf("ln_all")], writes=[SB[kt // 8]])
            for ci in range(8):
                ct = cth * 8 + ci
                pw = ct % 2
                ws, wof = SLOT[4], pw * 4096
                wb = SB4[pw]
                kb.op("S", "QA", lambda q, ct=ct, ws=ws, wof=wof: q.dma_start(
                    out=ws[:, wof:wof + 4096], in_=w1t[ct]),
                    reads=[Bf("w1t")], writes=[wb])
                for half in range(2):
                    pst, pB = ((psA, Bf("psA")) if half == 0 else (psB, Bf("psB")))
                    for kt in range(NKT):
                        s, o = SLOT[kt // 8], (kt % 8) * 1024
                        kb.op("T", "T", lambda t, s=s, o=o, ws=ws, wof=wof, kt=kt,
                              half=half, pst=pst: t.matmul(
                            pst[:, :], ws[:, wof + kt * 128:wof + (kt + 1) * 128],
                            s[:, o + half * 512:o + (half + 1) * 512],
                            start=(kt == 0), stop=(kt == NKT - 1)),
                            reads=[SB[kt // 8], wb], writes=[pB])
                    if ct < 8:
                        dtc = ct
                        for tq in range(2):
                            gtq = th * 4 + half * 2 + tq
                            p = gtq % 2
                            xo = p * (TQ + 3)
                            cB = Bf("xconv")
                            kb.op("S", "S", lambda s, xo=xo, pst=pst, tq=tq: s.copy(
                                xconv[:, xo + 3:xo + 3 + TQ],
                                pst[:, tq * TQ:(tq + 1) * TQ]),
                                reads=[pB], writes=[cB])
                            if gtq == 0:
                                kb.op("V", "V", lambda v, xo=xo: v.memset(
                                    xconv[:, xo:xo + 3], 0.0), writes=[cB])
                            else:
                                kb.op("V", "V", lambda v, xo=xo, dtc=dtc:
                                      v.tensor_copy(
                                    xconv[:, xo:xo + 3],
                                    tails[:, dtc * 3:dtc * 3 + 3]),
                                    reads=[Bf("tails")], writes=[cB])
                            kb.op("V", "V", lambda v, xo=xo, dtc=dtc: v.tensor_copy(
                                tails[:, dtc * 3:dtc * 3 + 3],
                                xconv[:, xo + TQ:xo + TQ + 3]),
                                reads=[cB], writes=[Bf("tails")])
                            kb.op("V", "V", lambda v, xo=xo, dtc=dtc: v.tensor_scalar(
                                cacc[:, 0:TQ], xconv[:, xo:xo + TQ],
                                cw_sb[:, dtc * 4:dtc * 4 + 1],
                                cb_sb[:, dtc:dtc + 1], ALU.mult, ALU.add),
                                reads=[cB, Bf("cw_sb"), Bf("cb_sb")],
                                writes=[CAB[0]])
                            for k in (1, 2, 3):
                                src, dst = ((k + 1) % 2) * TQ, (k % 2) * TQ
                                kb.op("V", "V", lambda v, xo=xo, dtc=dtc, k=k,
                                      src=src, dst=dst: v.scalar_tensor_tensor(
                                    cacc[:, dst:dst + TQ],
                                    xconv[:, xo + k:xo + k + TQ],
                                    cw_sb[:, dtc * 4 + k:dtc * 4 + k + 1],
                                    cacc[:, src:src + TQ], ALU.mult, ALU.add),
                                    reads=[cB, Bf("cw_sb"), CAB[(k + 1) % 2]],
                                    writes=[CAB[k % 2]])
                            kb.op("S", "S", lambda s, tq=tq: s.activation(
                                xcw[:, tq * TQ:(tq + 1) * TQ],
                                cacc[:, TQ:2 * TQ], AF.Silu),
                                reads=[CAB[1]], writes=[Bf("xcw")])
                            if tq == 1:
                                kb.op("Q", "QS", lambda q, dtc=dtc, b=b, gtq=gtq:
                                      q.dma_start(
                                    out=xcsp[b, dtc, :,
                                             (gtq - 1) * TQ:(gtq + 1) * TQ],
                                    in_=xcw[:, :]),
                                    reads=[Bf("xcw")], writes=[Bf("xcsp")])
                    else:
                        dtc = ct - 8
                        gp = half
                        go = gp * 512
                        kb.op("S", "S", lambda s, go=go, pst=pst: s.activation(
                            gev[:, go:go + 512], pst[:, :], AF.Silu),
                            reads=[pB], writes=[GEVB[gp]])
                        kb.op("Q", "QS", lambda q, go=go, gp=gp, dtc=dtc, b=b,
                              th=th, half=half: q.dma_start(
                            out=gsp[b, dtc, :, th * 1024 + half * 512:
                                    th * 1024 + (half + 1) * 512],
                            in_=gev[:, go:go + 512]),
                            reads=[GEVB[gp]], writes=[Bf("gsp")])
            if cth == 1:
                for ec in range(3):
                    for hh in range(2):
                        for dt in range(NDT):
                            kb.op("Q", "QS", lambda q, dt=dt, b=b,
                                  th=th, hh=hh: q.dma_start(
                                out=xcr[:, :],
                                in_=xcsp[b, dt, :, th * 1024 + hh * 512:
                                         th * 1024 + (hh + 1) * 512]),
                                reads=[Bf("xcsp")], writes=[XCRB[0]])
                            kb.op("T", "T", lambda t, ec=ec, dt=dt: t.matmul(
                                psE[:, :],
                                xpw_sb[:, dt * E3 + ec * 96:dt * E3 + (ec + 1) * 96],
                                xcr[:, :],
                                start=(dt == 0), stop=(dt == NDT - 1)),
                                reads=[Bf("xpw_sb"), XCRB[0]], writes=[Bf("psE")])
                        kb.op("S", "S", lambda s: s.copy(spsc[:, :], psE[:, :]),
                              reads=[Bf("psE")], writes=[Bf("spsc")])
                        kb.op("Q", "QS", lambda q, ec=ec, hh=hh, b=b, th=th:
                              q.dma_start(
                            out=arin[b][ec * 96:(ec + 1) * 96,
                                        th * 1024 + hh * 512:
                                        th * 1024 + (hh + 1) * 512],
                            in_=spsc[:, :]),
                            reads=[Bf("spsc")], writes=[Bf(f"arin{b}")])

        def p1_ar(b):
            fl = kb.op("Q", "Q", lambda q: q.nop(), reads=[Bf(f"arin{b}")])
            kb.op("G", f"CCA{b}", lambda g, b=b: g.collective_compute(
                "AllReduce", ALU.add, replica_groups=[list(range(NCK))],
                ins=[arin[b]], outs=[arout[b]]),
                reads=[Bf(f"arin{b}")], writes=[Bf(f"arout{b}")], extra=[fl])

        def rows_dt(b):
            kb.op("V", "V", lambda v: v.memset(hcar[:, :], 0.0),
                  writes=[Bf(f"hcar{d}") for d in range(NDT)])
            kb.op("G", "QG", lambda g, b=b: g.dma_start(
                out=spcB, in_=arout[b][256:272, :]),
                reads=[Bf(f"arout{b}")], writes=[Bf("wl1")])
            kb.op("G", "QG", lambda g, b=b: g.dma_start(
                out=spcC, in_=arout[b][272:288, :]),
                reads=[Bf(f"arout{b}")], writes=[Bf("wl1")])
            kb.op("S", "S", lambda s: s.activation(
                dAp[0:16, 0:L], spcB, AF.Square),
                reads=[Bf("wl1")], writes=[Bf("dAp")])
            kb.op("S", "S", lambda s: s.activation(
                dAp[0:16, L:2 * L], spcC, AF.Square),
                reads=[Bf("wl1")], writes=[Bf("dAp")])
            for (co, rbB_) in ((0, Bf("rb_rb")), (L, Bf("rb_rc"))):
                rb = rb_rb if co == 0 else rb_rc
                for ch in range(4):
                    kb.op("T", "T", lambda t, co=co, ch=ch: t.matmul(
                        psH[0:1, 0:512], ones_c[0:16, :],
                        dAp[0:16, co + ch * 512:co + (ch + 1) * 512],
                        start=True, stop=True),
                        reads=[Bf("dAp"), Bf("ones_c")], writes=[Bf("psH")])
                    kb.op("V", "V", lambda v: v.tensor_scalar(
                        s_scr[:, :], psH[0:1, 0:512], 1.0 / NST, EPS_RMS,
                        ALU.mult, ALU.add),
                        reads=[Bf("psH")], writes=[Bf("s_scr")])
                    kb.op("S", "S", lambda s: s.activation(
                        s_scr[:, :], s_scr[:, :], AF.Sqrt),
                        reads=[Bf("s_scr")], writes=[Bf("s_scr")])
                    kb.op("V", "V", lambda v: v.reciprocal(s_scr[:, :], s_scr[:, :]),
                          reads=[Bf("s_scr")], writes=[Bf("s_scr")])
                    kb.op("V", "V", lambda v, rb=rb, ch=ch: v.tensor_copy(
                        rb[0:1, ch * 512:(ch + 1) * 512], s_scr[:, :]),
                        reads=[Bf("s_scr")], writes=[rbB_])
            kb.op("V", "V", lambda v: v.tensor_scalar(
                bcrows[:, 0:L], spcB, bclnw[:, 0:1], -1.0, ALU.mult, ALU.mult),
                reads=[Bf("wl1"), Bf("bclnw")], writes=[Bf("bcrows")])
            kb.op("V", "V", lambda v: v.tensor_scalar(
                bcrows[:, L:2 * L], spcC, bclnw[:, 1:2], None, ALU.mult),
                reads=[Bf("wl1"), Bf("bclnw")], writes=[Bf("bcrows")])
            for (co, rb, rbB_) in ((0, rb_rb, Bf("rb_rb")),
                                   (L, rb_rc, Bf("rb_rc"))):
                for ch in range(4):
                    kb.op("T", "T", lambda t, co=co, ch=ch, rb=rb: t.matmul(
                        psH[0:16, 0:512], ones64[0:1, 0:16],
                        rb[0:1, ch * 512:(ch + 1) * 512],
                        start=True, stop=True),
                        reads=[rbB_, Bf("ones64")], writes=[Bf("psH")])
                    kb.op("S", "S", lambda s, ch=ch: s.copy(
                        hl2[0:16, ch * 512:(ch + 1) * 512],
                        psH[0:16, 0:512]),
                        reads=[Bf("psH")], writes=[Bf("hl2")])
                kb.op("V", "V", lambda v, co=co: v.tensor_tensor(
                    bcrows[:, co:co + L], bcrows[:, co:co + L],
                    hl2[0:16, 0:L], ALU.mult),
                    reads=[Bf("hl2")], writes=[Bf("bcrows")])
            for kt in range(2):
                for hh in range(2):
                    kb.op("G", "QG", lambda g, kt=kt, hh=hh, b=b: g.dma_start(
                        out=spd[:, :],
                        in_=arout[b][kt * 128:(kt + 1) * 128,
                                     hh * 1024:(hh + 1) * 1024]),
                        reads=[Bf(f"arout{b}")], writes=[Bf("spd")])
                    kb.op("S", "S", lambda s, kt=kt, hh=hh: s.activation(
                        wl1[:, kt * L + hh * 1024:kt * L + (hh + 1) * 1024],
                        spd[:, :], AF.Square),
                        reads=[Bf("spd")], writes=[Bf("wl1")])
            for ch in range(4):
                for kt in range(2):
                    kb.op("T", "T", lambda t, ch=ch, kt=kt: t.matmul(
                        psH[0:1, 0:512], ones_c[:, :],
                        wl1[:, kt * L + ch * 512:kt * L + (ch + 1) * 512],
                        start=(kt == 0), stop=(kt == 1)),
                        reads=[Bf("wl1"), Bf("ones_c")], writes=[Bf("psH")])
                kb.op("V", "V", lambda v: v.tensor_scalar(
                    s_scr[:, :], psH[0:1, 0:512], 1.0 / R, EPS_RMS,
                    ALU.mult, ALU.add),
                    reads=[Bf("psH")], writes=[Bf("s_scr")])
                kb.op("S", "S", lambda s: s.activation(
                    s_scr[:, :], s_scr[:, :], AF.Sqrt),
                    reads=[Bf("s_scr")], writes=[Bf("s_scr")])
                kb.op("V", "V", lambda v: v.reciprocal(s_scr[:, :], s_scr[:, :]),
                      reads=[Bf("s_scr")], writes=[Bf("s_scr")])
                kb.op("V", "V", lambda v, ch=ch: v.tensor_copy(
                    dAp[0:1, ch * 512:(ch + 1) * 512], s_scr[:, :]),
                    reads=[Bf("s_scr")], writes=[Bf("dAp")])
            for kt in range(2):
                for hh in range(2):
                    kb.op("G", "QG", lambda g, kt=kt, hh=hh, b=b: g.dma_start(
                        out=spd[:, :],
                        in_=arout[b][kt * 128:(kt + 1) * 128,
                                     hh * 1024:(hh + 1) * 1024]),
                        reads=[Bf(f"arout{b}")], writes=[Bf("spd")])
                    for c2 in range(2):
                        ch = hh * 2 + c2
                        kb.op("T", "T", lambda t, ch=ch: t.matmul(
                            psG[:, :], ones_r[:, :],
                            dAp[0:1, ch * 512:(ch + 1) * 512],
                            start=True, stop=True),
                            reads=[Bf("dAp"), Bf("ones_r")], writes=[Bf("psG")])
                        kb.op("V", "V", lambda v, kt=kt, ch=ch, c2=c2:
                              v.scalar_tensor_tensor(
                            hl2[:, kt * L + ch * 512:kt * L + (ch + 1) * 512],
                            spd[:, c2 * 512:(c2 + 1) * 512],
                            dtlnw_sb[:, kt:kt + 1], psG[:, :], ALU.mult, ALU.mult),
                            reads=[Bf("spd"), Bf("dtlnw_sb"), Bf("psG")],
                            writes=[Bf("hl2")])
            for dt in range(NDT):
                for tc in range(4):
                    for rt in range(2):
                        j = rt * NDT + dt
                        kb.op("T", "T", lambda t, j=j, rt=rt, tc=tc: t.matmul(
                            psF[:, :], dtp_sb[:, j * 128:(j + 1) * 128],
                            hl2[:, rt * L + tc * 512:rt * L + (tc + 1) * 512],
                            start=(rt == 0), stop=(rt == 1)),
                            reads=[Bf("dtp_sb"), Bf("hl2")], writes=[Bf("psF")])
                    kb.op("S", "S", lambda s, dt=dt: s.activation(
                        bct[:, 0:512], psF[:, :], AF.Sigmoid,
                        scale=-1.0, bias=ndtb[:, dt:dt + 1]),
                        reads=[Bf("psF"), Bf("ndtb")], writes=[Bf("bct")])
                    kb.op("S", "S", lambda s: s.activation(
                        gev[:, 0:512], bct[:, 0:512], AF.Ln),
                        reads=[Bf("bct")], writes=[GEVB[0]])
                    kb.op("Q", "QS", lambda q, b=b, dt=dt, tc=tc: q.dma_start(
                        out=dtlsp[b, dt, :, tc * 512:(tc + 1) * 512],
                        in_=gev[:, 0:512]),
                        reads=[GEVB[0]], writes=[Bf("dtlsp")])
            for n in range(NST):
                kb.op("V", "V", lambda v, n=n: v.memset(
                    dAp[:, n * TQS:n * TQS + 1], 0.0), writes=[Bf("dAp")])
            if b == 0:
                for n in range(NST):
                    kb.op("V", "V", lambda v, n=n: v.memset(
                        dApB[:, n * TQS:n * TQS + 1], 0.0),
                        writes=[Bf("dApB")])

        def bc_tq(b, tq):
            t0 = tq * TQ
            for (dst, dB, co) in ((bcB, Bf("bcB"), 0), (bcC, Bf("bcC"), L)):
                rows_v = bcrows[:, co + t0:co + t0 + TQ].unsqueeze(1) \
                    .broadcast_to([NST, NST, TQ])
                mask_v = dmask[:, :].unsqueeze(2).broadcast_to([NST, NST, TQ])
                diag_v = hl2[0:16, 0:NST * TQ].rearrange(
                    "p (n t) -> p n t", n=NST)
                kb.op("V", "V", lambda v, rows_v=rows_v, mask_v=mask_v,
                      diag_v=diag_v: v.tensor_tensor(
                    diag_v, rows_v, mask_v, ALU.mult),
                    reads=[Bf("bcrows"), Bf("dmask")], writes=[Bf("hl2")])
                for ch in range(4):
                    for c2 in range(2):
                        kb.op("T", "T", lambda t, ch=ch, c2=c2: t.matmul(
                            psH[:, c2 * 512:(c2 + 1) * 512], ones16[:, :],
                            hl2[0:16, ch * 1024 + c2 * 512:
                                ch * 1024 + (c2 + 1) * 512],
                            start=True, stop=True),
                            reads=[Bf("hl2"), Bf("ones16")], writes=[Bf("psH")])
                    kb.op("S", "S", lambda s, dst=dst, ch=ch: s.copy(
                        dst[:, ch * 1024:(ch + 1) * 1024], psH[:, :]),
                        reads=[Bf("psH")], writes=[dB])

        def scan_tq(b, tq):
            t0 = tq * TQ
            for dt in range(NDT):
                p = dt % 2
                po = p * TQ
                if p == 0:
                    for (dst, dB2, src, sB) in (
                            (dtl_t, (DTLB[0], DTLB[1]), dtlsp, Bf("dtlsp")),
                            (xc_t, (XCTB[0], XCTB[1]), xcsp, Bf("xcsp")),
                            (g_t, (GTB[0], GTB[1]), gsp, Bf("gsp"))):
                        kb.op("G", "QG", lambda g, dst=dst, src=src, b=b, dt=dt,
                              t0=t0: g.dma_start(
                            out=dst[:, :].rearrange("p (j t) -> p j t", j=2),
                            in_=src[b, dt:dt + 2, :, t0:t0 + TQ]
                                .transpose([1, 0, 2])),
                            reads=[sB], writes=[dB2[0], dB2[1]])
                kb.op("V", "V", lambda v, po=po: v.tensor_tensor(
                    u_t[:, :], dtl_t[:, po:po + TQ], xc_t[:, po:po + TQ], ALU.mult),
                    reads=[DTLB[p], XCTB[p]], writes=[Bf("u_t")])
                dApX, wlX, hlX = ((dAp, wl1, hl2), (dApB, wl1B, hl2))[dt % 2]
                dApN, wlN, hlN = (("dAp", "wl1", "hl2"),
                                  ("dApB", "wl1B", "hl2"))[dt % 2]
                hcN = f"hcar{dt}"
                kb.op("G", "QG", lambda q, dt=dt, wlX=wlX: q.dma_start(
                    out=wlX[:, :].rearrange("p (n t) -> p n t", n=NST)
                    [:, :, 0:1],
                    in_=hcar[:, dt * NST:(dt + 1) * NST].unsqueeze(2),
                    allow_slow_non_contiguous=True),
                    reads=[Bf(hcN)], writes=[Bf(wlN)])
                for n in range(NST):
                    kb.op("S", "S", lambda s, po=po, n=n, dt=dt, dApX=dApX:
                          s.activation(
                        dApX[:, n * TQS + 1:n * TQS + 1 + TQ],
                        dtl_t[:, po:po + TQ], AF.Exp,
                        scale=negA[:, dt * NST + n:dt * NST + n + 1]),
                        reads=[DTLB[p], Bf("negA")], writes=[Bf(dApN)])
                uview = u_t[:, :].unsqueeze(1).broadcast_to([128, NST, TQ])
                bview = bcB[:, :].rearrange("p (n t) -> p n t", n=NST)
                wview = wlX[:, :].rearrange("p (n t) -> p n t", n=NST)[:, :, 1:]
                kb.op("V", "V", lambda v, uview=uview, bview=bview, wview=wview:
                      v.tensor_tensor(wview, uview, bview, ALU.mult),
                      reads=[Bf("u_t"), Bf("bcB")], writes=[Bf(wlN)])
                kb.op("V", "V", lambda v, dApX=dApX, wlX=wlX, hlX=hlX:
                      v.tensor_tensor_scan(
                    hlX[:, :], dApX[:, :], wlX[:, :], 0.0, ALU.mult, ALU.add),
                    reads=[Bf(dApN), Bf(wlN)], writes=[Bf(hlN)])
                kb.op("G", "QG", lambda q, dt=dt, hlX=hlX: q.dma_start(
                    out=hcar[:, dt * NST:(dt + 1) * NST].unsqueeze(2),
                    in_=hlX[:, :].rearrange("p (n t) -> p n t", n=NST)
                    [:, :, TQ:TQ + 1],
                    allow_slow_non_contiguous=True),
                    reads=[Bf(hlN)], writes=[Bf(hcN)])
                cview = bcC[:, :].rearrange("p (n t) -> p n t", n=NST)
                hview = hlX[:, :].rearrange("p (n t) -> p n t", n=NST)[:, :, 1:]
                pview = dApX[:, :].rearrange("p (n t) -> p n t", n=NST)[:, :, 1:]
                kb.op("V", "V", lambda v, pview=pview, hview=hview, cview=cview:
                      v.tensor_tensor(pview, hview, cview, ALU.mult),
                      reads=[Bf(hlN), Bf("bcC")], writes=[Bf(dApN)])
                pv = dApX[:, :].rearrange("p (n t) -> p n t", n=NST)
                lv = wlX[:, :].rearrange("p (n t) -> p n t", n=NST)
                kb.op("V", "V", lambda v, pv=pv, lv=lv: v.tensor_tensor(
                    lv[:, 0:8, 1:], pv[:, 0:16:2, 1:], pv[:, 1:16:2, 1:],
                    ALU.add),
                    reads=[Bf(dApN)], writes=[Bf(wlN)])
                kb.op("V", "V", lambda v, lv=lv: v.tensor_tensor(
                    lv[:, 8:12, 1:], lv[:, 0:8:2, 1:], lv[:, 1:8:2, 1:],
                    ALU.add), writes=[Bf(wlN)])
                kb.op("V", "V", lambda v, lv=lv: v.tensor_tensor(
                    lv[:, 12:14, 1:], lv[:, 8:12:2, 1:], lv[:, 9:12:2, 1:],
                    ALU.add), writes=[Bf(wlN)])
                kb.op("V", "V", lambda v, wlX=wlX: v.tensor_tensor(
                    ys_t[:, :], wlX[:, 12 * TQS + 1:12 * TQS + 1 + TQ],
                    wlX[:, 13 * TQS + 1:13 * TQS + 1 + TQ], ALU.add),
                    reads=[Bf(wlN)], writes=[Bf("ys_t")])
                kb.op("V", "V", lambda v, po=po, dt=dt: v.scalar_tensor_tensor(
                    t2_t[:, :], xc_t[:, po:po + TQ], d_sb[:, dt:dt + 1], ys_t[:, :],
                    ALU.mult, ALU.add),
                    reads=[XCTB[p], Bf("d_sb"), Bf("ys_t")], writes=[Bf("t2_t")])
                kb.op("V", "V", lambda v, po=po: v.tensor_tensor(
                    ygd_t[:, po:po + TQ], t2_t[:, :], g_t[:, po:po + TQ], ALU.mult),
                    reads=[Bf("t2_t"), GTB[p]], writes=[YGDB[p]])
                kb.op("Q", "QS", lambda q, po=po, b=b, dt=dt, t0=t0: q.dma_start(
                    out=yin[b][dt * 128:(dt + 1) * 128, t0:t0 + TQ],
                    in_=ygd_t[:, po:po + TQ]),
                    reads=[YGDB[p]], writes=[Bf(f"yin{b}")])

        def y_rs(b):
            fl = kb.op("Q", "Q", lambda q: q.nop(), reads=[Bf(f"pmix{b}")])
            kb.op("G", f"CCR{b}", lambda g, b=b: g.collective_compute(
                "ReduceScatter", ALU.add, replica_groups=[list(range(NCK))],
                ins=[pmix[b]], outs=[rsout[b]]),
                reads=[Bf(f"pmix{b}")], writes=[Bf(f"rsout{b}")], extra=[fl])

        def ln_stats_T(mslot, mbuf, sqslot, sqbuf):
            for it in range(NKT):
                mo = it * TSL
                kb.op("S", "S", lambda s, mo=mo, mslot=mslot, sqslot=sqslot:
                      s.activation(sqslot[:, mo:mo + TSL], mslot[:, mo:mo + TSL],
                                   AF.Square),
                      reads=[mbuf], writes=[sqbuf])
            for it in range(NKT):
                mo = it * TSL
                kb.op("T", "T", lambda t, mo=mo, mslot=mslot, it=it: t.matmul(
                    psH[0:1, 0:TSL], ones_c[:, :], mslot[:, mo:mo + TSL],
                    start=(it == 0), stop=(it == NKT - 1)),
                    reads=[mbuf, Bf("ones_c")], writes=[Bf("psH")])
            kb.op("S", "S", lambda s: s.copy(c_sum[0:1, 0:TSL], psH[0:1, 0:TSL]),
                  reads=[Bf("psH")], writes=[Bf("c_sum")])
            for it in range(NKT):
                mo = it * TSL
                kb.op("T", "T", lambda t, mo=mo, sqslot=sqslot, it=it: t.matmul(
                    psH[0:1, TSL:2 * TSL], ones_c[:, :], sqslot[:, mo:mo + TSL],
                    start=(it == 0), stop=(it == NKT - 1)),
                    reads=[sqbuf, Bf("ones_c")], writes=[Bf("psH")])
            kb.op("S", "S", lambda s: s.copy(c_ssq[0:1, 0:TSL],
                                             psH[0:1, TSL:2 * TSL]),
                  reads=[Bf("psH")], writes=[Bf("c_ssq")])
            ln_rowmath(TSL)

        def stat_bcast(which):
            pairs = (((rb_rstd, 0), (rb_mrstd, 1)) if which != 1
                     else ((rb_rstd, 2),))
            for rb, pos in pairs:
                rbB = Bf("rb_rstd") if rb is rb_rstd else Bf("rb_mrstd")
                on = ones_r[:, :] if rb is rb_rstd else ones32
                onB = Bf("ones_r") if rb is rb_rstd else Bf("ones32")
                kb.op("T", "T", lambda t, rb=rb, on=on: t.matmul(
                    psG[:, 0:TSL], on, rb[0:1, 0:TSL],
                    start=True, stop=True),
                    reads=[rbB, onB], writes=[Bf("psG")])
                kb.op("S", "S", lambda s, pos=pos: s.copy(
                    bct[:, pos * TSL:(pos + 1) * TSL], psG[:, 0:TSL]),
                    reads=[Bf("psG")], writes=[Bf("bct")])

        def epi_chunk(b, part):
            if part in (0, 1):
                for sub in range(2):
                    tch = part * 2 + sub
                    t0 = tch * 512
                    rp = tch % 2
                    ro = rp * 4096
                    for dt in range(NDT):
                        kb.op("Q", "QS", lambda q, dt=dt, b=b, t0=t0, ro=ro:
                              q.dma_start(
                            out=SLOT[0][:, ro + dt * 512:ro + (dt + 1) * 512],
                            in_=yin[b][dt * 128:(dt + 1) * 128, t0:t0 + 512]),
                            reads=[Bf(f"yin{b}")], writes=[SB4[rp] if False
                                                           else SB[0]])
                    for htg in range(8):
                        wp = (tch * 8 + htg) % 2
                        wof = wp * 4096
                        kb.op("S", "QA", lambda q, htg=htg, wof=wof: q.dma_start(
                            out=SLOT[4][:, wof:wof + 4096], in_=wos[htg]),
                            reads=[Bf("wos")], writes=[SB4[wp]])
                        for hh in range(4):
                            pst, pB = ((psA, Bf("psA")), (psB, Bf("psB")),
                                       (psF, Bf("psF")), (psI, Bf("psI")))[hh]
                            for dt in range(NDT):
                                kb.op("T", "T", lambda t, pst=pst, wof=wof, dt=dt,
                                      hh=hh, ro=ro: t.matmul(
                                    pst[:, :],
                                    SLOT[4][:, wof + dt * 512 + hh * 128:
                                            wof + dt * 512 + (hh + 1) * 128],
                                    SLOT[0][:, ro + dt * 512:ro + (dt + 1) * 512],
                                    start=(dt == 0), stop=(dt == NDT - 1)),
                                    reads=[SB[0], SB4[wp]], writes=[pB])
                            ht = htg * 4 + hh
                            gp = hh % 2
                            go = gp * 512
                            kb.op("S", "S", lambda s, pst=pst, go=go: s.copy(
                                gev[:, go:go + 512], pst[:, :]),
                                reads=[pB], writes=[GEVB[gp]])
                            kb.op("Q", "QS", lambda q, b=b, tch=tch, ht=ht,
                                  go=go: q.dma_start(
                                out=pmix[b][tch * 2:tch * 2 + 2,
                                            ht * 128:(ht + 1) * 128, :]
                                    .transpose([1, 0, 2]),
                                in_=gev[:, go:go + 512]
                                    .rearrange("p (j t) -> p j t", j=2)),
                                reads=[GEVB[gp]], writes=[Bf(f"pmix{b}")])
            if part == 2:
                for ht in range(NKT):
                    mo = ht * TSL
                    kb.op("Q", "QS", lambda q, ht=ht, mo=mo, b=b: q.dma_start(
                        out=SLOT[2][:, mo:mo + TSL],
                        in_=rsout[b][ht * 128:(ht + 1) * 128, :]),
                        reads=[Bf(f"rsout{b}")], writes=[SB[2]])
            if part == 2:
                ln_stats_T(SLOT[2], SB[2], SLOT[1], SB[1])
                stat_bcast(0)
                for ht in range(NKT):
                    p = ht % 2
                    po = p * TSL
                    mo = ht * TSL
                    kb.op("Q", "QS", lambda q, ht=ht, b=b: q.dma_start(
                        out=x0t[:, :],
                        in_=hs_resT[b, ht * 128:(ht + 1) * 128, :]),
                        writes=[X0B[0]])
                    kb.op("V", "V", lambda v, po=po, mo=mo: v.tensor_tensor(
                        cacc[:, po:po + TSL], SLOT[2][:, mo:mo + TSL],
                        bct[:, 0:TSL], ALU.mult),
                        reads=[SB[2], Bf("bct")], writes=[CAB[p]])
                    kb.op("V", "V", lambda v, po=po: v.scalar_tensor_tensor(
                        cacc[:, po:po + TSL], cacc[:, po:po + TSL], 1.0,
                        bct[:, TSL:2 * TSL], ALU.mult, ALU.subtract),
                        reads=[Bf("bct")], writes=[CAB[p]])
                    kb.op("V", "V", lambda v, po=po: v.tensor_tensor(
                        x1t[:, po:po + TSL], cacc[:, po:po + TSL],
                        x0t[:, :], ALU.add),
                        reads=[CAB[p], X0B[0]], writes=[X1B[p]])
                    kb.op("Q", "QS", lambda q, po=po, ht=ht, b=b: q.dma_start(
                        out=x1sp[b, ht * 128:(ht + 1) * 128, :],
                        in_=x1t[:, po:po + TSL]),
                        reads=[X1B[p]], writes=[Bf("x1sp")])
                for it in range(NKT):
                    o = it * TSL
                    kb.op("G", "QG", lambda g, it=it, o=o, b=b: g.dma_start(
                        out=SLOT[0][:, o:o + TSL],
                        in_=x1sp[b, it * 128:(it + 1) * 128, :]),
                        reads=[Bf("x1sp")], writes=[SB[0]])
                ln_stats_T(SLOT[0], SB[0], SLOT[1], SB[1])
                stat_bcast(1)
                kb.op("V", "V", lambda v: v.tensor_scalar_mul(
                    rb_tmp[0:1, 0:TSL], c_m[0:1, 0:TSL], -1.0),
                    reads=[Bf("c_m")], writes=[Bf("rb_tmp")])
                kb.op("V", "V", lambda v: v.tensor_tensor(
                    aug2r[0:1, :], rb_tmp[0:1, 0:TSL], c_ssq[0:1, 0:TSL], ALU.mult),
                    reads=[Bf("rb_tmp"), Bf("c_ssq")], writes=[Bf("aug2r")])
                kb.op("G", "QG", lambda g: g.dma_start(
                    out=aug2r[1:2, :], in_=c_ssq[0:1, 0:TSL]),
                    reads=[Bf("c_ssq")], writes=[Bf("aug2r")])
                gates_lora(b)
            if part == 3:
                ffn_and_out(b)

        def gates_lora(b):
            for it in range(NKT):
                o = it * TSL
                kb.op("T", "T", lambda t, it=it, o=o: t.matmul(
                    psH[0:36, 0:TSL], awr_sb[:, it * 36:(it + 1) * 36],
                    SLOT[0][:, o:o + TSL], start=(it == 0), stop=False),
                    reads=[SB[0], Bf("awr_sb")], writes=[Bf("psH")])
            kb.op("T", "T", lambda t: t.matmul(
                psH[0:36, 0:TSL], colsA[:, :], aug2r[0:1, :],
                start=False, stop=True),
                reads=[Bf("colsA"), Bf("aug2r")], writes=[Bf("psH")])
            kb.op("V", "V", lambda v: v.tensor_copy(scr36[:, :], psH[0:36, 0:TSL]),
                  reads=[Bf("psH")], writes=[Bf("scr36")])
            kb.op("T", "T", lambda t: t.matmul(
                psG[0:4, 0:TSL], sel2[:, :], scr36[:, :], start=True, stop=True),
                reads=[Bf("sel2"), Bf("scr36")], writes=[Bf("psG")])
            kb.op("V", "V", lambda v: v.tensor_copy(gl4[:, 0:TSL], psG[0:4, 0:TSL]),
                  reads=[Bf("psG")], writes=[Bf("gl4")])
            kb.op("T", "T", lambda t: t.matmul(
                psG[0:4, 0:TSL], ones_r[:, 0:4], rb_rstd[0:1, 0:TSL],
                start=True, stop=True),
                reads=[Bf("rb_rstd"), Bf("ones_r")], writes=[Bf("psG")])
            kb.op("V", "V", lambda v: v.tensor_copy(bct4[:, :], psG[0:4, 0:TSL]),
                  reads=[Bf("psG")], writes=[Bf("bct4")])
            kb.op("V", "V", lambda v: v.tensor_tensor(
                gl4[:, 0:TSL], gl4[:, 0:TSL], bct4[:, :], ALU.mult),
                reads=[Bf("bct4")], writes=[Bf("gl4")])
            kb.op("S", "S", lambda s: s.activation(
                gb4[:, 0:TSL], gl4[:, 0:TSL], AF.Exp),
                reads=[Bf("gl4")], writes=[Bf("gb4")])
            kb.op("T", "T", lambda t: t.matmul(
                psG[0:1, 0:TSL], ones_c[0:4, :], gb4[:, 0:TSL],
                start=True, stop=True),
                reads=[Bf("gb4"), Bf("ones_c")], writes=[Bf("psG")])
            kb.op("V", "V", lambda v: v.tensor_copy(
                gl4[0:1, TSL:2 * TSL], psG[0:1, 0:TSL]),
                reads=[Bf("psG")], writes=[Bf("gl4")])
            kb.op("V", "V", lambda v: v.reciprocal(
                gl4[0:1, TSL:2 * TSL], gl4[0:1, TSL:2 * TSL]),
                reads=[Bf("gl4")], writes=[Bf("gl4")])
            kb.op("V", "V", lambda v: v.tensor_copy(
                rb_tmp[0:1, 0:TSL], gl4[0:1, TSL:2 * TSL]),
                reads=[Bf("gl4")], writes=[Bf("rb_tmp")])
            kb.op("T", "T", lambda t: t.matmul(
                psG[0:4, 0:TSL], ones_r[:, 0:4], rb_tmp[0:1, 0:TSL],
                start=True, stop=True),
                reads=[Bf("rb_tmp"), Bf("ones_r")], writes=[Bf("psG")])
            kb.op("V", "V", lambda v: v.tensor_copy(bct4[:, :], psG[0:4, 0:TSL]),
                  reads=[Bf("psG")], writes=[Bf("bct4")])
            kb.op("V", "V", lambda v: v.tensor_tensor(
                gb4[:, TSL:2 * TSL], gb4[:, 0:TSL], bct4[:, :], ALU.mult),
                reads=[Bf("bct4")], writes=[Bf("gb4")])
            kb.op("T", "T", lambda t: t.matmul(
                psH[0:32, 0:TSL], sel[:, :], gb4[:, TSL:2 * TSL],
                start=True, stop=True),
                reads=[Bf("sel"), Bf("gb4")], writes=[Bf("psH")])
            kb.op("V", "V", lambda v: v.tensor_copy(g32[:, :], psH[0:32, 0:TSL]),
                  reads=[Bf("psH")], writes=[Bf("g32")])
            kb.op("V", "V", lambda v: v.tensor_tensor(
                prs[:, :], scr36[0:32, :], g32[:, :], ALU.mult),
                reads=[Bf("scr36"), Bf("g32")], writes=[Bf("prs")])

        def ffn_and_out(b):
            for ot in range(NKT):
                pw = ot % 2
                ws, wof = SLOT[4], pw * 4096
                wb = SB4[pw]
                kb.op("S", "QA", lambda q, ot=ot, ws=ws, wof=wof: q.dma_start(
                    out=ws[:, wof:wof + 4096], in_=wf2[ot]),
                    reads=[Bf("wf2")], writes=[wb])
                kb.op("G", "QG", lambda g, ot=ot, pw=pw: g.dma_start(
                    out=bmt_t[:, pw * 128:(pw + 1) * 128],
                    in_=fBmT[:, ot * 128:(ot + 1) * 128]),
                    writes=[Bf("bmt_t")])
                for it in range(NKT):
                    o = it * TSL
                    kb.op("T", "T", lambda t, ws=ws, wof=wof, it=it, o=o: t.matmul(
                        psA[:, 0:TSL], ws[:, wof + it * 128:wof + (it + 1) * 128],
                        SLOT[0][:, o:o + TSL], start=(it == 0), stop=False),
                        reads=[SB[0], wb], writes=[Bf("psA")])
                kb.op("T", "T", lambda t, ot=ot: t.matmul(
                    psA[:, 0:TSL], aug2[:, ot * 128:(ot + 1) * 128],
                    aug2r[:, :], start=False, stop=False),
                    reads=[Bf("aug2"), Bf("aug2r")], writes=[Bf("psA")])
                kb.op("T", "T", lambda t, ot=ot, pw=pw: t.matmul(
                    psA[:, 0:TSL], bmt_t[:, pw * 128:(pw + 1) * 128], prs[:, :],
                    start=False, stop=True),
                    reads=[Bf("bmt_t"), Bf("prs")], writes=[Bf("psA")])
                mo = ot * TSL
                kb.op("V", "V", lambda v, mo=mo: v.tensor_tensor(
                    SLOT[2][:, mo:mo + TSL], psA[:, 0:TSL],
                    bct[:, 2 * TSL:3 * TSL], ALU.mult),
                    reads=[Bf("psA"), Bf("bct")], writes=[SB[2]])
            ln_stats_T(SLOT[2], SB[2], SLOT[1], SB[1])
            stat_bcast(2)
            for ht in range(NKT):
                p = ht % 2
                po = p * TSL
                mo = ht * TSL
                kb.op("G", "QG", lambda g, po=po, ht=ht, b=b: g.dma_start(
                    out=x1t[:, po:po + TSL],
                    in_=x1sp[b, ht * 128:(ht + 1) * 128, :]),
                    reads=[Bf("x1sp")], writes=[X1B[p]])
                kb.op("V", "V", lambda v, po=po, mo=mo: v.tensor_tensor(
                    cacc[:, po:po + TSL], SLOT[2][:, mo:mo + TSL],
                    bct[:, 0:TSL], ALU.mult),
                    reads=[SB[2], Bf("bct")], writes=[CAB[p]])
                kb.op("V", "V", lambda v, po=po: v.scalar_tensor_tensor(
                    cacc[:, po:po + TSL], cacc[:, po:po + TSL], 1.0,
                    bct[:, TSL:2 * TSL], ALU.mult, ALU.subtract),
                    reads=[Bf("bct")], writes=[CAB[p]])
                kb.op("V", "V", lambda v, po=po: v.tensor_tensor(
                    cacc[:, po:po + TSL], cacc[:, po:po + TSL],
                    x1t[:, po:po + TSL], ALU.add),
                    reads=[X1B[p]], writes=[CAB[p]])
                kb.op("Q", "QS", lambda q, po=po, ht=ht, b=b: q.dma_start(
                    out=out_ext[b, ht * 128:(ht + 1) * 128, :],
                    in_=cacc[:, po:po + TSL]),
                    reads=[CAB[p]], writes=[Bf("out")])

        for part in range(4):
            p1_chunk(0, part)
        p1_ar(0)
        for b in range(B):
            rows_dt(b)
            bc_tq(b, 0)
            scan_tq(b, 0)
            bc_tq(b, 1)
            scan_tq(b, 1)
            if b >= 1:
                epi_chunk(b - 1, 0)
            bc_tq(b, 2)
            scan_tq(b, 2)
            if b >= 1:
                epi_chunk(b - 1, 1)
                y_rs(b - 1)
            bc_tq(b, 3)
            scan_tq(b, 3)
            if b >= 1:
                epi_chunk(b - 1, 2)
                epi_chunk(b - 1, 3)
            bc_tq(b, 4)
            scan_tq(b, 4)
            bc_tq(b, 5)
            scan_tq(b, 5)
            if b + 1 < B:
                p1_chunk(b + 1, 0)
            bc_tq(b, 6)
            scan_tq(b, 6)
            if b + 1 < B:
                p1_chunk(b + 1, 1)
            bc_tq(b, 7)
            scan_tq(b, 7)
            if b + 1 < B:
                p1_chunk(b + 1, 2)
                p1_chunk(b + 1, 3)
                p1_ar(b + 1)
        epi_chunk(B - 1, 0)
        epi_chunk(B - 1, 1)
        y_rs(B - 1)
        epi_chunk(B - 1, 2)
        epi_chunk(B - 1, 3)

        kb.op("G", "G", lambda g: g.nop(),
              extra=[x for x in (bufs["out"].w,) if x is not None])

        kb.finalize()

        with contextlib.ExitStack() as st2:
            sems = {}
            ccs = ["CCL"] + [f"CCA{b}" for b in range(B)] + \
                  [f"CCR{b}" for b in range(B)]
            pools = [f"{q}{i}" for q, n in KB.POOLS.items() for i in range(n)]
            for s in ["T", "V", "S", "G", "Q", "QS", "QA", "QG"] + ccs + pools:
                sems[s] = st2.enter_context(nc.semaphore(f"sem_{s}"))
            with nc.Block() as block:
                @block.tensor
                def _(e):
                    kb.emit("T", e, sems)

                @block.vector
                def _(e):
                    kb.emit("V", e, sems)

                @block.scalar
                def _(e):
                    kb.emit("S", e, sems)

                @block.gpsimd
                def _(e):
                    kb.emit("G", e, sems)

                @block.sync
                def _(e):
                    kb.emit("Q", e, sems)
    return nc


_NC_CACHE = None


def kernel(**inputs):
    global _NC_CACHE
    f32 = np.float32

    hs = np.asarray(inputs["L0_attn_hidden_states"], f32)
    W1 = np.asarray(inputs["L0_attn_in_proj_w"], f32)
    convw = np.asarray(inputs["L0_attn_conv_w"], f32)
    convb = np.asarray(inputs["L0_attn_conv_b"], f32)
    xpw = np.asarray(inputs["L0_attn_x_proj_w"], f32)
    dtp = np.asarray(inputs["L0_attn_dt_proj_w"], f32)
    dtbias = np.asarray(inputs["L0_attn_dt_bias"], f32)
    alog = np.asarray(inputs["L0_attn_A_log"], f32)
    dparam = np.asarray(inputs["L0_attn_D_param"], f32)
    wo = np.asarray(inputs["L0_attn_out_proj_w"], f32)
    dtlnw = np.asarray(inputs["L0_attn_dt_ln_w"], f32)
    blnw = np.asarray(inputs["L0_attn_B_ln_w"], f32)
    clnw = np.asarray(inputs["L0_attn_C_ln_w"], f32)
    wf = np.asarray(inputs["L0_ffn_W"], f32)
    fbv = np.asarray(inputs["L0_ffn_b"], f32)
    fA = np.asarray(inputs["L0_ffn_A"], f32)
    fBm = np.asarray(inputs["L0_ffn_Bm"], f32)
    fWr = np.asarray(inputs["L0_ffn_Wr"], f32)

    hsT = np.ascontiguousarray(hs.reshape(B * L, H).T)
    woT_full = np.ascontiguousarray(wo.T)
    wfT_full = np.ascontiguousarray(wf.T)
    wf_sh = np.ascontiguousarray(
        wfT_full.reshape(32, 128, 32, 128).transpose(2, 1, 0, 3)
        .reshape(32, 128, 4096))
    dtpT_full = np.ascontiguousarray(dtp.T)
    xpwT_full = np.ascontiguousarray(xpw.T)
    fAWr = np.concatenate([fA.reshape(32, H), fWr], axis=0)
    fAWrT = np.ascontiguousarray(fAWr.T)
    fBmT = np.ascontiguousarray(fBm.transpose(0, 2, 1).reshape(32, H))

    in_maps = []
    for k in range(NCK):
        dsl = slice(k * DSH, (k + 1) * DSH)
        res = np.stack([
            np.ascontiguousarray(hsT[:, b * L + k * TSL:b * L + (k + 1) * TSL])
            for b in range(B)])
        w1k = np.concatenate(
            [W1[:, dsl], W1[:, DIN + k * DSH:DIN + (k + 1) * DSH]], axis=1)
        w1_sh = np.ascontiguousarray(
            w1k.reshape(32, 128, 16, 128).transpose(2, 1, 0, 3)
            .reshape(16, 128, 4096))
        wo_sh = np.ascontiguousarray(
            woT_full[dsl].reshape(8, 128, 8, 512).transpose(2, 1, 0, 3)
            .reshape(8, 128, 4096))
        in_maps.append({
            "hs_prepT": np.ascontiguousarray(hsT[:, k * 1024:(k + 1) * 1024]),
            "hs_resT": res,
            "w1": w1_sh,
            "woTs": wo_sh,
            "wfT": wf_sh,
            "convw": np.ascontiguousarray(convw[dsl]),
            "convb": np.ascontiguousarray(convb[dsl]),
            "dtbias": np.ascontiguousarray(dtbias[dsl]),
            "dparam": np.ascontiguousarray(dparam[dsl]),
            "alog": np.ascontiguousarray(alog[dsl]),
            "xpwT": np.ascontiguousarray(xpwT_full[dsl]),
            "dtpT": np.ascontiguousarray(dtpT_full[:, dsl]),
            "dtlnw": dtlnw, "blnw": blnw, "clnw": clnw,
            "fb": fbv, "fAWrT": fAWrT, "fBmT": fBmT,
        })

    if _NC_CACHE is None:
        _NC_CACHE = build_program()
    res = run_bass_kernel_spmd(_NC_CACHE, in_maps, core_ids=list(range(NCK)))
    full = np.empty((B, L, H), np.float32)
    for k in range(NCK):
        full[:, k * TSL:(k + 1) * TSL, :] = res.results[k]["out"].transpose(0, 2, 1)
    return full


# version 17
# speedup vs baseline: 1.2853x; 1.0053x over previous
import numpy as np
from collections import defaultdict
import contextlib

import concourse.bass as bass
import concourse.mybir as mybir
from concourse.bass_utils import run_bass_kernel_spmd

F32 = mybir.dt.float32
BF16 = mybir.dt.bfloat16
U32 = mybir.dt.uint32
AF = mybir.ActivationFunctionType
ALU = mybir.AluOpType

NCK = 8
H = 4096
DIN = 8192
DSH = DIN // NCK
NDT = DSH // 128
NST = 16
R = 256
E3 = 288
B = 4
L = 2048
TSL = L // NCK
NKT = H // 128
EPS_LN = 1e-5
EPS_RMS = 1e-6
SCALE = 2.0
TQ = 256
TQS = TQ + 1
NTQ = L // TQ


class Buf:
    __slots__ = ("name", "w", "r")

    def __init__(self, name):
        self.name = name
        self.w = None
        self.r = []


class KB:
    CSTREAMS = ("T", "V", "S", "G", "Q")

    def __init__(self):
        self.ops = []

    def op(self, eng, stream, fn, reads=(), writes=(), extra=()):
        idx = len(self.ops)
        waits = set(e for e in extra if e is not None)
        for b in reads:
            if b.w is not None:
                waits.add(b.w)
        for b in writes:
            if b.w is not None:
                waits.add(b.w)
            waits.update(b.r)
        for b in writes:
            b.w = idx
            b.r = []
        for b in reads:
            b.r.append(idx)
        amt = 16 if stream in ("QS", "QA", "QG") else 1
        self.ops.append([eng, stream, fn, waits, amt])
        return idx

    POOLS = {"QS": 24, "QG": 16, "QA": 8}
    STRICT = False

    def finalize(self):
        used = set()
        for _, _, _, waits, _ in self.ops:
            used.update(waits)
        val = {}
        cnt = defaultdict(int)
        pool_ctr = defaultdict(int)
        self.pre_wait = {}
        self.inc_flag = []
        for idx, (eng, stream, fn, waits, amt) in enumerate(self.ops):
            if stream in self.POOLS:
                n = self.POOLS[stream]
                c = pool_ctr[stream]
                pool_ctr[stream] += 1
                sem_name = f"{stream}{c % n}"
                u = c // n
                self.pre_wait[idx] = (sem_name, 16 * u)
                val[idx] = (sem_name, 16 * (u + 1))
                self.inc_flag.append(True)
                cnt[sem_name] = 16 * (u + 1)
            else:
                keep = self.STRICT or (stream not in self.CSTREAMS) or (idx in used)
                self.inc_flag.append(keep)
                if keep:
                    cnt[stream] += amt
                    val[idx] = (stream, cnt[stream])
        self.ev_val = val
        self.final_counts = dict(cnt)

    def emit(self, eng_key, handle, sems):
        waited = defaultdict(int)
        for idx, (eng, stream, fn, waits, amt) in enumerate(self.ops):
            if eng != eng_key:
                continue
            pw = self.pre_wait.get(idx)
            if pw is not None and pw[1] > 0 and waited[pw[0]] < pw[1]:
                handle.wait_ge(sems[pw[0]], pw[1])
                waited[pw[0]] = pw[1]
            for wev in sorted(waits):
                wsv = self.ev_val.get(wev)
                if wsv is None:
                    continue
                ws, wv = wsv
                if (not self.STRICT) and ws == eng and ws in self.CSTREAMS:
                    continue
                if waited[ws] >= wv:
                    continue
                handle.wait_ge(sems[ws], wv)
                waited[ws] = wv
            ins = fn(handle)
            if self.inc_flag[idx]:
                s, v = self.ev_val[idx]
                ins.then_inc(sems[s], amt)


def build_program():
    nc = bass.Bass("TRN2", target_bir_lowering=False, debug=False)

    def ein(name, shp, dt=F32):
        return nc.dram_tensor(name, shp, dt, kind="ExternalInput").ap()

    hs_prepT = ein("hs_prepT", [H, 1024])
    hs_resT = ein("hs_resT", [B, H, TSL])
    w1 = ein("w1", [16, 128, 4096])
    woTs = ein("woTs", [8, 128, 4096])
    wfT = ein("wfT", [32, 128, 4096])
    convw = ein("convw", [DSH, 4])
    convb = ein("convb", [DSH])
    dtbias = ein("dtbias", [DSH])
    dparam = ein("dparam", [DSH])
    alog = ein("alog", [DSH, NST])
    xpwT = ein("xpwT", [DSH, E3])
    dtpT = ein("dtpT", [R, DSH])
    dtlnw = ein("dtlnw", [R])
    blnw = ein("blnw", [NST])
    clnw = ein("clnw", [NST])
    fbi = ein("fb", [H])
    fAWrT = ein("fAWrT", [H, 36])
    fBmT = ein("fBmT", [32, H])
    out_ext = nc.dram_tensor("out", [B, H, TSL], F32, kind="ExternalOutput").ap()

    dr = lambda n, shp, dt: nc.dram_tensor(n, shp, dt).ap()
    ln_in = dr("ln_in", [H, 1024], BF16)
    ln_all = dr("ln_all", [NCK, H, 1024], BF16)
    w1t = dr("w1t", [16, 128, 4096], BF16)
    wos = dr("wos", [8, 128, 4096], BF16)
    wf2 = dr("wf2", [NKT, 128, 4096], BF16)
    arin = [dr(f"arin{b}", [E3, L], F32) for b in range(B)]
    arout = [dr(f"arout{b}", [E3, L], F32) for b in range(B)]
    yin = [dr(f"yin{b}", [DSH, L], BF16) for b in range(B)]
    pmix = [dr(f"pmix{b}", [NCK, H, TSL], BF16) for b in range(B)]
    rsout = [dr(f"rsout{b}", [H, TSL], BF16) for b in range(B)]
    xcsp = dr("xcsp", [B, NDT, 128, L], BF16)
    gsp = dr("gsp", [B, NDT, 128, L], BF16)
    dtlsp = dr("dtlsp", [B, NDT, 128, L], BF16)
    x1sp = dr("x1sp", [B, H, TSL], F32)

    kb = KB()
    st = contextlib.ExitStack()
    T = {}

    def sb(name, shp, dt):
        t = st.enter_context(nc.sbuf_tensor(name, shp, dt))
        T[name] = t
        return t

    def psm(name, shp):
        t = st.enter_context(nc.psum_tensor(name, shp, F32))
        T[name] = t
        return t

    with st:
        SLOT = [sb(f"sl{i}", [128, 8192], BF16) for i in range(5)]
        dAp = sb("dAp", [128, NST * TQS], BF16)
        wl1 = sb("wl1", [128, NST * TQS], BF16)
        hl2 = sb("hl2", [128, NST * TQS], BF16)
        dApB = sb("dApB", [128, NST * TQS], BF16)
        wl1B = sb("wl1B", [128, NST * TQS], BF16)
        hl2B = hl2
        bcB = sb("bcB", [128, NST * TQ], BF16)
        bcC = sb("bcC", [128, NST * TQ], BF16)
        dtl_t = sb("dtl_t", [128, 2 * TQ], BF16)
        xc_t = sb("xc_t", [128, 2 * TQ], BF16)
        g_t = sb("g_t", [128, 2 * TQ], BF16)
        u_t = sb("u_t", [128, TQ], BF16)
        ys_t = sb("ys_t", [128, TQ], F32)
        t2_t = sb("t2_t", [128, TQ], BF16)
        ygd_t = sb("ygd_t", [128, 2 * TQ], BF16)
        hcar = sb("hcar", [128, NDT * NST], BF16)
        xconv = sb("xconv", [128, 2 * (TQ + 3)], BF16)
        tails = sb("tails", [128, NDT * 3], BF16)
        cacc = sb("cacc", [128, 2 * TQ], F32)
        xcw = sb("xcw", [128, 2 * TQ], BF16)
        gev = sb("gev", [128, 2 * 512], BF16)
        xcr = sb("xcr", [128, 512], BF16)
        spsc = sb("spsc", [96, 512], F32)
        rowpack = sb("rowpack", [65, 2 * L + 128 + 1024 + 128], BF16)
        bcrows = rowpack[0:16, :]
        spd = sb("spd", [128, 1024], BF16)
        c_sum = sb("c_sum", [1, 512], F32)
        c_ssq = sb("c_ssq", [1, 512], F32)
        c_m = sb("c_m", [1, 512], F32)
        s_scr = sb("s_scr", [1, 512], F32)
        rb_rstd = rowpack[0:1, 4224:5248]
        rb_mrstd = rowpack[32:33, 4224:5248]
        ones32 = rowpack[32:33, 5248:5376]
        rb_rb = rowpack[64:65, 0:L]
        rb_rc = rowpack[64:65, L:2 * L]
        ones64 = rowpack[64:65, 2 * L:2 * L + 128]
        rb_tmp = sb("rb_tmp", [1, TSL], BF16)
        gl4 = sb("gl4", [4, 2 * TSL], F32)
        gb4 = sb("gb4", [4, 2 * TSL], BF16)
        bct4 = sb("bct4", [4, TSL], F32)
        g32 = sb("g32", [32, TSL], BF16)
        prs = sb("prs", [32, TSL], BF16)
        scr36 = sb("scr36", [36, TSL], BF16)
        negA = sb("negA", [128, NDT * NST], F32)
        cw_sb = sb("cw_sb", [128, NDT * 4], F32)
        cb_sb = sb("cb_sb", [128, NDT], F32)
        ndtb = sb("ndtb", [128, NDT], F32)
        d_sb = sb("d_sb", [128, NDT], F32)
        dtlnw_sb = sb("dtlnw_sb", [128, 2], F32)
        bclnw = sb("bclnw", [16, 2], F32)
        dtp_sb = sb("dtp_sb", [128, 16 * 128], BF16)
        xpw_sb = sb("xpw_sb", [128, NDT * E3], BF16)
        awr_sb = sb("awr_sb", [128, NKT * 36], BF16)
        bmt_t = sb("bmt_t", [32, 2 * 128], BF16)
        aug2r = sb("aug2r", [2, TSL], BF16)
        colsA = sb("colsA", [1, 36], BF16)
        ones_c = sb("ones_c", [128, 1], BF16)
        ones_r = sb("ones_r", [1, 128], BF16)
        ones16 = sb("ones16", [16, 128], BF16)

        srow = sb("srow", [1, TQ], BF16)
        sel = sb("sel", [4, 32], BF16)
        sel2 = sb("sel2", [36, 4], BF16)
        x0t = sb("x0t", [128, TSL], F32)
        x1t = sb("x1t", [128, 2 * TSL], F32)
        bct = sb("bct", [128, 3 * TSL], F32)
        aug2 = sb("aug2", [2, H], BF16)
        dmask = sb("dmask", [16, NST], BF16)

        psA = psm("psA", [128, 512])
        psB = psm("psB", [128, 512])
        psE = psm("psE", [96, 512])
        psF = psm("psF", [128, 512])
        psG = psm("psG", [128, 512])
        psH = psm("psH", [128, 1024])
        psI = psm("psI", [128, 512])

        bufs = {}

        def Bf(n):
            if n not in bufs:
                bufs[n] = Buf(n)
            return bufs[n]

        for n in list(T):
            Bf(n)
        for n in ("ln_in", "ln_all", "w1t", "wos", "wf2", "xcsp", "gsp",
                  "dtlsp", "x1sp", "out", "sl4a", "sl4b",
                  "dtl0", "dtl1", "xct0", "xct1", "gt0", "gt1", "ygd0", "ygd1",
                  "x0t0", "x0t1", "x1t0", "x1t1", "ca0", "ca1", "gev0", "gev1",
                  "xcr0", "xcr1"):
            Bf(n)
        for b in range(B):
            for n in ("arin", "arout", "yin", "pmix", "rsout"):
                Bf(f"{n}{b}")
        SB = [Bf(f"sl{i}") for i in range(5)]
        SB4 = [Bf("sl4a"), Bf("sl4b")]
        DTLB = [Bf("dtl0"), Bf("dtl1")]
        XCTB = [Bf("xct0"), Bf("xct1")]
        GTB = [Bf("gt0"), Bf("gt1")]
        YGDB = [Bf("ygd0"), Bf("ygd1")]
        X0B = [Bf("x0t0"), Bf("x0t1")]
        X1B = [Bf("x1t0"), Bf("x1t1")]
        CAB = [Bf("ca0"), Bf("ca1")]
        GEVB = [Bf("gev0"), Bf("gev1")]
        XCRB = [Bf("xcr0"), Bf("xcr1")]
        spcB = wl1[0:16, 0:L]
        spcC = wl1[0:16, L:2 * L]

        kb.op("V", "V", lambda v: v.memset(ones_c[:, :], 1.0), writes=[Bf("ones_c")])
        kb.op("V", "V", lambda v: v.memset(ones_r[:, :], 1.0), writes=[Bf("ones_r")])
        kb.op("V", "V", lambda v: v.memset(ones64, 1.0), writes=[Bf("ones64")])
        kb.op("V", "V", lambda v: v.memset(ones32, 1.0), writes=[Bf("ones32")])
        kb.op("V", "V", lambda v: v.memset(ones16[:, :], 1.0), writes=[Bf("ones16")])
        kb.op("V", "V", lambda v: v.memset(tails[:, :], 0.0), writes=[Bf("tails")])
        kb.op("V", "V", lambda v: v.memset(sel[:, :], 0.0), writes=[Bf("sel")])
        kb.op("V", "V", lambda v: v.memset(dmask[:, :], 0.0), writes=[Bf("dmask")])
        kb.op("V", "V", lambda v: v.memset(sel2[:, :], 0.0), writes=[Bf("sel2")])
        kb.op("V", "V", lambda v: v.memset(srow[:, :], 1.0), writes=[Bf("srow")])
        for e in range(4):
            kb.op("G", "QG", lambda g, e=e: g.dma_start(
                out=sel[e:e + 1, e * 8:(e + 1) * 8].rearrange("o f -> o f") if False
                else sel[e:e + 1, e * 8:(e + 1) * 8],
                in_=srow[0:1, 0:8]), reads=[Bf("srow")], writes=[Bf("sel")])
            kb.op("G", "QG", lambda g, e=e: g.dma_start(
                out=sel2[32 + e:33 + e, e:e + 1], in_=srow[0:1, 0:1]),
                reads=[Bf("srow")], writes=[Bf("sel2")])
        for n in range(NST):
            kb.op("G", "QG", lambda g, n=n: g.dma_start(
                out=dmask[n:n + 1, n:n + 1], in_=srow[0:1, 0:1]),
                reads=[Bf("srow")], writes=[Bf("dmask")])
        kb.op("V", "V", lambda v: v.tensor_scalar_mul(sel[:, :], sel[:, :], SCALE),
              reads=[Bf("sel")], writes=[Bf("sel")])

        for dt in range(NDT):
            kb.op("G", "QG", lambda g, dt=dt: g.dma_start(
                out=negA[:, dt * NST:(dt + 1) * NST],
                in_=alog[dt * 128:(dt + 1) * 128, :]), writes=[Bf("negA")])
            kb.op("G", "QG", lambda g, dt=dt: g.dma_start(
                out=cw_sb[:, dt * 4:(dt + 1) * 4],
                in_=convw[dt * 128:(dt + 1) * 128, :]), writes=[Bf("cw_sb")])
            kb.op("G", "QG", lambda g, dt=dt: g.dma_start(
                out=cb_sb[:, dt:dt + 1],
                in_=convb[dt * 128:(dt + 1) * 128].unsqueeze(1)), writes=[Bf("cb_sb")])
            kb.op("G", "QG", lambda g, dt=dt: g.dma_start(
                out=ndtb[:, dt:dt + 1],
                in_=dtbias[dt * 128:(dt + 1) * 128].unsqueeze(1)), writes=[Bf("ndtb")])
            kb.op("G", "QG", lambda g, dt=dt: g.dma_start(
                out=d_sb[:, dt:dt + 1],
                in_=dparam[dt * 128:(dt + 1) * 128].unsqueeze(1)), writes=[Bf("d_sb")])
            for ec in range(3):
                kb.op("G", "QG", lambda g, dt=dt, ec=ec: g.dma_start(
                    out=xpw_sb[:, dt * E3 + ec * 96:dt * E3 + (ec + 1) * 96],
                    in_=xpwT[dt * 128:(dt + 1) * 128, ec * 96:(ec + 1) * 96]),
                    writes=[Bf("xpw_sb")])
        kb.op("S", "S", lambda s: s.activation(negA[:, :], negA[:, :], AF.Exp),
              reads=[Bf("negA")], writes=[Bf("negA")])
        kb.op("V", "V", lambda v: v.tensor_scalar_mul(ndtb[:, :], ndtb[:, :], -1.0),
              reads=[Bf("ndtb")], writes=[Bf("ndtb")])
        for c in range(2):
            kb.op("G", "QG", lambda g, c=c: g.dma_start(
                out=dtlnw_sb[:, c:c + 1],
                in_=dtlnw[c * 128:(c + 1) * 128].unsqueeze(1)),
                writes=[Bf("dtlnw_sb")])
        kb.op("G", "QG", lambda g: g.dma_start(out=bclnw[:, 0:1],
                                               in_=blnw.unsqueeze(1)),
              writes=[Bf("bclnw")])
        kb.op("G", "QG", lambda g: g.dma_start(out=bclnw[:, 1:2],
                                               in_=clnw.unsqueeze(1)),
              writes=[Bf("bclnw")])
        for rt in range(2):
            for dt in range(NDT):
                j = rt * NDT + dt
                kb.op("G", "QG", lambda g, rt=rt, dt=dt, j=j: g.dma_start(
                    out=dtp_sb[:, j * 128:(j + 1) * 128],
                    in_=dtpT[rt * 128:(rt + 1) * 128, dt * 128:(dt + 1) * 128]),
                    writes=[Bf("dtp_sb")])
        for it in range(NKT):
            kb.op("G", "QG", lambda g, it=it: g.dma_start(
                out=awr_sb[:, it * 36:(it + 1) * 36],
                in_=fAWrT[it * 128:(it + 1) * 128, :]), writes=[Bf("awr_sb")])
        kb.op("G", "QG", lambda g: g.dma_start(out=aug2[1:2, :], in_=fbi.unsqueeze(0)),
              writes=[Bf("aug2")])

        def ln_rowmath(width, off=0):
            kb.op("V", "V", lambda v: v.tensor_scalar_mul(
                c_m[0:1, 0:width], c_sum[0:1, 0:width], 1.0 / H),
                reads=[Bf("c_sum")], writes=[Bf("c_m")])
            kb.op("S", "S", lambda s: s.activation(
                c_sum[0:1, 0:width], c_m[0:1, 0:width], AF.Square),
                reads=[Bf("c_m")], writes=[Bf("c_sum")])
            kb.op("V", "V", lambda v: v.scalar_tensor_tensor(
                c_ssq[0:1, 0:width], c_ssq[0:1, 0:width], 1.0 / H,
                c_sum[0:1, 0:width], ALU.mult, ALU.subtract),
                reads=[Bf("c_sum")], writes=[Bf("c_ssq")])
            kb.op("V", "V", lambda v: v.tensor_scalar_add(
                c_ssq[0:1, 0:width], c_ssq[0:1, 0:width], EPS_LN),
                reads=[Bf("c_ssq")], writes=[Bf("c_ssq")])
            kb.op("S", "S", lambda s: s.activation(
                c_ssq[0:1, 0:width], c_ssq[0:1, 0:width], AF.Sqrt),
                reads=[Bf("c_ssq")], writes=[Bf("c_ssq")])
            kb.op("V", "V", lambda v: v.reciprocal(
                c_sum[0:1, 0:width], c_ssq[0:1, 0:width]),
                reads=[Bf("c_ssq")], writes=[Bf("c_sum")])
            kb.op("V", "V", lambda v: v.tensor_copy(
                rb_rstd[0:1, off:off + width], c_sum[0:1, 0:width]),
                reads=[Bf("c_sum")], writes=[Bf("rb_rstd")])
            kb.op("V", "V", lambda v: v.tensor_mul(
                c_m[0:1, 0:width], c_m[0:1, 0:width], c_sum[0:1, 0:width]),
                reads=[Bf("c_sum")], writes=[Bf("c_m")])
            kb.op("V", "V", lambda v: v.tensor_copy(
                rb_mrstd[0:1, off:off + width], c_m[0:1, 0:width]),
                reads=[Bf("c_m")], writes=[Bf("rb_mrstd")])

        for it in range(NKT):
            s, o = SLOT[it // 8], (it % 8) * 1024
            kb.op("G", "QG", lambda g, it=it, s=s, o=o: g.dma_start(
                out=s[:, o:o + 1024], in_=hs_prepT[it * 128:(it + 1) * 128, :]),
                writes=[SB[it // 8]])
        for it in range(NKT):
            s, o = SLOT[it // 8], (it % 8) * 1024
            for ch in range(2):
                kb.op("T", "T", lambda t, s=s, o=o, it=it, ch=ch: t.matmul(
                    psH[0:1, ch * 512:(ch + 1) * 512], ones_c[:, :],
                    s[:, o + ch * 512:o + (ch + 1) * 512],
                    start=(it == 0), stop=(it == NKT - 1)),
                    reads=[SB[it // 8], Bf("ones_c")], writes=[Bf("psH")])
        kb.op("S", "S", lambda s: s.copy(wl1.bitcast(F32)[0:1, 0:1024],
                                         psH[0:1, :]),
              reads=[Bf("psH")], writes=[Bf("wl1")])
        for it in range(NKT):
            s, o = SLOT[it // 8], (it % 8) * 1024
            q, qo = SLOT[4], (it % 4) * 1024
            kb.op("S", "S", lambda sc, s=s, o=o, q=q, qo=qo: sc.activation(
                q[:, qo:qo + 1024], s[:, o:o + 1024], AF.Square),
                reads=[SB[it // 8]], writes=[SB[4]])
            for ch in range(2):
                kb.op("T", "T", lambda t, q=q, qo=qo, it=it, ch=ch: t.matmul(
                    psH[0:1, ch * 512:(ch + 1) * 512], ones_c[:, :],
                    q[:, qo + ch * 512:qo + (ch + 1) * 512],
                    start=(it == 0), stop=(it == NKT - 1)),
                    reads=[SB[4], Bf("ones_c")], writes=[Bf("psH")])
        for ch in range(2):
            kb.op("S", "S", lambda s, ch=ch: s.copy(
                c_sum[:, :], wl1.bitcast(F32)[0:1, ch * 512:(ch + 1) * 512]),
                reads=[Bf("wl1")], writes=[Bf("c_sum")])
            kb.op("S", "S", lambda s, ch=ch: s.copy(
                c_ssq[:, :], psH[0:1, ch * 512:(ch + 1) * 512]),
                reads=[Bf("psH")], writes=[Bf("c_ssq")])
            ln_rowmath(512, off=ch * 512)
        for ch in range(2):
            kb.op("T", "T", lambda t, ch=ch: t.matmul(
                psG[:, :], ones_r[:, :], rb_rstd[0:1, ch * 512:(ch + 1) * 512],
                start=True, stop=True),
                reads=[Bf("rb_rstd"), Bf("ones_r")], writes=[Bf("psG")])
            kb.op("T", "T", lambda t, ch=ch: t.matmul(
                psF[:, :], ones32, rb_mrstd[0:1, ch * 512:(ch + 1) * 512],
                start=True, stop=True),
                reads=[Bf("rb_mrstd"), Bf("ones32")], writes=[Bf("psF")])
            for it in range(NKT):
                s, o = SLOT[it // 8], (it % 8) * 1024 + ch * 512
                q, qo = SLOT[4], 4096 + (it % 4) * 1024 + ch * 512
                kb.op("V", "V", lambda v, s=s, o=o, q=q, qo=qo: v.tensor_tensor(
                    q[:, qo:qo + 512], s[:, o:o + 512], psG[:, :], ALU.mult),
                    reads=[SB[it // 8], Bf("psG")], writes=[SB[4]])
                kb.op("V", "V", lambda v, q=q, qo=qo: v.tensor_tensor(
                    q[:, qo:qo + 512], q[:, qo:qo + 512], psF[:, :], ALU.subtract),
                    reads=[Bf("psF")], writes=[SB[4]])
                kb.op("Q", "QS", lambda qq, it=it, q=q, qo=qo, ch=ch: qq.dma_start(
                    out=ln_in[it * 128:(it + 1) * 128, ch * 512:(ch + 1) * 512],
                    in_=q[:, qo:qo + 512]),
                    reads=[SB[4]], writes=[Bf("ln_in")])
        fl = kb.op("Q", "Q", lambda q: q.nop(), reads=[Bf("ln_in")])
        kb.op("G", "CCL", lambda g: g.collective_compute(
            "AllGather", ALU.bypass, replica_groups=[list(range(NCK))],
            ins=[ln_in], outs=[ln_all]),
            reads=[Bf("ln_in")], writes=[Bf("ln_all")], extra=[fl])

        for kt in range(16):
            s = SLOT[kt % 2]
            kb.op("G", "QG", lambda g, kt=kt, s=s: g.dma_start(
                out=s[:, 0:4096], in_=w1[kt]), writes=[SB[kt % 2]])
            kb.op("Q", "QS", lambda q, kt=kt, s=s: q.dma_start(
                out=w1t[kt], in_=s[:, 0:4096]),
                reads=[SB[kt % 2]], writes=[Bf("w1t")])
        for dt in range(NDT):
            s = SLOT[dt % 2]
            kb.op("G", "QG", lambda g, dt=dt, s=s: g.dma_start(
                out=s[:, 0:H], in_=woTs[dt]), writes=[SB[dt % 2]])
            kb.op("Q", "QS", lambda q, dt=dt, s=s: q.dma_start(
                out=wos[dt], in_=s[:, 0:H]),
                reads=[SB[dt % 2]], writes=[Bf("wos")])
        kb.op("V", "V", lambda v: v.memset(aug2[0:1, :], 0.0), writes=[Bf("aug2")])
        for it in range(NKT):
            s = SLOT[2 + it % 2]
            kb.op("G", "QG", lambda g, it=it, s=s: g.dma_start(
                out=s[:, 0:H], in_=wfT[it]), writes=[SB[2 + it % 2]])
            kb.op("Q", "QS", lambda q, it=it, s=s: q.dma_start(
                out=wf2[it], in_=s[:, 0:H]),
                reads=[SB[2 + it % 2]], writes=[Bf("wf2")])
            for ch in range(NKT):
                kb.op("T", "T", lambda t, s=s, ch=ch, it=it: t.matmul(
                    psH[0:1, 0:128], ones_c[:, :], s[:, ch * 128:(ch + 1) * 128],
                    start=(ch == 0), stop=(ch == NKT - 1)),
                    reads=[SB[2 + it % 2], Bf("ones_c")], writes=[Bf("psH")])
            kb.op("V", "V", lambda v, it=it: v.tensor_copy(
                aug2[0:1, it * 128:(it + 1) * 128], psH[0:1, 0:128]),
                reads=[Bf("psH")], writes=[Bf("aug2")])
        for it in range(NKT):
            kb.op("T", "T", lambda t, it=it: t.matmul(
                psH[0:1, 0:36], ones_c[:, :], awr_sb[:, it * 36:(it + 1) * 36],
                start=(it == 0), stop=(it == NKT - 1)),
                reads=[Bf("awr_sb"), Bf("ones_c")], writes=[Bf("psH")])
        kb.op("V", "V", lambda v: v.tensor_copy(colsA[:, :], psH[0:1, 0:36]),
              reads=[Bf("psH")], writes=[Bf("colsA")])

        def p1_chunk(b, part):
            th, cth = divmod(part, 2)
            if cth == 0:
                for kt in range(NKT):
                    s, o = SLOT[kt // 8], (kt % 8) * 1024
                    chunk = 2 * b + th
                    kb.op("Q", "QS", lambda q, kt=kt, s=s, o=o, chunk=chunk:
                          q.dma_start(
                        out=s[:, o:o + 1024],
                        in_=ln_all[chunk, kt * 128:(kt + 1) * 128, :]),
                        reads=[Bf("ln_all")], writes=[SB[kt // 8]])
            for ci in range(8):
                ct = cth * 8 + ci
                pw = ct % 2
                ws, wof = SLOT[4], pw * 4096
                wb = SB4[pw]
                kb.op("S", "QA", lambda q, ct=ct, ws=ws, wof=wof: q.dma_start(
                    out=ws[:, wof:wof + 4096], in_=w1t[ct]),
                    reads=[Bf("w1t")], writes=[wb])
                for half in range(2):
                    pst, pB = ((psA, Bf("psA")) if half == 0 else (psB, Bf("psB")))
                    for kt in range(NKT):
                        s, o = SLOT[kt // 8], (kt % 8) * 1024
                        kb.op("T", "T", lambda t, s=s, o=o, ws=ws, wof=wof, kt=kt,
                              half=half, pst=pst: t.matmul(
                            pst[:, :], ws[:, wof + kt * 128:wof + (kt + 1) * 128],
                            s[:, o + half * 512:o + (half + 1) * 512],
                            start=(kt == 0), stop=(kt == NKT - 1)),
                            reads=[SB[kt // 8], wb], writes=[pB])
                    if ct < 8:
                        dtc = ct
                        for tq in range(2):
                            gtq = th * 4 + half * 2 + tq
                            p = gtq % 2
                            xo = p * (TQ + 3)
                            cB = Bf("xconv")
                            kb.op("S", "S", lambda s, xo=xo, pst=pst, tq=tq: s.copy(
                                xconv[:, xo + 3:xo + 3 + TQ],
                                pst[:, tq * TQ:(tq + 1) * TQ]),
                                reads=[pB], writes=[cB])
                            if gtq == 0:
                                kb.op("V", "V", lambda v, xo=xo: v.memset(
                                    xconv[:, xo:xo + 3], 0.0), writes=[cB])
                            else:
                                kb.op("V", "V", lambda v, xo=xo, dtc=dtc:
                                      v.tensor_copy(
                                    xconv[:, xo:xo + 3],
                                    tails[:, dtc * 3:dtc * 3 + 3]),
                                    reads=[Bf("tails")], writes=[cB])
                            kb.op("V", "V", lambda v, xo=xo, dtc=dtc: v.tensor_copy(
                                tails[:, dtc * 3:dtc * 3 + 3],
                                xconv[:, xo + TQ:xo + TQ + 3]),
                                reads=[cB], writes=[Bf("tails")])
                            kb.op("V", "V", lambda v, xo=xo, dtc=dtc: v.tensor_scalar(
                                cacc[:, 0:TQ], xconv[:, xo:xo + TQ],
                                cw_sb[:, dtc * 4:dtc * 4 + 1],
                                cb_sb[:, dtc:dtc + 1], ALU.mult, ALU.add),
                                reads=[cB, Bf("cw_sb"), Bf("cb_sb")],
                                writes=[CAB[0]])
                            for k in (1, 2, 3):
                                src, dst = ((k + 1) % 2) * TQ, (k % 2) * TQ
                                kb.op("V", "V", lambda v, xo=xo, dtc=dtc, k=k,
                                      src=src, dst=dst: v.scalar_tensor_tensor(
                                    cacc[:, dst:dst + TQ],
                                    xconv[:, xo + k:xo + k + TQ],
                                    cw_sb[:, dtc * 4 + k:dtc * 4 + k + 1],
                                    cacc[:, src:src + TQ], ALU.mult, ALU.add),
                                    reads=[cB, Bf("cw_sb"), CAB[(k + 1) % 2]],
                                    writes=[CAB[k % 2]])
                            kb.op("S", "S", lambda s, tq=tq: s.activation(
                                xcw[:, tq * TQ:(tq + 1) * TQ],
                                cacc[:, TQ:2 * TQ], AF.Silu),
                                reads=[CAB[1]], writes=[Bf("xcw")])
                            if tq == 1:
                                kb.op("Q", "QS", lambda q, dtc=dtc, b=b, gtq=gtq:
                                      q.dma_start(
                                    out=xcsp[b, dtc, :,
                                             (gtq - 1) * TQ:(gtq + 1) * TQ],
                                    in_=xcw[:, :]),
                                    reads=[Bf("xcw")], writes=[Bf("xcsp")])
                    else:
                        dtc = ct - 8
                        gp = half
                        go = gp * 512
                        kb.op("S", "S", lambda s, go=go, pst=pst: s.activation(
                            gev[:, go:go + 512], pst[:, :], AF.Silu),
                            reads=[pB], writes=[GEVB[gp]])
                        kb.op("Q", "QS", lambda q, go=go, gp=gp, dtc=dtc, b=b,
                              th=th, half=half: q.dma_start(
                            out=gsp[b, dtc, :, th * 1024 + half * 512:
                                    th * 1024 + (half + 1) * 512],
                            in_=gev[:, go:go + 512]),
                            reads=[GEVB[gp]], writes=[Bf("gsp")])
            if cth == 1:
                for ec in range(3):
                    for hh in range(2):
                        for dt in range(NDT):
                            kb.op("Q", "QS", lambda q, dt=dt, b=b,
                                  th=th, hh=hh: q.dma_start(
                                out=xcr[:, :],
                                in_=xcsp[b, dt, :, th * 1024 + hh * 512:
                                         th * 1024 + (hh + 1) * 512]),
                                reads=[Bf("xcsp")], writes=[XCRB[0]])
                            kb.op("T", "T", lambda t, ec=ec, dt=dt: t.matmul(
                                psE[:, :],
                                xpw_sb[:, dt * E3 + ec * 96:dt * E3 + (ec + 1) * 96],
                                xcr[:, :],
                                start=(dt == 0), stop=(dt == NDT - 1)),
                                reads=[Bf("xpw_sb"), XCRB[0]], writes=[Bf("psE")])
                        kb.op("S", "S", lambda s: s.copy(spsc[:, :], psE[:, :]),
                              reads=[Bf("psE")], writes=[Bf("spsc")])
                        kb.op("Q", "QS", lambda q, ec=ec, hh=hh, b=b, th=th:
                              q.dma_start(
                            out=arin[b][ec * 96:(ec + 1) * 96,
                                        th * 1024 + hh * 512:
                                        th * 1024 + (hh + 1) * 512],
                            in_=spsc[:, :]),
                            reads=[Bf("spsc")], writes=[Bf(f"arin{b}")])

        def p1_ar(b):
            fl = kb.op("Q", "Q", lambda q: q.nop(), reads=[Bf(f"arin{b}")])
            kb.op("G", f"CCA{b}", lambda g, b=b: g.collective_compute(
                "AllReduce", ALU.add, replica_groups=[list(range(NCK))],
                ins=[arin[b]], outs=[arout[b]]),
                reads=[Bf(f"arin{b}")], writes=[Bf(f"arout{b}")], extra=[fl])

        def rows_dt(b):
            kb.op("V", "V", lambda v: v.memset(hcar[:, :], 0.0),
                  writes=[Bf(f"hcar{d}") for d in range(NDT)])
            kb.op("G", "QG", lambda g, b=b: g.dma_start(
                out=spcB, in_=arout[b][256:272, :]),
                reads=[Bf(f"arout{b}")], writes=[Bf("wl1")])
            kb.op("G", "QG", lambda g, b=b: g.dma_start(
                out=spcC, in_=arout[b][272:288, :]),
                reads=[Bf(f"arout{b}")], writes=[Bf("wl1")])
            kb.op("S", "S", lambda s: s.activation(
                dAp[0:16, 0:L], spcB, AF.Square),
                reads=[Bf("wl1")], writes=[Bf("dAp")])
            kb.op("S", "S", lambda s: s.activation(
                dAp[0:16, L:2 * L], spcC, AF.Square),
                reads=[Bf("wl1")], writes=[Bf("dAp")])
            for (co, rbB_) in ((0, Bf("rb_rb")), (L, Bf("rb_rc"))):
                rb = rb_rb if co == 0 else rb_rc
                for ch in range(4):
                    kb.op("T", "T", lambda t, co=co, ch=ch: t.matmul(
                        psH[0:1, 0:512], ones_c[0:16, :],
                        dAp[0:16, co + ch * 512:co + (ch + 1) * 512],
                        start=True, stop=True),
                        reads=[Bf("dAp"), Bf("ones_c")], writes=[Bf("psH")])
                    kb.op("V", "V", lambda v: v.tensor_scalar(
                        s_scr[:, :], psH[0:1, 0:512], 1.0 / NST, EPS_RMS,
                        ALU.mult, ALU.add),
                        reads=[Bf("psH")], writes=[Bf("s_scr")])
                    kb.op("S", "S", lambda s: s.activation(
                        s_scr[:, :], s_scr[:, :], AF.Sqrt),
                        reads=[Bf("s_scr")], writes=[Bf("s_scr")])
                    kb.op("V", "V", lambda v: v.reciprocal(s_scr[:, :], s_scr[:, :]),
                          reads=[Bf("s_scr")], writes=[Bf("s_scr")])
                    kb.op("V", "V", lambda v, rb=rb, ch=ch: v.tensor_copy(
                        rb[0:1, ch * 512:(ch + 1) * 512], s_scr[:, :]),
                        reads=[Bf("s_scr")], writes=[rbB_])
            kb.op("V", "V", lambda v: v.tensor_scalar(
                bcrows[:, 0:L], spcB, bclnw[:, 0:1], -1.0, ALU.mult, ALU.mult),
                reads=[Bf("wl1"), Bf("bclnw")], writes=[Bf("bcrows")])
            kb.op("V", "V", lambda v: v.tensor_scalar(
                bcrows[:, L:2 * L], spcC, bclnw[:, 1:2], None, ALU.mult),
                reads=[Bf("wl1"), Bf("bclnw")], writes=[Bf("bcrows")])
            for (co, rb, rbB_) in ((0, rb_rb, Bf("rb_rb")),
                                   (L, rb_rc, Bf("rb_rc"))):
                for ch in range(4):
                    kb.op("T", "T", lambda t, co=co, ch=ch, rb=rb: t.matmul(
                        psH[0:16, 0:512], ones64[0:1, 0:16],
                        rb[0:1, ch * 512:(ch + 1) * 512],
                        start=True, stop=True),
                        reads=[rbB_, Bf("ones64")], writes=[Bf("psH")])
                    kb.op("S", "S", lambda s, ch=ch: s.copy(
                        hl2[0:16, ch * 512:(ch + 1) * 512],
                        psH[0:16, 0:512]),
                        reads=[Bf("psH")], writes=[Bf("hl2")])
                kb.op("V", "V", lambda v, co=co: v.tensor_tensor(
                    bcrows[:, co:co + L], bcrows[:, co:co + L],
                    hl2[0:16, 0:L], ALU.mult),
                    reads=[Bf("hl2")], writes=[Bf("bcrows")])
            for kt in range(2):
                for hh in range(2):
                    kb.op("G", "QG", lambda g, kt=kt, hh=hh, b=b: g.dma_start(
                        out=spd[:, :],
                        in_=arout[b][kt * 128:(kt + 1) * 128,
                                     hh * 1024:(hh + 1) * 1024]),
                        reads=[Bf(f"arout{b}")], writes=[Bf("spd")])
                    kb.op("S", "S", lambda s, kt=kt, hh=hh: s.activation(
                        wl1[:, kt * L + hh * 1024:kt * L + (hh + 1) * 1024],
                        spd[:, :], AF.Square),
                        reads=[Bf("spd")], writes=[Bf("wl1")])
            for ch in range(4):
                for kt in range(2):
                    kb.op("T", "T", lambda t, ch=ch, kt=kt: t.matmul(
                        psH[0:1, 0:512], ones_c[:, :],
                        wl1[:, kt * L + ch * 512:kt * L + (ch + 1) * 512],
                        start=(kt == 0), stop=(kt == 1)),
                        reads=[Bf("wl1"), Bf("ones_c")], writes=[Bf("psH")])
                kb.op("V", "V", lambda v: v.tensor_scalar(
                    s_scr[:, :], psH[0:1, 0:512], 1.0 / R, EPS_RMS,
                    ALU.mult, ALU.add),
                    reads=[Bf("psH")], writes=[Bf("s_scr")])
                kb.op("S", "S", lambda s: s.activation(
                    s_scr[:, :], s_scr[:, :], AF.Sqrt),
                    reads=[Bf("s_scr")], writes=[Bf("s_scr")])
                kb.op("V", "V", lambda v: v.reciprocal(s_scr[:, :], s_scr[:, :]),
                      reads=[Bf("s_scr")], writes=[Bf("s_scr")])
                kb.op("V", "V", lambda v, ch=ch: v.tensor_copy(
                    dAp[0:1, ch * 512:(ch + 1) * 512], s_scr[:, :]),
                    reads=[Bf("s_scr")], writes=[Bf("dAp")])
            for kt in range(2):
                for hh in range(2):
                    kb.op("G", "QG", lambda g, kt=kt, hh=hh, b=b: g.dma_start(
                        out=spd[:, :],
                        in_=arout[b][kt * 128:(kt + 1) * 128,
                                     hh * 1024:(hh + 1) * 1024]),
                        reads=[Bf(f"arout{b}")], writes=[Bf("spd")])
                    for c2 in range(2):
                        ch = hh * 2 + c2
                        kb.op("T", "T", lambda t, ch=ch: t.matmul(
                            psG[:, :], ones_r[:, :],
                            dAp[0:1, ch * 512:(ch + 1) * 512],
                            start=True, stop=True),
                            reads=[Bf("dAp"), Bf("ones_r")], writes=[Bf("psG")])
                        kb.op("V", "V", lambda v, kt=kt, ch=ch, c2=c2:
                              v.scalar_tensor_tensor(
                            hl2[:, kt * L + ch * 512:kt * L + (ch + 1) * 512],
                            spd[:, c2 * 512:(c2 + 1) * 512],
                            dtlnw_sb[:, kt:kt + 1], psG[:, :], ALU.mult, ALU.mult),
                            reads=[Bf("spd"), Bf("dtlnw_sb"), Bf("psG")],
                            writes=[Bf("hl2")])
            for dt in range(NDT):
                for tc in range(4):
                    for rt in range(2):
                        j = rt * NDT + dt
                        kb.op("T", "T", lambda t, j=j, rt=rt, tc=tc: t.matmul(
                            psF[:, :], dtp_sb[:, j * 128:(j + 1) * 128],
                            hl2[:, rt * L + tc * 512:rt * L + (tc + 1) * 512],
                            start=(rt == 0), stop=(rt == 1)),
                            reads=[Bf("dtp_sb"), Bf("hl2")], writes=[Bf("psF")])
                    kb.op("S", "S", lambda s, dt=dt: s.activation(
                        bct[:, 0:512], psF[:, :], AF.Sigmoid,
                        scale=-1.0, bias=ndtb[:, dt:dt + 1]),
                        reads=[Bf("psF"), Bf("ndtb")], writes=[Bf("bct")])
                    kb.op("S", "S", lambda s: s.activation(
                        gev[:, 0:512], bct[:, 0:512], AF.Ln),
                        reads=[Bf("bct")], writes=[GEVB[0]])
                    kb.op("Q", "QS", lambda q, b=b, dt=dt, tc=tc: q.dma_start(
                        out=dtlsp[b, dt, :, tc * 512:(tc + 1) * 512],
                        in_=gev[:, 0:512]),
                        reads=[GEVB[0]], writes=[Bf("dtlsp")])
            for n in range(NST):
                kb.op("V", "V", lambda v, n=n: v.memset(
                    dAp[:, n * TQS:n * TQS + 1], 0.0), writes=[Bf("dAp")])
            if b == 0:
                for n in range(NST):
                    kb.op("V", "V", lambda v, n=n: v.memset(
                        dApB[:, n * TQS:n * TQS + 1], 0.0),
                        writes=[Bf("dApB")])

        def bc_tq(b, tq):
            t0 = tq * TQ
            for (dst, dB, co) in ((bcB, Bf("bcB"), 0), (bcC, Bf("bcC"), L)):
                rows_v = bcrows[:, co + t0:co + t0 + TQ].unsqueeze(1) \
                    .broadcast_to([NST, NST, TQ])
                mask_v = dmask[:, :].unsqueeze(2).broadcast_to([NST, NST, TQ])
                diag_v = hl2[0:16, 0:NST * TQ].rearrange(
                    "p (n t) -> p n t", n=NST)
                kb.op("V", "V", lambda v, rows_v=rows_v, mask_v=mask_v,
                      diag_v=diag_v: v.tensor_tensor(
                    diag_v, rows_v, mask_v, ALU.mult),
                    reads=[Bf("bcrows"), Bf("dmask")], writes=[Bf("hl2")])
                for ch in range(4):
                    for c2 in range(2):
                        kb.op("T", "T", lambda t, ch=ch, c2=c2: t.matmul(
                            psH[:, c2 * 512:(c2 + 1) * 512], ones16[:, :],
                            hl2[0:16, ch * 1024 + c2 * 512:
                                ch * 1024 + (c2 + 1) * 512],
                            start=True, stop=True),
                            reads=[Bf("hl2"), Bf("ones16")], writes=[Bf("psH")])
                    kb.op("S", "S", lambda s, dst=dst, ch=ch: s.copy(
                        dst[:, ch * 1024:(ch + 1) * 1024], psH[:, :]),
                        reads=[Bf("psH")], writes=[dB])

        def scan_tq(b, tq):
            t0 = tq * TQ
            for dt in range(NDT):
                p = dt % 2
                po = p * TQ
                if p == 0:
                    for (dst, dB2, src, sB) in (
                            (dtl_t, (DTLB[0], DTLB[1]), dtlsp, Bf("dtlsp")),
                            (xc_t, (XCTB[0], XCTB[1]), xcsp, Bf("xcsp")),
                            (g_t, (GTB[0], GTB[1]), gsp, Bf("gsp"))):
                        kb.op("G", "QG", lambda g, dst=dst, src=src, b=b, dt=dt,
                              t0=t0: g.dma_start(
                            out=dst[:, :].rearrange("p (j t) -> p j t", j=2),
                            in_=src[b, dt:dt + 2, :, t0:t0 + TQ]
                                .transpose([1, 0, 2])),
                            reads=[sB], writes=[dB2[0], dB2[1]])
                kb.op("V", "V", lambda v, po=po: v.tensor_tensor(
                    u_t[:, :], dtl_t[:, po:po + TQ], xc_t[:, po:po + TQ], ALU.mult),
                    reads=[DTLB[p], XCTB[p]], writes=[Bf("u_t")])
                dApX, wlX, hlX = ((dAp, wl1, hl2), (dApB, wl1B, hl2))[dt % 2]
                dApN, wlN, hlN = (("dAp", "wl1", "hl2"),
                                  ("dApB", "wl1B", "hl2"))[dt % 2]
                hcN = f"hcar{dt}"
                kb.op("G", "QG", lambda q, dt=dt, wlX=wlX: q.dma_start(
                    out=wlX[:, :].rearrange("p (n t) -> p n t", n=NST)
                    [:, :, 0:1],
                    in_=hcar[:, dt * NST:(dt + 1) * NST].unsqueeze(2),
                    allow_slow_non_contiguous=True),
                    reads=[Bf(hcN)], writes=[Bf(wlN)])
                for n in range(NST):
                    kb.op("S", "S", lambda s, po=po, n=n, dt=dt, dApX=dApX:
                          s.activation(
                        dApX[:, n * TQS + 1:n * TQS + 1 + TQ],
                        dtl_t[:, po:po + TQ], AF.Exp,
                        scale=negA[:, dt * NST + n:dt * NST + n + 1]),
                        reads=[DTLB[p], Bf("negA")], writes=[Bf(dApN)])
                uview = u_t[:, :].unsqueeze(1).broadcast_to([128, NST, TQ])
                bview = bcB[:, :].rearrange("p (n t) -> p n t", n=NST)
                wview = wlX[:, :].rearrange("p (n t) -> p n t", n=NST)[:, :, 1:]
                kb.op("V", "V", lambda v, uview=uview, bview=bview, wview=wview:
                      v.tensor_tensor(wview, uview, bview, ALU.mult),
                      reads=[Bf("u_t"), Bf("bcB")], writes=[Bf(wlN)])
                kb.op("V", "V", lambda v, dApX=dApX, wlX=wlX, hlX=hlX:
                      v.tensor_tensor_scan(
                    hlX[:, :], dApX[:, :], wlX[:, :], 0.0, ALU.mult, ALU.add),
                    reads=[Bf(dApN), Bf(wlN)], writes=[Bf(hlN)])
                kb.op("G", "QG", lambda q, dt=dt, hlX=hlX: q.dma_start(
                    out=hcar[:, dt * NST:(dt + 1) * NST].unsqueeze(2),
                    in_=hlX[:, :].rearrange("p (n t) -> p n t", n=NST)
                    [:, :, TQ:TQ + 1],
                    allow_slow_non_contiguous=True),
                    reads=[Bf(hlN)], writes=[Bf(hcN)])
                cview = bcC[:, :].rearrange("p (n t) -> p n t", n=NST)
                hview = hlX[:, :].rearrange("p (n t) -> p n t", n=NST)[:, :, 1:]
                pview = dApX[:, :].rearrange("p (n t) -> p n t", n=NST)[:, :, 1:]
                kb.op("V", "V", lambda v, pview=pview, hview=hview, cview=cview:
                      v.tensor_tensor(pview, hview, cview, ALU.mult),
                      reads=[Bf(hlN), Bf("bcC")], writes=[Bf(dApN)])
                pv = dApX[:, :].rearrange("p (n t) -> p n t", n=NST)
                lv = wlX[:, :].rearrange("p (n t) -> p n t", n=NST)
                kb.op("V", "V", lambda v, pv=pv, lv=lv: v.tensor_tensor(
                    lv[:, 0:8, 1:], pv[:, 0:16:2, 1:], pv[:, 1:16:2, 1:],
                    ALU.add),
                    reads=[Bf(dApN)], writes=[Bf(wlN)])
                kb.op("V", "V", lambda v, lv=lv: v.tensor_tensor(
                    lv[:, 8:12, 1:], lv[:, 0:8:2, 1:], lv[:, 1:8:2, 1:],
                    ALU.add), writes=[Bf(wlN)])
                kb.op("V", "V", lambda v, lv=lv: v.tensor_tensor(
                    lv[:, 12:14, 1:], lv[:, 8:12:2, 1:], lv[:, 9:12:2, 1:],
                    ALU.add), writes=[Bf(wlN)])
                kb.op("V", "V", lambda v, wlX=wlX: v.tensor_tensor(
                    ys_t[:, :], wlX[:, 12 * TQS + 1:12 * TQS + 1 + TQ],
                    wlX[:, 13 * TQS + 1:13 * TQS + 1 + TQ], ALU.add),
                    reads=[Bf(wlN)], writes=[Bf("ys_t")])
                kb.op("V", "V", lambda v, po=po, dt=dt: v.scalar_tensor_tensor(
                    t2_t[:, :], xc_t[:, po:po + TQ], d_sb[:, dt:dt + 1], ys_t[:, :],
                    ALU.mult, ALU.add),
                    reads=[XCTB[p], Bf("d_sb"), Bf("ys_t")], writes=[Bf("t2_t")])
                kb.op("V", "V", lambda v, po=po: v.tensor_tensor(
                    ygd_t[:, po:po + TQ], t2_t[:, :], g_t[:, po:po + TQ], ALU.mult),
                    reads=[Bf("t2_t"), GTB[p]], writes=[YGDB[p]])
                kb.op("Q", "QS", lambda q, po=po, b=b, dt=dt, t0=t0: q.dma_start(
                    out=yin[b][dt * 128:(dt + 1) * 128, t0:t0 + TQ],
                    in_=ygd_t[:, po:po + TQ]),
                    reads=[YGDB[p]], writes=[Bf(f"yin{b}")])

        def y_rs(b):
            fl = kb.op("Q", "Q", lambda q: q.nop(), reads=[Bf(f"pmix{b}")])
            kb.op("G", f"CCR{b}", lambda g, b=b: g.collective_compute(
                "ReduceScatter", ALU.add, replica_groups=[list(range(NCK))],
                ins=[pmix[b]], outs=[rsout[b]]),
                reads=[Bf(f"pmix{b}")], writes=[Bf(f"rsout{b}")], extra=[fl])

        def ln_stats_T(mslot, mbuf, sqslot, sqbuf):
            for it in range(NKT):
                mo = it * TSL
                kb.op("S", "S", lambda s, mo=mo, mslot=mslot, sqslot=sqslot:
                      s.activation(sqslot[:, mo:mo + TSL], mslot[:, mo:mo + TSL],
                                   AF.Square),
                      reads=[mbuf], writes=[sqbuf])
            for it in range(NKT):
                mo = it * TSL
                kb.op("T", "T", lambda t, mo=mo, mslot=mslot, it=it: t.matmul(
                    psH[0:1, 0:TSL], ones_c[:, :], mslot[:, mo:mo + TSL],
                    start=(it == 0), stop=(it == NKT - 1)),
                    reads=[mbuf, Bf("ones_c")], writes=[Bf("psH")])
            kb.op("S", "S", lambda s: s.copy(c_sum[0:1, 0:TSL], psH[0:1, 0:TSL]),
                  reads=[Bf("psH")], writes=[Bf("c_sum")])
            for it in range(NKT):
                mo = it * TSL
                kb.op("T", "T", lambda t, mo=mo, sqslot=sqslot, it=it: t.matmul(
                    psH[0:1, TSL:2 * TSL], ones_c[:, :], sqslot[:, mo:mo + TSL],
                    start=(it == 0), stop=(it == NKT - 1)),
                    reads=[sqbuf, Bf("ones_c")], writes=[Bf("psH")])
            kb.op("S", "S", lambda s: s.copy(c_ssq[0:1, 0:TSL],
                                             psH[0:1, TSL:2 * TSL]),
                  reads=[Bf("psH")], writes=[Bf("c_ssq")])
            ln_rowmath(TSL)

        def stat_bcast(which):
            pairs = (((rb_rstd, 0), (rb_mrstd, 1)) if which != 1
                     else ((rb_rstd, 2),))
            for rb, pos in pairs:
                rbB = Bf("rb_rstd") if rb is rb_rstd else Bf("rb_mrstd")
                on = ones_r[:, :] if rb is rb_rstd else ones32
                onB = Bf("ones_r") if rb is rb_rstd else Bf("ones32")
                kb.op("T", "T", lambda t, rb=rb, on=on: t.matmul(
                    psG[:, 0:TSL], on, rb[0:1, 0:TSL],
                    start=True, stop=True),
                    reads=[rbB, onB], writes=[Bf("psG")])
                kb.op("S", "S", lambda s, pos=pos: s.copy(
                    bct[:, pos * TSL:(pos + 1) * TSL], psG[:, 0:TSL]),
                    reads=[Bf("psG")], writes=[Bf("bct")])

        def epi_chunk(b, part):
            if part in (0, 1):
                for sub in range(2):
                    tch = part * 2 + sub
                    t0 = tch * 512
                    rp = tch % 2
                    ro = rp * 4096
                    for dt in range(NDT):
                        kb.op("Q", "QS", lambda q, dt=dt, b=b, t0=t0, ro=ro:
                              q.dma_start(
                            out=SLOT[0][:, ro + dt * 512:ro + (dt + 1) * 512],
                            in_=yin[b][dt * 128:(dt + 1) * 128, t0:t0 + 512]),
                            reads=[Bf(f"yin{b}")], writes=[SB4[rp] if False
                                                           else SB[0]])
                    for htg in range(8):
                        wp = (tch * 8 + htg) % 2
                        wof = wp * 4096
                        kb.op("S", "QA", lambda q, htg=htg, wof=wof: q.dma_start(
                            out=SLOT[4][:, wof:wof + 4096], in_=wos[htg]),
                            reads=[Bf("wos")], writes=[SB4[wp]])
                        for hh in range(4):
                            pst, pB = ((psA, Bf("psA")), (psB, Bf("psB")),
                                       (psF, Bf("psF")), (psI, Bf("psI")))[hh]
                            for dt in range(NDT):
                                kb.op("T", "T", lambda t, pst=pst, wof=wof, dt=dt,
                                      hh=hh, ro=ro: t.matmul(
                                    pst[:, :],
                                    SLOT[4][:, wof + dt * 512 + hh * 128:
                                            wof + dt * 512 + (hh + 1) * 128],
                                    SLOT[0][:, ro + dt * 512:ro + (dt + 1) * 512],
                                    start=(dt == 0), stop=(dt == NDT - 1)),
                                    reads=[SB[0], SB4[wp]], writes=[pB])
                            ht = htg * 4 + hh
                            gp = hh % 2
                            go = gp * 512
                            kb.op("S", "S", lambda s, pst=pst, go=go: s.copy(
                                gev[:, go:go + 512], pst[:, :]),
                                reads=[pB], writes=[GEVB[gp]])
                            kb.op("S", "QA", lambda q, b=b, tch=tch, ht=ht,
                                  go=go: q.dma_start(
                                out=pmix[b][tch * 2:tch * 2 + 2,
                                            ht * 128:(ht + 1) * 128, :]
                                    .transpose([1, 0, 2]),
                                in_=gev[:, go:go + 512]
                                    .rearrange("p (j t) -> p j t", j=2)),
                                reads=[GEVB[gp]], writes=[Bf(f"pmix{b}")])
            if part == 2:
                for ht in range(NKT):
                    mo = ht * TSL
                    kb.op("Q", "QS", lambda q, ht=ht, mo=mo, b=b: q.dma_start(
                        out=SLOT[2][:, mo:mo + TSL],
                        in_=rsout[b][ht * 128:(ht + 1) * 128, :]),
                        reads=[Bf(f"rsout{b}")], writes=[SB[2]])
            if part == 2:
                ln_stats_T(SLOT[2], SB[2], SLOT[1], SB[1])
                stat_bcast(0)
                for ht in range(NKT):
                    p = ht % 2
                    po = p * TSL
                    mo = ht * TSL
                    kb.op("Q", "QS", lambda q, ht=ht, b=b: q.dma_start(
                        out=x0t[:, :],
                        in_=hs_resT[b, ht * 128:(ht + 1) * 128, :]),
                        writes=[X0B[0]])
                    kb.op("V", "V", lambda v, po=po, mo=mo: v.tensor_tensor(
                        cacc[:, po:po + TSL], SLOT[2][:, mo:mo + TSL],
                        bct[:, 0:TSL], ALU.mult),
                        reads=[SB[2], Bf("bct")], writes=[CAB[p]])
                    kb.op("V", "V", lambda v, po=po: v.scalar_tensor_tensor(
                        cacc[:, po:po + TSL], cacc[:, po:po + TSL], 1.0,
                        bct[:, TSL:2 * TSL], ALU.mult, ALU.subtract),
                        reads=[Bf("bct")], writes=[CAB[p]])
                    kb.op("V", "V", lambda v, po=po: v.tensor_tensor(
                        x1t[:, po:po + TSL], cacc[:, po:po + TSL],
                        x0t[:, :], ALU.add),
                        reads=[CAB[p], X0B[0]], writes=[X1B[p]])
                    kb.op("Q", "QS", lambda q, po=po, ht=ht, b=b: q.dma_start(
                        out=x1sp[b, ht * 128:(ht + 1) * 128, :],
                        in_=x1t[:, po:po + TSL]),
                        reads=[X1B[p]], writes=[Bf("x1sp")])
                for it in range(NKT):
                    o = it * TSL
                    kb.op("G", "QG", lambda g, it=it, o=o, b=b: g.dma_start(
                        out=SLOT[0][:, o:o + TSL],
                        in_=x1sp[b, it * 128:(it + 1) * 128, :]),
                        reads=[Bf("x1sp")], writes=[SB[0]])
                ln_stats_T(SLOT[0], SB[0], SLOT[1], SB[1])
                stat_bcast(1)
                kb.op("V", "V", lambda v: v.tensor_scalar_mul(
                    rb_tmp[0:1, 0:TSL], c_m[0:1, 0:TSL], -1.0),
                    reads=[Bf("c_m")], writes=[Bf("rb_tmp")])
                kb.op("V", "V", lambda v: v.tensor_tensor(
                    aug2r[0:1, :], rb_tmp[0:1, 0:TSL], c_ssq[0:1, 0:TSL], ALU.mult),
                    reads=[Bf("rb_tmp"), Bf("c_ssq")], writes=[Bf("aug2r")])
                kb.op("G", "QG", lambda g: g.dma_start(
                    out=aug2r[1:2, :], in_=c_ssq[0:1, 0:TSL]),
                    reads=[Bf("c_ssq")], writes=[Bf("aug2r")])
                gates_lora(b)
            if part == 3:
                ffn_and_out(b)

        def gates_lora(b):
            for it in range(NKT):
                o = it * TSL
                kb.op("T", "T", lambda t, it=it, o=o: t.matmul(
                    psH[0:36, 0:TSL], awr_sb[:, it * 36:(it + 1) * 36],
                    SLOT[0][:, o:o + TSL], start=(it == 0), stop=False),
                    reads=[SB[0], Bf("awr_sb")], writes=[Bf("psH")])
            kb.op("T", "T", lambda t: t.matmul(
                psH[0:36, 0:TSL], colsA[:, :], aug2r[0:1, :],
                start=False, stop=True),
                reads=[Bf("colsA"), Bf("aug2r")], writes=[Bf("psH")])
            kb.op("V", "V", lambda v: v.tensor_copy(scr36[:, :], psH[0:36, 0:TSL]),
                  reads=[Bf("psH")], writes=[Bf("scr36")])
            kb.op("T", "T", lambda t: t.matmul(
                psG[0:4, 0:TSL], sel2[:, :], scr36[:, :], start=True, stop=True),
                reads=[Bf("sel2"), Bf("scr36")], writes=[Bf("psG")])
            kb.op("V", "V", lambda v: v.tensor_copy(gl4[:, 0:TSL], psG[0:4, 0:TSL]),
                  reads=[Bf("psG")], writes=[Bf("gl4")])
            kb.op("T", "T", lambda t: t.matmul(
                psG[0:4, 0:TSL], ones_r[:, 0:4], rb_rstd[0:1, 0:TSL],
                start=True, stop=True),
                reads=[Bf("rb_rstd"), Bf("ones_r")], writes=[Bf("psG")])
            kb.op("V", "V", lambda v: v.tensor_copy(bct4[:, :], psG[0:4, 0:TSL]),
                  reads=[Bf("psG")], writes=[Bf("bct4")])
            kb.op("V", "V", lambda v: v.tensor_tensor(
                gl4[:, 0:TSL], gl4[:, 0:TSL], bct4[:, :], ALU.mult),
                reads=[Bf("bct4")], writes=[Bf("gl4")])
            kb.op("S", "S", lambda s: s.activation(
                gb4[:, 0:TSL], gl4[:, 0:TSL], AF.Exp),
                reads=[Bf("gl4")], writes=[Bf("gb4")])
            kb.op("T", "T", lambda t: t.matmul(
                psG[0:1, 0:TSL], ones_c[0:4, :], gb4[:, 0:TSL],
                start=True, stop=True),
                reads=[Bf("gb4"), Bf("ones_c")], writes=[Bf("psG")])
            kb.op("V", "V", lambda v: v.tensor_copy(
                gl4[0:1, TSL:2 * TSL], psG[0:1, 0:TSL]),
                reads=[Bf("psG")], writes=[Bf("gl4")])
            kb.op("V", "V", lambda v: v.reciprocal(
                gl4[0:1, TSL:2 * TSL], gl4[0:1, TSL:2 * TSL]),
                reads=[Bf("gl4")], writes=[Bf("gl4")])
            kb.op("V", "V", lambda v: v.tensor_copy(
                rb_tmp[0:1, 0:TSL], gl4[0:1, TSL:2 * TSL]),
                reads=[Bf("gl4")], writes=[Bf("rb_tmp")])
            kb.op("T", "T", lambda t: t.matmul(
                psG[0:4, 0:TSL], ones_r[:, 0:4], rb_tmp[0:1, 0:TSL],
                start=True, stop=True),
                reads=[Bf("rb_tmp"), Bf("ones_r")], writes=[Bf("psG")])
            kb.op("V", "V", lambda v: v.tensor_copy(bct4[:, :], psG[0:4, 0:TSL]),
                  reads=[Bf("psG")], writes=[Bf("bct4")])
            kb.op("V", "V", lambda v: v.tensor_tensor(
                gb4[:, TSL:2 * TSL], gb4[:, 0:TSL], bct4[:, :], ALU.mult),
                reads=[Bf("bct4")], writes=[Bf("gb4")])
            kb.op("T", "T", lambda t: t.matmul(
                psH[0:32, 0:TSL], sel[:, :], gb4[:, TSL:2 * TSL],
                start=True, stop=True),
                reads=[Bf("sel"), Bf("gb4")], writes=[Bf("psH")])
            kb.op("V", "V", lambda v: v.tensor_copy(g32[:, :], psH[0:32, 0:TSL]),
                  reads=[Bf("psH")], writes=[Bf("g32")])
            kb.op("V", "V", lambda v: v.tensor_tensor(
                prs[:, :], scr36[0:32, :], g32[:, :], ALU.mult),
                reads=[Bf("scr36"), Bf("g32")], writes=[Bf("prs")])

        def ffn_and_out(b):
            for ot in range(NKT):
                pw = ot % 2
                ws, wof = SLOT[4], pw * 4096
                wb = SB4[pw]
                kb.op("S", "QA", lambda q, ot=ot, ws=ws, wof=wof: q.dma_start(
                    out=ws[:, wof:wof + 4096], in_=wf2[ot]),
                    reads=[Bf("wf2")], writes=[wb])
                kb.op("G", "QG", lambda g, ot=ot, pw=pw: g.dma_start(
                    out=bmt_t[:, pw * 128:(pw + 1) * 128],
                    in_=fBmT[:, ot * 128:(ot + 1) * 128]),
                    writes=[Bf("bmt_t")])
                for it in range(NKT):
                    o = it * TSL
                    kb.op("T", "T", lambda t, ws=ws, wof=wof, it=it, o=o: t.matmul(
                        psA[:, 0:TSL], ws[:, wof + it * 128:wof + (it + 1) * 128],
                        SLOT[0][:, o:o + TSL], start=(it == 0), stop=False),
                        reads=[SB[0], wb], writes=[Bf("psA")])
                kb.op("T", "T", lambda t, ot=ot: t.matmul(
                    psA[:, 0:TSL], aug2[:, ot * 128:(ot + 1) * 128],
                    aug2r[:, :], start=False, stop=False),
                    reads=[Bf("aug2"), Bf("aug2r")], writes=[Bf("psA")])
                kb.op("T", "T", lambda t, ot=ot, pw=pw: t.matmul(
                    psA[:, 0:TSL], bmt_t[:, pw * 128:(pw + 1) * 128], prs[:, :],
                    start=False, stop=True),
                    reads=[Bf("bmt_t"), Bf("prs")], writes=[Bf("psA")])
                mo = ot * TSL
                kb.op("V", "V", lambda v, mo=mo: v.tensor_tensor(
                    SLOT[2][:, mo:mo + TSL], psA[:, 0:TSL],
                    bct[:, 2 * TSL:3 * TSL], ALU.mult),
                    reads=[Bf("psA"), Bf("bct")], writes=[SB[2]])
            ln_stats_T(SLOT[2], SB[2], SLOT[1], SB[1])
            stat_bcast(2)
            for ht in range(NKT):
                p = ht % 2
                po = p * TSL
                mo = ht * TSL
                kb.op("G", "QG", lambda g, po=po, ht=ht, b=b: g.dma_start(
                    out=x1t[:, po:po + TSL],
                    in_=x1sp[b, ht * 128:(ht + 1) * 128, :]),
                    reads=[Bf("x1sp")], writes=[X1B[p]])
                kb.op("V", "V", lambda v, po=po, mo=mo: v.tensor_tensor(
                    cacc[:, po:po + TSL], SLOT[2][:, mo:mo + TSL],
                    bct[:, 0:TSL], ALU.mult),
                    reads=[SB[2], Bf("bct")], writes=[CAB[p]])
                kb.op("V", "V", lambda v, po=po: v.scalar_tensor_tensor(
                    cacc[:, po:po + TSL], cacc[:, po:po + TSL], 1.0,
                    bct[:, TSL:2 * TSL], ALU.mult, ALU.subtract),
                    reads=[Bf("bct")], writes=[CAB[p]])
                kb.op("V", "V", lambda v, po=po: v.tensor_tensor(
                    cacc[:, po:po + TSL], cacc[:, po:po + TSL],
                    x1t[:, po:po + TSL], ALU.add),
                    reads=[X1B[p]], writes=[CAB[p]])
                kb.op("Q", "QS", lambda q, po=po, ht=ht, b=b: q.dma_start(
                    out=out_ext[b, ht * 128:(ht + 1) * 128, :],
                    in_=cacc[:, po:po + TSL]),
                    reads=[CAB[p]], writes=[Bf("out")])

        for part in range(4):
            p1_chunk(0, part)
        p1_ar(0)
        for b in range(B):
            rows_dt(b)
            bc_tq(b, 0)
            scan_tq(b, 0)
            bc_tq(b, 1)
            scan_tq(b, 1)
            if b >= 1:
                epi_chunk(b - 1, 0)
            bc_tq(b, 2)
            scan_tq(b, 2)
            if b >= 1:
                epi_chunk(b - 1, 1)
                y_rs(b - 1)
            bc_tq(b, 3)
            scan_tq(b, 3)
            if b >= 1:
                epi_chunk(b - 1, 2)
                epi_chunk(b - 1, 3)
            bc_tq(b, 4)
            scan_tq(b, 4)
            bc_tq(b, 5)
            scan_tq(b, 5)
            if b + 1 < B:
                p1_chunk(b + 1, 0)
            bc_tq(b, 6)
            scan_tq(b, 6)
            if b + 1 < B:
                p1_chunk(b + 1, 1)
            bc_tq(b, 7)
            scan_tq(b, 7)
            if b + 1 < B:
                p1_chunk(b + 1, 2)
                p1_chunk(b + 1, 3)
                p1_ar(b + 1)
        epi_chunk(B - 1, 0)
        epi_chunk(B - 1, 1)
        y_rs(B - 1)
        epi_chunk(B - 1, 2)
        epi_chunk(B - 1, 3)

        kb.op("G", "G", lambda g: g.nop(),
              extra=[x for x in (bufs["out"].w,) if x is not None])

        kb.finalize()

        with contextlib.ExitStack() as st2:
            sems = {}
            ccs = ["CCL"] + [f"CCA{b}" for b in range(B)] + \
                  [f"CCR{b}" for b in range(B)]
            pools = [f"{q}{i}" for q, n in KB.POOLS.items() for i in range(n)]
            for s in ["T", "V", "S", "G", "Q", "QS", "QA", "QG"] + ccs + pools:
                sems[s] = st2.enter_context(nc.semaphore(f"sem_{s}"))
            with nc.Block() as block:
                @block.tensor
                def _(e):
                    kb.emit("T", e, sems)

                @block.vector
                def _(e):
                    kb.emit("V", e, sems)

                @block.scalar
                def _(e):
                    kb.emit("S", e, sems)

                @block.gpsimd
                def _(e):
                    kb.emit("G", e, sems)

                @block.sync
                def _(e):
                    kb.emit("Q", e, sems)
    return nc


_NC_CACHE = None


def kernel(**inputs):
    global _NC_CACHE
    f32 = np.float32

    hs = np.asarray(inputs["L0_attn_hidden_states"], f32)
    W1 = np.asarray(inputs["L0_attn_in_proj_w"], f32)
    convw = np.asarray(inputs["L0_attn_conv_w"], f32)
    convb = np.asarray(inputs["L0_attn_conv_b"], f32)
    xpw = np.asarray(inputs["L0_attn_x_proj_w"], f32)
    dtp = np.asarray(inputs["L0_attn_dt_proj_w"], f32)
    dtbias = np.asarray(inputs["L0_attn_dt_bias"], f32)
    alog = np.asarray(inputs["L0_attn_A_log"], f32)
    dparam = np.asarray(inputs["L0_attn_D_param"], f32)
    wo = np.asarray(inputs["L0_attn_out_proj_w"], f32)
    dtlnw = np.asarray(inputs["L0_attn_dt_ln_w"], f32)
    blnw = np.asarray(inputs["L0_attn_B_ln_w"], f32)
    clnw = np.asarray(inputs["L0_attn_C_ln_w"], f32)
    wf = np.asarray(inputs["L0_ffn_W"], f32)
    fbv = np.asarray(inputs["L0_ffn_b"], f32)
    fA = np.asarray(inputs["L0_ffn_A"], f32)
    fBm = np.asarray(inputs["L0_ffn_Bm"], f32)
    fWr = np.asarray(inputs["L0_ffn_Wr"], f32)

    hsT = np.ascontiguousarray(hs.reshape(B * L, H).T)
    woT_full = np.ascontiguousarray(wo.T)
    wfT_full = np.ascontiguousarray(wf.T)
    wf_sh = np.ascontiguousarray(
        wfT_full.reshape(32, 128, 32, 128).transpose(2, 1, 0, 3)
        .reshape(32, 128, 4096))
    dtpT_full = np.ascontiguousarray(dtp.T)
    xpwT_full = np.ascontiguousarray(xpw.T)
    fAWr = np.concatenate([fA.reshape(32, H), fWr], axis=0)
    fAWrT = np.ascontiguousarray(fAWr.T)
    fBmT = np.ascontiguousarray(fBm.transpose(0, 2, 1).reshape(32, H))

    in_maps = []
    for k in range(NCK):
        dsl = slice(k * DSH, (k + 1) * DSH)
        res = np.stack([
            np.ascontiguousarray(hsT[:, b * L + k * TSL:b * L + (k + 1) * TSL])
            for b in range(B)])
        w1k = np.concatenate(
            [W1[:, dsl], W1[:, DIN + k * DSH:DIN + (k + 1) * DSH]], axis=1)
        w1_sh = np.ascontiguousarray(
            w1k.reshape(32, 128, 16, 128).transpose(2, 1, 0, 3)
            .reshape(16, 128, 4096))
        wo_sh = np.ascontiguousarray(
            woT_full[dsl].reshape(8, 128, 8, 512).transpose(2, 1, 0, 3)
            .reshape(8, 128, 4096))
        in_maps.append({
            "hs_prepT": np.ascontiguousarray(hsT[:, k * 1024:(k + 1) * 1024]),
            "hs_resT": res,
            "w1": w1_sh,
            "woTs": wo_sh,
            "wfT": wf_sh,
            "convw": np.ascontiguousarray(convw[dsl]),
            "convb": np.ascontiguousarray(convb[dsl]),
            "dtbias": np.ascontiguousarray(dtbias[dsl]),
            "dparam": np.ascontiguousarray(dparam[dsl]),
            "alog": np.ascontiguousarray(alog[dsl]),
            "xpwT": np.ascontiguousarray(xpwT_full[dsl]),
            "dtpT": np.ascontiguousarray(dtpT_full[:, dsl]),
            "dtlnw": dtlnw, "blnw": blnw, "clnw": clnw,
            "fb": fbv, "fAWrT": fAWrT, "fBmT": fBmT,
        })

    if _NC_CACHE is None:
        _NC_CACHE = build_program()
    res = run_bass_kernel_spmd(_NC_CACHE, in_maps, core_ids=list(range(NCK)))
    full = np.empty((B, L, H), np.float32)
    for k in range(NCK):
        full[:, k * TSL:(k + 1) * TSL, :] = res.results[k]["out"].transpose(0, 2, 1)
    return full
